# version 04
# baseline (speedup 1.0000x reference)
import math
import contextlib
import numpy as np
import ml_dtypes
import concourse.bass as bass
import concourse.mybir as mybir
from concourse.bass_utils import run_bass_kernel_spmd

F32 = mybir.dt.float32
BF16 = mybir.dt.bfloat16
AF = mybir.ActivationFunctionType
ALU = mybir.AluOpType
AX = mybir.AxisListType

NCORES = 8
BPC = 2
D = 1024
NCH = D // 128
T_FULL = 2048
NEG = -1e30

A_NH, A_DH, A_CHUNK = 8, 128, 128
A_SCALE = 1.0 / math.sqrt(A_DH)
LN_A_SCALE = math.log(A_SCALE)

MOE_H, MOE_ES, MOE_ER, MOE_K = 4096, 2, 8, 2

C_NH, C_HD = 16, 64
C_SCALE = (1.0 / math.sqrt(C_HD)) / 0.1

F_HEADS, F_DH, F_WINDOW, F_HID = 8, 64, 512, 2730
F_QK_SCALE = 8.0

WEIGHT_NAMES = [
    "L0_attn_norm_w", "L0_attn_Wq", "L0_attn_Wk", "L0_attn_Wv",
    "L0_attn_Wbeta", "L0_attn_bbeta", "L0_attn_Wig", "L0_attn_big",
    "L0_attn_Wog", "L0_attn_bog", "L0_attn_Wout", "L0_attn_bout",
    "L0_ffn_g_shared", "L0_ffn_W1_s", "L0_ffn_b1_s", "L0_ffn_W2_s", "L0_ffn_b2_s",
    "L0_ffn_g_routed", "L0_ffn_W1_r", "L0_ffn_b1_r", "L0_ffn_W2_r", "L0_ffn_b2_r",
    "L0_ffn_centroids",
    "L1_attn_ln_g", "L1_attn_ln_b", "L1_attn_w_qkv", "L1_attn_b_qkv",
    "L1_attn_w_ent", "L1_attn_b_ent", "L1_attn_w_out", "L1_attn_b_out",
    "L1_ffn_wqkv", "L1_ffn_q_scale", "L1_ffn_k_scale", "L1_ffn_wo",
    "L1_ffn_ln_g", "L1_ffn_ln_b", "L1_ffn_w1", "L1_ffn_b1", "L1_ffn_w2", "L1_ffn_b2",
]

INPUT_SHAPES = {
    "L0_attn_norm_w": (1024,), "L0_attn_Wq": (1024, 1024), "L0_attn_Wk": (1024, 1024),
    "L0_attn_Wv": (1024, 1024), "L0_attn_Wbeta": (8, 1024), "L0_attn_bbeta": (8,),
    "L0_attn_Wig": (8, 1024), "L0_attn_big": (8,), "L0_attn_Wog": (8, 1024),
    "L0_attn_bog": (8,), "L0_attn_Wout": (1024, 1024), "L0_attn_bout": (1024,),
    "L0_ffn_g_shared": (1024,), "L0_ffn_W1_s": (2, 1024, 4096), "L0_ffn_b1_s": (2, 4096),
    "L0_ffn_W2_s": (2, 4096, 1024), "L0_ffn_b2_s": (2, 1024),
    "L0_ffn_g_routed": (1024,), "L0_ffn_W1_r": (8, 1024, 4096), "L0_ffn_b1_r": (8, 4096),
    "L0_ffn_W2_r": (8, 4096, 1024), "L0_ffn_b2_r": (8, 1024),
    "L0_ffn_centroids": (1024, 8),
    "L1_attn_ln_g": (1024,), "L1_attn_ln_b": (1024,), "L1_attn_w_qkv": (3072, 1024),
    "L1_attn_b_qkv": (3072,), "L1_attn_w_ent": (1, 1024), "L1_attn_b_ent": (1,),
    "L1_attn_w_out": (1024, 1024), "L1_attn_b_out": (1024,),
    "L1_ffn_wqkv": (2, 1024, 1536), "L1_ffn_q_scale": (2, 64), "L1_ffn_k_scale": (2, 64),
    "L1_ffn_wo": (2, 512, 1024), "L1_ffn_ln_g": (2, 1024), "L1_ffn_ln_b": (2, 1024),
    "L1_ffn_w1": (2, 1024, 5460), "L1_ffn_b1": (2, 5460), "L1_ffn_w2": (2, 2730, 1024),
    "L1_ffn_b2": (2, 1024),
}


class Ctr:
    def __init__(self, sem, step=1):
        self.sem, self.v, self.step = sem, 0, step

    def inc(self, n=1):
        self.v += n * self.step
        return self.v


class Ctx:
    pass


class SelfWaitBlock:

    def __init__(self, blk, ctx):
        self._blk = blk
        self._ctx = ctx

    def _wrap(self, dec, ctrs):
        ctx = self._ctx
        if getattr(ctx, "paranoid", False):
            ctrs = (ctx.ld, ctx.lw, ctx.st, ctx.pe, ctx.dve, ctx.act, ctx.pool)

        def decorate(fn):
            vals = [(c.sem, c.v) for c in ctrs if c.v > 0]

            def wrapped(e):
                for sem, v in vals:
                    e.wait_ge(sem, v)
                fn(e)
            dec(wrapped)
        return decorate

    @property
    def tensor(self):
        return self._wrap(self._blk.tensor, (self._ctx.pe,))

    @property
    def vector(self):
        return self._wrap(self._blk.vector, (self._ctx.dve,))

    @property
    def scalar(self):
        return self._wrap(self._blk.scalar, (self._ctx.act, self._ctx.st))

    @property
    def sync(self):
        return self._wrap(self._blk.sync, (self._ctx.ld,))

    @property
    def gpsimd(self):
        return self._wrap(self._blk.gpsimd, (self._ctx.pool, self._ctx.lw))


def make_host_consts(T):
    c = {}
    j = np.arange(64, dtype=np.float64)
    inv_freq = 1.0 / (10000.0 ** (j / 64.0))
    t = np.arange(T, dtype=np.float64)
    ang = inv_freq[:, None] * t[None, :]
    c["rope0_cos"] = np.vstack([np.cos(ang), np.cos(ang)]).astype(ml_dtypes.bfloat16)
    c["rope0_sin"] = np.vstack([np.sin(ang), np.sin(ang)]).astype(ml_dtypes.bfloat16)

    half_idx = np.arange(0, F_DH, 2, dtype=np.float64)
    inv_freq2 = 1.0 / (10000.0 ** (half_idx / F_DH))
    ang2 = t[None, :] * inv_freq2[:, None]
    base = (half_idx + 0.4 * F_DH) / (1.4 * F_DH)
    power = (t - T // 2) / float(F_WINDOW // 2)
    scale = base[:, None] ** power[None, :]
    cos2, sin2 = np.cos(ang2), np.sin(ang2)
    def dup(a):
        return np.vstack([a, a]).astype(ml_dtypes.bfloat16)
    c["xpos_qcos"] = dup(cos2 * scale * F_QK_SCALE)
    c["xpos_qsin"] = dup(sin2 * scale * F_QK_SCALE)
    c["xpos_kcos"] = dup(cos2 / scale)
    c["xpos_ksin"] = dup(sin2 / scale)
    return c


def declare_ios(nc, T):
    io = {}
    io["x"] = nc.dram_tensor("x", [BPC, D, T], F32, kind="ExternalInput").ap()
    io["out"] = nc.dram_tensor("out", [BPC, D, T], F32, kind="ExternalOutput").ap()
    for name in WEIGHT_NAMES:
        io[name] = nc.dram_tensor(name, list(INPUT_SHAPES[name]), F32,
                                  kind="ExternalInput").ap()
    for name in ["rope0_cos", "rope0_sin"]:
        io[name] = nc.dram_tensor(name, [128, T], BF16, kind="ExternalInput").ap()
    for name in ["xpos_qcos", "xpos_qsin", "xpos_kcos", "xpos_ksin"]:
        io[name] = nc.dram_tensor(name, [64, T], BF16, kind="ExternalInput").ap()
    io["x_cur"] = nc.dram_tensor("x_cur", [BPC, D, T], F32, kind="Internal").ap()
    io["dbg_gates"] = nc.dram_tensor("dbg_gates", [96, T], F32, kind="ExternalOutput").ap()
    io["dbg_lcum"] = nc.dram_tensor("dbg_lcum", [8, T], F32, kind="ExternalOutput").ap()
    io["dbg_ltm"] = nc.dram_tensor("dbg_ltm", [128, (T // 128) * 24], F32, kind="ExternalOutput").ap()
    io["dbg_yall"] = nc.dram_tensor("dbg_yall", [128, NCH * T], BF16, kind="ExternalOutput").ap()
    io["dbg_attn"] = nc.dram_tensor("dbg_attn", [128, NCH * T], BF16, kind="ExternalOutput").ap()
    io["dbg_qh"] = nc.dram_tensor("dbg_qh", [128, T], BF16, kind="ExternalOutput").ap()
    io["dbg_u"] = nc.dram_tensor("dbg_u", [BPC, D, T], F32, kind="ExternalOutput").ap()
    io["dbg_ubf"] = nc.dram_tensor("dbg_ubf", [D, T], BF16, kind="ExternalOutput").ap()
    io["dbg_gv"] = nc.dram_tensor("dbg_gv", [128, (T // 128) * 8], F32, kind="ExternalOutput").ap()
    io["dbg_sc"] = nc.dram_tensor("dbg_sc", [128, (T // 128) * 8], F32, kind="ExternalOutput").ap()
    io["dbg_gem"] = nc.dram_tensor("dbg_gem", [8, T], F32, kind="ExternalOutput").ap()
    io["dbg_acc"] = nc.dram_tensor("dbg_acc", [BPC, D, T], F32, kind="ExternalOutput").ap()
    io["dbg_h"] = nc.dram_tensor("dbg_h", [128, NCH * T], BF16, kind="ExternalOutput").ap()
    io["w1bf"] = nc.dram_tensor("w1bf", [10, 1024, 4096], BF16, kind="Internal").ap()
    io["fqkv"] = nc.dram_tensor("fqkv", [2, 1024, 1536], BF16, kind="Internal").ap()
    io["fwo"] = nc.dram_tensor("fwo", [2, 512, 1024], BF16, kind="Internal").ap()
    io["fw1"] = nc.dram_tensor("fw1", [2, 1024, 5460], BF16, kind="Internal").ap()
    io["fw2"] = nc.dram_tensor("fw2", [2, 2730, 1024], BF16, kind="Internal").ap()
    io["w2bf"] = nc.dram_tensor("w2bf", [10, 4096, 1024], BF16, kind="Internal").ap()
    io["wqT"] = nc.dram_tensor("wqT", [1024, 1024], BF16, kind="Internal").ap()
    io["wkT"] = nc.dram_tensor("wkT", [1024, 1024], BF16, kind="Internal").ap()
    io["wvT"] = nc.dram_tensor("wvT", [1024, 1024], BF16, kind="Internal").ap()
    io["woT"] = nc.dram_tensor("woT", [1024, 1024], BF16, kind="Internal").ap()
    io["qkvT"] = nc.dram_tensor("qkvT", [1024, 3072], BF16, kind="Internal").ap()
    io["woutT"] = nc.dram_tensor("woutT", [1024, 1024], BF16, kind="Internal").ap()
    return io


def barrier(ctx):
    blk = ctx.block
    ctrs = (ctx.ld, ctx.lw, ctx.st, ctx.pe, ctx.dve, ctx.act, ctx.pool)
    vals = [(c.sem, c.v) for c in ctrs]
    for dec in (blk.tensor, blk.vector, blk.scalar, blk.sync, blk.gpsimd):
        @dec
        def _(e):
            for sem, v in vals:
                if v > 0:
                    e.wait_ge(sem, v)



def emit_prep(ctx, io):
    nc, blk = ctx.nc, ctx.block
    stack = contextlib.ExitStack()
    stage_row = stack.enter_context(nc.sbuf_tensor("stage_row", [1, 4096], F32))
    wg_rows = stack.enter_context(nc.sbuf_tensor("wg_rows", [96, 1024], F32))
    wt_stage = stack.enter_context(nc.sbuf_tensor("wt_stage", [128, 256], BF16))
    wt_out = stack.enter_context(nc.sbuf_tensor("wt_out", [128, 256], BF16))

    @blk.gpsimd
    def _(g):
        memsets = [(ctx.ident_bf[:], 0.0), (ctx.ident_f32[:], 0.0),
                   (ctx.mask_up[:], 0.0), (ctx.mask_low[:], 0.0),
                   (ctx.sel8[:], 0.0), (wg_rows[:], 0.0),
                   (ctx.ones_bf[:], 1.0), (ctx.ones_f32[:], 1.0),
                   (stage_row[:], 0.0), (ctx.zero_col[:], 0.0)]
        for ap, val in memsets:
            g.memset(ap, val).then_inc(ctx.pool.sem, 1)
        g.memset(ctx.zero_col[:, 1:2], LN_A_SCALE).then_inc(ctx.pool.sem, 1)
        n1 = ctx.pool.inc(len(memsets) + 1)
        g.wait_ge(ctx.pool.sem, n1)
        for ident in (ctx.ident_bf, ctx.ident_f32):
            g.affine_select(out=ident[:], in_=ident[:], compare_op=ALU.not_equal,
                            fill=1.0, base=0, pattern=[[-1, 128]],
                            channel_multiplier=1).then_inc(ctx.pool.sem, 1)
        g.affine_select(out=ctx.mask_up[:], in_=ctx.mask_up[:], compare_op=ALU.is_ge,
                        fill=NEG, base=0, pattern=[[1, 128]],
                        channel_multiplier=-1).then_inc(ctx.pool.sem, 1)
        g.affine_select(out=ctx.mask_low[:], in_=ctx.mask_low[:], compare_op=ALU.is_ge,
                        fill=NEG, base=0, pattern=[[-1, 128]],
                        channel_multiplier=1).then_inc(ctx.pool.sem, 1)
        for h in range(8):
            sl = ctx.sel8[:, h * 128:(h + 1) * 128]
            g.affine_select(out=sl, in_=sl, compare_op=ALU.not_equal, fill=1.0,
                            base=-h, pattern=[[0, 128]],
                            channel_multiplier=1).then_inc(ctx.pool.sem, 1)
        ctx.pool.inc(12)
        g.wait_ge(ctx.pool.sem, ctx.pool.v)

    reg = {}
    specs = []
    col = 0

    def add_vec(name, ap_flat, length):
        nonlocal col
        reg[name] = col
        specs.append((col, ap_flat, length))
        col += (length + 127) // 128

    add_vec("a_norm_w", io["L0_attn_norm_w"], 1024)
    add_vec("a_bout", io["L0_attn_bout"], 1024)
    add_vec("m_gs", io["L0_ffn_g_shared"], 1024)
    add_vec("m_gr", io["L0_ffn_g_routed"], 1024)
    for e in range(MOE_ES):
        add_vec(f"m_b1_s{e}", io["L0_ffn_b1_s"][e], 4096)
        add_vec(f"m_b2_s{e}", io["L0_ffn_b2_s"][e], 1024)
    for e in range(MOE_ER):
        add_vec(f"m_b1_r{e}", io["L0_ffn_b1_r"][e], 4096)
        add_vec(f"m_b2_r{e}", io["L0_ffn_b2_r"][e], 1024)
    add_vec("c_lng", io["L1_attn_ln_g"], 1024)
    add_vec("c_lnb", io["L1_attn_ln_b"], 1024)
    add_vec("c_bqkv", io["L1_attn_b_qkv"], 3072)
    add_vec("c_went", io["L1_attn_w_ent"][0], 1024)
    add_vec("c_bent", io["L1_attn_b_ent"], 1)
    add_vec("c_bout", io["L1_attn_b_out"], 1024)
    for l in range(2):
        add_vec(f"f_qs{l}", io["L1_ffn_q_scale"][l], 64)
        add_vec(f"f_ks{l}", io["L1_ffn_k_scale"][l], 64)
        add_vec(f"f_lng{l}", io["L1_ffn_ln_g"][l], 1024)
        add_vec(f"f_lnb{l}", io["L1_ffn_ln_b"][l], 1024)
        add_vec(f"f_b1a{l}", io["L1_ffn_b1"][l][0:2730], 2730)
        add_vec(f"f_b1g{l}", io["L1_ffn_b1"][l][2730:5460], 2730)
        add_vec(f"f_b2{l}", io["L1_ffn_b2"][l], 1024)
    reg["a_gbias"] = col
    gb_col = col
    col += 1
    ctx.bias_reg = reg
    assert col <= ctx.bias_pool.shape[1], (col, tuple(ctx.bias_pool.shape))

    @blk.sync
    def _(s):
        s.wait_ge(ctx.pool.sem, ctx.pool.v)
        s.dma_start(out=stage_row[0:1, 0:8],
                    in_=io["L0_attn_bbeta"].rearrange("(a b) -> a b", a=1)).then_inc(ctx.ld.sem, 16)
        s.dma_start(out=stage_row[0:1, 32:40],
                    in_=io["L0_attn_big"].rearrange("(a b) -> a b", a=1)).then_inc(ctx.ld.sem, 16)
        s.dma_start(out=stage_row[0:1, 64:72],
                    in_=io["L0_attn_bog"].rearrange("(a b) -> a b", a=1)).then_inc(ctx.ld.sem, 16)
    ctx.ld.inc(3)

    @blk.tensor
    def _(t):
        t.wait_ge(ctx.ld.sem, ctx.ld.v)
        t.wait_ge(ctx.pool.sem, ctx.pool.v)
        t.transpose(ctx.ps_a[:, 0:1], stage_row[0:1, 0:128],
                    ctx.ident_f32[0:1, 0:1]).then_inc(ctx.pe.sem, 1)
    ctx.pe.inc(1)

    @blk.vector
    def _(v):
        v.wait_ge(ctx.pe.sem, ctx.pe.v)
        v.tensor_copy(ctx.bias_pool[:, gb_col:gb_col + 1],
                      ctx.ps_a[:, 0:1]).then_inc(ctx.dve.sem, 1)
    ctx.dve.inc(1)

    group, used, groups = [], 0, []
    for s_ in specs:
        ncols = (s_[2] + 127) // 128
        if used + ncols > 32:
            groups.append(group)
            group, used = [], 0
        group.append((s_, used))
        used += ncols
    groups.append(group)

    for grp in groups:
        @blk.sync
        def _(s, grp=grp):
            s.wait_ge(ctx.dve.sem, ctx.dve.v)
            for (c0, ap_flat, length), off in grp:
                s.dma_start(out=stage_row[0:1, off * 128: off * 128 + length],
                            in_=ap_flat.rearrange("(a b) -> a b", a=1)
                            ).then_inc(ctx.ld.sem, 16)
        ctx.ld.inc(len(grp))

        ncols_g = sum((sp[2] + 127) // 128 for sp, _ in grp)

        @blk.tensor
        def _(t, ncols_g=ncols_g):
            t.wait_ge(ctx.ld.sem, ctx.ld.v)
            t.wait_ge(ctx.dve.sem, ctx.dve.v)
            last = None
            for k in range(ncols_g):
                last = t.transpose(ctx.ps_a[:, k:k + 1],
                                   stage_row[0:1, k * 128:(k + 1) * 128],
                                   ctx.ident_f32[0:1, 0:1])
            last.then_inc(ctx.pe.sem, 1)
        ctx.pe.inc(1)

        @blk.vector
        def _(v, grp=grp):
            v.wait_ge(ctx.pe.sem, ctx.pe.v)
            for n_, ((c0, ap_flat, length), off) in enumerate(grp):
                nch = (length + 127) // 128
                cc = v.tensor_copy(ctx.bias_pool[:, c0:c0 + nch],
                                   ctx.ps_a[:, off:off + nch])
                if n_ == len(grp) - 1:
                    cc.then_inc(ctx.dve.sem, 1)
        ctx.dve.inc(1)

    @blk.sync
    def _(s):
        s.wait_ge(ctx.pool.sem, ctx.pool.v)
        s.dma_start(out=wg_rows[0:8, :], in_=io["L0_attn_Wbeta"]).then_inc(ctx.ld.sem, 16)
        s.dma_start(out=wg_rows[32:40, :], in_=io["L0_attn_Wig"]).then_inc(ctx.ld.sem, 16)
        s.dma_start(out=wg_rows[64:72, :], in_=io["L0_attn_Wog"]).then_inc(ctx.ld.sem, 16)
        s.dma_start(out=ctx.cent_sb[:].rearrange("p (ch e) -> p ch e", ch=8),
                    in_=io["L0_ffn_centroids"].rearrange(
                        "(ch p) e -> p ch e", p=128)).then_inc(ctx.ld.sem, 16)
    ctx.ld.inc(4)

    nw = reg["a_norm_w"]
    for rnd in range(2):
        @blk.tensor
        def _(t, rnd=rnd):
            t.wait_ge(ctx.ld.sem, ctx.ld.v)
            t.wait_ge(ctx.dve.sem, ctx.dve.v)
            last = None
            for k in range(4):
                last = t.transpose(ctx.ps_a[:, k * 96:k * 96 + 96],
                                   wg_rows[0:96, (rnd * 4 + k) * 128:(rnd * 4 + k + 1) * 128],
                                   ctx.ident_f32[0:96, 0:96])
            last.then_inc(ctx.pe.sem, 1)
        ctx.pe.inc(1)

        @blk.vector
        def _(v, rnd=rnd):
            v.wait_ge(ctx.pe.sem, ctx.pe.v)
            for k in range(4):
                kk = rnd * 4 + k
                cc = v.tensor_scalar(ctx.wgA[:, kk * 96:(kk + 1) * 96],
                                     ctx.ps_a[:, k * 96:k * 96 + 96],
                                     ctx.bias_pool[:, nw + kk:nw + kk + 1], None, ALU.mult)
                if k == 3:
                    cc.then_inc(ctx.dve.sem, 1)
        ctx.dve.inc(1)

    @blk.tensor
    def _(t):
        t.wait_ge(ctx.dve.sem, ctx.dve.v)
        for k in range(8):
            mm = t.matmul(ctx.ps_a[0:96, 0:1], ctx.wgA[:, k * 96:(k + 1) * 96],
                          ctx.ones_f32[:, 0:1], start=(k == 0), stop=(k == 7))
        mm.then_inc(ctx.pe.sem, 1)
    ctx.pe.inc(1)

    @blk.vector
    def _(v):
        v.wait_ge(ctx.pe.sem, ctx.pe.v)
        v.tensor_copy(ctx.wg_colsum[0:96, 0:1], ctx.ps_a[0:96, 0:1])
        c = ctx.bias_reg["c_bout"]
        v.tensor_scalar(ctx.bias_pool[:, c:c + 8], ctx.bias_pool[:, c:c + 8],
                        0.1, None, ALU.mult)
        c = ctx.bias_reg["c_went"]
        v.tensor_copy(ctx.went_bf[:], ctx.bias_pool[:, c:c + 8]).then_inc(ctx.dve.sem, 1)
    ctx.dve.inc(1)

    wcv = stack.enter_context(nc.sbuf_tensor("wcv", [128, 2 * 4096], BF16))
    moe_list = ([("L0_ffn_W1_s", "L0_ffn_W2_s", "m_gs", MOE_ES, 0)] +
                [("L0_ffn_W1_r", "L0_ffn_W2_r", "m_gr", MOE_ER, MOE_ES)])
    cvt_jobs = []
    for w1n, w2n, gn, ne, base in moe_list:
        for e in range(ne):
            for ch in range(8):
                cvt_jobs.append((io[w1n][e][ch * 128:(ch + 1) * 128, :],
                                 io["w1bf"][base + e][ch * 128:(ch + 1) * 128, :],
                                 reg[gn] + ch, 4096))
            for hc in range(32):
                cvt_jobs.append((io[w2n][e][hc * 128:(hc + 1) * 128, :],
                                 io["w2bf"][base + e][hc * 128:(hc + 1) * 128, :],
                                 None, 1024))
    for l in range(2):
        for ch in range(8):
            cvt_jobs.append((io["L1_ffn_wqkv"][l][ch * 128:(ch + 1) * 128, :],
                             io["fqkv"][l][ch * 128:(ch + 1) * 128, :], None, 1536))
        for ch in range(4):
            cvt_jobs.append((io["L1_ffn_wo"][l][ch * 128:(ch + 1) * 128, :],
                             io["fwo"][l][ch * 128:(ch + 1) * 128, :], None, 1024))
        for ch in range(8):
            cvt_jobs.append((io["L1_ffn_w1"][l][ch * 128:(ch + 1) * 128, 0:2730],
                             io["fw1"][l][ch * 128:(ch + 1) * 128, 0:2730], None, 2730))
            cvt_jobs.append((io["L1_ffn_w1"][l][ch * 128:(ch + 1) * 128, 2730:5460],
                             io["fw1"][l][ch * 128:(ch + 1) * 128, 2730:5460], None, 2730))
        for ch in range(21):
            cvt_jobs.append((io["L1_ffn_w2"][l][ch * 128:(ch + 1) * 128, :],
                             io["fw2"][l][ch * 128:(ch + 1) * 128, :], None, 1024))
        cvt_jobs.append((io["L1_ffn_w2"][l][2688:2730, :],
                         io["fw2"][l][2688:2730, :], None, 1024))
    for j, (srcap, dstap, gcol, width) in enumerate(cvt_jobs):
        slot = j % 2
        nprt = srcap.shape[0]
        buf = wcv[0:nprt, slot * 4096: slot * 4096 + width]

        @blk.gpsimd
        def _(g, buf=buf, srcap=srcap):
            g.wait_ge(ctx.st.sem, max(ctx.st.v - 16, 0))
            g.dma_start(out=buf, in_=srcap).then_inc(ctx.lw.sem, 16)
        ctx.lw.inc(1)

        if gcol is not None:
            @blk.vector
            def _(v, buf=buf, gcol=gcol):
                v.wait_ge(ctx.lw.sem, ctx.lw.v)
                v.tensor_scalar(buf, buf, ctx.bias_pool[:, gcol:gcol + 1], None,
                                ALU.mult).then_inc(ctx.dve.sem, 1)
            ctx.dve.inc(1)

        @blk.scalar
        def _(s, buf=buf, dstap=dstap, gcol=gcol):
            if gcol is not None:
                s.wait_ge(ctx.dve.sem, ctx.dve.v)
            else:
                s.wait_ge(ctx.lw.sem, ctx.lw.v)
            s.dma_start(out=dstap, in_=buf).then_inc(ctx.st.sem, 16)
        ctx.st.inc(1)

    jobs = [
        (io["L0_attn_Wq"], io["wqT"], 1024, 1024, 1.0),
        (io["L0_attn_Wk"], io["wkT"], 1024, 1024, 1.0),
        (io["L0_attn_Wv"], io["wvT"], 1024, 1024, 1.0),
        (io["L0_attn_Wout"], io["woT"], 1024, 1024, 1.0),
        (io["L1_attn_w_qkv"], io["qkvT"], 3072, 1024, 1.0),
        (io["L1_attn_w_out"], io["woutT"], 1024, 1024, 0.1),
    ]
    for src, dst, R, C, scale in jobs:
        for rb in range(R // 128):
            for cb in range(C // 128):
                slot = (rb * (C // 128) + cb) % 2
                stg = wt_stage[:, slot * 128:(slot + 1) * 128]
                out_sb = wt_out[:, slot * 128:(slot + 1) * 128]
                psl = ctx.ps_tr[:, slot * 128:(slot + 1) * 128]

                @blk.gpsimd
                def _(g, stg=stg, src=src, rb=rb, cb=cb):
                    g.wait_ge(ctx.pe.sem, max(ctx.pe.v - 1, 0))
                    g.dma_start(out=stg,
                                in_=src[rb * 128:(rb + 1) * 128, cb * 128:(cb + 1) * 128]
                                ).then_inc(ctx.lw.sem, 16)
                ctx.lw.inc(1)

                @blk.tensor
                def _(t, psl=psl, stg=stg):
                    t.wait_ge(ctx.lw.sem, ctx.lw.v)
                    t.wait_ge(ctx.dve.sem, max(ctx.dve.v - 1, 0))
                    t.transpose(psl, stg, ctx.ident_bf[:]).then_inc(ctx.pe.sem, 1)
                ctx.pe.inc(1)

                @blk.vector
                def _(v, out_sb=out_sb, psl=psl, scale=scale):
                    v.wait_ge(ctx.pe.sem, ctx.pe.v)
                    v.wait_ge(ctx.st.sem, max(ctx.st.v - 16, 0))
                    if scale == 1.0:
                        v.tensor_copy(out_sb, psl).then_inc(ctx.dve.sem, 1)
                    else:
                        v.tensor_scalar(out_sb, psl, scale, None,
                                        ALU.mult).then_inc(ctx.dve.sem, 1)
                ctx.dve.inc(1)

                @blk.scalar
                def _(sc, dst=dst, rb=rb, cb=cb, out_sb=out_sb):
                    sc.wait_ge(ctx.dve.sem, ctx.dve.v)
                    sc.dma_start(out=dst[cb * 128:(cb + 1) * 128, rb * 128:(rb + 1) * 128],
                                 in_=out_sb).then_inc(ctx.st.sem, 16)
                ctx.st.inc(1)

    barrier(ctx)
    stack.close()



def emit_tile_stats(ctx, get_chunk, sq_scr, is_f32, extra_mm=None):
    blk = ctx.block
    ones = ctx.ones_f32 if is_f32 else ctx.ones_bf
    pe_hist = {}
    for ch in range(NCH):
        src = get_chunk(ch)
        slot = ch % 2
        scr = sq_scr[:, slot * 512:(slot + 1) * 512]
        wait_pe = pe_hist.get(ch - 2, 0)

        @blk.scalar
        def _(s, src=src, scr=scr, wait_pe=wait_pe, ch=ch):
            if ch == 0:
                s.wait_ge(ctx.dve.sem, ctx.dve.v)
            if wait_pe:
                s.wait_ge(ctx.pe.sem, wait_pe)
            s.activation(scr, src, AF.Square).then_inc(ctx.act.sem, 1)
        ctx.act.inc(1)

        @blk.tensor
        def _(t, src=src, scr=scr, ch=ch):
            t.wait_ge(ctx.act.sem, ctx.act.v)
            if ch == 0:
                t.wait_ge(ctx.dve.sem, ctx.dve.v)
            t.matmul(ctx.ps_b[0:1, 0:512], ones[:, 0:1], src,
                     start=(ch == 0), stop=(ch == NCH - 1))
            mm = t.matmul(ctx.ps_c[0:1, 0:512], ones[:, 0:1], scr,
                          start=(ch == 0), stop=(ch == NCH - 1))
            if extra_mm is not None:
                mm = extra_mm(t, ch) or mm
            mm.then_inc(ctx.pe.sem, 1)
        ctx.pe.inc(1)
        pe_hist[ch] = ctx.pe.v


def emit_tile_rows(ctx, eps):
    blk = ctx.block

    @blk.vector
    def _(v):
        v.wait_ge(ctx.pe.sem, ctx.pe.v)
        v.tensor_scalar(ctx.row_a[0:1, :], ctx.ps_b[0:1, 0:512], 1.0 / D, None, ALU.mult)
        v.tensor_scalar(ctx.row_b[0:1, :], ctx.ps_c[0:1, 0:512], 1.0 / D, None, ALU.mult)
        v.tensor_tensor(ctx.row_d[0:1, :], ctx.row_a[0:1, :], ctx.row_a[0:1, :], ALU.mult)
        v.tensor_tensor(ctx.row_b[0:1, :], ctx.row_b[0:1, :], ctx.row_d[0:1, :], ALU.subtract)
        v.tensor_scalar(ctx.row_c[0:1, :], ctx.row_b[0:1, :], eps, None,
                        ALU.add).then_inc(ctx.dve.sem, 1)
    ctx.dve.inc(1)

    @blk.scalar
    def _(s):
        s.wait_ge(ctx.dve.sem, ctx.dve.v)
        s.activation(ctx.row_c[0:1, :], ctx.row_c[0:1, :], AF.Sqrt).then_inc(ctx.act.sem, 1)
    ctx.act.inc(1)

    @blk.vector
    def _(v):
        v.wait_ge(ctx.act.sem, ctx.act.v)
        v.reciprocal(ctx.row_c[0:1, :], ctx.row_c[0:1, :]).then_inc(ctx.dve.sem, 1)
    ctx.dve.inc(1)


def emit_tile_bcast(ctx, rows_out):
    blk = ctx.block

    @blk.tensor
    def _(t):
        t.wait_ge(ctx.dve.sem, ctx.dve.v)
        mm = None
        for i, (row, out) in enumerate(rows_out):
            psx = (ctx.ps_b, ctx.ps_c)[i % 2]
            mm = t.matmul(psx[:, 0:512], ctx.ones_f32[0:1, :], row[0:1, :],
                          start=True, stop=True)
        mm.then_inc(ctx.pe.sem, 1)
    ctx.pe.inc(1)

    @blk.vector
    def _(v):
        v.wait_ge(ctx.pe.sem, ctx.pe.v)
        cc = None
        for i, (row, out) in enumerate(rows_out):
            psx = (ctx.ps_b, ctx.ps_c)[i % 2]
            cc = v.tensor_copy(out[:, 0:512], psx[:, 0:512])
        cc.then_inc(ctx.dve.sem, 1)
    ctx.dve.inc(1)



def emit_phase_a(ctx, io, b, src="x", dst="x_cur"):
    nc, blk, T = ctx.nc, ctx.block, ctx.T
    NT = T // 512
    NCK = T // A_CHUNK
    stack = contextlib.ExitStack()

    def sb(name, shape, dt):
        return stack.enter_context(nc.sbuf_tensor(f"a{b}_{name}", shape, dt))

    h = sb("h", [128, NCH * T], BF16)
    yall = sb("yall", [128, NCH * T], BF16)
    x_it = sb("x_it", [128, NCH * 512], F32)
    gates = sb("gates", [96, T], F32)
    lcum = sb("lcum", [8, T], F32)
    ogr = sb("ogr", [8, T], F32)
    ones8 = sb("ones8", [8, T], F32)
    ltm = sb("ltm", [128, NCK * 24], F32)
    lnig = sb("lnig", [8, T], F32)
    qh = sb("qh", [128, T], BF16)
    kh = sb("kh", [128, T], BF16)
    vh = sb("vh", [128, T], BF16)
    ropec = sb("ropec", [128, T], BF16)
    ropes = sb("ropes", [128, T], BF16)
    rsc1 = sb("rsc1", [128, T], BF16)
    rsc2 = sb("rsc2", [128, T], BF16)
    sqf = sb("sqf", [128, 1024], F32)
    sqb = sb("sqb", [128, 1024], BF16)
    whb = sb("whb", [128, 3 * 1024], BF16)
    wob = sb("wob", [128, 1024], BF16)
    m_f32 = sb("m_f32", [128, 128], F32)
    m_bf = sb("m_bf", [128, 128], BF16)
    cols = sb("cols", [128, 8], F32)
    ogb = sb("ogb", [128, 128], F32)
    darg = sb("darg", [128, 128], F32)
    aarg = sb("aarg", [128, 128], F32)
    aexp = sb("aexp", [128, 128], F32)
    pt_bf = sb("pt_bf", [128, 128], BF16)
    kt_bf = sb("kt_bf", [128, 128], BF16)
    vt_bf = sb("vt_bf", [128, 128], BF16)
    scr128 = sb("scr128", [128, 128], F32)
    ctx.row_a = sb("row_a", [1, 512], F32)
    ctx.row_b = sb("row_b", [1, 512], F32)
    ctx.row_c = sb("row_c", [1, 512], F32)
    ctx.row_d = sb("row_d", [1, 512], F32)
    mu_bt = sb("mu_bt", [128, 512], F32)
    rho_bt = sb("rho_bt", [128, 512], F32)
    scr_t = sb("scr_t", [128, 512], F32)

    @blk.gpsimd
    def _(g):
        g.memset(ones8[:], 1.0).then_inc(ctx.pool.sem, 1)
    ctx.pool.inc(1)

    gb = ctx.bias_reg["a_gbias"]
    nw = ctx.bias_reg["a_norm_w"]

    for it in range(NT):
        tsl = slice(it * 512, (it + 1) * 512)

        @blk.sync
        def _(s, tsl=tsl):
            s.wait_ge(ctx.dve.sem, ctx.dve.v)
            for ch in range(NCH):
                s.dma_start(out=x_it[:, ch * 512:(ch + 1) * 512],
                            in_=io[src][b, ch * 128:(ch + 1) * 128, tsl]
                            ).then_inc(ctx.ld.sem, 16)
        ctx.ld.inc(NCH)

        @blk.scalar
        def _(s):
            s.wait_ge(ctx.ld.sem, ctx.ld.v)
            s.sem_inc(ctx.act.sem, 1)
        ctx.act.inc(1)

        @blk.tensor
        def _(t):
            t.wait_ge(ctx.ld.sem, ctx.ld.v)
            t.sem_inc(ctx.pe.sem, 1)
        ctx.pe.inc(1)

        def gate_mm(t, ch):
            return t.matmul(ctx.ps_a[0:96, 0:512], ctx.wgA[:, ch * 96:(ch + 1) * 96],
                            x_it[:, ch * 512:(ch + 1) * 512],
                            start=(ch == 0), stop=(ch == NCH - 1))
        emit_tile_stats(ctx, lambda ch: x_it[:, ch * 512:(ch + 1) * 512],
                        sqf, is_f32=True, extra_mm=gate_mm)

        @blk.vector
        def _(v):
            v.wait_ge(ctx.pe.sem, ctx.pe.v)
            v.tensor_scalar(ctx.row_a[0:1, :], ctx.ps_b[0:1, 0:512], 1.0 / D, None, ALU.mult)
            v.tensor_scalar(ctx.row_b[0:1, :], ctx.ps_c[0:1, 0:512], 1.0 / D, None, ALU.mult)
            v.tensor_tensor(ctx.row_d[0:1, :], ctx.row_a[0:1, :], ctx.row_a[0:1, :], ALU.mult)
            v.tensor_tensor(ctx.row_b[0:1, :], ctx.row_b[0:1, :], ctx.row_d[0:1, :], ALU.subtract)
            v.tensor_scalar(ctx.row_c[0:1, :], ctx.row_b[0:1, :], 1e-5, None, ALU.add)
            v.reciprocal(ctx.row_d[0:1, :], ctx.row_c[0:1, :])
            v.tensor_tensor(ctx.row_d[0:1, :], ctx.row_b[0:1, :], ctx.row_d[0:1, :], ALU.mult)
            v.tensor_scalar(ctx.row_d[0:1, :], ctx.row_d[0:1, :], 1e-6, None, ALU.add)
            v.tensor_tensor(ctx.row_c[0:1, :], ctx.row_c[0:1, :], ctx.row_d[0:1, :],
                            ALU.mult).then_inc(ctx.dve.sem, 1)
        ctx.dve.inc(1)

        @blk.scalar
        def _(s):
            s.wait_ge(ctx.dve.sem, ctx.dve.v)
            s.activation(ctx.row_c[0:1, :], ctx.row_c[0:1, :], AF.Sqrt).then_inc(ctx.act.sem, 1)
        ctx.act.inc(1)

        @blk.vector
        def _(v):
            v.wait_ge(ctx.act.sem, ctx.act.v)
            v.reciprocal(ctx.row_c[0:1, :], ctx.row_c[0:1, :]).then_inc(ctx.dve.sem, 1)
        ctx.dve.inc(1)

        emit_tile_bcast(ctx, [(ctx.row_a, mu_bt), (ctx.row_c, rho_bt)])

        @blk.vector
        def _(v):
            v.scalar_tensor_tensor(scr_t[0:96, :], mu_bt[0:96, 0:512],
                                   ctx.wg_colsum[0:96, 0:1], ctx.ps_a[0:96, 0:512],
                                   op0=ALU.mult, op1=ALU.subtract)
            v.tensor_tensor(scr_t[0:96, :], scr_t[0:96, :], rho_bt[0:96, 0:512],
                            ALU.mult).then_inc(ctx.dve.sem, 1)
        ctx.dve.inc(1)

        @blk.scalar
        def _(s, tsl=tsl):
            s.wait_ge(ctx.dve.sem, ctx.dve.v)
            s.activation(gates[0:96, tsl], scr_t[0:96, :], AF.Sigmoid,
                         bias=ctx.bias_pool[0:96, gb:gb + 1],
                         scale=-1.0).then_inc(ctx.act.sem, 1)
        ctx.act.inc(1)

        for ch in range(NCH):
            @blk.vector
            def _(v, ch=ch, it=it):
                if ch == 0:
                    v.wait_ge(ctx.act.sem, ctx.act.v)
                v.tensor_tensor(scr_t[:, 0:512], x_it[:, ch * 512:(ch + 1) * 512],
                                mu_bt[:, 0:512], ALU.subtract)
                v.scalar_tensor_tensor(h[:, ch * T + it * 512: ch * T + (it + 1) * 512],
                                       scr_t[:, 0:512],
                                       ctx.bias_pool[:, nw + ch:nw + ch + 1],
                                       rho_bt[:, 0:512],
                                       op0=ALU.mult, op1=ALU.mult).then_inc(ctx.dve.sem, 1)
            ctx.dve.inc(1)

    @blk.vector
    def _(v):
        v.wait_ge(ctx.act.sem, ctx.act.v)
        v.wait_ge(ctx.pool.sem, ctx.pool.v)
        v.tensor_scalar(lcum[0:8, :T], gates[0:8, :T], 1e-15, None,
                        ALU.max).then_inc(ctx.dve.sem, 1)
    ctx.dve.inc(1)

    @blk.scalar
    def _(s):
        s.wait_ge(ctx.dve.sem, ctx.dve.v)
        s.activation(lcum[0:8, :T], lcum[0:8, :T], AF.Ln).then_inc(ctx.act.sem, 1)
    ctx.act.inc(1)

    @blk.vector
    def _(v):
        v.wait_ge(ctx.act.sem, ctx.act.v)
        v.tensor_tensor_scan(lcum[0:8, :T], ones8[0:8, :T], lcum[0:8, :T], 0.0,
                             op0=ALU.mult, op1=ALU.add).then_inc(ctx.dve.sem, 1)
    ctx.dve.inc(1)

    @blk.gpsimd
    def _(g):
        g.wait_ge(ctx.act.sem, ctx.act.v)
        g.dma_start(out=ogr[0:8, :T], in_=gates[64:72, :T]).then_inc(ctx.lw.sem, 16)
    ctx.lw.inc(1)

    @blk.scalar
    def _(s):
        s.activation(lnig[0:8, :T], gates[32:40, :T], AF.Ln).then_inc(ctx.act.sem, 1)
    ctx.act.inc(1)

    @blk.vector
    def _(v):
        v.wait_ge(ctx.act.sem, ctx.act.v)
        v.tensor_scalar(lnig[0:8, :T], lnig[0:8, :T], LN_A_SCALE, None,
                        ALU.add).then_inc(ctx.dve.sem, 1)
    ctx.dve.inc(1)

    for ck in range(NCK):
        bank_pos = ck % 21

        @blk.tensor
        def _(t, ck=ck, bank_pos=bank_pos):
            t.wait_ge(ctx.dve.sem, ctx.dve.v)
            t.transpose(ctx.ps_b[:, bank_pos * 24:bank_pos * 24 + 8],
                        lcum[0:8, ck * 128:(ck + 1) * 128], ctx.ident_f32[0:8, 0:8])
            t.transpose(ctx.ps_b[:, bank_pos * 24 + 8:bank_pos * 24 + 16],
                        gates[32:40, ck * 128:(ck + 1) * 128],
                        ctx.ident_f32[32:40, 32:40])
            t.transpose(ctx.ps_b[:, bank_pos * 24 + 16:bank_pos * 24 + 24],
                        lnig[0:8, ck * 128:(ck + 1) * 128],
                        ctx.ident_f32[0:8, 0:8]).then_inc(ctx.pe.sem, 1)
        ctx.pe.inc(1)

        if bank_pos == 20 or ck == NCK - 1:
            n_in = bank_pos + 1
            c0 = ck - n_in + 1

            @blk.vector
            def _(v, n_in=n_in, c0=c0):
                v.wait_ge(ctx.pe.sem, ctx.pe.v)
                v.tensor_copy(ltm[:, c0 * 24:(c0 + n_in) * 24],
                              ctx.ps_b[:, 0:n_in * 24]).then_inc(ctx.dve.sem, 1)
            ctx.dve.inc(1)

    @blk.sync
    def _(s):
        s.dma_start(out=ropec[:, :T], in_=io["rope0_cos"]).then_inc(ctx.ld.sem, 16)
        s.dma_start(out=ropes[:, :T], in_=io["rope0_sin"]).then_inc(ctx.ld.sem, 16)
    ctx.ld.inc(2)

    if getattr(ctx, "debug", False) and b == 0:
        @blk.scalar
        def _(s):
            s.wait_ge(ctx.dve.sem, ctx.dve.v)
            s.wait_ge(ctx.act.sem, ctx.act.v)
            s.dma_start(out=io["dbg_gates"], in_=gates[0:96, :T]).then_inc(ctx.st.sem, 16)
            s.dma_start(out=io["dbg_lcum"], in_=lcum[0:8, :T]).then_inc(ctx.st.sem, 16)
            s.dma_start(out=io["dbg_ltm"], in_=ltm[:]).then_inc(ctx.st.sem, 16)
            s.dma_start(out=io["dbg_h"], in_=h[:]).then_inc(ctx.st.sem, 16)
        ctx.st.inc(4)

    for head in range(A_NH):
        @blk.sync
        def _(s, head=head):
            s.wait_ge(ctx.dve.sem, ctx.dve.v)
            for wi, name in enumerate(("wqT", "wkT", "wvT")):
                s.dma_start(
                    out=whb[:, wi * 1024:(wi + 1) * 1024].rearrange(
                        "p (c e) -> p c e", e=128),
                    in_=io[name].rearrange("(c p) e -> p c e", p=128)[
                        :, :, head * 128:(head + 1) * 128]).then_inc(ctx.ld.sem, 16)
        ctx.ld.inc(3)

        for wi, dstbuf in ((0, qh), (1, kh), (2, vh)):
            for it in range(NT):
                @blk.tensor
                def _(t, wi=wi, it=it):
                    t.wait_ge(ctx.ld.sem, ctx.ld.v)
                    t.wait_ge(ctx.dve.sem, ctx.dve.v)
                    for ch in range(NCH):
                        mm = t.matmul(ctx.ps_a[:, 0:512],
                                      whb[:, wi * 1024 + ch * 128: wi * 1024 + (ch + 1) * 128],
                                      h[:, ch * T + it * 512: ch * T + (it + 1) * 512],
                                      start=(ch == 0), stop=(ch == NCH - 1))
                    mm.then_inc(ctx.pe.sem, 1)
                ctx.pe.inc(1)

                @blk.vector
                def _(v, dstbuf=dstbuf, it=it):
                    v.wait_ge(ctx.pe.sem, ctx.pe.v)
                    v.tensor_copy(dstbuf[:, it * 512:(it + 1) * 512],
                                  ctx.ps_a[:, 0:512]).then_inc(ctx.dve.sem, 1)
                ctx.dve.inc(1)

        for buf in (qh, kh):
            @blk.vector
            def _(v, buf=buf):
                v.tensor_tensor(rsc2[0:64, :T], buf[64:128, :T], ropes[64:128, :T], ALU.mult)
                v.tensor_tensor(rsc1[64:128, :T], buf[0:64, :T], ropes[0:64, :T], ALU.mult)
                v.tensor_tensor(buf[0:64, :T], buf[0:64, :T], ropec[0:64, :T], ALU.mult)
                v.tensor_tensor(buf[64:128, :T], buf[64:128, :T], ropec[64:128, :T], ALU.mult)
                v.tensor_tensor(buf[0:64, :T], buf[0:64, :T], rsc2[0:64, :T], ALU.subtract)
                v.tensor_tensor(buf[64:128, :T], buf[64:128, :T], rsc1[64:128, :T],
                                ALU.add).then_inc(ctx.dve.sem, 1)
            ctx.dve.inc(1)

        @blk.vector
        def _(v):
            v.memset(m_f32[:], 0.0)
            v.memset(m_bf[:], 0.0).then_inc(ctx.dve.sem, 1)
        ctx.dve.inc(1)

        for ck in range(NCK):
            c0 = ck * 128
            csl = slice(c0, c0 + 128)
            lcol = ltm[:, ck * 24 + head: ck * 24 + head + 1]
            igcol = ltm[:, ck * 24 + 8 + head: ck * 24 + 8 + head + 1]
            lnigcol = ltm[:, ck * 24 + 16 + head: ck * 24 + 16 + head + 1]
            sel = ctx.sel8[:, head * 128:(head + 1) * 128]

            @blk.tensor
            def _(t, ck=ck, c0=c0, csl=csl, sel=sel):
                t.wait_ge(ctx.dve.sem, ctx.dve.v)
                t.wait_ge(ctx.lw.sem, ctx.lw.v)
                t.matmul(ctx.ps_misc[:, 0:128], sel, lcum[0:8, csl],
                         start=True, stop=True)
                t.matmul(ctx.ps_misc[:, 128:256], sel, ogr[0:8, csl],
                         start=True, stop=True)
                if ck > 0:
                    t.matmul(ctx.ps_misc[:, 256:257], sel, lcum[0:8, c0 - 1:c0],
                             start=True, stop=True)
                t.matmul(ctx.ps_misc[:, 257:258], sel, lcum[0:8, c0 + 127:c0 + 128],
                         start=True, stop=True)
                t.matmul(ctx.ps_st[:, 0:128], kh[:, csl], qh[:, csl],
                         start=True, stop=True)
                t.transpose(ctx.ps_kv[:, 0:128], kh[:, csl], ctx.ident_bf[:])
                t.transpose(ctx.ps_kv[:, 128:256], vh[:, csl],
                            ctx.ident_bf[:]).then_inc(ctx.pe.sem, 1)
            ctx.pe.inc(1)

            @blk.vector
            def _(v, ck=ck):
                v.wait_ge(ctx.pe.sem, ctx.pe.v)
                if ck > 0:
                    v.tensor_copy(cols[:, 0:2], ctx.ps_misc[:, 256:258])
                else:
                    v.tensor_copy(cols[:, 0:1], ctx.zero_col[:, 0:1])
                    v.tensor_copy(cols[:, 1:2], ctx.ps_misc[:, 257:258])
                v.tensor_copy(ogb[:, 0:128], ctx.ps_misc[:, 128:256]).then_inc(ctx.dve.sem, 1)
            ctx.dve.inc(1)

            @blk.vector
            def _(v, lcol=lcol):
                v.scalar_tensor_tensor(darg[:, 0:128], ctx.ps_misc[:, 0:128],
                                       lcol, ctx.mask_up[:],
                                       op0=ALU.subtract, op1=ALU.add)
                v.tensor_scalar(aarg[:, 0:128], ctx.ps_misc[:, 0:128],
                                cols[:, 0:1], None, ALU.subtract)
                v.tensor_tensor(cols[:, 3:4], cols[:, 1:2], lcol, ALU.subtract)
                v.tensor_tensor(cols[:, 4:5], cols[:, 1:2], cols[:, 0:1],
                                ALU.subtract).then_inc(ctx.dve.sem, 1)
            ctx.dve.inc(1)

            @blk.scalar
            def _(s, lnigcol=lnigcol):
                s.wait_ge(ctx.dve.sem, ctx.dve.v)
                s.activation(darg[:, 0:128], darg[:, 0:128], AF.Exp,
                             bias=ctx.zero_col[:, 1:2])
                s.activation(aexp[:, 0:128], aarg[:, 0:128], AF.Exp)
                s.activation(cols[:, 5:6], cols[:, 3:4], AF.Exp, bias=lnigcol)
                s.activation(cols[:, 7:8], cols[:, 4:5], AF.Exp).then_inc(ctx.act.sem, 1)
            ctx.act.inc(1)

            @blk.vector
            def _(v, igcol=igcol):
                v.wait_ge(ctx.act.sem, ctx.act.v)
                v.scalar_tensor_tensor(pt_bf[:, 0:128], ctx.ps_st[:, 0:128],
                                       igcol, darg[:, 0:128],
                                       op0=ALU.mult, op1=ALU.mult)
                v.tensor_scalar(kt_bf[:, 0:128], ctx.ps_kv[:, 0:128],
                                cols[:, 5:6], None, ALU.mult)
                v.tensor_copy(vt_bf[:, 0:128], ctx.ps_kv[:, 128:256]).then_inc(ctx.dve.sem, 1)
            ctx.dve.inc(1)

            @blk.tensor
            def _(t, csl=csl):
                t.wait_ge(ctx.dve.sem, ctx.dve.v)
                t.matmul(ctx.ps_y3[:, 0:128], vt_bf[:, 0:128], pt_bf[:, 0:128],
                         start=True, stop=True)
                t.matmul(ctx.ps_y3[:, 128:256], m_bf[:], qh[:, csl],
                         start=True, stop=True)
                t.matmul(ctx.ps_y3[:, 256:384], kt_bf[:, 0:128], vt_bf[:, 0:128],
                         start=True, stop=True).then_inc(ctx.pe.sem, 1)
            ctx.pe.inc(1)

            @blk.vector
            def _(v, head=head, c0=c0):
                v.wait_ge(ctx.pe.sem, ctx.pe.v)
                v.tensor_tensor(scr128[:, 0:128], ctx.ps_y3[:, 128:256],
                                aexp[:, 0:128], ALU.mult)
                v.tensor_tensor(scr128[:, 0:128], scr128[:, 0:128],
                                ctx.ps_y3[:, 0:128], ALU.add)
                v.tensor_tensor(yall[:, head * T + c0: head * T + c0 + 128],
                                scr128[:, 0:128], ogb[:, 0:128], ALU.mult)
                v.tensor_scalar(scr128[:, 0:128], m_f32[:], cols[:, 7:8],
                                None, ALU.mult)
                v.tensor_tensor(m_f32[:], scr128[:, 0:128],
                                ctx.ps_y3[:, 256:384], ALU.add)
                v.tensor_copy(m_bf[:], m_f32[:]).then_inc(ctx.dve.sem, 1)
            ctx.dve.inc(1)

    if getattr(ctx, "debug", False) and b == 0:
        @blk.scalar
        def _(s):
            s.wait_ge(ctx.dve.sem, ctx.dve.v)
            s.dma_start(out=io["dbg_yall"], in_=yall[:]).then_inc(ctx.st.sem, 16)
            s.dma_start(out=io["dbg_qh"], in_=qh[:, :T]).then_inc(ctx.st.sem, 16)
        ctx.st.inc(2)

    attn = h
    bout = ctx.bias_reg["a_bout"]
    for ch in range(NCH):
        @blk.sync
        def _(s, ch=ch):
            s.wait_ge(ctx.dve.sem, ctx.dve.v)
            s.dma_start(
                out=wob[:].rearrange("p (e d) -> p e d", d=128),
                in_=io["woT"].rearrange("(e p) d -> p e d", p=128)[
                    :, :, ch * 128:(ch + 1) * 128]).then_inc(ctx.ld.sem, 16)
        ctx.ld.inc(1)

        for it in range(NT):
            @blk.tensor
            def _(t, ch=ch, it=it):
                t.wait_ge(ctx.ld.sem, ctx.ld.v)
                t.wait_ge(ctx.dve.sem, ctx.dve.v)
                for e in range(NCH):
                    mm = t.matmul(ctx.ps_a[:, 0:512],
                                  wob[:, e * 128:(e + 1) * 128],
                                  yall[:, e * T + it * 512: e * T + (it + 1) * 512],
                                  start=(e == 0), stop=(e == NCH - 1))
                mm.then_inc(ctx.pe.sem, 1)
            ctx.pe.inc(1)

            @blk.vector
            def _(v, ch=ch, it=it):
                v.wait_ge(ctx.pe.sem, ctx.pe.v)
                v.tensor_scalar(attn[:, ch * T + it * 512: ch * T + (it + 1) * 512],
                                ctx.ps_a[:, 0:512],
                                ctx.bias_pool[:, bout + ch:bout + ch + 1], None,
                                ALU.add).then_inc(ctx.dve.sem, 1)
            ctx.dve.inc(1)

    if getattr(ctx, "debug", False) and b == 0:
        @blk.scalar
        def _(s):
            s.wait_ge(ctx.dve.sem, ctx.dve.v)
            s.dma_start(out=io["dbg_attn"], in_=attn[:]).then_inc(ctx.st.sem, 16)
        ctx.st.inc(1)

    for it in range(NT):
        tsl = slice(it * 512, (it + 1) * 512)
        emit_tile_stats(ctx,
                        lambda ch, it=it: attn[:, ch * T + it * 512: ch * T + (it + 1) * 512],
                        sqb, is_f32=False)
        emit_tile_rows(ctx, 1e-5)
        emit_tile_bcast(ctx, [(ctx.row_a, mu_bt), (ctx.row_c, rho_bt)])

        @blk.sync
        def _(s, tsl=tsl):
            s.wait_ge(ctx.dve.sem, ctx.dve.v)
            for ch in range(NCH):
                s.dma_start(out=x_it[:, ch * 512:(ch + 1) * 512],
                            in_=io[src][b, ch * 128:(ch + 1) * 128, tsl]
                            ).then_inc(ctx.ld.sem, 16)
        ctx.ld.inc(NCH)

        for ch in range(NCH):
            @blk.vector
            def _(v, ch=ch, it=it):
                if ch == 0:
                    v.wait_ge(ctx.ld.sem, ctx.ld.v)
                v.tensor_tensor(scr_t[:, 0:512],
                                attn[:, ch * T + it * 512: ch * T + (it + 1) * 512],
                                mu_bt[:, 0:512], ALU.subtract)
                v.tensor_tensor(scr_t[:, 0:512], scr_t[:, 0:512], rho_bt[:, 0:512], ALU.mult)
                v.tensor_tensor(x_it[:, ch * 512:(ch + 1) * 512],
                                x_it[:, ch * 512:(ch + 1) * 512],
                                scr_t[:, 0:512], ALU.add).then_inc(ctx.dve.sem, 1)
            ctx.dve.inc(1)

        @blk.scalar
        def _(s, tsl=tsl):
            s.wait_ge(ctx.dve.sem, ctx.dve.v)
            for ch in range(NCH):
                s.dma_start(out=io[dst][b, ch * 128:(ch + 1) * 128, tsl],
                            in_=x_it[:, ch * 512:(ch + 1) * 512]).then_inc(ctx.st.sem, 16)
        ctx.st.inc(NCH)

    barrier(ctx)
    stack.close()


def emit_phase_b(ctx, io, b, src="x_cur", dst="x_cur"):
    nc, blk, T = ctx.nc, ctx.block, ctx.T
    NT = T // 512
    NTC = T // 128
    EPSF = 1.1920929e-07
    stack = contextlib.ExitStack()

    def sb(name, shape, dt):
        return stack.enter_context(nc.sbuf_tensor(f"b{b}_{name}", shape, dt))

    acc = sb("acc", [128, NCH * T], F32)
    u_bf = sb("u_bf", [128, NCH * T], BF16)
    act = sb("act", [128, 8 * T], BF16)
    w1b = sb("w1b", [128, 2 * 1024], BF16)
    w2q = sb("w2q", [128, 8 * 1024], BF16)
    gbc_t = sb("gbc_t", [128, 512], F32)
    gate_em = sb("gate_em", [8, T], F32)
    scores = sb("scores", [128, NTC * 8], F32)
    gv_tm = sb("gv_tm", [128, NTC * 8], F32)
    x_it = sb("x_it", [128, NCH * 512], F32)
    sqf = sb("sqf", [128, 1024], F32)
    ctx.row_a = sb("row_a", [1, 512], F32)
    ctx.row_b = sb("row_b", [1, 512], F32)
    ctx.row_c = sb("row_c", [1, 512], F32)
    ctx.row_d = sb("row_d", [1, 512], F32)
    mu_bt = sb("mu_bt", [128, 512], F32)
    rho_bt = sb("rho_bt", [128, 512], F32)
    rho2_bt = sb("rho2_bt", [128, 512], F32)
    scr_t = sb("scr_t", [128, 512], F32)
    col_t = sb("col_t", [128, 4 * (T // 128)], F32)

    for it in range(NT):
        tsl = slice(it * 512, (it + 1) * 512)

        @blk.sync
        def _(s, tsl=tsl):
            s.wait_ge(ctx.dve.sem, ctx.dve.v)
            for ch in range(NCH):
                s.dma_start(out=x_it[:, ch * 512:(ch + 1) * 512],
                            in_=io[src][b, ch * 128:(ch + 1) * 128, tsl]
                            ).then_inc(ctx.ld.sem, 16)
        ctx.ld.inc(NCH)

        @blk.scalar
        def _(s):
            s.wait_ge(ctx.ld.sem, ctx.ld.v)
            s.sem_inc(ctx.act.sem, 1)
        ctx.act.inc(1)

        emit_tile_stats(ctx, lambda ch: x_it[:, ch * 512:(ch + 1) * 512],
                        sqf, is_f32=True)

        @blk.vector
        def _(v):
            v.wait_ge(ctx.pe.sem, ctx.pe.v)
            v.tensor_scalar(ctx.row_a[0:1, :], ctx.ps_b[0:1, 0:512], 1.0 / D, None, ALU.mult)
            v.tensor_scalar(ctx.row_b[0:1, :], ctx.ps_c[0:1, 0:512], 1.0 / D, None, ALU.mult)
            v.tensor_tensor(ctx.row_d[0:1, :], ctx.row_a[0:1, :], ctx.row_a[0:1, :], ALU.mult)
            v.tensor_tensor(ctx.row_b[0:1, :], ctx.row_b[0:1, :], ctx.row_d[0:1, :], ALU.subtract)
            v.tensor_scalar(ctx.row_c[0:1, :], ctx.row_b[0:1, :], 1e-5, None, ALU.add)
            v.reciprocal(ctx.row_d[0:1, :], ctx.row_c[0:1, :])
            v.tensor_tensor(ctx.row_d[0:1, :], ctx.row_b[0:1, :], ctx.row_d[0:1, :], ALU.mult)
            v.tensor_scalar(ctx.row_d[0:1, :], ctx.row_d[0:1, :], EPSF, None,
                            ALU.add).then_inc(ctx.dve.sem, 1)
        ctx.dve.inc(1)

        @blk.scalar
        def _(s):
            s.wait_ge(ctx.dve.sem, ctx.dve.v)
            s.activation(ctx.row_c[0:1, :], ctx.row_c[0:1, :], AF.Sqrt)
            s.activation(ctx.row_d[0:1, :], ctx.row_d[0:1, :], AF.Sqrt).then_inc(ctx.act.sem, 1)
        ctx.act.inc(1)

        @blk.vector
        def _(v):
            v.wait_ge(ctx.act.sem, ctx.act.v)
            v.reciprocal(ctx.row_c[0:1, :], ctx.row_c[0:1, :])
            v.reciprocal(ctx.row_d[0:1, :], ctx.row_d[0:1, :]).then_inc(ctx.dve.sem, 1)
        ctx.dve.inc(1)

        emit_tile_bcast(ctx, [(ctx.row_a, mu_bt), (ctx.row_c, rho_bt)])
        emit_tile_bcast(ctx, [(ctx.row_d, rho2_bt)])

        for ch in range(NCH):
            @blk.vector
            def _(v, ch=ch, it=it):
                v.tensor_tensor(scr_t[:, 0:512], x_it[:, ch * 512:(ch + 1) * 512],
                                mu_bt[:, 0:512], ALU.subtract)
                v.tensor_tensor(acc[:, ch * T + it * 512: ch * T + (it + 1) * 512],
                                scr_t[:, 0:512], rho_bt[:, 0:512], ALU.mult)
                v.tensor_tensor(u_bf[:, ch * T + it * 512: ch * T + (it + 1) * 512],
                                acc[:, ch * T + it * 512: ch * T + (it + 1) * 512],
                                rho2_bt[:, 0:512], ALU.mult).then_inc(ctx.dve.sem, 1)
            ctx.dve.inc(1)

    for tc in range(NTC):
        @blk.tensor
        def _(t, tc=tc):
            t.wait_ge(ctx.dve.sem, ctx.dve.v)
            t.wait_ge(ctx.act.sem, ctx.act.v)
            for ch in range(NCH):
                mm = t.matmul(ctx.ps_misc[0:128, 0:8],
                              acc[:, ch * T + tc * 128: ch * T + (tc + 1) * 128],
                              ctx.cent_sb[:, ch * 8:(ch + 1) * 8],
                              start=(ch == 0), stop=(ch == NCH - 1))
            mm.then_inc(ctx.pe.sem, 1)
        ctx.pe.inc(1)

        @blk.scalar
        def _(s, tc=tc):
            s.wait_ge(ctx.pe.sem, ctx.pe.v)
            s.activation(scores[:, tc * 8:(tc + 1) * 8], ctx.ps_misc[0:128, 0:8],
                         AF.Sigmoid).then_inc(ctx.act.sem, 1)
        ctx.act.inc(1)

    ct = col_t

    @blk.vector
    def _(v):
        v.wait_ge(ctx.act.sem, ctx.act.v)
        for tc in range(NTC):
            cc = v.tensor_reduce(ct[:, tc * 4:tc * 4 + 1],
                                 scores[:, tc * 8:(tc + 1) * 8], axis=AX.X, op=ALU.max)
            cc2 = v.tensor_reduce(ct[:, tc * 4 + 2:tc * 4 + 3],
                                  scores[:, tc * 8:(tc + 1) * 8], axis=AX.X, op=ALU.add)
            if tc == NTC - 1:
                cc2.then_inc(ctx.dve.sem, 1)
    ctx.dve.inc(1)

    @blk.vector
    def _(v):
        for tc in range(NTC):
            cc = v.tensor_scalar(gv_tm[:, tc * 8:(tc + 1) * 8],
                                 scores[:, tc * 8:(tc + 1) * 8],
                                 ct[:, tc * 4:tc * 4 + 1], 1e30,
                                 ALU.is_equal, op1=ALU.mult)
            if tc == NTC - 1:
                cc.then_inc(ctx.dve.sem, 1)
    ctx.dve.inc(1)

    @blk.vector
    def _(v):
        for tc in range(NTC):
            cc = v.tensor_tensor(gv_tm[:, tc * 8:(tc + 1) * 8],
                                 scores[:, tc * 8:(tc + 1) * 8],
                                 gv_tm[:, tc * 8:(tc + 1) * 8], ALU.subtract)
            if tc == NTC - 1:
                cc.then_inc(ctx.dve.sem, 1)
    ctx.dve.inc(1)

    @blk.vector
    def _(v):
        for tc in range(NTC):
            cc = v.tensor_reduce(ct[:, tc * 4 + 1:tc * 4 + 2],
                                 gv_tm[:, tc * 8:(tc + 1) * 8], axis=AX.X, op=ALU.max)
            cc2 = v.reciprocal(ct[:, tc * 4 + 3:tc * 4 + 4],
                               ct[:, tc * 4 + 2:tc * 4 + 3])
            if tc == NTC - 1:
                cc2.then_inc(ctx.dve.sem, 1)
    ctx.dve.inc(1)

    @blk.vector
    def _(v):
        for tc in range(NTC):
            cc = v.tensor_scalar(gv_tm[:, tc * 8:(tc + 1) * 8],
                                 scores[:, tc * 8:(tc + 1) * 8],
                                 ct[:, tc * 4 + 1:tc * 4 + 2], None, ALU.is_ge)
            if tc == NTC - 1:
                cc.then_inc(ctx.dve.sem, 1)
    ctx.dve.inc(1)

    @blk.vector
    def _(v):
        for tc in range(NTC):
            cc = v.tensor_tensor(gv_tm[:, tc * 8:(tc + 1) * 8],
                                 scores[:, tc * 8:(tc + 1) * 8],
                                 gv_tm[:, tc * 8:(tc + 1) * 8], ALU.mult)
            if tc == NTC - 1:
                cc.then_inc(ctx.dve.sem, 1)
    ctx.dve.inc(1)

    @blk.vector
    def _(v):
        for tc in range(NTC):
            cc = v.tensor_scalar(gv_tm[:, tc * 8:(tc + 1) * 8],
                                 gv_tm[:, tc * 8:(tc + 1) * 8],
                                 ct[:, tc * 4 + 3:tc * 4 + 4], None, ALU.mult)
            if tc == NTC - 1:
                cc.then_inc(ctx.dve.sem, 1)
    ctx.dve.inc(1)

    for tc in range(NTC):
        bank_pos = tc % 4

        @blk.tensor
        def _(t, tc=tc, bank_pos=bank_pos):
            t.wait_ge(ctx.dve.sem, ctx.dve.v)
            t.transpose(ctx.ps_b[0:8, bank_pos * 128:(bank_pos + 1) * 128],
                        gv_tm[:, tc * 8:(tc + 1) * 8],
                        ctx.ident_f32[:]).then_inc(ctx.pe.sem, 1)
        ctx.pe.inc(1)

        if bank_pos == 3 or tc == NTC - 1:
            n_in = bank_pos + 1
            c0 = tc - n_in + 1

            @blk.vector
            def _(v, n_in=n_in, c0=c0):
                v.wait_ge(ctx.pe.sem, ctx.pe.v)
                v.tensor_copy(gate_em[0:8, c0 * 128:(c0 + n_in) * 128],
                              ctx.ps_b[0:8, 0:n_in * 128]).then_inc(ctx.dve.sem, 1)
            ctx.dve.inc(1)

    if getattr(ctx, "debug", False) and b == 0:
        @blk.scalar
        def _(s):
            s.wait_ge(ctx.dve.sem, ctx.dve.v)
            for ch in range(NCH):
                s.dma_start(out=io["dbg_u"][0, ch * 128:(ch + 1) * 128, :],
                            in_=acc[:, ch * T:(ch + 1) * T]).then_inc(ctx.st.sem, 16)
            s.dma_start(out=io["dbg_gv"], in_=gv_tm[:, 0:(T // 128) * 8]).then_inc(ctx.st.sem, 16)
            s.dma_start(out=io["dbg_sc"], in_=scores[:, 0:(T // 128) * 8]).then_inc(ctx.st.sem, 16)
            s.dma_start(out=io["dbg_gem"], in_=gate_em[0:8, :T]).then_inc(ctx.st.sem, 16)
        ctx.st.inc(NCH + 3)

        @blk.gpsimd
        def _(g):
            g.wait_ge(ctx.dve.sem, ctx.dve.v)
            g.dma_start(out=io["dbg_ubf"].rearrange("(c p) t -> p c t", p=128),
                        in_=u_bf[:].rearrange("p (c t) -> p c t", t=T)).then_inc(ctx.lw.sem, 16)
        ctx.lw.inc(1)

    ctx.w2_evac_hist = {}
    ctx.w2_g2 = 0
    NQ = 4
    HQ = MOE_H // NQ
    QC = HQ // 128
    for e in range(10):
        routed = e >= 2
        b1c = ctx.bias_reg[f"m_b1_r{e - 2}" if routed else f"m_b1_s{e}"]
        b2c = ctx.bias_reg[f"m_b2_r{e - 2}" if routed else f"m_b2_s{e}"]
        for q in range(NQ):
            h0 = q * HQ
            @blk.sync
            def _(s, e=e, h0=h0):
                s.wait_ge(ctx.pe.sem, ctx.pe.v)
                s.dma_start(out=w2q[:].rearrange("p (c d) -> p c d", d=1024),
                            in_=io["w2bf"][e].rearrange("(c p) d -> p c d", p=128)[
                                :, h0 // 128:(h0 + HQ) // 128, :]
                            ).then_inc(ctx.ld.sem, 16)
            ctx.ld.inc(1)

            gelu_hist = {}
            qdve = ctx.dve.v
            for hc in range(QC):
                slot = hc % 2
                wsl = w1b[:, slot * 1024:(slot + 1) * 1024]

                @blk.sync
                def _(s, e=e, h0=h0, hc=hc, wsl=wsl):
                    s.wait_ge(ctx.pe.sem, max(ctx.pe.v - NT, 0))
                    s.dma_start(
                        out=wsl.rearrange("p (c h) -> p c h", h=128),
                        in_=io["w1bf"][e].rearrange("(c p) h -> p c h", p=128)[
                            :, :, h0 + hc * 128: h0 + (hc + 1) * 128]
                        ).then_inc(ctx.ld.sem, 16)
                ctx.ld.inc(1)

                for it in range(NT):
                    g = hc * NT + it
                    psx = (ctx.ps_a, ctx.ps_b, ctx.ps_c, ctx.ps_misc)[g % 4]
                    bank_free = gelu_hist.get(g - 4, 0)

                    @blk.tensor
                    def _(t, wsl=wsl, psx=psx, bank_free=bank_free, qdve=qdve):
                        t.wait_ge(ctx.ld.sem, ctx.ld.v)
                        t.wait_ge(ctx.dve.sem, qdve)
                        if bank_free:
                            t.wait_ge(ctx.act.sem, bank_free)
                        for ch in range(NCH):
                            mm = t.matmul(psx[:, 0:512],
                                          wsl[:, ch * 128:(ch + 1) * 128],
                                          u_bf[:, ch * T + it * 512: ch * T + (it + 1) * 512],
                                          start=(ch == 0), stop=(ch == NCH - 1))
                        mm.then_inc(ctx.pe.sem, 1)
                    ctx.pe.inc(1)

                    @blk.scalar
                    def _(s, hc=hc, it=it, b1c=b1c, q=q, psx=psx):
                        s.wait_ge(ctx.pe.sem, ctx.pe.v)
                        gcol = b1c + q * QC + hc
                        s.activation(act[:, hc * T + it * 512: hc * T + (it + 1) * 512],
                                     psx[:, 0:512], AF.Gelu,
                                     bias=ctx.bias_pool[:, gcol:gcol + 1]
                                     ).then_inc(ctx.act.sem, 1)
                    ctx.act.inc(1)
                    gelu_hist[g] = ctx.act.v

            for it in range(NT):
                if routed:
                    @blk.tensor
                    def _(t, e=e, it=it):
                        t.wait_ge(ctx.dve.sem, ctx.dve.v)
                        t.matmul(ctx.ps_y3[:, 0:128],
                                 ctx.sel8[:, (e - 2) * 128:(e - 1) * 128],
                                 gate_em[0:8, it * 512:it * 512 + 128],
                                 start=True, stop=True)
                        t.matmul(ctx.ps_y3[:, 128:256],
                                 ctx.sel8[:, (e - 2) * 128:(e - 1) * 128],
                                 gate_em[0:8, it * 512 + 128:it * 512 + 256],
                                 start=True, stop=True).then_inc(ctx.pe.sem, 1)
                        t.matmul(ctx.ps_st[:, 0:128],
                                 ctx.sel8[:, (e - 2) * 128:(e - 1) * 128],
                                 gate_em[0:8, it * 512 + 256:it * 512 + 384],
                                 start=True, stop=True)
                        t.matmul(ctx.ps_st[:, 0:128] if False else ctx.ps_y3[:, 256:384],
                                 ctx.sel8[:, (e - 2) * 128:(e - 1) * 128],
                                 gate_em[0:8, it * 512 + 384:it * 512 + 512],
                                 start=True, stop=True).then_inc(ctx.pe.sem, 1)
                    ctx.pe.inc(2)

                    @blk.vector
                    def _(v):
                        v.wait_ge(ctx.pe.sem, ctx.pe.v)
                        v.tensor_copy(gbc_t[:, 0:128], ctx.ps_y3[:, 0:128])
                        v.tensor_copy(gbc_t[:, 128:256], ctx.ps_y3[:, 128:256])
                        v.tensor_copy(gbc_t[:, 256:384], ctx.ps_st[:, 0:128])
                        v.tensor_copy(gbc_t[:, 384:512],
                                      ctx.ps_y3[:, 256:384]).then_inc(ctx.dve.sem, 1)
                    ctx.dve.inc(1)

                for dc in range(NCH):
                    g2 = ctx.w2_g2
                    ctx.w2_g2 += 1
                    psx = (ctx.ps_a, ctx.ps_b, ctx.ps_c, ctx.ps_misc)[g2 % 4]
                    bank_free2 = ctx.w2_evac_hist.get(g2 - 4, 0)

                    @blk.tensor
                    def _(t, dc=dc, it=it, psx=psx, bank_free2=bank_free2):
                        t.wait_ge(ctx.ld.sem, ctx.ld.v)
                        if bank_free2:
                            t.wait_ge(ctx.dve.sem, bank_free2)
                        t.wait_ge(ctx.act.sem, ctx.act.v)
                        for hc in range(QC):
                            mm = t.matmul(psx[:, 0:512],
                                          w2q[:, hc * 1024 + dc * 128: hc * 1024 + (dc + 1) * 128],
                                          act[:, hc * T + it * 512: hc * T + (it + 1) * 512],
                                          start=(hc == 0), stop=(hc == QC - 1))
                        mm.then_inc(ctx.pe.sem, 1)
                    ctx.pe.inc(1)

                    @blk.vector
                    def _(v, dc=dc, it=it, psx=psx, q=q, b2c=b2c, routed=routed):
                        v.wait_ge(ctx.pe.sem, ctx.pe.v)
                        dst_sl = acc[:, dc * T + it * 512: dc * T + (it + 1) * 512]
                        if routed:
                            if q == 0:
                                v.scalar_tensor_tensor(scr_t[:, 0:512], psx[:, 0:512],
                                                       ctx.bias_pool[:, b2c + dc:b2c + dc + 1],
                                                       gbc_t[:, 0:512],
                                                       op0=ALU.add, op1=ALU.mult)
                            else:
                                v.tensor_tensor(scr_t[:, 0:512], psx[:, 0:512],
                                                gbc_t[:, 0:512], ALU.mult)
                            v.tensor_tensor(dst_sl, dst_sl, scr_t[:, 0:512],
                                            ALU.add).then_inc(ctx.dve.sem, 1)
                        else:
                            if q == 0:
                                v.tensor_scalar(scr_t[:, 0:512], psx[:, 0:512],
                                                ctx.bias_pool[:, b2c + dc:b2c + dc + 1],
                                                None, ALU.add)
                                v.tensor_tensor(dst_sl, dst_sl, scr_t[:, 0:512],
                                                ALU.add).then_inc(ctx.dve.sem, 1)
                            else:
                                v.tensor_tensor(dst_sl, dst_sl, psx[:, 0:512],
                                                ALU.add).then_inc(ctx.dve.sem, 1)
                    ctx.dve.inc(1)
                    ctx.w2_evac_hist[g2] = ctx.dve.v

    if getattr(ctx, "debug", False) and b == 0:
        @blk.scalar
        def _(s):
            s.wait_ge(ctx.dve.sem, ctx.dve.v)
            for ch in range(NCH):
                s.dma_start(out=io["dbg_acc"][0, ch * 128:(ch + 1) * 128, :],
                            in_=acc[:, ch * T:(ch + 1) * T]).then_inc(ctx.st.sem, 16)
        ctx.st.inc(NCH)

    for it in range(NT):
        tsl = slice(it * 512, (it + 1) * 512)
        emit_tile_stats(ctx, lambda ch, it=it: acc[:, ch * T + it * 512: ch * T + (it + 1) * 512],
                        sqf, is_f32=True)
        emit_tile_rows(ctx, 1e-5)
        emit_tile_bcast(ctx, [(ctx.row_a, mu_bt), (ctx.row_c, rho_bt)])

        @blk.sync
        def _(s, tsl=tsl):
            s.wait_ge(ctx.dve.sem, ctx.dve.v)
            for ch in range(NCH):
                s.dma_start(out=x_it[:, ch * 512:(ch + 1) * 512],
                            in_=io[src][b, ch * 128:(ch + 1) * 128, tsl]
                            ).then_inc(ctx.ld.sem, 16)
        ctx.ld.inc(NCH)

        for ch in range(NCH):
            @blk.vector
            def _(v, ch=ch, it=it):
                if ch == 0:
                    v.wait_ge(ctx.ld.sem, ctx.ld.v)
                v.tensor_tensor(scr_t[:, 0:512],
                                acc[:, ch * T + it * 512: ch * T + (it + 1) * 512],
                                mu_bt[:, 0:512], ALU.subtract)
                v.tensor_tensor(scr_t[:, 0:512], scr_t[:, 0:512], rho_bt[:, 0:512], ALU.mult)
                v.tensor_tensor(x_it[:, ch * 512:(ch + 1) * 512],
                                x_it[:, ch * 512:(ch + 1) * 512],
                                scr_t[:, 0:512], ALU.add).then_inc(ctx.dve.sem, 1)
            ctx.dve.inc(1)

        @blk.scalar
        def _(s, tsl=tsl):
            s.wait_ge(ctx.dve.sem, ctx.dve.v)
            for ch in range(NCH):
                s.dma_start(out=io[dst][b, ch * 128:(ch + 1) * 128, tsl],
                            in_=x_it[:, ch * 512:(ch + 1) * 512]).then_inc(ctx.st.sem, 16)
        ctx.st.inc(NCH)

    barrier(ctx)
    stack.close()


def emit_phase_c(ctx, io, b, src="x_cur", dst="x_cur"):
    nc, blk, T = ctx.nc, ctx.block, ctx.T
    NT = T // 512
    NTC = T // 128
    NQT = T // 128
    stack = contextlib.ExitStack()

    def sb(name, shape, dt):
        return stack.enter_context(nc.sbuf_tensor(f"c{b}_{name}", shape, dt))

    xn = sb("xn", [128, NCH * T], BF16)
    yall = sb("yall", [128, NCH * T], BF16)
    x_it = sb("x_it", [128, NCH * 512], F32)
    q2 = sb("q2", [128, T], BF16)
    k2 = sb("k2", [128, T], BF16)
    vtm = sb("vtm", [128, T], BF16)
    p_sb = sb("p_sb", [128, T], BF16)
    ptb = sb("ptb", [128, 16 * 128], BF16)
    wpair = sb("wpair", [128, 3 * 1024], BF16)
    wob = sb("wob", [128, 1024], BF16)
    ge_row = sb("ge_row", [1, T], F32)
    gcol = sb("gcol", [128, NTC], F32)
    mzcols = sb("mzcols", [128, 12], F32)
    sqf = sb("sqf", [128, 1024], F32)
    sqb = sb("sqb", [128, 1024], BF16)
    ctx.row_a = sb("row_a", [1, 512], F32)
    ctx.row_b = sb("row_b", [1, 512], F32)
    ctx.row_c = sb("row_c", [1, 512], F32)
    ctx.row_d = sb("row_d", [1, 512], F32)
    mu_bt = sb("mu_bt", [128, 512], F32)
    rho_bt = sb("rho_bt", [128, 512], F32)
    scr_t = sb("scr_t", [128, 512], F32)

    lng = ctx.bias_reg["c_lng"]
    lnb = ctx.bias_reg["c_lnb"]
    bqkv = ctx.bias_reg["c_bqkv"]
    bent = ctx.bias_reg["c_bent"]
    bout = ctx.bias_reg["c_bout"]

    for it in range(NT):
        tsl = slice(it * 512, (it + 1) * 512)

        @blk.sync
        def _(s, tsl=tsl):
            s.wait_ge(ctx.dve.sem, ctx.dve.v)
            for ch in range(NCH):
                s.dma_start(out=x_it[:, ch * 512:(ch + 1) * 512],
                            in_=io[src][b, ch * 128:(ch + 1) * 128, tsl]
                            ).then_inc(ctx.ld.sem, 16)
        ctx.ld.inc(NCH)

        @blk.scalar
        def _(s):
            s.wait_ge(ctx.ld.sem, ctx.ld.v)
            s.sem_inc(ctx.act.sem, 1)
        ctx.act.inc(1)

        emit_tile_stats(ctx, lambda ch: x_it[:, ch * 512:(ch + 1) * 512],
                        sqf, is_f32=True)

        @blk.vector
        def _(v):
            v.wait_ge(ctx.pe.sem, ctx.pe.v)
            v.tensor_scalar(ctx.row_a[0:1, :], ctx.ps_b[0:1, 0:512], 1.0 / D, None, ALU.mult)
            v.tensor_scalar(ctx.row_b[0:1, :], ctx.ps_c[0:1, 0:512], 1.0 / D, None, ALU.mult)
            v.tensor_tensor(ctx.row_d[0:1, :], ctx.row_a[0:1, :], ctx.row_a[0:1, :], ALU.mult)
            v.tensor_tensor(ctx.row_b[0:1, :], ctx.row_b[0:1, :], ctx.row_d[0:1, :], ALU.subtract)
            v.tensor_scalar(ctx.row_c[0:1, :], ctx.row_b[0:1, :], 1e-5, None, ALU.add)
            v.reciprocal(ctx.row_d[0:1, :], ctx.row_c[0:1, :])
            v.tensor_tensor(ctx.row_d[0:1, :], ctx.row_b[0:1, :], ctx.row_d[0:1, :], ALU.mult)
            v.tensor_scalar(ctx.row_d[0:1, :], ctx.row_d[0:1, :], 1e-6, None, ALU.add)
            v.tensor_tensor(ctx.row_c[0:1, :], ctx.row_c[0:1, :], ctx.row_d[0:1, :],
                            ALU.mult).then_inc(ctx.dve.sem, 1)
        ctx.dve.inc(1)

        @blk.scalar
        def _(s):
            s.wait_ge(ctx.dve.sem, ctx.dve.v)
            s.activation(ctx.row_c[0:1, :], ctx.row_c[0:1, :], AF.Sqrt).then_inc(ctx.act.sem, 1)
        ctx.act.inc(1)

        @blk.vector
        def _(v):
            v.wait_ge(ctx.act.sem, ctx.act.v)
            v.reciprocal(ctx.row_c[0:1, :], ctx.row_c[0:1, :]).then_inc(ctx.dve.sem, 1)
        ctx.dve.inc(1)

        emit_tile_bcast(ctx, [(ctx.row_a, mu_bt), (ctx.row_c, rho_bt)])

        for ch in range(NCH):
            @blk.vector
            def _(v, ch=ch, it=it):
                v.tensor_tensor(scr_t[:, 0:512], x_it[:, ch * 512:(ch + 1) * 512],
                                mu_bt[:, 0:512], ALU.subtract)
                v.scalar_tensor_tensor(scr_t[:, 0:512], scr_t[:, 0:512],
                                       ctx.bias_pool[:, lng + ch:lng + ch + 1],
                                       rho_bt[:, 0:512], op0=ALU.mult, op1=ALU.mult)
                v.tensor_scalar(xn[:, ch * T + it * 512: ch * T + (it + 1) * 512],
                                scr_t[:, 0:512],
                                ctx.bias_pool[:, lnb + ch:lnb + ch + 1], None,
                                ALU.add).then_inc(ctx.dve.sem, 1)
            ctx.dve.inc(1)

    for it in range(NT):
        @blk.tensor
        def _(t, it=it):
            t.wait_ge(ctx.dve.sem, ctx.dve.v)
            t.wait_ge(ctx.act.sem, ctx.act.v)
            for ch in range(NCH):
                mm = t.matmul(ctx.ps_misc[0:1, 0:512],
                              ctx.went_bf[:, ch:ch + 1],
                              xn[:, ch * T + it * 512: ch * T + (it + 1) * 512],
                              start=(ch == 0), stop=(ch == NCH - 1))
            mm.then_inc(ctx.pe.sem, 1)
        ctx.pe.inc(1)

        @blk.scalar
        def _(s, it=it):
            s.wait_ge(ctx.pe.sem, ctx.pe.v)
            s.activation(ge_row[0:1, it * 512:(it + 1) * 512], ctx.ps_misc[0:1, 0:512],
                         AF.Sigmoid,
                         bias=ctx.bias_pool[0:1, bent:bent + 1]).then_inc(ctx.act.sem, 1)
        ctx.act.inc(1)

    @blk.vector
    def _(v):
        v.wait_ge(ctx.act.sem, ctx.act.v)
        v.tensor_scalar(ge_row[0:1, :T], ge_row[0:1, :T], 2.0, 0.1, ALU.min,
                        op1=ALU.max).then_inc(ctx.dve.sem, 1)
    ctx.dve.inc(1)

    for tc in range(NTC):
        bank_pos = tc % 32

        @blk.tensor
        def _(t, tc=tc, bank_pos=bank_pos):
            t.wait_ge(ctx.dve.sem, ctx.dve.v)
            t.transpose(ctx.ps_b[:, bank_pos:bank_pos + 1],
                        ge_row[0:1, tc * 128:(tc + 1) * 128],
                        ctx.ident_f32[0:1, 0:1]).then_inc(ctx.pe.sem, 1)
        ctx.pe.inc(1)

        if bank_pos == 31 or tc == NTC - 1:
            n_in = bank_pos + 1
            c0 = tc - n_in + 1

            @blk.vector
            def _(v, n_in=n_in, c0=c0):
                v.wait_ge(ctx.pe.sem, ctx.pe.v)
                v.tensor_copy(gcol[:, c0:c0 + n_in],
                              ctx.ps_b[:, 0:n_in]).then_inc(ctx.dve.sem, 1)
            ctx.dve.inc(1)

    for hp in range(C_NH // 2):
        @blk.sync
        def _(s, hp=hp):
            s.wait_ge(ctx.dve.sem, ctx.dve.v)
            for wi in range(3):
                s.dma_start(
                    out=wpair[:, wi * 1024:(wi + 1) * 1024].rearrange(
                        "p (c e) -> p c e", e=128),
                    in_=io["qkvT"].rearrange("(c p) e -> p c e", p=128)[
                        :, :, wi * 1024 + hp * 128: wi * 1024 + (hp + 1) * 128]
                    ).then_inc(ctx.ld.sem, 16)
        ctx.ld.inc(3)

        for wi, dstbuf in ((0, q2), (1, k2)):
            for it in range(NT):
                @blk.tensor
                def _(t, wi=wi, it=it):
                    t.wait_ge(ctx.ld.sem, ctx.ld.v)
                    t.wait_ge(ctx.dve.sem, ctx.dve.v)
                    for ch in range(NCH):
                        mm = t.matmul(ctx.ps_a[:, 0:512],
                                      wpair[:, wi * 1024 + ch * 128: wi * 1024 + (ch + 1) * 128],
                                      xn[:, ch * T + it * 512: ch * T + (it + 1) * 512],
                                      start=(ch == 0), stop=(ch == NCH - 1))
                    mm.then_inc(ctx.pe.sem, 1)
                ctx.pe.inc(1)

                @blk.vector
                def _(v, wi=wi, dstbuf=dstbuf, it=it, hp=hp):
                    v.wait_ge(ctx.pe.sem, ctx.pe.v)
                    bcol = bqkv + (wi * 1024 + hp * 128) // 128
                    if wi == 0:
                        v.tensor_scalar(dstbuf[:, it * 512:(it + 1) * 512],
                                        ctx.ps_a[:, 0:512],
                                        ctx.bias_pool[:, bcol:bcol + 1], C_SCALE,
                                        ALU.add, op1=ALU.mult).then_inc(ctx.dve.sem, 1)
                    else:
                        v.tensor_scalar(dstbuf[:, it * 512:(it + 1) * 512],
                                        ctx.ps_a[:, 0:512],
                                        ctx.bias_pool[:, bcol:bcol + 1], None,
                                        ALU.add).then_inc(ctx.dve.sem, 1)
                ctx.dve.inc(1)

        for it in range(NT):
            @blk.tensor
            def _(t, it=it):
                t.wait_ge(ctx.ld.sem, ctx.ld.v)
                t.wait_ge(ctx.dve.sem, ctx.dve.v)
                for ch in range(NCH):
                    mm = t.matmul(ctx.ps_a[:, 0:512],
                                  wpair[:, 2 * 1024 + ch * 128: 2 * 1024 + (ch + 1) * 128],
                                  xn[:, ch * T + it * 512: ch * T + (it + 1) * 512],
                                  start=(ch == 0), stop=(ch == NCH - 1))
                mm.then_inc(ctx.pe.sem, 1)
            ctx.pe.inc(1)

            @blk.vector
            def _(v, it=it, hp=hp):
                v.wait_ge(ctx.pe.sem, ctx.pe.v)
                bcol = bqkv + (2 * 1024 + hp * 128) // 128
                v.tensor_scalar(p_sb[:, it * 512:(it + 1) * 512],
                                ctx.ps_a[:, 0:512],
                                ctx.bias_pool[:, bcol:bcol + 1], None,
                                ALU.add).then_inc(ctx.dve.sem, 1)
            ctx.dve.inc(1)
        for tc in range(NTC):
            slot = tc % 2

            @blk.tensor
            def _(t, tc=tc, slot=slot):
                t.wait_ge(ctx.dve.sem, ctx.dve.v)
                t.transpose(ctx.ps_kv[:, slot * 128:(slot + 1) * 128],
                            p_sb[:, tc * 128:(tc + 1) * 128],
                            ctx.ident_bf[:]).then_inc(ctx.pe.sem, 1)
            ctx.pe.inc(1)

            @blk.vector
            def _(v, tc=tc, slot=slot):
                v.wait_ge(ctx.pe.sem, ctx.pe.v)
                v.tensor_scalar(vtm[:, tc * 128:(tc + 1) * 128],
                                ctx.ps_kv[:, slot * 128:(slot + 1) * 128],
                                gcol[:, tc:tc + 1], None,
                                ALU.mult).then_inc(ctx.dve.sem, 1)
            ctx.dve.inc(1)

        for hh in range(2):
            base = 64 * hh
            for qi in range(NQT):
                nk = (qi + 1) * 128
                nb = (nk + 511) // 512
                sbanks = (ctx.ps_a, ctx.ps_b, ctx.ps_c, ctx.ps_misc)

                @blk.tensor
                def _(t, base=base, qi=qi, nk=nk, nb=nb):
                    t.wait_ge(ctx.dve.sem, ctx.dve.v)
                    t.wait_ge(ctx.act.sem, ctx.act.v)
                    last = None
                    for bk in range(nb):
                        kw = min(512, nk - bk * 512)
                        last = t.matmul(sbanks[bk][:, 0:kw],
                                        q2[base:base + 64, qi * 128:(qi + 1) * 128],
                                        k2[base:base + 64, bk * 512:bk * 512 + kw],
                                        start=True, stop=True)
                    last.then_inc(ctx.pe.sem, 1)
                ctx.pe.inc(1)

                @blk.vector
                def _(v, qi=qi, nb=nb, nk=nk):
                    v.wait_ge(ctx.pe.sem, ctx.pe.v)
                    off = (qi * 128) % 512
                    v.tensor_tensor(sbanks[nb - 1][:, off:off + 128],
                                    sbanks[nb - 1][:, off:off + 128],
                                    ctx.mask_low[:], ALU.add).then_inc(ctx.dve.sem, 1)
                ctx.dve.inc(1)

                @blk.vector
                def _(v, nb=nb, nk=nk):
                    last = None
                    for bk in range(nb):
                        kw = min(512, nk - bk * 512)
                        last = v.tensor_reduce(mzcols[:, bk:bk + 1],
                                               sbanks[bk][:, 0:kw], axis=AX.X,
                                               op=ALU.max)
                    last.then_inc(ctx.dve.sem, 1)
                ctx.dve.inc(1)

                @blk.vector
                def _(v, nb=nb):
                    v.tensor_reduce(mzcols[:, 4:5], mzcols[:, 0:nb], axis=AX.X,
                                    op=ALU.max, negate=True).then_inc(ctx.dve.sem, 1)
                ctx.dve.inc(1)

                @blk.scalar
                def _(s, qi=qi, nb=nb, nk=nk):
                    s.wait_ge(ctx.dve.sem, ctx.dve.v)
                    last = None
                    for bk in range(nb):
                        kw = min(512, nk - bk * 512)
                        last = s.activation(p_sb[:, bk * 512:bk * 512 + kw],
                                            sbanks[bk][:, 0:kw], AF.Exp,
                                            bias=mzcols[:, 4:5],
                                            accum_out=mzcols[:, 5 + bk:6 + bk])
                    last.then_inc(ctx.act.sem, 1)
                ctx.act.inc(1)

                @blk.vector
                def _(v, nb=nb):
                    v.wait_ge(ctx.act.sem, ctx.act.v)
                    v.tensor_reduce(mzcols[:, 9:10], mzcols[:, 5:5 + nb], axis=AX.X,
                                    op=ALU.add).then_inc(ctx.dve.sem, 1)
                ctx.dve.inc(1)

                @blk.vector
                def _(v):
                    v.reciprocal(mzcols[:, 10:11], mzcols[:, 9:10]).then_inc(ctx.dve.sem, 1)
                ctx.dve.inc(1)

                @blk.vector
                def _(v, nk=nk):
                    v.tensor_scalar(p_sb[:, 0:nk], p_sb[:, 0:nk],
                                    mzcols[:, 10:11], None,
                                    ALU.mult).then_inc(ctx.dve.sem, 1)
                ctx.dve.inc(1)

                nkb = qi + 1
                for kb2 in range(0, nkb, 2):
                    npair = min(2, nkb - kb2)

                    @blk.tensor
                    def _(t, kb2=kb2, npair=npair):
                        t.wait_ge(ctx.dve.sem, ctx.dve.v)
                        last = None
                        for j in range(npair):
                            last = t.transpose(
                                ctx.ps_kv[:, j * 128:(j + 1) * 128],
                                p_sb[:, (kb2 + j) * 128:(kb2 + j + 1) * 128],
                                ctx.ident_bf[:])
                        last.then_inc(ctx.pe.sem, 1)
                    ctx.pe.inc(1)

                    @blk.vector
                    def _(v, kb2=kb2, npair=npair):
                        v.wait_ge(ctx.pe.sem, ctx.pe.v)
                        v.tensor_copy(ptb[:, kb2 * 128:(kb2 + npair) * 128],
                                      ctx.ps_kv[:, 0:npair * 128]).then_inc(ctx.dve.sem, 1)
                    ctx.dve.inc(1)

                @blk.tensor
                def _(t, base=base, qi=qi, nkb=nkb):
                    t.wait_ge(ctx.dve.sem, ctx.dve.v)
                    for kb in range(nkb):
                        mm = t.matmul(ctx.ps_y3[0:64, 0:128],
                                      vtm[:, kb * 128 + base: kb * 128 + base + 64],
                                      ptb[:, kb * 128:(kb + 1) * 128],
                                      start=(kb == 0), stop=(kb == nkb - 1))
                    mm.then_inc(ctx.pe.sem, 1)
                ctx.pe.inc(1)

                @blk.vector
                def _(v, hp=hp, hh=hh, base=base, qi=qi):
                    v.wait_ge(ctx.pe.sem, ctx.pe.v)
                    v.tensor_copy(yall[base:base + 64,
                                       hp * T + qi * 128: hp * T + (qi + 1) * 128],
                                  ctx.ps_y3[0:64, 0:128]).then_inc(ctx.dve.sem, 1)
                ctx.dve.inc(1)

    attn = xn
    for ch in range(NCH):
        @blk.sync
        def _(s, ch=ch):
            s.wait_ge(ctx.dve.sem, ctx.dve.v)
            s.dma_start(
                out=wob[:].rearrange("p (e d) -> p e d", d=128),
                in_=io["woutT"].rearrange("(e p) d -> p e d", p=128)[
                    :, :, ch * 128:(ch + 1) * 128]).then_inc(ctx.ld.sem, 16)
        ctx.ld.inc(1)

        for it in range(NT):
            @blk.tensor
            def _(t, ch=ch, it=it):
                t.wait_ge(ctx.ld.sem, ctx.ld.v)
                t.wait_ge(ctx.dve.sem, ctx.dve.v)
                for e in range(NCH):
                    mm = t.matmul(ctx.ps_a[:, 0:512],
                                  wob[:, e * 128:(e + 1) * 128],
                                  yall[:, e * T + it * 512: e * T + (it + 1) * 512],
                                  start=(e == 0), stop=(e == NCH - 1))
                mm.then_inc(ctx.pe.sem, 1)
            ctx.pe.inc(1)

            @blk.vector
            def _(v, ch=ch, it=it):
                v.wait_ge(ctx.pe.sem, ctx.pe.v)
                v.tensor_scalar(attn[:, ch * T + it * 512: ch * T + (it + 1) * 512],
                                ctx.ps_a[:, 0:512],
                                ctx.bias_pool[:, bout + ch:bout + ch + 1], None,
                                ALU.add).then_inc(ctx.dve.sem, 1)
            ctx.dve.inc(1)

    for it in range(NT):
        tsl = slice(it * 512, (it + 1) * 512)
        emit_tile_stats(ctx,
                        lambda ch, it=it: attn[:, ch * T + it * 512: ch * T + (it + 1) * 512],
                        sqb, is_f32=False)
        emit_tile_rows(ctx, 1e-5)
        emit_tile_bcast(ctx, [(ctx.row_a, mu_bt), (ctx.row_c, rho_bt)])

        @blk.sync
        def _(s, tsl=tsl):
            s.wait_ge(ctx.dve.sem, ctx.dve.v)
            for ch in range(NCH):
                s.dma_start(out=x_it[:, ch * 512:(ch + 1) * 512],
                            in_=io[src][b, ch * 128:(ch + 1) * 128, tsl]
                            ).then_inc(ctx.ld.sem, 16)
        ctx.ld.inc(NCH)

        for ch in range(NCH):
            @blk.vector
            def _(v, ch=ch, it=it):
                if ch == 0:
                    v.wait_ge(ctx.ld.sem, ctx.ld.v)
                v.tensor_tensor(scr_t[:, 0:512],
                                attn[:, ch * T + it * 512: ch * T + (it + 1) * 512],
                                mu_bt[:, 0:512], ALU.subtract)
                v.tensor_tensor(scr_t[:, 0:512], scr_t[:, 0:512], rho_bt[:, 0:512], ALU.mult)
                v.tensor_tensor(x_it[:, ch * 512:(ch + 1) * 512],
                                x_it[:, ch * 512:(ch + 1) * 512],
                                scr_t[:, 0:512], ALU.add).then_inc(ctx.dve.sem, 1)
            ctx.dve.inc(1)

        @blk.scalar
        def _(s, tsl=tsl):
            s.wait_ge(ctx.dve.sem, ctx.dve.v)
            for ch in range(NCH):
                s.dma_start(out=io[dst][b, ch * 128:(ch + 1) * 128, tsl],
                            in_=x_it[:, ch * 512:(ch + 1) * 512]).then_inc(ctx.st.sem, 16)
        ctx.st.inc(NCH)

    barrier(ctx)
    stack.close()


def emit_phase_d(ctx, io, b, src="x_cur", dst="x_cur"):
    nc, blk, T = ctx.nc, ctx.block, ctx.T
    NT = T // 512
    NTC = T // 128
    VW = T + 8
    NW = T // F_WINDOW
    stack = contextlib.ExitStack()

    def sb(name, shape, dt):
        return stack.enter_context(nc.sbuf_tensor(f"d{b}_{name}", shape, dt))

    v = sb("v", [128, NCH * VW], F32)
    h_bf = sb("h_bf", [128, NCH * T], BF16)
    act = sb("act", [128, 11 * 512], BF16)
    yp = sb("yp", [128, T], BF16)
    qh = sb("qh", [64, T], BF16)
    kh = sb("kh", [64, T], BF16)
    vfm = sb("vfm", [128, T], BF16)
    vtm = sb("vtm", [128, T], BF16)
    p_sb = sb("p_sb", [128, 1024], BF16)
    ptb = sb("ptb", [128, 5 * 128], BF16)
    wpair = sb("wpair", [128, 3 * 1024], BF16)
    wo_sl = sb("wo_sl", [128, 1024], BF16)
    w1a = sb("w1a", [128, 2 * 1024], BF16)
    w1g = sb("w1g", [128, 2 * 1024], BF16)
    w2sl = sb("w2sl", [128, 2 * 11 * 128], BF16)
    gelu_sb = sb("gelu_sb", [128, 512], BF16)
    xq_c = sb("xq_c", [64, T], BF16)
    xq_s = sb("xq_s", [64, T], BF16)
    xk_c = sb("xk_c", [64, T], BF16)
    xk_s = sb("xk_s", [64, T], BF16)
    rs1 = sb("rs1", [64, 512], BF16)
    rs2 = sb("rs2", [64, 512], BF16)
    mzcols = sb("mzcols", [128, 16], F32)
    sqf = sb("sqf", [128, 1024], F32)
    sqb = sb("sqb", [64, 512], BF16)
    xtile = sb("xtile", [128, 2 * 512], F32)
    ctx.row_a = sb("row_a", [1, 512], F32)
    ctx.row_b = sb("row_b", [1, 512], F32)
    ctx.row_c = sb("row_c", [1, 512], F32)
    ctx.row_d = sb("row_d", [1, 512], F32)
    mu_bt = sb("mu_bt", [128, 512], F32)
    rho_bt = sb("rho_bt", [128, 512], F32)
    scr_t = sb("scr_t", [128, 512], F32)

    def vsl(ch, t0, n, shift=False):
        off = 0 if (shift and ch >= NCH // 2) else 1
        return v[:, ch * VW + off + t0: ch * VW + off + t0 + n]

    @blk.gpsimd
    def _(g):
        g.memset(v[:], 0.0).then_inc(ctx.pool.sem, 1)
    ctx.pool.inc(1)

    @blk.sync
    def _(s):
        s.wait_ge(ctx.pool.sem, ctx.pool.v)
        s.dma_start(out=xq_c[:, :T], in_=io["xpos_qcos"]).then_inc(ctx.ld.sem, 16)
        s.dma_start(out=xq_s[:, :T], in_=io["xpos_qsin"]).then_inc(ctx.ld.sem, 16)
        s.dma_start(out=xk_c[:, :T], in_=io["xpos_kcos"]).then_inc(ctx.ld.sem, 16)
        s.dma_start(out=xk_s[:, :T], in_=io["xpos_ksin"]).then_inc(ctx.ld.sem, 16)
        for ch in range(NCH):
            for it in range(NT):
                s.dma_start(out=vsl(ch, it * 512, 512),
                            in_=io[src][b, ch * 128:(ch + 1) * 128,
                                        it * 512:(it + 1) * 512]).then_inc(ctx.ld.sem, 16)
    ctx.ld.inc(4 + NCH * NT)

    @blk.scalar
    def _(s):
        s.wait_ge(ctx.ld.sem, ctx.ld.v)
        s.sem_inc(ctx.act.sem, 1)
    ctx.act.inc(1)

    for it in range(NT):
        emit_tile_stats(ctx, lambda ch, it=it: vsl(ch, it * 512, 512),
                        sqf, is_f32=True)
        emit_tile_rows(ctx, 1e-5)
        emit_tile_bcast(ctx, [(ctx.row_a, mu_bt), (ctx.row_c, rho_bt)])
        for ch in range(NCH):
            @blk.vector
            def _(vv, ch=ch, it=it):
                vv.tensor_tensor(scr_t[:, 0:512], vsl(ch, it * 512, 512),
                                 mu_bt[:, 0:512], ALU.subtract)
                vv.tensor_tensor(vsl(ch, it * 512, 512), scr_t[:, 0:512],
                                 rho_bt[:, 0:512], ALU.mult).then_inc(ctx.dve.sem, 1)
            ctx.dve.inc(1)

    for l in range(2):
        qsc = ctx.bias_reg[f"f_qs{l}"]
        ksc = ctx.bias_reg[f"f_ks{l}"]
        lngc = ctx.bias_reg[f"f_lng{l}"]
        lnbc = ctx.bias_reg[f"f_lnb{l}"]
        b1ac = ctx.bias_reg[f"f_b1a{l}"]
        b1gc = ctx.bias_reg[f"f_b1g{l}"]
        b2c = ctx.bias_reg[f"f_b2{l}"]

        for ch in range(NCH):
            @blk.vector
            def _(vv, ch=ch):
                last = None
                for it in range(NT):
                    last = vv.tensor_copy(
                        h_bf[:, ch * T + it * 512: ch * T + (it + 1) * 512],
                        vsl(ch, it * 512, 512, shift=True))
                last.then_inc(ctx.dve.sem, 1)
            ctx.dve.inc(1)

        for hp in range(F_HEADS // 2):
            @blk.sync
            def _(s, hp=hp, l=l):
                s.wait_ge(ctx.dve.sem, ctx.dve.v)
                for wi in range(3):
                    s.dma_start(
                        out=wpair[:, wi * 1024:(wi + 1) * 1024].rearrange(
                            "p (c e) -> p c e", e=128),
                        in_=io["fqkv"][l].rearrange("(c p) e -> p c e", p=128)[
                            :, :, wi * 512 + hp * 128: wi * 512 + (hp + 1) * 128]
                        ).then_inc(ctx.ld.sem, 16)
            ctx.ld.inc(3)

            for it in range(NT):
                @blk.tensor
                def _(t, it=it):
                    t.wait_ge(ctx.ld.sem, ctx.ld.v)
                    t.wait_ge(ctx.dve.sem, ctx.dve.v)
                    for ch in range(NCH):
                        mm = t.matmul(ctx.ps_a[:, 0:512],
                                      wpair[:, 2 * 1024 + ch * 128: 2 * 1024 + (ch + 1) * 128],
                                      h_bf[:, ch * T + it * 512: ch * T + (it + 1) * 512],
                                      start=(ch == 0), stop=(ch == NCH - 1))
                    mm.then_inc(ctx.pe.sem, 1)
                ctx.pe.inc(1)

                @blk.vector
                def _(vv, it=it):
                    vv.wait_ge(ctx.pe.sem, ctx.pe.v)
                    vv.tensor_copy(vfm[:, it * 512:(it + 1) * 512],
                                   ctx.ps_a[:, 0:512]).then_inc(ctx.dve.sem, 1)
                ctx.dve.inc(1)
            for tc in range(NTC):
                slot = tc % 2

                @blk.tensor
                def _(t, tc=tc, slot=slot):
                    t.wait_ge(ctx.dve.sem, ctx.dve.v)
                    t.transpose(ctx.ps_kv[:, slot * 128:(slot + 1) * 128],
                                vfm[:, tc * 128:(tc + 1) * 128],
                                ctx.ident_bf[:]).then_inc(ctx.pe.sem, 1)
                ctx.pe.inc(1)

                @blk.vector
                def _(vv, tc=tc, slot=slot):
                    vv.wait_ge(ctx.pe.sem, ctx.pe.v)
                    vv.tensor_copy(vtm[:, tc * 128:(tc + 1) * 128],
                                   ctx.ps_kv[:, slot * 128:(slot + 1) * 128]
                                   ).then_inc(ctx.dve.sem, 1)
                ctx.dve.inc(1)

            for hh in range(2):
                base = 64 * hh
                for (wi, fin, tabc, tabs, scol) in (
                        (0, qh, xq_c, xq_s, qsc), (1, kh, xk_c, xk_s, ksc)):
                    apply_hist = {}
                    for it in range(NT):
                        psx = (ctx.ps_a, ctx.ps_misc)[it % 2]
                        bank_free = apply_hist.get(it - 2, 0)

                        @blk.tensor
                        def _(t, wi=wi, it=it, base=base, psx=psx, bank_free=bank_free):
                            if bank_free:
                                t.wait_ge(ctx.dve.sem, bank_free)
                            t.wait_ge(ctx.dve.sem, max(ctx.dve.v - 2, 0))
                            for ch in range(NCH):
                                mm = t.matmul(
                                    psx[0:64, 0:512],
                                    wpair[:, wi * 1024 + ch * 128 + base:
                                          wi * 1024 + ch * 128 + base + 64],
                                    h_bf[:, ch * T + it * 512: ch * T + (it + 1) * 512],
                                    start=(ch == 0), stop=(ch == NCH - 1))
                            mm.then_inc(ctx.pe.sem, 1)
                        ctx.pe.inc(1)

                        @blk.scalar
                        def _(s, it=it, psx=psx):
                            s.wait_ge(ctx.pe.sem, ctx.pe.v)
                            s.activation(sqb[0:64, 0:512], psx[0:64, 0:512],
                                         AF.Square).then_inc(ctx.act.sem, 1)
                        ctx.act.inc(1)

                        @blk.tensor
                        def _(t, it=it):
                            t.wait_ge(ctx.act.sem, ctx.act.v)
                            t.wait_ge(ctx.dve.sem, ctx.dve.v)
                            t.matmul(ctx.ps_b[0:1, 0:512], ctx.ones_bf[0:64, 0:1],
                                     sqb[0:64, 0:512], start=True,
                                     stop=True).then_inc(ctx.pe.sem, 1)
                        ctx.pe.inc(1)

                        @blk.vector
                        def _(vv):
                            vv.wait_ge(ctx.pe.sem, ctx.pe.v)
                            vv.tensor_copy(ctx.row_a[0:1, :], ctx.ps_b[0:1, 0:512]
                                           ).then_inc(ctx.dve.sem, 1)
                        ctx.dve.inc(1)

                        @blk.scalar
                        def _(s):
                            s.wait_ge(ctx.dve.sem, ctx.dve.v)
                            s.activation(ctx.row_a[0:1, :], ctx.row_a[0:1, :],
                                         AF.Sqrt).then_inc(ctx.act.sem, 1)
                        ctx.act.inc(1)

                        @blk.vector
                        def _(vv):
                            vv.wait_ge(ctx.act.sem, ctx.act.v)
                            vv.tensor_scalar(ctx.row_a[0:1, :], ctx.row_a[0:1, :],
                                             1e-12, None, ALU.max)
                            vv.reciprocal(ctx.row_a[0:1, :],
                                          ctx.row_a[0:1, :]).then_inc(ctx.dve.sem, 1)
                        ctx.dve.inc(1)

                        @blk.tensor
                        def _(t):
                            t.wait_ge(ctx.dve.sem, ctx.dve.v)
                            t.matmul(ctx.ps_b[0:64, 0:512], ctx.ones_f32[0:1, 0:64],
                                     ctx.row_a[0:1, :], start=True,
                                     stop=True).then_inc(ctx.pe.sem, 1)
                        ctx.pe.inc(1)

                        @blk.vector
                        def _(vv):
                            vv.wait_ge(ctx.pe.sem, ctx.pe.v)
                            vv.tensor_copy(mu_bt[0:64, 0:512],
                                           ctx.ps_b[0:64, 0:512]).then_inc(ctx.dve.sem, 1)
                        ctx.dve.inc(1)

                        @blk.vector
                        def _(vv, fin=fin, scol=scol, it=it, psx=psx):
                            vv.scalar_tensor_tensor(fin[:, it * 512:(it + 1) * 512],
                                                    psx[0:64, 0:512],
                                                    ctx.bias_pool[0:64, scol:scol + 1],
                                                    mu_bt[0:64, 0:512],
                                                    op0=ALU.mult,
                                                    op1=ALU.mult).then_inc(ctx.dve.sem, 1)
                        ctx.dve.inc(1)
                        apply_hist[it] = ctx.dve.v

                    for it in range(NT):
                        ts2 = slice(it * 512, (it + 1) * 512)

                        @blk.vector
                        def _(vv, fin=fin, tabc=tabc, tabs=tabs, ts2=ts2):
                            vv.tensor_tensor(rs2[0:32, :], fin[32:64, ts2],
                                             tabs[32:64, ts2], ALU.mult)
                            vv.tensor_tensor(rs1[32:64, :], fin[0:32, ts2],
                                             tabs[0:32, ts2], ALU.mult)
                            vv.tensor_tensor(fin[0:32, ts2], fin[0:32, ts2],
                                             tabc[0:32, ts2], ALU.mult)
                            vv.tensor_tensor(fin[32:64, ts2], fin[32:64, ts2],
                                             tabc[32:64, ts2], ALU.mult)
                            vv.tensor_tensor(fin[0:32, ts2], fin[0:32, ts2],
                                             rs2[0:32, :], ALU.subtract)
                            vv.tensor_tensor(fin[32:64, ts2], fin[32:64, ts2],
                                             rs1[32:64, :], ALU.add).then_inc(ctx.dve.sem, 1)
                        ctx.dve.inc(1)

                tiles = []
                for w in range(NW):
                    for qti in range(4):
                        tiles.append((w, qti))
                ntl = len(tiles)
                s_hist, mm_hist, exp_hist = {}, {}, {}
                qk_ready = ctx.dve.v

                def emit_S(g):
                    w, qti = tiles[g]
                    qt = w * 4 + qti
                    if w == 0:
                        k0, nkb = 0, qti + 1
                    else:
                        k0, nkb = (w - 1) * 512 + qti * 128, 5
                    nk = nkb * 128
                    nb = (nk + 511) // 512
                    par = g % 2
                    sbk = ((ctx.ps_a, ctx.ps_b), (ctx.ps_c, ctx.ps_misc))[par]

                    @blk.tensor
                    def _(t, qt=qt, k0=k0, nk=nk, nb=nb, sbk=sbk, g=g):
                        t.wait_ge(ctx.dve.sem, max(mm_hist.get(g - 2, 0), qk_ready))
                        if exp_hist.get(g - 2, 0):
                            t.wait_ge(ctx.act.sem, exp_hist[g - 2])
                        last = None
                        for bk in range(nb):
                            kw = min(512, nk - bk * 512)
                            last = t.matmul(sbk[bk][:, 0:kw],
                                            qh[:, qt * 128:(qt + 1) * 128],
                                            kh[:, k0 + bk * 512: k0 + bk * 512 + kw],
                                            start=True, stop=True)
                        last.then_inc(ctx.pe.sem, 1)
                    ctx.pe.inc(1)
                    s_hist[g] = ctx.pe.v

                def emit_stages(g, base):
                    w, qti = tiles[g]
                    qt = w * 4 + qti
                    if w == 0:
                        k0, nkb = 0, qti + 1
                    else:
                        k0, nkb = (w - 1) * 512 + qti * 128, 5
                    nk = nkb * 128
                    nb = (nk + 511) // 512
                    par = g % 2
                    sbk = ((ctx.ps_a, ctx.ps_b), (ctx.ps_c, ctx.ps_misc))[par]
                    mz = par * 8
                    pp = 0
                    pt0 = 0

                    @blk.vector
                    def _(vv, nkb=nkb, nb=nb, nk=nk, sbk=sbk, g=g, w=w, mz=mz):
                        vv.wait_ge(ctx.pe.sem, s_hist[g])
                        off = ((nkb - 1) * 128) % 512
                        vv.tensor_tensor(sbk[(nkb - 1) * 128 // 512][:, off:off + 128],
                                         sbk[(nkb - 1) * 128 // 512][:, off:off + 128],
                                         ctx.mask_low[:], ALU.add)
                        if w > 0:
                            vv.tensor_tensor(sbk[0][:, 0:128], sbk[0][:, 0:128],
                                             ctx.mask_up[:], ALU.add)
                        last = None
                        for bk in range(nb):
                            kw = min(512, nk - bk * 512)
                            last = vv.tensor_reduce(mzcols[:, mz + bk:mz + bk + 1],
                                                    sbk[bk][:, 0:kw], axis=AX.X,
                                                    op=ALU.max)
                        last.then_inc(ctx.dve.sem, 1)
                    ctx.dve.inc(1)
                    mm_hist[g] = ctx.dve.v

                    @blk.vector
                    def _(vv, nb=nb, mz=mz):
                        vv.tensor_reduce(mzcols[:, mz + 2:mz + 3], mzcols[:, mz:mz + nb],
                                         axis=AX.X, op=ALU.max,
                                         negate=True).then_inc(ctx.dve.sem, 1)
                    ctx.dve.inc(1)

                    @blk.scalar
                    def _(s, nb=nb, nk=nk, sbk=sbk, mz=mz, pp=pp):
                        s.wait_ge(ctx.dve.sem, ctx.dve.v)
                        last = None
                        for bk in range(nb):
                            kw = min(512, nk - bk * 512)
                            last = s.activation(p_sb[:, pp + bk * 512: pp + bk * 512 + kw],
                                                sbk[bk][:, 0:kw], AF.Exp,
                                                bias=mzcols[:, mz + 2:mz + 3],
                                                accum_out=mzcols[:, mz + 3 + bk:mz + 4 + bk])
                        last.then_inc(ctx.act.sem, 1)
                    ctx.act.inc(1)
                    exp_hist[g] = ctx.act.v

                    @blk.vector
                    def _(vv, nb=nb, mz=mz):
                        vv.wait_ge(ctx.act.sem, ctx.act.v)
                        vv.tensor_reduce(mzcols[:, mz + 5:mz + 6], mzcols[:, mz + 3:mz + 3 + nb],
                                         axis=AX.X, op=ALU.add).then_inc(ctx.dve.sem, 1)
                    ctx.dve.inc(1)

                    @blk.vector
                    def _(vv, mz=mz):
                        vv.reciprocal(mzcols[:, mz + 6:mz + 7],
                                      mzcols[:, mz + 5:mz + 6]).then_inc(ctx.dve.sem, 1)
                    ctx.dve.inc(1)

                    @blk.vector
                    def _(vv, nk=nk, mz=mz, pp=pp):
                        vv.tensor_scalar(p_sb[:, pp:pp + nk], p_sb[:, pp:pp + nk],
                                         mzcols[:, mz + 6:mz + 7], None,
                                         ALU.mult).then_inc(ctx.dve.sem, 1)
                    ctx.dve.inc(1)

                    for kb in range(nkb):
                        @blk.tensor
                        def _(t, kb=kb, par=par, pp=pp):
                            t.wait_ge(ctx.dve.sem, ctx.dve.v)
                            t.transpose(ctx.ps_kv[:, par * 128:(par + 1) * 128],
                                        p_sb[:, pp + kb * 128: pp + (kb + 1) * 128],
                                        ctx.ident_bf[:]).then_inc(ctx.pe.sem, 1)
                        ctx.pe.inc(1)

                        @blk.vector
                        def _(vv, kb=kb, par=par, pt0=pt0):
                            vv.wait_ge(ctx.pe.sem, ctx.pe.v)
                            vv.tensor_copy(ptb[:, pt0 + kb * 128: pt0 + (kb + 1) * 128],
                                           ctx.ps_kv[:, par * 128:(par + 1) * 128]
                                           ).then_inc(ctx.dve.sem, 1)
                        ctx.dve.inc(1)

                    @blk.tensor
                    def _(t, base=base, k0=k0, nkb=nkb, par=par, pt0=pt0):
                        t.wait_ge(ctx.dve.sem, ctx.dve.v)
                        for kb in range(nkb):
                            kbg = k0 // 128 + kb
                            mm = t.matmul(ctx.ps_y3[0:64, par * 128:(par + 1) * 128],
                                          vtm[:, kbg * 128 + base: kbg * 128 + base + 64],
                                          ptb[:, pt0 + kb * 128: pt0 + (kb + 1) * 128],
                                          start=(kb == 0), stop=(kb == nkb - 1))
                        mm.then_inc(ctx.pe.sem, 1)
                    ctx.pe.inc(1)

                    @blk.vector
                    def _(vv, base=base, qt=qt, par=par):
                        vv.wait_ge(ctx.pe.sem, ctx.pe.v)
                        vv.tensor_copy(yp[base:base + 64, qt * 128:(qt + 1) * 128],
                                       ctx.ps_y3[0:64, par * 128:(par + 1) * 128]
                                       ).then_inc(ctx.dve.sem, 1)
                    ctx.dve.inc(1)

                for g in range(ntl + 1):
                    if g < ntl:
                        emit_S(g)
                    if g >= 1:
                        emit_stages(g - 1, base)

            @blk.sync
            def _(s, hp=hp, l=l):
                s.wait_ge(ctx.dve.sem, ctx.dve.v)
                s.dma_start(out=wo_sl[:], in_=io["fwo"][l][hp * 128:(hp + 1) * 128, :]
                            ).then_inc(ctx.ld.sem, 16)
            ctx.ld.inc(1)

            for dc in range(NCH):
                for it in range(NT):
                    psx = (ctx.ps_c, ctx.ps_misc)[(dc * NT + it) % 2]

                    @blk.tensor
                    def _(t, dc=dc, it=it, psx=psx):
                        t.wait_ge(ctx.ld.sem, ctx.ld.v)
                        t.wait_ge(ctx.dve.sem, max(ctx.dve.v - 1, 0))
                        t.matmul(psx[:, 0:512], wo_sl[:, dc * 128:(dc + 1) * 128],
                                 yp[:, it * 512:(it + 1) * 512],
                                 start=True, stop=True).then_inc(ctx.pe.sem, 1)
                    ctx.pe.inc(1)

                    @blk.vector
                    def _(vv, dc=dc, it=it, psx=psx):
                        vv.wait_ge(ctx.pe.sem, ctx.pe.v)
                        vv.tensor_tensor(vsl(dc, it * 512, 512), vsl(dc, it * 512, 512),
                                         psx[:, 0:512], ALU.add).then_inc(ctx.dve.sem, 1)
                    ctx.dve.inc(1)

        for it in range(NT):
            emit_tile_stats(ctx, lambda ch, it=it: vsl(ch, it * 512, 512, shift=True),
                            sqf, is_f32=True)
            emit_tile_rows(ctx, 1e-5)
            emit_tile_bcast(ctx, [(ctx.row_a, mu_bt), (ctx.row_c, rho_bt)])
            for ch in range(NCH):
                @blk.vector
                def _(vv, ch=ch, it=it):
                    vv.tensor_tensor(scr_t[:, 0:512], vsl(ch, it * 512, 512, shift=True),
                                     mu_bt[:, 0:512], ALU.subtract)
                    vv.scalar_tensor_tensor(scr_t[:, 0:512], scr_t[:, 0:512],
                                            ctx.bias_pool[:, lngc + ch:lngc + ch + 1],
                                            rho_bt[:, 0:512], op0=ALU.mult, op1=ALU.mult)
                    vv.tensor_scalar(h_bf[:, ch * T + it * 512: ch * T + (it + 1) * 512],
                                     scr_t[:, 0:512],
                                     ctx.bias_pool[:, lnbc + ch:lnbc + ch + 1], None,
                                     ALU.add).then_inc(ctx.dve.sem, 1)
                ctx.dve.inc(1)

            for jh in range(2):
                for jc in range(11):
                    jcg = jh * 11 + jc
                    cw = 42 if jcg == 21 else 128
                    slot = jc % 2

                    @blk.sync
                    def _(g, l=l, jcg=jcg, cw=cw, slot=slot):
                        g.wait_ge(ctx.pe.sem, max(ctx.pe.v - 1, 0))
                        g.dma_start(
                            out=w1a[:, slot * 1024:slot * 1024 + NCH * cw].rearrange(
                                "p (c o) -> p c o", o=cw),
                            in_=io["fw1"][l].rearrange("(c p) o -> p c o", p=128)[
                                :, :, jcg * 128: jcg * 128 + cw]).then_inc(ctx.ld.sem, 16)
                        g.dma_start(
                            out=w1g[:, slot * 1024:slot * 1024 + NCH * cw].rearrange(
                                "p (c o) -> p c o", o=cw),
                            in_=io["fw1"][l].rearrange("(c p) o -> p c o", p=128)[
                                :, :, 2730 + jcg * 128: 2730 + jcg * 128 + cw]
                            ).then_inc(ctx.ld.sem, 16)
                    ctx.ld.inc(2)

                    @blk.tensor
                    def _(t, it=it, jcg=jcg, cw=cw, slot=slot):
                        t.wait_ge(ctx.ld.sem, ctx.ld.v)
                        t.wait_ge(ctx.dve.sem, ctx.dve.v)
                        t.wait_ge(ctx.act.sem, ctx.act.v)
                        for ch in range(NCH):
                            t.matmul(ctx.ps_a[0:cw, 0:512],
                                     w1a[:, slot * 1024 + ch * cw: slot * 1024 + (ch + 1) * cw],
                                     h_bf[:, ch * T + it * 512: ch * T + (it + 1) * 512],
                                     start=(ch == 0), stop=(ch == NCH - 1))
                            mm = t.matmul(ctx.ps_b[0:cw, 0:512],
                                          w1g[:, slot * 1024 + ch * cw: slot * 1024 + (ch + 1) * cw],
                                          h_bf[:, ch * T + it * 512: ch * T + (it + 1) * 512],
                                          start=(ch == 0), stop=(ch == NCH - 1))
                        mm.then_inc(ctx.pe.sem, 1)
                    ctx.pe.inc(1)

                    @blk.scalar
                    def _(s, jcg=jcg, cw=cw):
                        s.wait_ge(ctx.pe.sem, ctx.pe.v)
                        s.activation(gelu_sb[0:cw, 0:512], ctx.ps_b[0:cw, 0:512],
                                     AF.Gelu,
                                     bias=ctx.bias_pool[0:cw, b1gc + jcg:b1gc + jcg + 1]
                                     ).then_inc(ctx.act.sem, 1)
                    ctx.act.inc(1)

                    @blk.vector
                    def _(vv, jc=jc, jcg=jcg, cw=cw):
                        vv.wait_ge(ctx.act.sem, ctx.act.v)
                        vv.scalar_tensor_tensor(act[0:cw, jc * 512:(jc + 1) * 512],
                                                ctx.ps_a[0:cw, 0:512],
                                                ctx.bias_pool[0:cw, b1ac + jcg:b1ac + jcg + 1],
                                                gelu_sb[0:cw, 0:512],
                                                op0=ALU.add, op1=ALU.mult
                                                ).then_inc(ctx.dve.sem, 1)
                    ctx.dve.inc(1)

                for dc in range(NCH):
                    slot = dc % 2

                    @blk.scalar
                    def _(s, l=l, jh=jh, dc=dc, slot=slot):
                        s.wait_ge(ctx.st.sem, ctx.st.v)
                        s.wait_ge(ctx.pe.sem, max(ctx.pe.v - 1, 0))
                        r0 = jh * 1408
                        nfull = 11 if jh == 0 else 10
                        s.dma_start(
                            out=w2sl[:, slot * 1408: slot * 1408 + nfull * 128].rearrange(
                                "p (c d) -> p c d", d=128),
                            in_=io["fw2"][l][0:2688, :].rearrange(
                                "(x p) d -> p x d", p=128)[
                                :, r0 // 128: r0 // 128 + nfull,
                                dc * 128:(dc + 1) * 128]).then_inc(ctx.st.sem, 16)
                        if jh == 1:
                            s.dma_start(
                                out=w2sl[0:42, slot * 1408 + 10 * 128: slot * 1408 + 11 * 128],
                                in_=io["fw2"][l][2688:2730, dc * 128:(dc + 1) * 128]
                                ).then_inc(ctx.st.sem, 16)
                    ctx.st.inc(2 if jh == 1 else 1)

                    psx = (ctx.ps_c, ctx.ps_misc)[dc % 2]

                    @blk.tensor
                    def _(t, it=it, jh=jh, dc=dc, slot=slot, psx=psx):
                        t.wait_ge(ctx.st.sem, ctx.st.v)
                        t.wait_ge(ctx.dve.sem, max(ctx.dve.v - 1, 0))
                        for jc in range(11):
                            jcg = jh * 11 + jc
                            cw = 42 if jcg == 21 else 128
                            mm = t.matmul(psx[:, 0:512],
                                          w2sl[0:cw, slot * 1408 + jc * 128: slot * 1408 + jc * 128 + 128],
                                          act[0:cw, jc * 512:(jc + 1) * 512],
                                          start=(jc == 0), stop=(jc == 10))
                        mm.then_inc(ctx.pe.sem, 1)
                    ctx.pe.inc(1)

                    @blk.vector
                    def _(vv, it=it, jh=jh, dc=dc, psx=psx):
                        vv.wait_ge(ctx.pe.sem, ctx.pe.v)
                        if jh == 1:
                            vv.tensor_scalar(scr_t[:, 0:512], psx[:, 0:512],
                                             ctx.bias_pool[:, b2c + dc:b2c + dc + 1],
                                             None, ALU.add)
                            vv.tensor_tensor(vsl(dc, it * 512, 512),
                                             vsl(dc, it * 512, 512),
                                             scr_t[:, 0:512], ALU.add).then_inc(ctx.dve.sem, 1)
                        else:
                            vv.tensor_tensor(vsl(dc, it * 512, 512),
                                             vsl(dc, it * 512, 512),
                                             psx[:, 0:512], ALU.add).then_inc(ctx.dve.sem, 1)
                    ctx.dve.inc(1)

    for it in range(NT):
        emit_tile_stats(ctx, lambda ch, it=it: vsl(ch, it * 512, 512),
                        sqf, is_f32=True)
        emit_tile_rows(ctx, 1e-5)
        emit_tile_bcast(ctx, [(ctx.row_a, mu_bt), (ctx.row_c, rho_bt)])

        for ch in range(NCH):
            slot = ch % 2

            @blk.sync
            def _(s, ch=ch, it=it, slot=slot):
                s.wait_ge(ctx.dve.sem, ctx.dve.v)
                s.wait_ge(ctx.st.sem, ctx.st.v)
                s.dma_start(out=xtile[:, slot * 512:(slot + 1) * 512],
                            in_=io[src][b, ch * 128:(ch + 1) * 128,
                                        it * 512:(it + 1) * 512]).then_inc(ctx.ld.sem, 16)
            ctx.ld.inc(1)

            @blk.vector
            def _(vv, ch=ch, it=it, slot=slot):
                vv.wait_ge(ctx.ld.sem, ctx.ld.v)
                vv.tensor_tensor(scr_t[:, 0:512], vsl(ch, it * 512, 512),
                                 mu_bt[:, 0:512], ALU.subtract)
                vv.tensor_tensor(scr_t[:, 0:512], scr_t[:, 0:512], rho_bt[:, 0:512],
                                 ALU.mult)
                vv.tensor_tensor(xtile[:, slot * 512:(slot + 1) * 512],
                                 xtile[:, slot * 512:(slot + 1) * 512],
                                 scr_t[:, 0:512], ALU.add).then_inc(ctx.dve.sem, 1)
            ctx.dve.inc(1)

            @blk.scalar
            def _(s, ch=ch, it=it, slot=slot):
                s.wait_ge(ctx.dve.sem, ctx.dve.v)
                s.dma_start(out=io[dst][b, ch * 128:(ch + 1) * 128,
                                        it * 512:(it + 1) * 512],
                            in_=xtile[:, slot * 512:(slot + 1) * 512]).then_inc(ctx.st.sem, 16)
            ctx.st.inc(1)

    barrier(ctx)
    stack.close()


def emit_copy_out(ctx, io, src="x_cur"):
    blk = ctx.block
    for b in range(BPC):
        @blk.scalar
        def _(s, b=b):
            s.dma_start(out=io["out"][b], in_=io[src][b]).then_inc(ctx.st.sem, 16)
        ctx.st.inc(1)
    barrier(ctx)



PHASE_ORDER = ("a", "b", "c", "d")


def build_program(T=T_FULL, phases=("prep", "a"), batches=(0, 1), debug=False,
                  paranoid=False):
    nc = bass.Bass("TRN2", target_bir_lowering=False)
    io = declare_ios(nc, T)
    ctx = Ctx()
    ctx.nc = nc
    ctx.T = T
    ctx.debug = debug
    ctx.paranoid = paranoid

    stack = contextlib.ExitStack()

    def sb(name, shape, dt):
        return stack.enter_context(nc.sbuf_tensor(name, shape, dt))

    def ps(name, shape, dt=F32):
        return stack.enter_context(nc.psum_tensor(name, shape, dt))

    with stack:
        ctx.ident_bf = sb("ident_bf", [128, 128], BF16)
        ctx.ident_f32 = sb("ident_f32", [128, 128], F32)
        ctx.mask_up = sb("mask_up", [128, 128], F32)
        ctx.mask_low = sb("mask_low", [128, 128], F32)
        ctx.sel8 = sb("sel8", [8, 8 * 128], F32)
        ctx.ones_bf = sb("ones_bf", [128, 128], BF16)
        ctx.ones_f32 = sb("ones_f32", [128, 128], F32)
        ctx.bias_pool = sb("bias_pool", [128, 704], F32)
        ctx.wgA = sb("wgA", [128, 8 * 96], F32)
        ctx.wg_colsum = sb("wg_colsum", [96, 1], F32)
        ctx.went_bf = sb("went_bf", [128, 8], BF16)
        ctx.cent_sb = sb("cent_sb", [128, 8 * 8], F32)
        ctx.zero_col = sb("zero_col", [128, 8], F32)

        ctx.ps_a = ps("ps_a", [128, 512])
        ctx.ps_b = ps("ps_b", [128, 512])
        ctx.ps_c = ps("ps_c", [128, 512])
        ctx.ps_misc = ps("ps_misc", [128, 512])
        ctx.ps_st = ps("ps_st", [128, 128])
        ctx.ps_kv = ps("ps_kv", [128, 256], BF16)
        ctx.ps_y3 = ps("ps_y3", [128, 384])
        ctx.ps_tr = ps("ps_tr", [128, 256], BF16)

        with (
            nc.semaphore("s_ld") as s_ld,
            nc.semaphore("s_lw") as s_lw,
            nc.semaphore("s_st") as s_st,
            nc.semaphore("s_pe") as s_pe,
            nc.semaphore("s_dve") as s_dve,
            nc.semaphore("s_act") as s_act,
            nc.semaphore("s_pool") as s_pool,
            nc.Block() as block,
        ):
            ctx.ld = Ctr(s_ld, 16)
            ctx.lw = Ctr(s_lw, 16)
            ctx.st = Ctr(s_st, 16)
            ctx.pe = Ctr(s_pe)
            ctx.dve = Ctr(s_dve)
            ctx.act = Ctr(s_act)
            ctx.pool = Ctr(s_pool)
            ctx.block = SelfWaitBlock(block, ctx)

            if "prep" in phases:
                emit_prep(ctx, io)
            active = [p for p in PHASE_ORDER if p in phases]
            cur = "x"
            for pi, p in enumerate(active):
                for b in batches:
                    if p == "a":
                        emit_phase_a(ctx, io, b, src=cur, dst="x_cur")
                    elif p == "b":
                        emit_phase_b(ctx, io, b, src=cur, dst="x_cur")
                    elif p == "c":
                        emit_phase_c(ctx, io, b, src=cur, dst="x_cur")
                    elif p == "d":
                        emit_phase_d(ctx, io, b, src=cur, dst="x_cur")
                cur = "x_cur"
            emit_copy_out(ctx, io, src=cur if active else "x")

    return nc, io, ctx


def kernel(**inputs):
    T = T_FULL
    nc, io, ctx = build_program(T=T, phases=("prep", "a", "b", "c", "d"), batches=(0, 1))
    consts = make_host_consts(T)
    x = inputs["L0_attn_x"]
    x_fm = np.ascontiguousarray(np.transpose(x, (0, 2, 1)))
    in_maps = []
    for c in range(NCORES):
        m = {"x": x_fm[c * BPC:(c + 1) * BPC]}
        for name in WEIGHT_NAMES:
            m[name] = np.ascontiguousarray(inputs[name])
        m.update(consts)
        in_maps.append(m)
    res = run_bass_kernel_spmd(nc, in_maps, core_ids=list(range(NCORES)))
    out_fm = np.stack([r["out"] for r in res.results]).reshape(NCORES * BPC, D, T)
    return np.ascontiguousarray(np.transpose(out_fm, (0, 2, 1)))


# version 05
# speedup vs baseline: 1.0028x; 1.0028x over previous
import math
import contextlib
import numpy as np
import ml_dtypes
import concourse.bass as bass
import concourse.mybir as mybir
from concourse.bass_utils import run_bass_kernel_spmd

F32 = mybir.dt.float32
BF16 = mybir.dt.bfloat16
AF = mybir.ActivationFunctionType
ALU = mybir.AluOpType
AX = mybir.AxisListType

NCORES = 8
BPC = 2
D = 1024
NCH = D // 128
T_FULL = 2048
NEG = -1e30

A_NH, A_DH, A_CHUNK = 8, 128, 128
A_SCALE = 1.0 / math.sqrt(A_DH)
LN_A_SCALE = math.log(A_SCALE)

MOE_H, MOE_ES, MOE_ER, MOE_K = 4096, 2, 8, 2

C_NH, C_HD = 16, 64
C_SCALE = (1.0 / math.sqrt(C_HD)) / 0.1

F_HEADS, F_DH, F_WINDOW, F_HID = 8, 64, 512, 2730
F_QK_SCALE = 8.0

WEIGHT_NAMES = [
    "L0_attn_norm_w", "L0_attn_Wq", "L0_attn_Wk", "L0_attn_Wv",
    "L0_attn_Wbeta", "L0_attn_bbeta", "L0_attn_Wig", "L0_attn_big",
    "L0_attn_Wog", "L0_attn_bog", "L0_attn_Wout", "L0_attn_bout",
    "L0_ffn_g_shared", "L0_ffn_W1_s", "L0_ffn_b1_s", "L0_ffn_W2_s", "L0_ffn_b2_s",
    "L0_ffn_g_routed", "L0_ffn_W1_r", "L0_ffn_b1_r", "L0_ffn_W2_r", "L0_ffn_b2_r",
    "L0_ffn_centroids",
    "L1_attn_ln_g", "L1_attn_ln_b", "L1_attn_w_qkv", "L1_attn_b_qkv",
    "L1_attn_w_ent", "L1_attn_b_ent", "L1_attn_w_out", "L1_attn_b_out",
    "L1_ffn_wqkv", "L1_ffn_q_scale", "L1_ffn_k_scale", "L1_ffn_wo",
    "L1_ffn_ln_g", "L1_ffn_ln_b", "L1_ffn_w1", "L1_ffn_b1", "L1_ffn_w2", "L1_ffn_b2",
]

INPUT_SHAPES = {
    "L0_attn_norm_w": (1024,), "L0_attn_Wq": (1024, 1024), "L0_attn_Wk": (1024, 1024),
    "L0_attn_Wv": (1024, 1024), "L0_attn_Wbeta": (8, 1024), "L0_attn_bbeta": (8,),
    "L0_attn_Wig": (8, 1024), "L0_attn_big": (8,), "L0_attn_Wog": (8, 1024),
    "L0_attn_bog": (8,), "L0_attn_Wout": (1024, 1024), "L0_attn_bout": (1024,),
    "L0_ffn_g_shared": (1024,), "L0_ffn_W1_s": (2, 1024, 4096), "L0_ffn_b1_s": (2, 4096),
    "L0_ffn_W2_s": (2, 4096, 1024), "L0_ffn_b2_s": (2, 1024),
    "L0_ffn_g_routed": (1024,), "L0_ffn_W1_r": (8, 1024, 4096), "L0_ffn_b1_r": (8, 4096),
    "L0_ffn_W2_r": (8, 4096, 1024), "L0_ffn_b2_r": (8, 1024),
    "L0_ffn_centroids": (1024, 8),
    "L1_attn_ln_g": (1024,), "L1_attn_ln_b": (1024,), "L1_attn_w_qkv": (3072, 1024),
    "L1_attn_b_qkv": (3072,), "L1_attn_w_ent": (1, 1024), "L1_attn_b_ent": (1,),
    "L1_attn_w_out": (1024, 1024), "L1_attn_b_out": (1024,),
    "L1_ffn_wqkv": (2, 1024, 1536), "L1_ffn_q_scale": (2, 64), "L1_ffn_k_scale": (2, 64),
    "L1_ffn_wo": (2, 512, 1024), "L1_ffn_ln_g": (2, 1024), "L1_ffn_ln_b": (2, 1024),
    "L1_ffn_w1": (2, 1024, 5460), "L1_ffn_b1": (2, 5460), "L1_ffn_w2": (2, 2730, 1024),
    "L1_ffn_b2": (2, 1024),
}


class Ctr:
    def __init__(self, sem, step=1):
        self.sem, self.v, self.step = sem, 0, step

    def inc(self, n=1):
        self.v += n * self.step
        return self.v


class Ctx:
    pass


class SelfWaitBlock:

    def __init__(self, blk, ctx):
        self._blk = blk
        self._ctx = ctx

    def _wrap(self, dec, ctrs):
        ctx = self._ctx
        if getattr(ctx, "paranoid", False):
            ctrs = (ctx.ld, ctx.lw, ctx.st, ctx.pe, ctx.dve, ctx.act, ctx.pool)

        def decorate(fn):
            vals = [(c.sem, c.v) for c in ctrs if c.v > 0]

            def wrapped(e):
                for sem, v in vals:
                    e.wait_ge(sem, v)
                fn(e)
            dec(wrapped)
        return decorate

    @property
    def tensor(self):
        return self._wrap(self._blk.tensor, (self._ctx.pe,))

    @property
    def vector(self):
        return self._wrap(self._blk.vector, (self._ctx.dve,))

    @property
    def scalar(self):
        return self._wrap(self._blk.scalar, (self._ctx.act, self._ctx.st))

    @property
    def sync(self):
        return self._wrap(self._blk.sync, (self._ctx.ld,))

    @property
    def gpsimd(self):
        return self._wrap(self._blk.gpsimd, (self._ctx.pool, self._ctx.lw))


def make_host_consts(T):
    c = {}
    j = np.arange(64, dtype=np.float64)
    inv_freq = 1.0 / (10000.0 ** (j / 64.0))
    t = np.arange(T, dtype=np.float64)
    ang = inv_freq[:, None] * t[None, :]
    c["rope0_cos"] = np.vstack([np.cos(ang), np.cos(ang)]).astype(ml_dtypes.bfloat16)
    c["rope0_sin"] = np.vstack([np.sin(ang), np.sin(ang)]).astype(ml_dtypes.bfloat16)

    half_idx = np.arange(0, F_DH, 2, dtype=np.float64)
    inv_freq2 = 1.0 / (10000.0 ** (half_idx / F_DH))
    ang2 = t[None, :] * inv_freq2[:, None]
    base = (half_idx + 0.4 * F_DH) / (1.4 * F_DH)
    power = (t - T // 2) / float(F_WINDOW // 2)
    scale = base[:, None] ** power[None, :]
    cos2, sin2 = np.cos(ang2), np.sin(ang2)
    def dup(a):
        return np.vstack([a, a]).astype(ml_dtypes.bfloat16)
    c["xpos_qcos"] = dup(cos2 * scale * F_QK_SCALE)
    c["xpos_qsin"] = dup(sin2 * scale * F_QK_SCALE)
    c["xpos_kcos"] = dup(cos2 / scale)
    c["xpos_ksin"] = dup(sin2 / scale)
    return c


def declare_ios(nc, T):
    io = {}
    io["x"] = nc.dram_tensor("x", [BPC, D, T], F32, kind="ExternalInput").ap()
    io["out"] = nc.dram_tensor("out", [BPC, D, T], F32, kind="ExternalOutput").ap()
    for name in WEIGHT_NAMES:
        io[name] = nc.dram_tensor(name, list(INPUT_SHAPES[name]), F32,
                                  kind="ExternalInput").ap()
    for name in ["rope0_cos", "rope0_sin"]:
        io[name] = nc.dram_tensor(name, [128, T], BF16, kind="ExternalInput").ap()
    for name in ["xpos_qcos", "xpos_qsin", "xpos_kcos", "xpos_ksin"]:
        io[name] = nc.dram_tensor(name, [64, T], BF16, kind="ExternalInput").ap()
    io["x_cur"] = nc.dram_tensor("x_cur", [BPC, D, T], F32, kind="Internal").ap()
    io["dbg_gates"] = nc.dram_tensor("dbg_gates", [96, T], F32, kind="ExternalOutput").ap()
    io["dbg_lcum"] = nc.dram_tensor("dbg_lcum", [8, T], F32, kind="ExternalOutput").ap()
    io["dbg_ltm"] = nc.dram_tensor("dbg_ltm", [128, (T // 128) * 24], F32, kind="ExternalOutput").ap()
    io["dbg_yall"] = nc.dram_tensor("dbg_yall", [128, NCH * T], BF16, kind="ExternalOutput").ap()
    io["dbg_attn"] = nc.dram_tensor("dbg_attn", [128, NCH * T], BF16, kind="ExternalOutput").ap()
    io["dbg_qh"] = nc.dram_tensor("dbg_qh", [128, T], BF16, kind="ExternalOutput").ap()
    io["dbg_u"] = nc.dram_tensor("dbg_u", [BPC, D, T], F32, kind="ExternalOutput").ap()
    io["dbg_ubf"] = nc.dram_tensor("dbg_ubf", [D, T], BF16, kind="ExternalOutput").ap()
    io["dbg_gv"] = nc.dram_tensor("dbg_gv", [128, (T // 128) * 8], F32, kind="ExternalOutput").ap()
    io["dbg_sc"] = nc.dram_tensor("dbg_sc", [128, (T // 128) * 8], F32, kind="ExternalOutput").ap()
    io["dbg_gem"] = nc.dram_tensor("dbg_gem", [8, T], F32, kind="ExternalOutput").ap()
    io["dbg_acc"] = nc.dram_tensor("dbg_acc", [BPC, D, T], F32, kind="ExternalOutput").ap()
    io["dbg_h"] = nc.dram_tensor("dbg_h", [128, NCH * T], BF16, kind="ExternalOutput").ap()
    io["w1bf"] = nc.dram_tensor("w1bf", [10, 1024, 4096], BF16, kind="Internal").ap()
    io["fqkv"] = nc.dram_tensor("fqkv", [2, 1024, 1536], BF16, kind="Internal").ap()
    io["fwo"] = nc.dram_tensor("fwo", [2, 512, 1024], BF16, kind="Internal").ap()
    io["fw1"] = nc.dram_tensor("fw1", [2, 1024, 5460], BF16, kind="Internal").ap()
    io["fw2"] = nc.dram_tensor("fw2", [2, 2730, 1024], BF16, kind="Internal").ap()
    io["w2bf"] = nc.dram_tensor("w2bf", [10, 4096, 1024], BF16, kind="Internal").ap()
    io["wqT"] = nc.dram_tensor("wqT", [1024, 1024], BF16, kind="Internal").ap()
    io["wkT"] = nc.dram_tensor("wkT", [1024, 1024], BF16, kind="Internal").ap()
    io["wvT"] = nc.dram_tensor("wvT", [1024, 1024], BF16, kind="Internal").ap()
    io["woT"] = nc.dram_tensor("woT", [1024, 1024], BF16, kind="Internal").ap()
    io["qkvT"] = nc.dram_tensor("qkvT", [1024, 3072], BF16, kind="Internal").ap()
    io["woutT"] = nc.dram_tensor("woutT", [1024, 1024], BF16, kind="Internal").ap()
    return io


def barrier(ctx):
    blk = ctx.block
    ctrs = (ctx.ld, ctx.lw, ctx.st, ctx.pe, ctx.dve, ctx.act, ctx.pool)
    vals = [(c.sem, c.v) for c in ctrs]
    for dec in (blk.tensor, blk.vector, blk.scalar, blk.sync, blk.gpsimd):
        @dec
        def _(e):
            for sem, v in vals:
                if v > 0:
                    e.wait_ge(sem, v)



def emit_prep(ctx, io):
    nc, blk = ctx.nc, ctx.block
    stack = contextlib.ExitStack()
    stage_row = stack.enter_context(nc.sbuf_tensor("stage_row", [1, 4096], F32))
    wg_rows = stack.enter_context(nc.sbuf_tensor("wg_rows", [96, 1024], F32))
    wt_stage = stack.enter_context(nc.sbuf_tensor("wt_stage", [128, 256], BF16))
    wt_out = stack.enter_context(nc.sbuf_tensor("wt_out", [128, 256], BF16))

    @blk.gpsimd
    def _(g):
        memsets = [(ctx.ident_bf[:], 0.0), (ctx.ident_f32[:], 0.0),
                   (ctx.mask_up[:], 0.0), (ctx.mask_low[:], 0.0),
                   (ctx.sel8[:], 0.0), (wg_rows[:], 0.0),
                   (ctx.ones_bf[:], 1.0), (ctx.ones_f32[:], 1.0),
                   (stage_row[:], 0.0), (ctx.zero_col[:], 0.0)]
        for ap, val in memsets:
            g.memset(ap, val).then_inc(ctx.pool.sem, 1)
        g.memset(ctx.zero_col[:, 1:2], LN_A_SCALE).then_inc(ctx.pool.sem, 1)
        n1 = ctx.pool.inc(len(memsets) + 1)
        g.wait_ge(ctx.pool.sem, n1)
        for ident in (ctx.ident_bf, ctx.ident_f32):
            g.affine_select(out=ident[:], in_=ident[:], compare_op=ALU.not_equal,
                            fill=1.0, base=0, pattern=[[-1, 128]],
                            channel_multiplier=1).then_inc(ctx.pool.sem, 1)
        g.affine_select(out=ctx.mask_up[:], in_=ctx.mask_up[:], compare_op=ALU.is_ge,
                        fill=NEG, base=0, pattern=[[1, 128]],
                        channel_multiplier=-1).then_inc(ctx.pool.sem, 1)
        g.affine_select(out=ctx.mask_low[:], in_=ctx.mask_low[:], compare_op=ALU.is_ge,
                        fill=NEG, base=0, pattern=[[-1, 128]],
                        channel_multiplier=1).then_inc(ctx.pool.sem, 1)
        for h in range(8):
            sl = ctx.sel8[:, h * 128:(h + 1) * 128]
            g.affine_select(out=sl, in_=sl, compare_op=ALU.not_equal, fill=1.0,
                            base=-h, pattern=[[0, 128]],
                            channel_multiplier=1).then_inc(ctx.pool.sem, 1)
        ctx.pool.inc(12)
        g.wait_ge(ctx.pool.sem, ctx.pool.v)

    reg = {}
    specs = []
    col = 0

    def add_vec(name, ap_flat, length):
        nonlocal col
        reg[name] = col
        specs.append((col, ap_flat, length))
        col += (length + 127) // 128

    add_vec("a_norm_w", io["L0_attn_norm_w"], 1024)
    add_vec("a_bout", io["L0_attn_bout"], 1024)
    add_vec("m_gs", io["L0_ffn_g_shared"], 1024)
    add_vec("m_gr", io["L0_ffn_g_routed"], 1024)
    for e in range(MOE_ES):
        add_vec(f"m_b1_s{e}", io["L0_ffn_b1_s"][e], 4096)
        add_vec(f"m_b2_s{e}", io["L0_ffn_b2_s"][e], 1024)
    for e in range(MOE_ER):
        add_vec(f"m_b1_r{e}", io["L0_ffn_b1_r"][e], 4096)
        add_vec(f"m_b2_r{e}", io["L0_ffn_b2_r"][e], 1024)
    add_vec("c_lng", io["L1_attn_ln_g"], 1024)
    add_vec("c_lnb", io["L1_attn_ln_b"], 1024)
    add_vec("c_bqkv", io["L1_attn_b_qkv"], 3072)
    add_vec("c_went", io["L1_attn_w_ent"][0], 1024)
    add_vec("c_bent", io["L1_attn_b_ent"], 1)
    add_vec("c_bout", io["L1_attn_b_out"], 1024)
    for l in range(2):
        add_vec(f"f_qs{l}", io["L1_ffn_q_scale"][l], 64)
        add_vec(f"f_ks{l}", io["L1_ffn_k_scale"][l], 64)
        add_vec(f"f_lng{l}", io["L1_ffn_ln_g"][l], 1024)
        add_vec(f"f_lnb{l}", io["L1_ffn_ln_b"][l], 1024)
        add_vec(f"f_b1a{l}", io["L1_ffn_b1"][l][0:2730], 2730)
        add_vec(f"f_b1g{l}", io["L1_ffn_b1"][l][2730:5460], 2730)
        add_vec(f"f_b2{l}", io["L1_ffn_b2"][l], 1024)
    reg["a_gbias"] = col
    gb_col = col
    col += 1
    ctx.bias_reg = reg
    assert col <= ctx.bias_pool.shape[1], (col, tuple(ctx.bias_pool.shape))

    @blk.sync
    def _(s):
        s.wait_ge(ctx.pool.sem, ctx.pool.v)
        s.dma_start(out=stage_row[0:1, 0:8],
                    in_=io["L0_attn_bbeta"].rearrange("(a b) -> a b", a=1)).then_inc(ctx.ld.sem, 16)
        s.dma_start(out=stage_row[0:1, 32:40],
                    in_=io["L0_attn_big"].rearrange("(a b) -> a b", a=1)).then_inc(ctx.ld.sem, 16)
        s.dma_start(out=stage_row[0:1, 64:72],
                    in_=io["L0_attn_bog"].rearrange("(a b) -> a b", a=1)).then_inc(ctx.ld.sem, 16)
    ctx.ld.inc(3)

    @blk.tensor
    def _(t):
        t.wait_ge(ctx.ld.sem, ctx.ld.v)
        t.wait_ge(ctx.pool.sem, ctx.pool.v)
        t.transpose(ctx.ps_a[:, 0:1], stage_row[0:1, 0:128],
                    ctx.ident_f32[0:1, 0:1]).then_inc(ctx.pe.sem, 1)
    ctx.pe.inc(1)

    @blk.vector
    def _(v):
        v.wait_ge(ctx.pe.sem, ctx.pe.v)
        v.tensor_copy(ctx.bias_pool[:, gb_col:gb_col + 1],
                      ctx.ps_a[:, 0:1]).then_inc(ctx.dve.sem, 1)
    ctx.dve.inc(1)

    group, used, groups = [], 0, []
    for s_ in specs:
        ncols = (s_[2] + 127) // 128
        if used + ncols > 32:
            groups.append(group)
            group, used = [], 0
        group.append((s_, used))
        used += ncols
    groups.append(group)

    for grp in groups:
        @blk.sync
        def _(s, grp=grp):
            s.wait_ge(ctx.dve.sem, ctx.dve.v)
            for (c0, ap_flat, length), off in grp:
                s.dma_start(out=stage_row[0:1, off * 128: off * 128 + length],
                            in_=ap_flat.rearrange("(a b) -> a b", a=1)
                            ).then_inc(ctx.ld.sem, 16)
        ctx.ld.inc(len(grp))

        ncols_g = sum((sp[2] + 127) // 128 for sp, _ in grp)

        @blk.tensor
        def _(t, ncols_g=ncols_g):
            t.wait_ge(ctx.ld.sem, ctx.ld.v)
            t.wait_ge(ctx.dve.sem, ctx.dve.v)
            last = None
            for k in range(ncols_g):
                last = t.transpose(ctx.ps_a[:, k:k + 1],
                                   stage_row[0:1, k * 128:(k + 1) * 128],
                                   ctx.ident_f32[0:1, 0:1])
            last.then_inc(ctx.pe.sem, 1)
        ctx.pe.inc(1)

        @blk.vector
        def _(v, grp=grp):
            v.wait_ge(ctx.pe.sem, ctx.pe.v)
            for n_, ((c0, ap_flat, length), off) in enumerate(grp):
                nch = (length + 127) // 128
                cc = v.tensor_copy(ctx.bias_pool[:, c0:c0 + nch],
                                   ctx.ps_a[:, off:off + nch])
                if n_ == len(grp) - 1:
                    cc.then_inc(ctx.dve.sem, 1)
        ctx.dve.inc(1)

    @blk.sync
    def _(s):
        s.wait_ge(ctx.pool.sem, ctx.pool.v)
        s.dma_start(out=wg_rows[0:8, :], in_=io["L0_attn_Wbeta"]).then_inc(ctx.ld.sem, 16)
        s.dma_start(out=wg_rows[32:40, :], in_=io["L0_attn_Wig"]).then_inc(ctx.ld.sem, 16)
        s.dma_start(out=wg_rows[64:72, :], in_=io["L0_attn_Wog"]).then_inc(ctx.ld.sem, 16)
        s.dma_start(out=ctx.cent_sb[:].rearrange("p (ch e) -> p ch e", ch=8),
                    in_=io["L0_ffn_centroids"].rearrange(
                        "(ch p) e -> p ch e", p=128)).then_inc(ctx.ld.sem, 16)
    ctx.ld.inc(4)

    nw = reg["a_norm_w"]
    for rnd in range(2):
        @blk.tensor
        def _(t, rnd=rnd):
            t.wait_ge(ctx.ld.sem, ctx.ld.v)
            t.wait_ge(ctx.dve.sem, ctx.dve.v)
            last = None
            for k in range(4):
                last = t.transpose(ctx.ps_a[:, k * 96:k * 96 + 96],
                                   wg_rows[0:96, (rnd * 4 + k) * 128:(rnd * 4 + k + 1) * 128],
                                   ctx.ident_f32[0:96, 0:96])
            last.then_inc(ctx.pe.sem, 1)
        ctx.pe.inc(1)

        @blk.vector
        def _(v, rnd=rnd):
            v.wait_ge(ctx.pe.sem, ctx.pe.v)
            for k in range(4):
                kk = rnd * 4 + k
                cc = v.tensor_scalar(ctx.wgA[:, kk * 96:(kk + 1) * 96],
                                     ctx.ps_a[:, k * 96:k * 96 + 96],
                                     ctx.bias_pool[:, nw + kk:nw + kk + 1], None, ALU.mult)
                if k == 3:
                    cc.then_inc(ctx.dve.sem, 1)
        ctx.dve.inc(1)

    @blk.tensor
    def _(t):
        t.wait_ge(ctx.dve.sem, ctx.dve.v)
        for k in range(8):
            mm = t.matmul(ctx.ps_a[0:96, 0:1], ctx.wgA[:, k * 96:(k + 1) * 96],
                          ctx.ones_f32[:, 0:1], start=(k == 0), stop=(k == 7))
        mm.then_inc(ctx.pe.sem, 1)
    ctx.pe.inc(1)

    @blk.vector
    def _(v):
        v.wait_ge(ctx.pe.sem, ctx.pe.v)
        v.tensor_copy(ctx.wg_colsum[0:96, 0:1], ctx.ps_a[0:96, 0:1])
        c = ctx.bias_reg["c_bout"]
        v.tensor_scalar(ctx.bias_pool[:, c:c + 8], ctx.bias_pool[:, c:c + 8],
                        0.1, None, ALU.mult)
        c = ctx.bias_reg["c_went"]
        v.tensor_copy(ctx.went_bf[:], ctx.bias_pool[:, c:c + 8]).then_inc(ctx.dve.sem, 1)
    ctx.dve.inc(1)

    wcv = stack.enter_context(nc.sbuf_tensor("wcv", [128, 2 * 4096], BF16))
    moe_list = ([("L0_ffn_W1_s", "L0_ffn_W2_s", "m_gs", MOE_ES, 0)] +
                [("L0_ffn_W1_r", "L0_ffn_W2_r", "m_gr", MOE_ER, MOE_ES)])
    cvt_jobs = []
    for w1n, w2n, gn, ne, base in moe_list:
        for e in range(ne):
            for ch in range(8):
                cvt_jobs.append((io[w1n][e][ch * 128:(ch + 1) * 128, :],
                                 io["w1bf"][base + e][ch * 128:(ch + 1) * 128, :],
                                 reg[gn] + ch, 4096))
            for hc in range(32):
                cvt_jobs.append((io[w2n][e][hc * 128:(hc + 1) * 128, :],
                                 io["w2bf"][base + e][hc * 128:(hc + 1) * 128, :],
                                 None, 1024))
    for l in range(2):
        for ch in range(8):
            cvt_jobs.append((io["L1_ffn_wqkv"][l][ch * 128:(ch + 1) * 128, :],
                             io["fqkv"][l][ch * 128:(ch + 1) * 128, :], None, 1536))
        for ch in range(4):
            cvt_jobs.append((io["L1_ffn_wo"][l][ch * 128:(ch + 1) * 128, :],
                             io["fwo"][l][ch * 128:(ch + 1) * 128, :], None, 1024))
        for ch in range(8):
            cvt_jobs.append((io["L1_ffn_w1"][l][ch * 128:(ch + 1) * 128, 0:2730],
                             io["fw1"][l][ch * 128:(ch + 1) * 128, 0:2730], None, 2730))
            cvt_jobs.append((io["L1_ffn_w1"][l][ch * 128:(ch + 1) * 128, 2730:5460],
                             io["fw1"][l][ch * 128:(ch + 1) * 128, 2730:5460], None, 2730))
        for ch in range(21):
            cvt_jobs.append((io["L1_ffn_w2"][l][ch * 128:(ch + 1) * 128, :],
                             io["fw2"][l][ch * 128:(ch + 1) * 128, :], None, 1024))
        cvt_jobs.append((io["L1_ffn_w2"][l][2688:2730, :],
                         io["fw2"][l][2688:2730, :], None, 1024))
    for j, (srcap, dstap, gcol, width) in enumerate(cvt_jobs):
        slot = j % 2
        nprt = srcap.shape[0]
        buf = wcv[0:nprt, slot * 4096: slot * 4096 + width]

        @blk.gpsimd
        def _(g, buf=buf, srcap=srcap):
            g.wait_ge(ctx.st.sem, max(ctx.st.v - 16, 0))
            g.dma_start(out=buf, in_=srcap).then_inc(ctx.lw.sem, 16)
        ctx.lw.inc(1)

        if gcol is not None:
            @blk.vector
            def _(v, buf=buf, gcol=gcol):
                v.wait_ge(ctx.lw.sem, ctx.lw.v)
                v.tensor_scalar(buf, buf, ctx.bias_pool[:, gcol:gcol + 1], None,
                                ALU.mult).then_inc(ctx.dve.sem, 1)
            ctx.dve.inc(1)

        @blk.scalar
        def _(s, buf=buf, dstap=dstap, gcol=gcol):
            if gcol is not None:
                s.wait_ge(ctx.dve.sem, ctx.dve.v)
            else:
                s.wait_ge(ctx.lw.sem, ctx.lw.v)
            s.dma_start(out=dstap, in_=buf).then_inc(ctx.st.sem, 16)
        ctx.st.inc(1)

    jobs = [
        (io["L0_attn_Wq"], io["wqT"], 1024, 1024, 1.0),
        (io["L0_attn_Wk"], io["wkT"], 1024, 1024, 1.0),
        (io["L0_attn_Wv"], io["wvT"], 1024, 1024, 1.0),
        (io["L0_attn_Wout"], io["woT"], 1024, 1024, 1.0),
        (io["L1_attn_w_qkv"], io["qkvT"], 3072, 1024, 1.0),
        (io["L1_attn_w_out"], io["woutT"], 1024, 1024, 0.1),
    ]
    for src, dst, R, C, scale in jobs:
        for rb in range(R // 128):
            for cb in range(C // 128):
                slot = (rb * (C // 128) + cb) % 2
                stg = wt_stage[:, slot * 128:(slot + 1) * 128]
                out_sb = wt_out[:, slot * 128:(slot + 1) * 128]
                psl = ctx.ps_tr[:, slot * 128:(slot + 1) * 128]

                @blk.gpsimd
                def _(g, stg=stg, src=src, rb=rb, cb=cb):
                    g.wait_ge(ctx.pe.sem, max(ctx.pe.v - 1, 0))
                    g.dma_start(out=stg,
                                in_=src[rb * 128:(rb + 1) * 128, cb * 128:(cb + 1) * 128]
                                ).then_inc(ctx.lw.sem, 16)
                ctx.lw.inc(1)

                @blk.tensor
                def _(t, psl=psl, stg=stg):
                    t.wait_ge(ctx.lw.sem, ctx.lw.v)
                    t.wait_ge(ctx.dve.sem, max(ctx.dve.v - 1, 0))
                    t.transpose(psl, stg, ctx.ident_bf[:]).then_inc(ctx.pe.sem, 1)
                ctx.pe.inc(1)

                @blk.vector
                def _(v, out_sb=out_sb, psl=psl, scale=scale):
                    v.wait_ge(ctx.pe.sem, ctx.pe.v)
                    v.wait_ge(ctx.st.sem, max(ctx.st.v - 16, 0))
                    if scale == 1.0:
                        v.tensor_copy(out_sb, psl).then_inc(ctx.dve.sem, 1)
                    else:
                        v.tensor_scalar(out_sb, psl, scale, None,
                                        ALU.mult).then_inc(ctx.dve.sem, 1)
                ctx.dve.inc(1)

                @blk.scalar
                def _(sc, dst=dst, rb=rb, cb=cb, out_sb=out_sb):
                    sc.wait_ge(ctx.dve.sem, ctx.dve.v)
                    sc.dma_start(out=dst[cb * 128:(cb + 1) * 128, rb * 128:(rb + 1) * 128],
                                 in_=out_sb).then_inc(ctx.st.sem, 16)
                ctx.st.inc(1)

    barrier(ctx)
    stack.close()



def emit_tile_stats(ctx, get_chunk, sq_scr, is_f32, extra_mm=None):
    blk = ctx.block
    ones = ctx.ones_f32 if is_f32 else ctx.ones_bf
    pe_hist = {}
    for ch in range(NCH):
        src = get_chunk(ch)
        slot = ch % 2
        scr = sq_scr[:, slot * 512:(slot + 1) * 512]
        wait_pe = pe_hist.get(ch - 2, 0)

        @blk.scalar
        def _(s, src=src, scr=scr, wait_pe=wait_pe, ch=ch):
            if ch == 0:
                s.wait_ge(ctx.dve.sem, ctx.dve.v)
            if wait_pe:
                s.wait_ge(ctx.pe.sem, wait_pe)
            s.activation(scr, src, AF.Square).then_inc(ctx.act.sem, 1)
        ctx.act.inc(1)

        @blk.tensor
        def _(t, src=src, scr=scr, ch=ch):
            t.wait_ge(ctx.act.sem, ctx.act.v)
            if ch == 0:
                t.wait_ge(ctx.dve.sem, ctx.dve.v)
            t.matmul(ctx.ps_b[0:1, 0:512], ones[:, 0:1], src,
                     start=(ch == 0), stop=(ch == NCH - 1))
            mm = t.matmul(ctx.ps_c[0:1, 0:512], ones[:, 0:1], scr,
                          start=(ch == 0), stop=(ch == NCH - 1))
            if extra_mm is not None:
                mm = extra_mm(t, ch) or mm
            mm.then_inc(ctx.pe.sem, 1)
        ctx.pe.inc(1)
        pe_hist[ch] = ctx.pe.v


def emit_tile_rows(ctx, eps):
    blk = ctx.block

    @blk.vector
    def _(v):
        v.wait_ge(ctx.pe.sem, ctx.pe.v)
        v.tensor_scalar(ctx.row_a[0:1, :], ctx.ps_b[0:1, 0:512], 1.0 / D, None, ALU.mult)
        v.tensor_scalar(ctx.row_b[0:1, :], ctx.ps_c[0:1, 0:512], 1.0 / D, None, ALU.mult)
        v.tensor_tensor(ctx.row_d[0:1, :], ctx.row_a[0:1, :], ctx.row_a[0:1, :], ALU.mult)
        v.tensor_tensor(ctx.row_b[0:1, :], ctx.row_b[0:1, :], ctx.row_d[0:1, :], ALU.subtract)
        v.tensor_scalar(ctx.row_c[0:1, :], ctx.row_b[0:1, :], eps, None,
                        ALU.add).then_inc(ctx.dve.sem, 1)
    ctx.dve.inc(1)

    @blk.scalar
    def _(s):
        s.wait_ge(ctx.dve.sem, ctx.dve.v)
        s.activation(ctx.row_c[0:1, :], ctx.row_c[0:1, :], AF.Sqrt).then_inc(ctx.act.sem, 1)
    ctx.act.inc(1)

    @blk.vector
    def _(v):
        v.wait_ge(ctx.act.sem, ctx.act.v)
        v.reciprocal(ctx.row_c[0:1, :], ctx.row_c[0:1, :]).then_inc(ctx.dve.sem, 1)
    ctx.dve.inc(1)


def emit_tile_bcast(ctx, rows_out):
    blk = ctx.block

    @blk.tensor
    def _(t):
        t.wait_ge(ctx.dve.sem, ctx.dve.v)
        mm = None
        for i, (row, out) in enumerate(rows_out):
            psx = (ctx.ps_b, ctx.ps_c)[i % 2]
            mm = t.matmul(psx[:, 0:512], ctx.ones_f32[0:1, :], row[0:1, :],
                          start=True, stop=True)
        mm.then_inc(ctx.pe.sem, 1)
    ctx.pe.inc(1)

    @blk.vector
    def _(v):
        v.wait_ge(ctx.pe.sem, ctx.pe.v)
        cc = None
        for i, (row, out) in enumerate(rows_out):
            psx = (ctx.ps_b, ctx.ps_c)[i % 2]
            cc = v.tensor_copy(out[:, 0:512], psx[:, 0:512])
        cc.then_inc(ctx.dve.sem, 1)
    ctx.dve.inc(1)



def emit_phase_a(ctx, io, b, src="x", dst="x_cur"):
    nc, blk, T = ctx.nc, ctx.block, ctx.T
    NT = T // 512
    NCK = T // A_CHUNK
    stack = contextlib.ExitStack()

    def sb(name, shape, dt):
        return stack.enter_context(nc.sbuf_tensor(f"a{b}_{name}", shape, dt))

    h = sb("h", [128, NCH * T], BF16)
    yall = sb("yall", [128, NCH * T], BF16)
    x_it = sb("x_it", [128, NCH * 512], F32)
    gates = sb("gates", [96, T], F32)
    lcum = sb("lcum", [8, T], F32)
    ogr = sb("ogr", [8, T], F32)
    ones8 = sb("ones8", [8, T], F32)
    ltm = sb("ltm", [128, NCK * 24], F32)
    lnig = sb("lnig", [8, T], F32)
    qh = sb("qh", [128, T], BF16)
    kh = sb("kh", [128, T], BF16)
    vh = sb("vh", [128, T], BF16)
    ropec = sb("ropec", [128, T], BF16)
    ropes = sb("ropes", [128, T], BF16)
    rsc1 = sb("rsc1", [128, T], BF16)
    rsc2 = sb("rsc2", [128, T], BF16)
    sqf = sb("sqf", [128, 1024], F32)
    sqb = sb("sqb", [128, 1024], BF16)
    whb = sb("whb", [128, 3 * 1024], BF16)
    wob = sb("wob", [128, 1024], BF16)
    m_f32 = sb("m_f32", [128, 128], F32)
    m_bf = sb("m_bf", [128, 128], BF16)
    cols = sb("cols", [128, 8], F32)
    ogb = sb("ogb", [128, 128], F32)
    darg = sb("darg", [128, 128], F32)
    aarg = sb("aarg", [128, 128], F32)
    aexp = sb("aexp", [128, 128], F32)
    pt_bf = sb("pt_bf", [128, 128], BF16)
    kt_bf = sb("kt_bf", [128, 128], BF16)
    vt_bf = sb("vt_bf", [128, 128], BF16)
    scr128 = sb("scr128", [128, 128], F32)
    ctx.row_a = sb("row_a", [1, 512], F32)
    ctx.row_b = sb("row_b", [1, 512], F32)
    ctx.row_c = sb("row_c", [1, 512], F32)
    ctx.row_d = sb("row_d", [1, 512], F32)
    mu_bt = sb("mu_bt", [128, 512], F32)
    rho_bt = sb("rho_bt", [128, 512], F32)
    scr_t = sb("scr_t", [128, 512], F32)

    @blk.gpsimd
    def _(g):
        g.memset(ones8[:], 1.0).then_inc(ctx.pool.sem, 1)
    ctx.pool.inc(1)

    gb = ctx.bias_reg["a_gbias"]
    nw = ctx.bias_reg["a_norm_w"]

    for it in range(NT):
        tsl = slice(it * 512, (it + 1) * 512)

        @blk.sync
        def _(s, tsl=tsl):
            s.wait_ge(ctx.dve.sem, ctx.dve.v)
            for ch in range(NCH):
                s.dma_start(out=x_it[:, ch * 512:(ch + 1) * 512],
                            in_=io[src][b, ch * 128:(ch + 1) * 128, tsl]
                            ).then_inc(ctx.ld.sem, 16)
        ctx.ld.inc(NCH)

        @blk.scalar
        def _(s):
            s.wait_ge(ctx.ld.sem, ctx.ld.v)
            s.sem_inc(ctx.act.sem, 1)
        ctx.act.inc(1)

        @blk.tensor
        def _(t):
            t.wait_ge(ctx.ld.sem, ctx.ld.v)
            t.sem_inc(ctx.pe.sem, 1)
        ctx.pe.inc(1)

        def gate_mm(t, ch):
            return t.matmul(ctx.ps_a[0:96, 0:512], ctx.wgA[:, ch * 96:(ch + 1) * 96],
                            x_it[:, ch * 512:(ch + 1) * 512],
                            start=(ch == 0), stop=(ch == NCH - 1))
        emit_tile_stats(ctx, lambda ch: x_it[:, ch * 512:(ch + 1) * 512],
                        sqf, is_f32=True, extra_mm=gate_mm)

        @blk.vector
        def _(v):
            v.wait_ge(ctx.pe.sem, ctx.pe.v)
            v.tensor_scalar(ctx.row_a[0:1, :], ctx.ps_b[0:1, 0:512], 1.0 / D, None, ALU.mult)
            v.tensor_scalar(ctx.row_b[0:1, :], ctx.ps_c[0:1, 0:512], 1.0 / D, None, ALU.mult)
            v.tensor_tensor(ctx.row_d[0:1, :], ctx.row_a[0:1, :], ctx.row_a[0:1, :], ALU.mult)
            v.tensor_tensor(ctx.row_b[0:1, :], ctx.row_b[0:1, :], ctx.row_d[0:1, :], ALU.subtract)
            v.tensor_scalar(ctx.row_c[0:1, :], ctx.row_b[0:1, :], 1e-5, None, ALU.add)
            v.reciprocal(ctx.row_d[0:1, :], ctx.row_c[0:1, :])
            v.tensor_tensor(ctx.row_d[0:1, :], ctx.row_b[0:1, :], ctx.row_d[0:1, :], ALU.mult)
            v.tensor_scalar(ctx.row_d[0:1, :], ctx.row_d[0:1, :], 1e-6, None, ALU.add)
            v.tensor_tensor(ctx.row_c[0:1, :], ctx.row_c[0:1, :], ctx.row_d[0:1, :],
                            ALU.mult).then_inc(ctx.dve.sem, 1)
        ctx.dve.inc(1)

        @blk.scalar
        def _(s):
            s.wait_ge(ctx.dve.sem, ctx.dve.v)
            s.activation(ctx.row_c[0:1, :], ctx.row_c[0:1, :], AF.Sqrt).then_inc(ctx.act.sem, 1)
        ctx.act.inc(1)

        @blk.vector
        def _(v):
            v.wait_ge(ctx.act.sem, ctx.act.v)
            v.reciprocal(ctx.row_c[0:1, :], ctx.row_c[0:1, :]).then_inc(ctx.dve.sem, 1)
        ctx.dve.inc(1)

        emit_tile_bcast(ctx, [(ctx.row_a, mu_bt), (ctx.row_c, rho_bt)])

        @blk.vector
        def _(v):
            v.scalar_tensor_tensor(scr_t[0:96, :], mu_bt[0:96, 0:512],
                                   ctx.wg_colsum[0:96, 0:1], ctx.ps_a[0:96, 0:512],
                                   op0=ALU.mult, op1=ALU.subtract)
            v.tensor_tensor(scr_t[0:96, :], scr_t[0:96, :], rho_bt[0:96, 0:512],
                            ALU.mult).then_inc(ctx.dve.sem, 1)
        ctx.dve.inc(1)

        @blk.scalar
        def _(s, tsl=tsl):
            s.wait_ge(ctx.dve.sem, ctx.dve.v)
            s.activation(gates[0:96, tsl], scr_t[0:96, :], AF.Sigmoid,
                         bias=ctx.bias_pool[0:96, gb:gb + 1],
                         scale=-1.0).then_inc(ctx.act.sem, 1)
        ctx.act.inc(1)

        for ch in range(NCH):
            @blk.vector
            def _(v, ch=ch, it=it):
                if ch == 0:
                    v.wait_ge(ctx.act.sem, ctx.act.v)
                v.tensor_tensor(scr_t[:, 0:512], x_it[:, ch * 512:(ch + 1) * 512],
                                mu_bt[:, 0:512], ALU.subtract)
                v.scalar_tensor_tensor(h[:, ch * T + it * 512: ch * T + (it + 1) * 512],
                                       scr_t[:, 0:512],
                                       ctx.bias_pool[:, nw + ch:nw + ch + 1],
                                       rho_bt[:, 0:512],
                                       op0=ALU.mult, op1=ALU.mult).then_inc(ctx.dve.sem, 1)
            ctx.dve.inc(1)

    @blk.vector
    def _(v):
        v.wait_ge(ctx.act.sem, ctx.act.v)
        v.wait_ge(ctx.pool.sem, ctx.pool.v)
        v.tensor_scalar(lcum[0:8, :T], gates[0:8, :T], 1e-15, None,
                        ALU.max).then_inc(ctx.dve.sem, 1)
    ctx.dve.inc(1)

    @blk.scalar
    def _(s):
        s.wait_ge(ctx.dve.sem, ctx.dve.v)
        s.activation(lcum[0:8, :T], lcum[0:8, :T], AF.Ln).then_inc(ctx.act.sem, 1)
    ctx.act.inc(1)

    @blk.vector
    def _(v):
        v.wait_ge(ctx.act.sem, ctx.act.v)
        v.tensor_tensor_scan(lcum[0:8, :T], ones8[0:8, :T], lcum[0:8, :T], 0.0,
                             op0=ALU.mult, op1=ALU.add).then_inc(ctx.dve.sem, 1)
    ctx.dve.inc(1)

    @blk.gpsimd
    def _(g):
        g.wait_ge(ctx.act.sem, ctx.act.v)
        g.dma_start(out=ogr[0:8, :T], in_=gates[64:72, :T]).then_inc(ctx.lw.sem, 16)
    ctx.lw.inc(1)

    @blk.scalar
    def _(s):
        s.activation(lnig[0:8, :T], gates[32:40, :T], AF.Ln).then_inc(ctx.act.sem, 1)
    ctx.act.inc(1)

    @blk.vector
    def _(v):
        v.wait_ge(ctx.act.sem, ctx.act.v)
        v.tensor_scalar(lnig[0:8, :T], lnig[0:8, :T], LN_A_SCALE, None,
                        ALU.add).then_inc(ctx.dve.sem, 1)
    ctx.dve.inc(1)

    for ck in range(NCK):
        bank_pos = ck % 21

        @blk.tensor
        def _(t, ck=ck, bank_pos=bank_pos):
            t.wait_ge(ctx.dve.sem, ctx.dve.v)
            t.transpose(ctx.ps_b[:, bank_pos * 24:bank_pos * 24 + 8],
                        lcum[0:8, ck * 128:(ck + 1) * 128], ctx.ident_f32[0:8, 0:8])
            t.transpose(ctx.ps_b[:, bank_pos * 24 + 8:bank_pos * 24 + 16],
                        gates[32:40, ck * 128:(ck + 1) * 128],
                        ctx.ident_f32[32:40, 32:40])
            t.transpose(ctx.ps_b[:, bank_pos * 24 + 16:bank_pos * 24 + 24],
                        lnig[0:8, ck * 128:(ck + 1) * 128],
                        ctx.ident_f32[0:8, 0:8]).then_inc(ctx.pe.sem, 1)
        ctx.pe.inc(1)

        if bank_pos == 20 or ck == NCK - 1:
            n_in = bank_pos + 1
            c0 = ck - n_in + 1

            @blk.vector
            def _(v, n_in=n_in, c0=c0):
                v.wait_ge(ctx.pe.sem, ctx.pe.v)
                v.tensor_copy(ltm[:, c0 * 24:(c0 + n_in) * 24],
                              ctx.ps_b[:, 0:n_in * 24]).then_inc(ctx.dve.sem, 1)
            ctx.dve.inc(1)

    @blk.sync
    def _(s):
        s.dma_start(out=ropec[:, :T], in_=io["rope0_cos"]).then_inc(ctx.ld.sem, 16)
        s.dma_start(out=ropes[:, :T], in_=io["rope0_sin"]).then_inc(ctx.ld.sem, 16)
    ctx.ld.inc(2)

    if getattr(ctx, "debug", False) and b == 0:
        @blk.scalar
        def _(s):
            s.wait_ge(ctx.dve.sem, ctx.dve.v)
            s.wait_ge(ctx.act.sem, ctx.act.v)
            s.dma_start(out=io["dbg_gates"], in_=gates[0:96, :T]).then_inc(ctx.st.sem, 16)
            s.dma_start(out=io["dbg_lcum"], in_=lcum[0:8, :T]).then_inc(ctx.st.sem, 16)
            s.dma_start(out=io["dbg_ltm"], in_=ltm[:]).then_inc(ctx.st.sem, 16)
            s.dma_start(out=io["dbg_h"], in_=h[:]).then_inc(ctx.st.sem, 16)
        ctx.st.inc(4)

    whb_free = {}
    for head in range(A_NH):
        @blk.sync
        def _(s, head=head):
            if whb_free.get(head - 1):
                s.wait_ge(ctx.pe.sem, whb_free[head - 1])
            else:
                s.wait_ge(ctx.dve.sem, ctx.dve.v)
            for wi, name in enumerate(("wqT", "wkT", "wvT")):
                s.dma_start(
                    out=whb[:, wi * 1024:(wi + 1) * 1024].rearrange(
                        "p (c e) -> p c e", e=128),
                    in_=io[name].rearrange("(c p) e -> p c e", p=128)[
                        :, :, head * 128:(head + 1) * 128]).then_inc(ctx.ld.sem, 16)
        ctx.ld.inc(3)

        for wi, dstbuf in ((0, qh), (1, kh), (2, vh)):
            for it in range(NT):
                @blk.tensor
                def _(t, wi=wi, it=it):
                    t.wait_ge(ctx.ld.sem, ctx.ld.v)
                    t.wait_ge(ctx.dve.sem, ctx.dve.v)
                    for ch in range(NCH):
                        mm = t.matmul(ctx.ps_a[:, 0:512],
                                      whb[:, wi * 1024 + ch * 128: wi * 1024 + (ch + 1) * 128],
                                      h[:, ch * T + it * 512: ch * T + (it + 1) * 512],
                                      start=(ch == 0), stop=(ch == NCH - 1))
                    mm.then_inc(ctx.pe.sem, 1)
                ctx.pe.inc(1)

                @blk.vector
                def _(v, dstbuf=dstbuf, it=it):
                    v.wait_ge(ctx.pe.sem, ctx.pe.v)
                    v.tensor_copy(dstbuf[:, it * 512:(it + 1) * 512],
                                  ctx.ps_a[:, 0:512]).then_inc(ctx.dve.sem, 1)
                ctx.dve.inc(1)

        whb_free[head] = ctx.pe.v

        for buf in (qh, kh):
            @blk.vector
            def _(v, buf=buf):
                v.tensor_tensor(rsc2[0:64, :T], buf[64:128, :T], ropes[64:128, :T], ALU.mult)
                v.tensor_tensor(rsc1[64:128, :T], buf[0:64, :T], ropes[0:64, :T], ALU.mult)
                v.tensor_tensor(buf[0:64, :T], buf[0:64, :T], ropec[0:64, :T], ALU.mult)
                v.tensor_tensor(buf[64:128, :T], buf[64:128, :T], ropec[64:128, :T], ALU.mult)
                v.tensor_tensor(buf[0:64, :T], buf[0:64, :T], rsc2[0:64, :T], ALU.subtract)
                v.tensor_tensor(buf[64:128, :T], buf[64:128, :T], rsc1[64:128, :T],
                                ALU.add).then_inc(ctx.dve.sem, 1)
            ctx.dve.inc(1)

        @blk.vector
        def _(v):
            v.memset(m_f32[:], 0.0)
            v.memset(m_bf[:], 0.0).then_inc(ctx.dve.sem, 1)
        ctx.dve.inc(1)

        for ck in range(NCK):
            c0 = ck * 128
            csl = slice(c0, c0 + 128)
            lcol = ltm[:, ck * 24 + head: ck * 24 + head + 1]
            igcol = ltm[:, ck * 24 + 8 + head: ck * 24 + 8 + head + 1]
            lnigcol = ltm[:, ck * 24 + 16 + head: ck * 24 + 16 + head + 1]
            sel = ctx.sel8[:, head * 128:(head + 1) * 128]

            @blk.tensor
            def _(t, ck=ck, c0=c0, csl=csl, sel=sel):
                t.wait_ge(ctx.dve.sem, ctx.dve.v)
                t.wait_ge(ctx.lw.sem, ctx.lw.v)
                t.matmul(ctx.ps_misc[:, 0:128], sel, lcum[0:8, csl],
                         start=True, stop=True)
                t.matmul(ctx.ps_misc[:, 128:256], sel, ogr[0:8, csl],
                         start=True, stop=True)
                if ck > 0:
                    t.matmul(ctx.ps_misc[:, 256:257], sel, lcum[0:8, c0 - 1:c0],
                             start=True, stop=True)
                t.matmul(ctx.ps_misc[:, 257:258], sel, lcum[0:8, c0 + 127:c0 + 128],
                         start=True, stop=True)
                t.matmul(ctx.ps_st[:, 0:128], kh[:, csl], qh[:, csl],
                         start=True, stop=True)
                t.transpose(ctx.ps_kv[:, 0:128], kh[:, csl], ctx.ident_bf[:])
                t.transpose(ctx.ps_kv[:, 128:256], vh[:, csl],
                            ctx.ident_bf[:]).then_inc(ctx.pe.sem, 1)
            ctx.pe.inc(1)

            @blk.vector
            def _(v, ck=ck):
                v.wait_ge(ctx.pe.sem, ctx.pe.v)
                if ck > 0:
                    v.tensor_copy(cols[:, 0:2], ctx.ps_misc[:, 256:258])
                else:
                    v.tensor_copy(cols[:, 0:1], ctx.zero_col[:, 0:1])
                    v.tensor_copy(cols[:, 1:2], ctx.ps_misc[:, 257:258])
                v.tensor_copy(ogb[:, 0:128], ctx.ps_misc[:, 128:256]).then_inc(ctx.dve.sem, 1)
            ctx.dve.inc(1)

            @blk.vector
            def _(v, lcol=lcol):
                v.scalar_tensor_tensor(darg[:, 0:128], ctx.ps_misc[:, 0:128],
                                       lcol, ctx.mask_up[:],
                                       op0=ALU.subtract, op1=ALU.add)
                v.tensor_scalar(aarg[:, 0:128], ctx.ps_misc[:, 0:128],
                                cols[:, 0:1], None, ALU.subtract)
                v.tensor_tensor(cols[:, 3:4], cols[:, 1:2], lcol, ALU.subtract)
                v.tensor_tensor(cols[:, 4:5], cols[:, 1:2], cols[:, 0:1],
                                ALU.subtract).then_inc(ctx.dve.sem, 1)
            ctx.dve.inc(1)

            @blk.scalar
            def _(s, lnigcol=lnigcol):
                s.wait_ge(ctx.dve.sem, ctx.dve.v)
                s.activation(darg[:, 0:128], darg[:, 0:128], AF.Exp,
                             bias=ctx.zero_col[:, 1:2])
                s.activation(aexp[:, 0:128], aarg[:, 0:128], AF.Exp)
                s.activation(cols[:, 5:6], cols[:, 3:4], AF.Exp, bias=lnigcol)
                s.activation(cols[:, 7:8], cols[:, 4:5], AF.Exp).then_inc(ctx.act.sem, 1)
            ctx.act.inc(1)

            @blk.vector
            def _(v, igcol=igcol):
                v.wait_ge(ctx.act.sem, ctx.act.v)
                v.scalar_tensor_tensor(pt_bf[:, 0:128], ctx.ps_st[:, 0:128],
                                       igcol, darg[:, 0:128],
                                       op0=ALU.mult, op1=ALU.mult)
                v.tensor_scalar(kt_bf[:, 0:128], ctx.ps_kv[:, 0:128],
                                cols[:, 5:6], None, ALU.mult)
                v.tensor_copy(vt_bf[:, 0:128], ctx.ps_kv[:, 128:256]).then_inc(ctx.dve.sem, 1)
            ctx.dve.inc(1)

            @blk.tensor
            def _(t, csl=csl):
                t.wait_ge(ctx.dve.sem, ctx.dve.v)
                t.matmul(ctx.ps_y3[:, 0:128], vt_bf[:, 0:128], pt_bf[:, 0:128],
                         start=True, stop=True)
                t.matmul(ctx.ps_y3[:, 128:256], m_bf[:], qh[:, csl],
                         start=True, stop=True)
                t.matmul(ctx.ps_y3[:, 256:384], kt_bf[:, 0:128], vt_bf[:, 0:128],
                         start=True, stop=True).then_inc(ctx.pe.sem, 1)
            ctx.pe.inc(1)

            @blk.vector
            def _(v, head=head, c0=c0):
                v.wait_ge(ctx.pe.sem, ctx.pe.v)
                v.tensor_tensor(scr128[:, 0:128], ctx.ps_y3[:, 128:256],
                                aexp[:, 0:128], ALU.mult)
                v.tensor_tensor(scr128[:, 0:128], scr128[:, 0:128],
                                ctx.ps_y3[:, 0:128], ALU.add)
                v.tensor_tensor(yall[:, head * T + c0: head * T + c0 + 128],
                                scr128[:, 0:128], ogb[:, 0:128], ALU.mult)
                v.tensor_scalar(scr128[:, 0:128], m_f32[:], cols[:, 7:8],
                                None, ALU.mult)
                v.tensor_tensor(m_f32[:], scr128[:, 0:128],
                                ctx.ps_y3[:, 256:384], ALU.add)
                v.tensor_copy(m_bf[:], m_f32[:]).then_inc(ctx.dve.sem, 1)
            ctx.dve.inc(1)

    if getattr(ctx, "debug", False) and b == 0:
        @blk.scalar
        def _(s):
            s.wait_ge(ctx.dve.sem, ctx.dve.v)
            s.dma_start(out=io["dbg_yall"], in_=yall[:]).then_inc(ctx.st.sem, 16)
            s.dma_start(out=io["dbg_qh"], in_=qh[:, :T]).then_inc(ctx.st.sem, 16)
        ctx.st.inc(2)

    attn = h
    bout = ctx.bias_reg["a_bout"]
    wob_free = {}
    for ch in range(NCH):
        @blk.sync
        def _(s, ch=ch):
            if wob_free.get(ch - 1):
                s.wait_ge(ctx.pe.sem, wob_free[ch - 1])
            else:
                s.wait_ge(ctx.dve.sem, ctx.dve.v)
            s.dma_start(
                out=wob[:].rearrange("p (e d) -> p e d", d=128),
                in_=io["woT"].rearrange("(e p) d -> p e d", p=128)[
                    :, :, ch * 128:(ch + 1) * 128]).then_inc(ctx.ld.sem, 16)
        ctx.ld.inc(1)

        for it in range(NT):
            @blk.tensor
            def _(t, ch=ch, it=it):
                t.wait_ge(ctx.ld.sem, ctx.ld.v)
                t.wait_ge(ctx.dve.sem, ctx.dve.v)
                for e in range(NCH):
                    mm = t.matmul(ctx.ps_a[:, 0:512],
                                  wob[:, e * 128:(e + 1) * 128],
                                  yall[:, e * T + it * 512: e * T + (it + 1) * 512],
                                  start=(e == 0), stop=(e == NCH - 1))
                mm.then_inc(ctx.pe.sem, 1)
            ctx.pe.inc(1)

            @blk.vector
            def _(v, ch=ch, it=it):
                v.wait_ge(ctx.pe.sem, ctx.pe.v)
                v.tensor_scalar(attn[:, ch * T + it * 512: ch * T + (it + 1) * 512],
                                ctx.ps_a[:, 0:512],
                                ctx.bias_pool[:, bout + ch:bout + ch + 1], None,
                                ALU.add).then_inc(ctx.dve.sem, 1)
            ctx.dve.inc(1)

    if getattr(ctx, "debug", False) and b == 0:
        @blk.scalar
        def _(s):
            s.wait_ge(ctx.dve.sem, ctx.dve.v)
            s.dma_start(out=io["dbg_attn"], in_=attn[:]).then_inc(ctx.st.sem, 16)
        ctx.st.inc(1)

    for it in range(NT):
        tsl = slice(it * 512, (it + 1) * 512)
        emit_tile_stats(ctx,
                        lambda ch, it=it: attn[:, ch * T + it * 512: ch * T + (it + 1) * 512],
                        sqb, is_f32=False)
        emit_tile_rows(ctx, 1e-5)
        emit_tile_bcast(ctx, [(ctx.row_a, mu_bt), (ctx.row_c, rho_bt)])

        @blk.sync
        def _(s, tsl=tsl):
            s.wait_ge(ctx.dve.sem, ctx.dve.v)
            for ch in range(NCH):
                s.dma_start(out=x_it[:, ch * 512:(ch + 1) * 512],
                            in_=io[src][b, ch * 128:(ch + 1) * 128, tsl]
                            ).then_inc(ctx.ld.sem, 16)
        ctx.ld.inc(NCH)

        for ch in range(NCH):
            @blk.vector
            def _(v, ch=ch, it=it):
                if ch == 0:
                    v.wait_ge(ctx.ld.sem, ctx.ld.v)
                v.tensor_tensor(scr_t[:, 0:512],
                                attn[:, ch * T + it * 512: ch * T + (it + 1) * 512],
                                mu_bt[:, 0:512], ALU.subtract)
                v.tensor_tensor(scr_t[:, 0:512], scr_t[:, 0:512], rho_bt[:, 0:512], ALU.mult)
                v.tensor_tensor(x_it[:, ch * 512:(ch + 1) * 512],
                                x_it[:, ch * 512:(ch + 1) * 512],
                                scr_t[:, 0:512], ALU.add).then_inc(ctx.dve.sem, 1)
            ctx.dve.inc(1)

        @blk.scalar
        def _(s, tsl=tsl):
            s.wait_ge(ctx.dve.sem, ctx.dve.v)
            for ch in range(NCH):
                s.dma_start(out=io[dst][b, ch * 128:(ch + 1) * 128, tsl],
                            in_=x_it[:, ch * 512:(ch + 1) * 512]).then_inc(ctx.st.sem, 16)
        ctx.st.inc(NCH)

    barrier(ctx)
    stack.close()


def emit_phase_b(ctx, io, b, src="x_cur", dst="x_cur"):
    nc, blk, T = ctx.nc, ctx.block, ctx.T
    NT = T // 512
    NTC = T // 128
    EPSF = 1.1920929e-07
    stack = contextlib.ExitStack()

    def sb(name, shape, dt):
        return stack.enter_context(nc.sbuf_tensor(f"b{b}_{name}", shape, dt))

    acc = sb("acc", [128, NCH * T], F32)
    u_bf = sb("u_bf", [128, NCH * T], BF16)
    act = sb("act", [128, 8 * T], BF16)
    w1b = sb("w1b", [128, 2 * 1024], BF16)
    w2q = sb("w2q", [128, 8 * 1024], BF16)
    gbc_t = sb("gbc_t", [128, 512], F32)
    gate_em = sb("gate_em", [8, T], F32)
    scores = sb("scores", [128, NTC * 8], F32)
    gv_tm = sb("gv_tm", [128, NTC * 8], F32)
    x_it = sb("x_it", [128, NCH * 512], F32)
    sqf = sb("sqf", [128, 1024], F32)
    ctx.row_a = sb("row_a", [1, 512], F32)
    ctx.row_b = sb("row_b", [1, 512], F32)
    ctx.row_c = sb("row_c", [1, 512], F32)
    ctx.row_d = sb("row_d", [1, 512], F32)
    mu_bt = sb("mu_bt", [128, 512], F32)
    rho_bt = sb("rho_bt", [128, 512], F32)
    rho2_bt = sb("rho2_bt", [128, 512], F32)
    scr_t = sb("scr_t", [128, 512], F32)
    col_t = sb("col_t", [128, 4 * (T // 128)], F32)

    for it in range(NT):
        tsl = slice(it * 512, (it + 1) * 512)

        @blk.sync
        def _(s, tsl=tsl):
            s.wait_ge(ctx.dve.sem, ctx.dve.v)
            for ch in range(NCH):
                s.dma_start(out=x_it[:, ch * 512:(ch + 1) * 512],
                            in_=io[src][b, ch * 128:(ch + 1) * 128, tsl]
                            ).then_inc(ctx.ld.sem, 16)
        ctx.ld.inc(NCH)

        @blk.scalar
        def _(s):
            s.wait_ge(ctx.ld.sem, ctx.ld.v)
            s.sem_inc(ctx.act.sem, 1)
        ctx.act.inc(1)

        emit_tile_stats(ctx, lambda ch: x_it[:, ch * 512:(ch + 1) * 512],
                        sqf, is_f32=True)

        @blk.vector
        def _(v):
            v.wait_ge(ctx.pe.sem, ctx.pe.v)
            v.tensor_scalar(ctx.row_a[0:1, :], ctx.ps_b[0:1, 0:512], 1.0 / D, None, ALU.mult)
            v.tensor_scalar(ctx.row_b[0:1, :], ctx.ps_c[0:1, 0:512], 1.0 / D, None, ALU.mult)
            v.tensor_tensor(ctx.row_d[0:1, :], ctx.row_a[0:1, :], ctx.row_a[0:1, :], ALU.mult)
            v.tensor_tensor(ctx.row_b[0:1, :], ctx.row_b[0:1, :], ctx.row_d[0:1, :], ALU.subtract)
            v.tensor_scalar(ctx.row_c[0:1, :], ctx.row_b[0:1, :], 1e-5, None, ALU.add)
            v.reciprocal(ctx.row_d[0:1, :], ctx.row_c[0:1, :])
            v.tensor_tensor(ctx.row_d[0:1, :], ctx.row_b[0:1, :], ctx.row_d[0:1, :], ALU.mult)
            v.tensor_scalar(ctx.row_d[0:1, :], ctx.row_d[0:1, :], EPSF, None,
                            ALU.add).then_inc(ctx.dve.sem, 1)
        ctx.dve.inc(1)

        @blk.scalar
        def _(s):
            s.wait_ge(ctx.dve.sem, ctx.dve.v)
            s.activation(ctx.row_c[0:1, :], ctx.row_c[0:1, :], AF.Sqrt)
            s.activation(ctx.row_d[0:1, :], ctx.row_d[0:1, :], AF.Sqrt).then_inc(ctx.act.sem, 1)
        ctx.act.inc(1)

        @blk.vector
        def _(v):
            v.wait_ge(ctx.act.sem, ctx.act.v)
            v.reciprocal(ctx.row_c[0:1, :], ctx.row_c[0:1, :])
            v.reciprocal(ctx.row_d[0:1, :], ctx.row_d[0:1, :]).then_inc(ctx.dve.sem, 1)
        ctx.dve.inc(1)

        emit_tile_bcast(ctx, [(ctx.row_a, mu_bt), (ctx.row_c, rho_bt)])
        emit_tile_bcast(ctx, [(ctx.row_d, rho2_bt)])

        for ch in range(NCH):
            @blk.vector
            def _(v, ch=ch, it=it):
                v.tensor_tensor(scr_t[:, 0:512], x_it[:, ch * 512:(ch + 1) * 512],
                                mu_bt[:, 0:512], ALU.subtract)
                v.tensor_tensor(acc[:, ch * T + it * 512: ch * T + (it + 1) * 512],
                                scr_t[:, 0:512], rho_bt[:, 0:512], ALU.mult)
                v.tensor_tensor(u_bf[:, ch * T + it * 512: ch * T + (it + 1) * 512],
                                acc[:, ch * T + it * 512: ch * T + (it + 1) * 512],
                                rho2_bt[:, 0:512], ALU.mult).then_inc(ctx.dve.sem, 1)
            ctx.dve.inc(1)

    for tc in range(NTC):
        @blk.tensor
        def _(t, tc=tc):
            t.wait_ge(ctx.dve.sem, ctx.dve.v)
            t.wait_ge(ctx.act.sem, ctx.act.v)
            for ch in range(NCH):
                mm = t.matmul(ctx.ps_misc[0:128, 0:8],
                              acc[:, ch * T + tc * 128: ch * T + (tc + 1) * 128],
                              ctx.cent_sb[:, ch * 8:(ch + 1) * 8],
                              start=(ch == 0), stop=(ch == NCH - 1))
            mm.then_inc(ctx.pe.sem, 1)
        ctx.pe.inc(1)

        @blk.scalar
        def _(s, tc=tc):
            s.wait_ge(ctx.pe.sem, ctx.pe.v)
            s.activation(scores[:, tc * 8:(tc + 1) * 8], ctx.ps_misc[0:128, 0:8],
                         AF.Sigmoid).then_inc(ctx.act.sem, 1)
        ctx.act.inc(1)

    ct = col_t

    @blk.vector
    def _(v):
        v.wait_ge(ctx.act.sem, ctx.act.v)
        for tc in range(NTC):
            cc = v.tensor_reduce(ct[:, tc * 4:tc * 4 + 1],
                                 scores[:, tc * 8:(tc + 1) * 8], axis=AX.X, op=ALU.max)
            cc2 = v.tensor_reduce(ct[:, tc * 4 + 2:tc * 4 + 3],
                                  scores[:, tc * 8:(tc + 1) * 8], axis=AX.X, op=ALU.add)
            if tc == NTC - 1:
                cc2.then_inc(ctx.dve.sem, 1)
    ctx.dve.inc(1)

    @blk.vector
    def _(v):
        for tc in range(NTC):
            cc = v.tensor_scalar(gv_tm[:, tc * 8:(tc + 1) * 8],
                                 scores[:, tc * 8:(tc + 1) * 8],
                                 ct[:, tc * 4:tc * 4 + 1], 1e30,
                                 ALU.is_equal, op1=ALU.mult)
            if tc == NTC - 1:
                cc.then_inc(ctx.dve.sem, 1)
    ctx.dve.inc(1)

    @blk.vector
    def _(v):
        for tc in range(NTC):
            cc = v.tensor_tensor(gv_tm[:, tc * 8:(tc + 1) * 8],
                                 scores[:, tc * 8:(tc + 1) * 8],
                                 gv_tm[:, tc * 8:(tc + 1) * 8], ALU.subtract)
            if tc == NTC - 1:
                cc.then_inc(ctx.dve.sem, 1)
    ctx.dve.inc(1)

    @blk.vector
    def _(v):
        for tc in range(NTC):
            cc = v.tensor_reduce(ct[:, tc * 4 + 1:tc * 4 + 2],
                                 gv_tm[:, tc * 8:(tc + 1) * 8], axis=AX.X, op=ALU.max)
            cc2 = v.reciprocal(ct[:, tc * 4 + 3:tc * 4 + 4],
                               ct[:, tc * 4 + 2:tc * 4 + 3])
            if tc == NTC - 1:
                cc2.then_inc(ctx.dve.sem, 1)
    ctx.dve.inc(1)

    @blk.vector
    def _(v):
        for tc in range(NTC):
            cc = v.tensor_scalar(gv_tm[:, tc * 8:(tc + 1) * 8],
                                 scores[:, tc * 8:(tc + 1) * 8],
                                 ct[:, tc * 4 + 1:tc * 4 + 2], None, ALU.is_ge)
            if tc == NTC - 1:
                cc.then_inc(ctx.dve.sem, 1)
    ctx.dve.inc(1)

    @blk.vector
    def _(v):
        for tc in range(NTC):
            cc = v.tensor_tensor(gv_tm[:, tc * 8:(tc + 1) * 8],
                                 scores[:, tc * 8:(tc + 1) * 8],
                                 gv_tm[:, tc * 8:(tc + 1) * 8], ALU.mult)
            if tc == NTC - 1:
                cc.then_inc(ctx.dve.sem, 1)
    ctx.dve.inc(1)

    @blk.vector
    def _(v):
        for tc in range(NTC):
            cc = v.tensor_scalar(gv_tm[:, tc * 8:(tc + 1) * 8],
                                 gv_tm[:, tc * 8:(tc + 1) * 8],
                                 ct[:, tc * 4 + 3:tc * 4 + 4], None, ALU.mult)
            if tc == NTC - 1:
                cc.then_inc(ctx.dve.sem, 1)
    ctx.dve.inc(1)

    for tc in range(NTC):
        bank_pos = tc % 4

        @blk.tensor
        def _(t, tc=tc, bank_pos=bank_pos):
            t.wait_ge(ctx.dve.sem, ctx.dve.v)
            t.transpose(ctx.ps_b[0:8, bank_pos * 128:(bank_pos + 1) * 128],
                        gv_tm[:, tc * 8:(tc + 1) * 8],
                        ctx.ident_f32[:]).then_inc(ctx.pe.sem, 1)
        ctx.pe.inc(1)

        if bank_pos == 3 or tc == NTC - 1:
            n_in = bank_pos + 1
            c0 = tc - n_in + 1

            @blk.vector
            def _(v, n_in=n_in, c0=c0):
                v.wait_ge(ctx.pe.sem, ctx.pe.v)
                v.tensor_copy(gate_em[0:8, c0 * 128:(c0 + n_in) * 128],
                              ctx.ps_b[0:8, 0:n_in * 128]).then_inc(ctx.dve.sem, 1)
            ctx.dve.inc(1)

    if getattr(ctx, "debug", False) and b == 0:
        @blk.scalar
        def _(s):
            s.wait_ge(ctx.dve.sem, ctx.dve.v)
            for ch in range(NCH):
                s.dma_start(out=io["dbg_u"][0, ch * 128:(ch + 1) * 128, :],
                            in_=acc[:, ch * T:(ch + 1) * T]).then_inc(ctx.st.sem, 16)
            s.dma_start(out=io["dbg_gv"], in_=gv_tm[:, 0:(T // 128) * 8]).then_inc(ctx.st.sem, 16)
            s.dma_start(out=io["dbg_sc"], in_=scores[:, 0:(T // 128) * 8]).then_inc(ctx.st.sem, 16)
            s.dma_start(out=io["dbg_gem"], in_=gate_em[0:8, :T]).then_inc(ctx.st.sem, 16)
        ctx.st.inc(NCH + 3)

        @blk.gpsimd
        def _(g):
            g.wait_ge(ctx.dve.sem, ctx.dve.v)
            g.dma_start(out=io["dbg_ubf"].rearrange("(c p) t -> p c t", p=128),
                        in_=u_bf[:].rearrange("p (c t) -> p c t", t=T)).then_inc(ctx.lw.sem, 16)
        ctx.lw.inc(1)

    ctx.w2_evac_hist = {}
    ctx.w2_g2 = 0
    NQ = 4
    HQ = MOE_H // NQ
    QC = HQ // 128
    for e in range(10):
        routed = e >= 2
        b1c = ctx.bias_reg[f"m_b1_r{e - 2}" if routed else f"m_b1_s{e}"]
        b2c = ctx.bias_reg[f"m_b2_r{e - 2}" if routed else f"m_b2_s{e}"]
        for q in range(NQ):
            h0 = q * HQ
            @blk.sync
            def _(s, e=e, h0=h0):
                s.wait_ge(ctx.pe.sem, ctx.pe.v)
                s.dma_start(out=w2q[:].rearrange("p (c d) -> p c d", d=1024),
                            in_=io["w2bf"][e].rearrange("(c p) d -> p c d", p=128)[
                                :, h0 // 128:(h0 + HQ) // 128, :]
                            ).then_inc(ctx.ld.sem, 16)
            ctx.ld.inc(1)

            gelu_hist = {}
            qdve = ctx.dve.v
            for hc in range(QC):
                slot = hc % 2
                wsl = w1b[:, slot * 1024:(slot + 1) * 1024]

                @blk.sync
                def _(s, e=e, h0=h0, hc=hc, wsl=wsl):
                    s.wait_ge(ctx.pe.sem, max(ctx.pe.v - NT, 0))
                    s.dma_start(
                        out=wsl.rearrange("p (c h) -> p c h", h=128),
                        in_=io["w1bf"][e].rearrange("(c p) h -> p c h", p=128)[
                            :, :, h0 + hc * 128: h0 + (hc + 1) * 128]
                        ).then_inc(ctx.ld.sem, 16)
                ctx.ld.inc(1)

                for it in range(NT):
                    g = hc * NT + it
                    psx = (ctx.ps_a, ctx.ps_b, ctx.ps_c, ctx.ps_misc)[g % 4]
                    bank_free = gelu_hist.get(g - 4, 0)

                    @blk.tensor
                    def _(t, wsl=wsl, psx=psx, bank_free=bank_free, qdve=qdve):
                        t.wait_ge(ctx.ld.sem, ctx.ld.v)
                        t.wait_ge(ctx.dve.sem, qdve)
                        if bank_free:
                            t.wait_ge(ctx.act.sem, bank_free)
                        for ch in range(NCH):
                            mm = t.matmul(psx[:, 0:512],
                                          wsl[:, ch * 128:(ch + 1) * 128],
                                          u_bf[:, ch * T + it * 512: ch * T + (it + 1) * 512],
                                          start=(ch == 0), stop=(ch == NCH - 1))
                        mm.then_inc(ctx.pe.sem, 1)
                    ctx.pe.inc(1)

                    @blk.scalar
                    def _(s, hc=hc, it=it, b1c=b1c, q=q, psx=psx):
                        s.wait_ge(ctx.pe.sem, ctx.pe.v)
                        gcol = b1c + q * QC + hc
                        s.activation(act[:, hc * T + it * 512: hc * T + (it + 1) * 512],
                                     psx[:, 0:512], AF.Gelu,
                                     bias=ctx.bias_pool[:, gcol:gcol + 1]
                                     ).then_inc(ctx.act.sem, 1)
                    ctx.act.inc(1)
                    gelu_hist[g] = ctx.act.v

            for it in range(NT):
                if routed:
                    @blk.tensor
                    def _(t, e=e, it=it):
                        t.wait_ge(ctx.dve.sem, ctx.dve.v)
                        t.matmul(ctx.ps_y3[:, 0:128],
                                 ctx.sel8[:, (e - 2) * 128:(e - 1) * 128],
                                 gate_em[0:8, it * 512:it * 512 + 128],
                                 start=True, stop=True)
                        t.matmul(ctx.ps_y3[:, 128:256],
                                 ctx.sel8[:, (e - 2) * 128:(e - 1) * 128],
                                 gate_em[0:8, it * 512 + 128:it * 512 + 256],
                                 start=True, stop=True).then_inc(ctx.pe.sem, 1)
                        t.matmul(ctx.ps_st[:, 0:128],
                                 ctx.sel8[:, (e - 2) * 128:(e - 1) * 128],
                                 gate_em[0:8, it * 512 + 256:it * 512 + 384],
                                 start=True, stop=True)
                        t.matmul(ctx.ps_st[:, 0:128] if False else ctx.ps_y3[:, 256:384],
                                 ctx.sel8[:, (e - 2) * 128:(e - 1) * 128],
                                 gate_em[0:8, it * 512 + 384:it * 512 + 512],
                                 start=True, stop=True).then_inc(ctx.pe.sem, 1)
                    ctx.pe.inc(2)

                    @blk.vector
                    def _(v):
                        v.wait_ge(ctx.pe.sem, ctx.pe.v)
                        v.tensor_copy(gbc_t[:, 0:128], ctx.ps_y3[:, 0:128])
                        v.tensor_copy(gbc_t[:, 128:256], ctx.ps_y3[:, 128:256])
                        v.tensor_copy(gbc_t[:, 256:384], ctx.ps_st[:, 0:128])
                        v.tensor_copy(gbc_t[:, 384:512],
                                      ctx.ps_y3[:, 256:384]).then_inc(ctx.dve.sem, 1)
                    ctx.dve.inc(1)

                for dc in range(NCH):
                    g2 = ctx.w2_g2
                    ctx.w2_g2 += 1
                    psx = (ctx.ps_a, ctx.ps_b, ctx.ps_c, ctx.ps_misc)[g2 % 4]
                    bank_free2 = ctx.w2_evac_hist.get(g2 - 4, 0)

                    @blk.tensor
                    def _(t, dc=dc, it=it, psx=psx, bank_free2=bank_free2):
                        t.wait_ge(ctx.ld.sem, ctx.ld.v)
                        if bank_free2:
                            t.wait_ge(ctx.dve.sem, bank_free2)
                        t.wait_ge(ctx.act.sem, ctx.act.v)
                        for hc in range(QC):
                            mm = t.matmul(psx[:, 0:512],
                                          w2q[:, hc * 1024 + dc * 128: hc * 1024 + (dc + 1) * 128],
                                          act[:, hc * T + it * 512: hc * T + (it + 1) * 512],
                                          start=(hc == 0), stop=(hc == QC - 1))
                        mm.then_inc(ctx.pe.sem, 1)
                    ctx.pe.inc(1)

                    @blk.vector
                    def _(v, dc=dc, it=it, psx=psx, q=q, b2c=b2c, routed=routed):
                        v.wait_ge(ctx.pe.sem, ctx.pe.v)
                        dst_sl = acc[:, dc * T + it * 512: dc * T + (it + 1) * 512]
                        if routed:
                            if q == 0:
                                v.scalar_tensor_tensor(scr_t[:, 0:512], psx[:, 0:512],
                                                       ctx.bias_pool[:, b2c + dc:b2c + dc + 1],
                                                       gbc_t[:, 0:512],
                                                       op0=ALU.add, op1=ALU.mult)
                            else:
                                v.tensor_tensor(scr_t[:, 0:512], psx[:, 0:512],
                                                gbc_t[:, 0:512], ALU.mult)
                            v.tensor_tensor(dst_sl, dst_sl, scr_t[:, 0:512],
                                            ALU.add).then_inc(ctx.dve.sem, 1)
                        else:
                            if q == 0:
                                v.tensor_scalar(scr_t[:, 0:512], psx[:, 0:512],
                                                ctx.bias_pool[:, b2c + dc:b2c + dc + 1],
                                                None, ALU.add)
                                v.tensor_tensor(dst_sl, dst_sl, scr_t[:, 0:512],
                                                ALU.add).then_inc(ctx.dve.sem, 1)
                            else:
                                v.tensor_tensor(dst_sl, dst_sl, psx[:, 0:512],
                                                ALU.add).then_inc(ctx.dve.sem, 1)
                    ctx.dve.inc(1)
                    ctx.w2_evac_hist[g2] = ctx.dve.v

    if getattr(ctx, "debug", False) and b == 0:
        @blk.scalar
        def _(s):
            s.wait_ge(ctx.dve.sem, ctx.dve.v)
            for ch in range(NCH):
                s.dma_start(out=io["dbg_acc"][0, ch * 128:(ch + 1) * 128, :],
                            in_=acc[:, ch * T:(ch + 1) * T]).then_inc(ctx.st.sem, 16)
        ctx.st.inc(NCH)

    for it in range(NT):
        tsl = slice(it * 512, (it + 1) * 512)
        emit_tile_stats(ctx, lambda ch, it=it: acc[:, ch * T + it * 512: ch * T + (it + 1) * 512],
                        sqf, is_f32=True)
        emit_tile_rows(ctx, 1e-5)
        emit_tile_bcast(ctx, [(ctx.row_a, mu_bt), (ctx.row_c, rho_bt)])

        @blk.sync
        def _(s, tsl=tsl):
            s.wait_ge(ctx.dve.sem, ctx.dve.v)
            for ch in range(NCH):
                s.dma_start(out=x_it[:, ch * 512:(ch + 1) * 512],
                            in_=io[src][b, ch * 128:(ch + 1) * 128, tsl]
                            ).then_inc(ctx.ld.sem, 16)
        ctx.ld.inc(NCH)

        for ch in range(NCH):
            @blk.vector
            def _(v, ch=ch, it=it):
                if ch == 0:
                    v.wait_ge(ctx.ld.sem, ctx.ld.v)
                v.tensor_tensor(scr_t[:, 0:512],
                                acc[:, ch * T + it * 512: ch * T + (it + 1) * 512],
                                mu_bt[:, 0:512], ALU.subtract)
                v.tensor_tensor(scr_t[:, 0:512], scr_t[:, 0:512], rho_bt[:, 0:512], ALU.mult)
                v.tensor_tensor(x_it[:, ch * 512:(ch + 1) * 512],
                                x_it[:, ch * 512:(ch + 1) * 512],
                                scr_t[:, 0:512], ALU.add).then_inc(ctx.dve.sem, 1)
            ctx.dve.inc(1)

        @blk.scalar
        def _(s, tsl=tsl):
            s.wait_ge(ctx.dve.sem, ctx.dve.v)
            for ch in range(NCH):
                s.dma_start(out=io[dst][b, ch * 128:(ch + 1) * 128, tsl],
                            in_=x_it[:, ch * 512:(ch + 1) * 512]).then_inc(ctx.st.sem, 16)
        ctx.st.inc(NCH)

    barrier(ctx)
    stack.close()


def emit_phase_c(ctx, io, b, src="x_cur", dst="x_cur"):
    nc, blk, T = ctx.nc, ctx.block, ctx.T
    NT = T // 512
    NTC = T // 128
    NQT = T // 128
    stack = contextlib.ExitStack()

    def sb(name, shape, dt):
        return stack.enter_context(nc.sbuf_tensor(f"c{b}_{name}", shape, dt))

    xn = sb("xn", [128, NCH * T], BF16)
    yall = sb("yall", [128, NCH * T], BF16)
    x_it = sb("x_it", [128, NCH * 512], F32)
    q2 = sb("q2", [128, T], BF16)
    k2 = sb("k2", [128, T], BF16)
    vtm = sb("vtm", [128, T], BF16)
    p_sb = sb("p_sb", [128, T], BF16)
    ptb = sb("ptb", [128, 16 * 128], BF16)
    wpair = sb("wpair", [128, 3 * 1024], BF16)
    wob = sb("wob", [128, 1024], BF16)
    ge_row = sb("ge_row", [1, T], F32)
    gcol = sb("gcol", [128, NTC], F32)
    mzcols = sb("mzcols", [128, 12], F32)
    sqf = sb("sqf", [128, 1024], F32)
    sqb = sb("sqb", [128, 1024], BF16)
    ctx.row_a = sb("row_a", [1, 512], F32)
    ctx.row_b = sb("row_b", [1, 512], F32)
    ctx.row_c = sb("row_c", [1, 512], F32)
    ctx.row_d = sb("row_d", [1, 512], F32)
    mu_bt = sb("mu_bt", [128, 512], F32)
    rho_bt = sb("rho_bt", [128, 512], F32)
    scr_t = sb("scr_t", [128, 512], F32)

    lng = ctx.bias_reg["c_lng"]
    lnb = ctx.bias_reg["c_lnb"]
    bqkv = ctx.bias_reg["c_bqkv"]
    bent = ctx.bias_reg["c_bent"]
    bout = ctx.bias_reg["c_bout"]

    for it in range(NT):
        tsl = slice(it * 512, (it + 1) * 512)

        @blk.sync
        def _(s, tsl=tsl):
            s.wait_ge(ctx.dve.sem, ctx.dve.v)
            for ch in range(NCH):
                s.dma_start(out=x_it[:, ch * 512:(ch + 1) * 512],
                            in_=io[src][b, ch * 128:(ch + 1) * 128, tsl]
                            ).then_inc(ctx.ld.sem, 16)
        ctx.ld.inc(NCH)

        @blk.scalar
        def _(s):
            s.wait_ge(ctx.ld.sem, ctx.ld.v)
            s.sem_inc(ctx.act.sem, 1)
        ctx.act.inc(1)

        emit_tile_stats(ctx, lambda ch: x_it[:, ch * 512:(ch + 1) * 512],
                        sqf, is_f32=True)

        @blk.vector
        def _(v):
            v.wait_ge(ctx.pe.sem, ctx.pe.v)
            v.tensor_scalar(ctx.row_a[0:1, :], ctx.ps_b[0:1, 0:512], 1.0 / D, None, ALU.mult)
            v.tensor_scalar(ctx.row_b[0:1, :], ctx.ps_c[0:1, 0:512], 1.0 / D, None, ALU.mult)
            v.tensor_tensor(ctx.row_d[0:1, :], ctx.row_a[0:1, :], ctx.row_a[0:1, :], ALU.mult)
            v.tensor_tensor(ctx.row_b[0:1, :], ctx.row_b[0:1, :], ctx.row_d[0:1, :], ALU.subtract)
            v.tensor_scalar(ctx.row_c[0:1, :], ctx.row_b[0:1, :], 1e-5, None, ALU.add)
            v.reciprocal(ctx.row_d[0:1, :], ctx.row_c[0:1, :])
            v.tensor_tensor(ctx.row_d[0:1, :], ctx.row_b[0:1, :], ctx.row_d[0:1, :], ALU.mult)
            v.tensor_scalar(ctx.row_d[0:1, :], ctx.row_d[0:1, :], 1e-6, None, ALU.add)
            v.tensor_tensor(ctx.row_c[0:1, :], ctx.row_c[0:1, :], ctx.row_d[0:1, :],
                            ALU.mult).then_inc(ctx.dve.sem, 1)
        ctx.dve.inc(1)

        @blk.scalar
        def _(s):
            s.wait_ge(ctx.dve.sem, ctx.dve.v)
            s.activation(ctx.row_c[0:1, :], ctx.row_c[0:1, :], AF.Sqrt).then_inc(ctx.act.sem, 1)
        ctx.act.inc(1)

        @blk.vector
        def _(v):
            v.wait_ge(ctx.act.sem, ctx.act.v)
            v.reciprocal(ctx.row_c[0:1, :], ctx.row_c[0:1, :]).then_inc(ctx.dve.sem, 1)
        ctx.dve.inc(1)

        emit_tile_bcast(ctx, [(ctx.row_a, mu_bt), (ctx.row_c, rho_bt)])

        for ch in range(NCH):
            @blk.vector
            def _(v, ch=ch, it=it):
                v.tensor_tensor(scr_t[:, 0:512], x_it[:, ch * 512:(ch + 1) * 512],
                                mu_bt[:, 0:512], ALU.subtract)
                v.scalar_tensor_tensor(scr_t[:, 0:512], scr_t[:, 0:512],
                                       ctx.bias_pool[:, lng + ch:lng + ch + 1],
                                       rho_bt[:, 0:512], op0=ALU.mult, op1=ALU.mult)
                v.tensor_scalar(xn[:, ch * T + it * 512: ch * T + (it + 1) * 512],
                                scr_t[:, 0:512],
                                ctx.bias_pool[:, lnb + ch:lnb + ch + 1], None,
                                ALU.add).then_inc(ctx.dve.sem, 1)
            ctx.dve.inc(1)

    for it in range(NT):
        @blk.tensor
        def _(t, it=it):
            t.wait_ge(ctx.dve.sem, ctx.dve.v)
            t.wait_ge(ctx.act.sem, ctx.act.v)
            for ch in range(NCH):
                mm = t.matmul(ctx.ps_misc[0:1, 0:512],
                              ctx.went_bf[:, ch:ch + 1],
                              xn[:, ch * T + it * 512: ch * T + (it + 1) * 512],
                              start=(ch == 0), stop=(ch == NCH - 1))
            mm.then_inc(ctx.pe.sem, 1)
        ctx.pe.inc(1)

        @blk.scalar
        def _(s, it=it):
            s.wait_ge(ctx.pe.sem, ctx.pe.v)
            s.activation(ge_row[0:1, it * 512:(it + 1) * 512], ctx.ps_misc[0:1, 0:512],
                         AF.Sigmoid,
                         bias=ctx.bias_pool[0:1, bent:bent + 1]).then_inc(ctx.act.sem, 1)
        ctx.act.inc(1)

    @blk.vector
    def _(v):
        v.wait_ge(ctx.act.sem, ctx.act.v)
        v.tensor_scalar(ge_row[0:1, :T], ge_row[0:1, :T], 2.0, 0.1, ALU.min,
                        op1=ALU.max).then_inc(ctx.dve.sem, 1)
    ctx.dve.inc(1)

    for tc in range(NTC):
        bank_pos = tc % 32

        @blk.tensor
        def _(t, tc=tc, bank_pos=bank_pos):
            t.wait_ge(ctx.dve.sem, ctx.dve.v)
            t.transpose(ctx.ps_b[:, bank_pos:bank_pos + 1],
                        ge_row[0:1, tc * 128:(tc + 1) * 128],
                        ctx.ident_f32[0:1, 0:1]).then_inc(ctx.pe.sem, 1)
        ctx.pe.inc(1)

        if bank_pos == 31 or tc == NTC - 1:
            n_in = bank_pos + 1
            c0 = tc - n_in + 1

            @blk.vector
            def _(v, n_in=n_in, c0=c0):
                v.wait_ge(ctx.pe.sem, ctx.pe.v)
                v.tensor_copy(gcol[:, c0:c0 + n_in],
                              ctx.ps_b[:, 0:n_in]).then_inc(ctx.dve.sem, 1)
            ctx.dve.inc(1)

    wpair_free = {}
    for hp in range(C_NH // 2):
        @blk.sync
        def _(s, hp=hp):
            if wpair_free.get(hp - 1):
                s.wait_ge(ctx.pe.sem, wpair_free[hp - 1])
            else:
                s.wait_ge(ctx.dve.sem, ctx.dve.v)
            for wi in range(3):
                s.dma_start(
                    out=wpair[:, wi * 1024:(wi + 1) * 1024].rearrange(
                        "p (c e) -> p c e", e=128),
                    in_=io["qkvT"].rearrange("(c p) e -> p c e", p=128)[
                        :, :, wi * 1024 + hp * 128: wi * 1024 + (hp + 1) * 128]
                    ).then_inc(ctx.ld.sem, 16)
        ctx.ld.inc(3)

        for wi, dstbuf in ((0, q2), (1, k2)):
            for it in range(NT):
                @blk.tensor
                def _(t, wi=wi, it=it):
                    t.wait_ge(ctx.ld.sem, ctx.ld.v)
                    t.wait_ge(ctx.dve.sem, ctx.dve.v)
                    for ch in range(NCH):
                        mm = t.matmul(ctx.ps_a[:, 0:512],
                                      wpair[:, wi * 1024 + ch * 128: wi * 1024 + (ch + 1) * 128],
                                      xn[:, ch * T + it * 512: ch * T + (it + 1) * 512],
                                      start=(ch == 0), stop=(ch == NCH - 1))
                    mm.then_inc(ctx.pe.sem, 1)
                ctx.pe.inc(1)

                @blk.vector
                def _(v, wi=wi, dstbuf=dstbuf, it=it, hp=hp):
                    v.wait_ge(ctx.pe.sem, ctx.pe.v)
                    bcol = bqkv + (wi * 1024 + hp * 128) // 128
                    if wi == 0:
                        v.tensor_scalar(dstbuf[:, it * 512:(it + 1) * 512],
                                        ctx.ps_a[:, 0:512],
                                        ctx.bias_pool[:, bcol:bcol + 1], C_SCALE,
                                        ALU.add, op1=ALU.mult).then_inc(ctx.dve.sem, 1)
                    else:
                        v.tensor_scalar(dstbuf[:, it * 512:(it + 1) * 512],
                                        ctx.ps_a[:, 0:512],
                                        ctx.bias_pool[:, bcol:bcol + 1], None,
                                        ALU.add).then_inc(ctx.dve.sem, 1)
                ctx.dve.inc(1)

        for it in range(NT):
            @blk.tensor
            def _(t, it=it):
                t.wait_ge(ctx.ld.sem, ctx.ld.v)
                t.wait_ge(ctx.dve.sem, ctx.dve.v)
                for ch in range(NCH):
                    mm = t.matmul(ctx.ps_a[:, 0:512],
                                  wpair[:, 2 * 1024 + ch * 128: 2 * 1024 + (ch + 1) * 128],
                                  xn[:, ch * T + it * 512: ch * T + (it + 1) * 512],
                                  start=(ch == 0), stop=(ch == NCH - 1))
                mm.then_inc(ctx.pe.sem, 1)
            ctx.pe.inc(1)

            @blk.vector
            def _(v, it=it, hp=hp):
                v.wait_ge(ctx.pe.sem, ctx.pe.v)
                bcol = bqkv + (2 * 1024 + hp * 128) // 128
                v.tensor_scalar(p_sb[:, it * 512:(it + 1) * 512],
                                ctx.ps_a[:, 0:512],
                                ctx.bias_pool[:, bcol:bcol + 1], None,
                                ALU.add).then_inc(ctx.dve.sem, 1)
            ctx.dve.inc(1)
        wpair_free[hp] = ctx.pe.v

        for tc in range(NTC):
            slot = tc % 2

            @blk.tensor
            def _(t, tc=tc, slot=slot):
                t.wait_ge(ctx.dve.sem, ctx.dve.v)
                t.transpose(ctx.ps_kv[:, slot * 128:(slot + 1) * 128],
                            p_sb[:, tc * 128:(tc + 1) * 128],
                            ctx.ident_bf[:]).then_inc(ctx.pe.sem, 1)
            ctx.pe.inc(1)

            @blk.vector
            def _(v, tc=tc, slot=slot):
                v.wait_ge(ctx.pe.sem, ctx.pe.v)
                v.tensor_scalar(vtm[:, tc * 128:(tc + 1) * 128],
                                ctx.ps_kv[:, slot * 128:(slot + 1) * 128],
                                gcol[:, tc:tc + 1], None,
                                ALU.mult).then_inc(ctx.dve.sem, 1)
            ctx.dve.inc(1)

        for hh in range(2):
            base = 64 * hh
            for qi in range(NQT):
                nk = (qi + 1) * 128
                nb = (nk + 511) // 512
                sbanks = (ctx.ps_a, ctx.ps_b, ctx.ps_c, ctx.ps_misc)

                @blk.tensor
                def _(t, base=base, qi=qi, nk=nk, nb=nb):
                    t.wait_ge(ctx.dve.sem, ctx.dve.v)
                    t.wait_ge(ctx.act.sem, ctx.act.v)
                    last = None
                    for bk in range(nb):
                        kw = min(512, nk - bk * 512)
                        last = t.matmul(sbanks[bk][:, 0:kw],
                                        q2[base:base + 64, qi * 128:(qi + 1) * 128],
                                        k2[base:base + 64, bk * 512:bk * 512 + kw],
                                        start=True, stop=True)
                    last.then_inc(ctx.pe.sem, 1)
                ctx.pe.inc(1)

                @blk.vector
                def _(v, qi=qi, nb=nb, nk=nk):
                    v.wait_ge(ctx.pe.sem, ctx.pe.v)
                    off = (qi * 128) % 512
                    v.tensor_tensor(sbanks[nb - 1][:, off:off + 128],
                                    sbanks[nb - 1][:, off:off + 128],
                                    ctx.mask_low[:], ALU.add).then_inc(ctx.dve.sem, 1)
                ctx.dve.inc(1)

                @blk.vector
                def _(v, nb=nb, nk=nk):
                    last = None
                    for bk in range(nb):
                        kw = min(512, nk - bk * 512)
                        last = v.tensor_reduce(mzcols[:, bk:bk + 1],
                                               sbanks[bk][:, 0:kw], axis=AX.X,
                                               op=ALU.max)
                    last.then_inc(ctx.dve.sem, 1)
                ctx.dve.inc(1)

                @blk.vector
                def _(v, nb=nb):
                    v.tensor_reduce(mzcols[:, 4:5], mzcols[:, 0:nb], axis=AX.X,
                                    op=ALU.max, negate=True).then_inc(ctx.dve.sem, 1)
                ctx.dve.inc(1)

                @blk.scalar
                def _(s, qi=qi, nb=nb, nk=nk):
                    s.wait_ge(ctx.dve.sem, ctx.dve.v)
                    last = None
                    for bk in range(nb):
                        kw = min(512, nk - bk * 512)
                        last = s.activation(p_sb[:, bk * 512:bk * 512 + kw],
                                            sbanks[bk][:, 0:kw], AF.Exp,
                                            bias=mzcols[:, 4:5],
                                            accum_out=mzcols[:, 5 + bk:6 + bk])
                    last.then_inc(ctx.act.sem, 1)
                ctx.act.inc(1)

                @blk.vector
                def _(v, nb=nb):
                    v.wait_ge(ctx.act.sem, ctx.act.v)
                    v.tensor_reduce(mzcols[:, 9:10], mzcols[:, 5:5 + nb], axis=AX.X,
                                    op=ALU.add).then_inc(ctx.dve.sem, 1)
                ctx.dve.inc(1)

                @blk.vector
                def _(v):
                    v.reciprocal(mzcols[:, 10:11], mzcols[:, 9:10]).then_inc(ctx.dve.sem, 1)
                ctx.dve.inc(1)

                @blk.vector
                def _(v, nk=nk):
                    v.tensor_scalar(p_sb[:, 0:nk], p_sb[:, 0:nk],
                                    mzcols[:, 10:11], None,
                                    ALU.mult).then_inc(ctx.dve.sem, 1)
                ctx.dve.inc(1)

                nkb = qi + 1
                for kb2 in range(0, nkb, 2):
                    npair = min(2, nkb - kb2)

                    @blk.tensor
                    def _(t, kb2=kb2, npair=npair):
                        t.wait_ge(ctx.dve.sem, ctx.dve.v)
                        last = None
                        for j in range(npair):
                            last = t.transpose(
                                ctx.ps_kv[:, j * 128:(j + 1) * 128],
                                p_sb[:, (kb2 + j) * 128:(kb2 + j + 1) * 128],
                                ctx.ident_bf[:])
                        last.then_inc(ctx.pe.sem, 1)
                    ctx.pe.inc(1)

                    @blk.vector
                    def _(v, kb2=kb2, npair=npair):
                        v.wait_ge(ctx.pe.sem, ctx.pe.v)
                        v.tensor_copy(ptb[:, kb2 * 128:(kb2 + npair) * 128],
                                      ctx.ps_kv[:, 0:npair * 128]).then_inc(ctx.dve.sem, 1)
                    ctx.dve.inc(1)

                @blk.tensor
                def _(t, base=base, qi=qi, nkb=nkb):
                    t.wait_ge(ctx.dve.sem, ctx.dve.v)
                    for kb in range(nkb):
                        mm = t.matmul(ctx.ps_y3[0:64, 0:128],
                                      vtm[:, kb * 128 + base: kb * 128 + base + 64],
                                      ptb[:, kb * 128:(kb + 1) * 128],
                                      start=(kb == 0), stop=(kb == nkb - 1))
                    mm.then_inc(ctx.pe.sem, 1)
                ctx.pe.inc(1)

                @blk.vector
                def _(v, hp=hp, hh=hh, base=base, qi=qi):
                    v.wait_ge(ctx.pe.sem, ctx.pe.v)
                    v.tensor_copy(yall[base:base + 64,
                                       hp * T + qi * 128: hp * T + (qi + 1) * 128],
                                  ctx.ps_y3[0:64, 0:128]).then_inc(ctx.dve.sem, 1)
                ctx.dve.inc(1)

    attn = xn
    wob_free = {}
    for ch in range(NCH):
        @blk.sync
        def _(s, ch=ch):
            if wob_free.get(ch - 1):
                s.wait_ge(ctx.pe.sem, wob_free[ch - 1])
            else:
                s.wait_ge(ctx.dve.sem, ctx.dve.v)
            s.dma_start(
                out=wob[:].rearrange("p (e d) -> p e d", d=128),
                in_=io["woutT"].rearrange("(e p) d -> p e d", p=128)[
                    :, :, ch * 128:(ch + 1) * 128]).then_inc(ctx.ld.sem, 16)
        ctx.ld.inc(1)

        for it in range(NT):
            @blk.tensor
            def _(t, ch=ch, it=it):
                t.wait_ge(ctx.ld.sem, ctx.ld.v)
                t.wait_ge(ctx.dve.sem, ctx.dve.v)
                for e in range(NCH):
                    mm = t.matmul(ctx.ps_a[:, 0:512],
                                  wob[:, e * 128:(e + 1) * 128],
                                  yall[:, e * T + it * 512: e * T + (it + 1) * 512],
                                  start=(e == 0), stop=(e == NCH - 1))
                mm.then_inc(ctx.pe.sem, 1)
            ctx.pe.inc(1)

            @blk.vector
            def _(v, ch=ch, it=it):
                v.wait_ge(ctx.pe.sem, ctx.pe.v)
                v.tensor_scalar(attn[:, ch * T + it * 512: ch * T + (it + 1) * 512],
                                ctx.ps_a[:, 0:512],
                                ctx.bias_pool[:, bout + ch:bout + ch + 1], None,
                                ALU.add).then_inc(ctx.dve.sem, 1)
            ctx.dve.inc(1)
        wob_free[ch] = ctx.pe.v

    for it in range(NT):
        tsl = slice(it * 512, (it + 1) * 512)
        emit_tile_stats(ctx,
                        lambda ch, it=it: attn[:, ch * T + it * 512: ch * T + (it + 1) * 512],
                        sqb, is_f32=False)
        emit_tile_rows(ctx, 1e-5)
        emit_tile_bcast(ctx, [(ctx.row_a, mu_bt), (ctx.row_c, rho_bt)])

        @blk.sync
        def _(s, tsl=tsl):
            s.wait_ge(ctx.dve.sem, ctx.dve.v)
            for ch in range(NCH):
                s.dma_start(out=x_it[:, ch * 512:(ch + 1) * 512],
                            in_=io[src][b, ch * 128:(ch + 1) * 128, tsl]
                            ).then_inc(ctx.ld.sem, 16)
        ctx.ld.inc(NCH)

        for ch in range(NCH):
            @blk.vector
            def _(v, ch=ch, it=it):
                if ch == 0:
                    v.wait_ge(ctx.ld.sem, ctx.ld.v)
                v.tensor_tensor(scr_t[:, 0:512],
                                attn[:, ch * T + it * 512: ch * T + (it + 1) * 512],
                                mu_bt[:, 0:512], ALU.subtract)
                v.tensor_tensor(scr_t[:, 0:512], scr_t[:, 0:512], rho_bt[:, 0:512], ALU.mult)
                v.tensor_tensor(x_it[:, ch * 512:(ch + 1) * 512],
                                x_it[:, ch * 512:(ch + 1) * 512],
                                scr_t[:, 0:512], ALU.add).then_inc(ctx.dve.sem, 1)
            ctx.dve.inc(1)

        @blk.scalar
        def _(s, tsl=tsl):
            s.wait_ge(ctx.dve.sem, ctx.dve.v)
            for ch in range(NCH):
                s.dma_start(out=io[dst][b, ch * 128:(ch + 1) * 128, tsl],
                            in_=x_it[:, ch * 512:(ch + 1) * 512]).then_inc(ctx.st.sem, 16)
        ctx.st.inc(NCH)

    barrier(ctx)
    stack.close()


def emit_phase_d(ctx, io, b, src="x_cur", dst="x_cur"):
    nc, blk, T = ctx.nc, ctx.block, ctx.T
    NT = T // 512
    NTC = T // 128
    VW = T + 8
    NW = T // F_WINDOW
    stack = contextlib.ExitStack()

    def sb(name, shape, dt):
        return stack.enter_context(nc.sbuf_tensor(f"d{b}_{name}", shape, dt))

    v = sb("v", [128, NCH * VW], F32)
    h_bf = sb("h_bf", [128, NCH * T], BF16)
    act = sb("act", [128, 11 * 512], BF16)
    yp = sb("yp", [128, T], BF16)
    qh = sb("qh", [64, T], BF16)
    kh = sb("kh", [64, T], BF16)
    vfm = sb("vfm", [128, T], BF16)
    vtm = sb("vtm", [128, T], BF16)
    p_sb = sb("p_sb", [128, 1024], BF16)
    ptb = sb("ptb", [128, 5 * 128], BF16)
    wpair = sb("wpair", [128, 3 * 1024], BF16)
    wo_sl = sb("wo_sl", [128, 1024], BF16)
    w1a = sb("w1a", [128, 2 * 1024], BF16)
    w1g = sb("w1g", [128, 2 * 1024], BF16)
    w2sl = sb("w2sl", [128, 2 * 11 * 128], BF16)
    gelu_sb = sb("gelu_sb", [128, 512], BF16)
    xq_c = sb("xq_c", [64, T], BF16)
    xq_s = sb("xq_s", [64, T], BF16)
    xk_c = sb("xk_c", [64, T], BF16)
    xk_s = sb("xk_s", [64, T], BF16)
    rs1 = sb("rs1", [64, 512], BF16)
    rs2 = sb("rs2", [64, 512], BF16)
    mzcols = sb("mzcols", [128, 16], F32)
    sqf = sb("sqf", [128, 1024], F32)
    sqb = sb("sqb", [64, 512], BF16)
    xtile = sb("xtile", [128, 2 * 512], F32)
    ctx.row_a = sb("row_a", [1, 512], F32)
    ctx.row_b = sb("row_b", [1, 512], F32)
    ctx.row_c = sb("row_c", [1, 512], F32)
    ctx.row_d = sb("row_d", [1, 512], F32)
    mu_bt = sb("mu_bt", [128, 512], F32)
    rho_bt = sb("rho_bt", [128, 512], F32)
    scr_t = sb("scr_t", [128, 512], F32)

    def vsl(ch, t0, n, shift=False):
        off = 0 if (shift and ch >= NCH // 2) else 1
        return v[:, ch * VW + off + t0: ch * VW + off + t0 + n]

    ctx.d_wpair_free = {}
    ctx.d_wosl_free = {}
    ctx.d_wpair_last = 0

    @blk.gpsimd
    def _(g):
        g.memset(v[:], 0.0).then_inc(ctx.pool.sem, 1)
    ctx.pool.inc(1)

    @blk.sync
    def _(s):
        s.wait_ge(ctx.pool.sem, ctx.pool.v)
        s.dma_start(out=xq_c[:, :T], in_=io["xpos_qcos"]).then_inc(ctx.ld.sem, 16)
        s.dma_start(out=xq_s[:, :T], in_=io["xpos_qsin"]).then_inc(ctx.ld.sem, 16)
        s.dma_start(out=xk_c[:, :T], in_=io["xpos_kcos"]).then_inc(ctx.ld.sem, 16)
        s.dma_start(out=xk_s[:, :T], in_=io["xpos_ksin"]).then_inc(ctx.ld.sem, 16)
        for ch in range(NCH):
            for it in range(NT):
                s.dma_start(out=vsl(ch, it * 512, 512),
                            in_=io[src][b, ch * 128:(ch + 1) * 128,
                                        it * 512:(it + 1) * 512]).then_inc(ctx.ld.sem, 16)
    ctx.ld.inc(4 + NCH * NT)

    @blk.scalar
    def _(s):
        s.wait_ge(ctx.ld.sem, ctx.ld.v)
        s.sem_inc(ctx.act.sem, 1)
    ctx.act.inc(1)

    for it in range(NT):
        emit_tile_stats(ctx, lambda ch, it=it: vsl(ch, it * 512, 512),
                        sqf, is_f32=True)
        emit_tile_rows(ctx, 1e-5)
        emit_tile_bcast(ctx, [(ctx.row_a, mu_bt), (ctx.row_c, rho_bt)])
        for ch in range(NCH):
            @blk.vector
            def _(vv, ch=ch, it=it):
                vv.tensor_tensor(scr_t[:, 0:512], vsl(ch, it * 512, 512),
                                 mu_bt[:, 0:512], ALU.subtract)
                vv.tensor_tensor(vsl(ch, it * 512, 512), scr_t[:, 0:512],
                                 rho_bt[:, 0:512], ALU.mult).then_inc(ctx.dve.sem, 1)
            ctx.dve.inc(1)

    for l in range(2):
        qsc = ctx.bias_reg[f"f_qs{l}"]
        ksc = ctx.bias_reg[f"f_ks{l}"]
        lngc = ctx.bias_reg[f"f_lng{l}"]
        lnbc = ctx.bias_reg[f"f_lnb{l}"]
        b1ac = ctx.bias_reg[f"f_b1a{l}"]
        b1gc = ctx.bias_reg[f"f_b1g{l}"]
        b2c = ctx.bias_reg[f"f_b2{l}"]

        for ch in range(NCH):
            @blk.vector
            def _(vv, ch=ch):
                last = None
                for it in range(NT):
                    last = vv.tensor_copy(
                        h_bf[:, ch * T + it * 512: ch * T + (it + 1) * 512],
                        vsl(ch, it * 512, 512, shift=True))
                last.then_inc(ctx.dve.sem, 1)
            ctx.dve.inc(1)

        for hp in range(F_HEADS // 2):
            @blk.sync
            def _(s, hp=hp, l=l):
                key = (l, hp - 1)
                if ctx.d_wpair_free.get(key):
                    s.wait_ge(ctx.pe.sem, ctx.d_wpair_free[key])
                else:
                    s.wait_ge(ctx.dve.sem, ctx.dve.v)
                for wi in range(3):
                    s.dma_start(
                        out=wpair[:, wi * 1024:(wi + 1) * 1024].rearrange(
                            "p (c e) -> p c e", e=128),
                        in_=io["fqkv"][l].rearrange("(c p) e -> p c e", p=128)[
                            :, :, wi * 512 + hp * 128: wi * 512 + (hp + 1) * 128]
                        ).then_inc(ctx.ld.sem, 16)
            ctx.ld.inc(3)

            for it in range(NT):
                @blk.tensor
                def _(t, it=it):
                    t.wait_ge(ctx.ld.sem, ctx.ld.v)
                    t.wait_ge(ctx.dve.sem, ctx.dve.v)
                    for ch in range(NCH):
                        mm = t.matmul(ctx.ps_a[:, 0:512],
                                      wpair[:, 2 * 1024 + ch * 128: 2 * 1024 + (ch + 1) * 128],
                                      h_bf[:, ch * T + it * 512: ch * T + (it + 1) * 512],
                                      start=(ch == 0), stop=(ch == NCH - 1))
                    mm.then_inc(ctx.pe.sem, 1)
                ctx.pe.inc(1)

                @blk.vector
                def _(vv, it=it):
                    vv.wait_ge(ctx.pe.sem, ctx.pe.v)
                    vv.tensor_copy(vfm[:, it * 512:(it + 1) * 512],
                                   ctx.ps_a[:, 0:512]).then_inc(ctx.dve.sem, 1)
                ctx.dve.inc(1)
            for tc in range(NTC):
                slot = tc % 2

                @blk.tensor
                def _(t, tc=tc, slot=slot):
                    t.wait_ge(ctx.dve.sem, ctx.dve.v)
                    t.transpose(ctx.ps_kv[:, slot * 128:(slot + 1) * 128],
                                vfm[:, tc * 128:(tc + 1) * 128],
                                ctx.ident_bf[:]).then_inc(ctx.pe.sem, 1)
                ctx.pe.inc(1)

                @blk.vector
                def _(vv, tc=tc, slot=slot):
                    vv.wait_ge(ctx.pe.sem, ctx.pe.v)
                    vv.tensor_copy(vtm[:, tc * 128:(tc + 1) * 128],
                                   ctx.ps_kv[:, slot * 128:(slot + 1) * 128]
                                   ).then_inc(ctx.dve.sem, 1)
                ctx.dve.inc(1)

            for hh in range(2):
                base = 64 * hh
                for (wi, fin, tabc, tabs, scol) in (
                        (0, qh, xq_c, xq_s, qsc), (1, kh, xk_c, xk_s, ksc)):
                    apply_hist = {}
                    for it in range(NT):
                        psx = (ctx.ps_a, ctx.ps_misc)[it % 2]
                        bank_free = apply_hist.get(it - 2, 0)

                        @blk.tensor
                        def _(t, wi=wi, it=it, base=base, psx=psx, bank_free=bank_free):
                            if bank_free:
                                t.wait_ge(ctx.dve.sem, bank_free)
                            t.wait_ge(ctx.dve.sem, max(ctx.dve.v - 2, 0))
                            for ch in range(NCH):
                                mm = t.matmul(
                                    psx[0:64, 0:512],
                                    wpair[:, wi * 1024 + ch * 128 + base:
                                          wi * 1024 + ch * 128 + base + 64],
                                    h_bf[:, ch * T + it * 512: ch * T + (it + 1) * 512],
                                    start=(ch == 0), stop=(ch == NCH - 1))
                            mm.then_inc(ctx.pe.sem, 1)
                        ctx.pe.inc(1)

                        @blk.scalar
                        def _(s, it=it, psx=psx):
                            s.wait_ge(ctx.pe.sem, ctx.pe.v)
                            s.activation(sqb[0:64, 0:512], psx[0:64, 0:512],
                                         AF.Square).then_inc(ctx.act.sem, 1)
                        ctx.act.inc(1)

                        @blk.tensor
                        def _(t, it=it):
                            t.wait_ge(ctx.act.sem, ctx.act.v)
                            t.wait_ge(ctx.dve.sem, ctx.dve.v)
                            t.matmul(ctx.ps_b[0:1, 0:512], ctx.ones_bf[0:64, 0:1],
                                     sqb[0:64, 0:512], start=True,
                                     stop=True).then_inc(ctx.pe.sem, 1)
                        ctx.pe.inc(1)

                        @blk.vector
                        def _(vv):
                            vv.wait_ge(ctx.pe.sem, ctx.pe.v)
                            vv.tensor_copy(ctx.row_a[0:1, :], ctx.ps_b[0:1, 0:512]
                                           ).then_inc(ctx.dve.sem, 1)
                        ctx.dve.inc(1)

                        @blk.scalar
                        def _(s):
                            s.wait_ge(ctx.dve.sem, ctx.dve.v)
                            s.activation(ctx.row_a[0:1, :], ctx.row_a[0:1, :],
                                         AF.Sqrt).then_inc(ctx.act.sem, 1)
                        ctx.act.inc(1)

                        @blk.vector
                        def _(vv):
                            vv.wait_ge(ctx.act.sem, ctx.act.v)
                            vv.tensor_scalar(ctx.row_a[0:1, :], ctx.row_a[0:1, :],
                                             1e-12, None, ALU.max)
                            vv.reciprocal(ctx.row_a[0:1, :],
                                          ctx.row_a[0:1, :]).then_inc(ctx.dve.sem, 1)
                        ctx.dve.inc(1)

                        @blk.tensor
                        def _(t):
                            t.wait_ge(ctx.dve.sem, ctx.dve.v)
                            t.matmul(ctx.ps_b[0:64, 0:512], ctx.ones_f32[0:1, 0:64],
                                     ctx.row_a[0:1, :], start=True,
                                     stop=True).then_inc(ctx.pe.sem, 1)
                        ctx.pe.inc(1)

                        @blk.vector
                        def _(vv):
                            vv.wait_ge(ctx.pe.sem, ctx.pe.v)
                            vv.tensor_copy(mu_bt[0:64, 0:512],
                                           ctx.ps_b[0:64, 0:512]).then_inc(ctx.dve.sem, 1)
                        ctx.dve.inc(1)

                        @blk.vector
                        def _(vv, fin=fin, scol=scol, it=it, psx=psx):
                            vv.scalar_tensor_tensor(fin[:, it * 512:(it + 1) * 512],
                                                    psx[0:64, 0:512],
                                                    ctx.bias_pool[0:64, scol:scol + 1],
                                                    mu_bt[0:64, 0:512],
                                                    op0=ALU.mult,
                                                    op1=ALU.mult).then_inc(ctx.dve.sem, 1)
                        ctx.dve.inc(1)
                        apply_hist[it] = ctx.dve.v

                    if wi == 1:
                        ctx.d_wpair_last = ctx.pe.v

                    for it in range(NT):
                        ts2 = slice(it * 512, (it + 1) * 512)

                        @blk.vector
                        def _(vv, fin=fin, tabc=tabc, tabs=tabs, ts2=ts2):
                            vv.tensor_tensor(rs2[0:32, :], fin[32:64, ts2],
                                             tabs[32:64, ts2], ALU.mult)
                            vv.tensor_tensor(rs1[32:64, :], fin[0:32, ts2],
                                             tabs[0:32, ts2], ALU.mult)
                            vv.tensor_tensor(fin[0:32, ts2], fin[0:32, ts2],
                                             tabc[0:32, ts2], ALU.mult)
                            vv.tensor_tensor(fin[32:64, ts2], fin[32:64, ts2],
                                             tabc[32:64, ts2], ALU.mult)
                            vv.tensor_tensor(fin[0:32, ts2], fin[0:32, ts2],
                                             rs2[0:32, :], ALU.subtract)
                            vv.tensor_tensor(fin[32:64, ts2], fin[32:64, ts2],
                                             rs1[32:64, :], ALU.add).then_inc(ctx.dve.sem, 1)
                        ctx.dve.inc(1)

                tiles = []
                for w in range(NW):
                    for qti in range(4):
                        tiles.append((w, qti))
                ntl = len(tiles)
                s_hist, mm_hist, exp_hist = {}, {}, {}
                qk_ready = ctx.dve.v

                def emit_S(g):
                    w, qti = tiles[g]
                    qt = w * 4 + qti
                    if w == 0:
                        k0, nkb = 0, qti + 1
                    else:
                        k0, nkb = (w - 1) * 512 + qti * 128, 5
                    nk = nkb * 128
                    nb = (nk + 511) // 512
                    par = g % 2
                    sbk = ((ctx.ps_a, ctx.ps_b), (ctx.ps_c, ctx.ps_misc))[par]

                    @blk.tensor
                    def _(t, qt=qt, k0=k0, nk=nk, nb=nb, sbk=sbk, g=g):
                        t.wait_ge(ctx.dve.sem, max(mm_hist.get(g - 2, 0), qk_ready))
                        if exp_hist.get(g - 2, 0):
                            t.wait_ge(ctx.act.sem, exp_hist[g - 2])
                        last = None
                        for bk in range(nb):
                            kw = min(512, nk - bk * 512)
                            last = t.matmul(sbk[bk][:, 0:kw],
                                            qh[:, qt * 128:(qt + 1) * 128],
                                            kh[:, k0 + bk * 512: k0 + bk * 512 + kw],
                                            start=True, stop=True)
                        last.then_inc(ctx.pe.sem, 1)
                    ctx.pe.inc(1)
                    s_hist[g] = ctx.pe.v

                def emit_stages(g, base):
                    w, qti = tiles[g]
                    qt = w * 4 + qti
                    if w == 0:
                        k0, nkb = 0, qti + 1
                    else:
                        k0, nkb = (w - 1) * 512 + qti * 128, 5
                    nk = nkb * 128
                    nb = (nk + 511) // 512
                    par = g % 2
                    sbk = ((ctx.ps_a, ctx.ps_b), (ctx.ps_c, ctx.ps_misc))[par]
                    mz = par * 8
                    pp = 0
                    pt0 = 0

                    @blk.vector
                    def _(vv, nkb=nkb, nb=nb, nk=nk, sbk=sbk, g=g, w=w, mz=mz):
                        vv.wait_ge(ctx.pe.sem, s_hist[g])
                        off = ((nkb - 1) * 128) % 512
                        vv.tensor_tensor(sbk[(nkb - 1) * 128 // 512][:, off:off + 128],
                                         sbk[(nkb - 1) * 128 // 512][:, off:off + 128],
                                         ctx.mask_low[:], ALU.add)
                        if w > 0:
                            vv.tensor_tensor(sbk[0][:, 0:128], sbk[0][:, 0:128],
                                             ctx.mask_up[:], ALU.add)
                        last = None
                        for bk in range(nb):
                            kw = min(512, nk - bk * 512)
                            last = vv.tensor_reduce(mzcols[:, mz + bk:mz + bk + 1],
                                                    sbk[bk][:, 0:kw], axis=AX.X,
                                                    op=ALU.max)
                        last.then_inc(ctx.dve.sem, 1)
                    ctx.dve.inc(1)
                    mm_hist[g] = ctx.dve.v

                    @blk.vector
                    def _(vv, nb=nb, mz=mz):
                        vv.tensor_reduce(mzcols[:, mz + 2:mz + 3], mzcols[:, mz:mz + nb],
                                         axis=AX.X, op=ALU.max,
                                         negate=True).then_inc(ctx.dve.sem, 1)
                    ctx.dve.inc(1)

                    @blk.scalar
                    def _(s, nb=nb, nk=nk, sbk=sbk, mz=mz, pp=pp):
                        s.wait_ge(ctx.dve.sem, ctx.dve.v)
                        last = None
                        for bk in range(nb):
                            kw = min(512, nk - bk * 512)
                            last = s.activation(p_sb[:, pp + bk * 512: pp + bk * 512 + kw],
                                                sbk[bk][:, 0:kw], AF.Exp,
                                                bias=mzcols[:, mz + 2:mz + 3],
                                                accum_out=mzcols[:, mz + 3 + bk:mz + 4 + bk])
                        last.then_inc(ctx.act.sem, 1)
                    ctx.act.inc(1)
                    exp_hist[g] = ctx.act.v

                    @blk.vector
                    def _(vv, nb=nb, mz=mz):
                        vv.wait_ge(ctx.act.sem, ctx.act.v)
                        vv.tensor_reduce(mzcols[:, mz + 5:mz + 6], mzcols[:, mz + 3:mz + 3 + nb],
                                         axis=AX.X, op=ALU.add).then_inc(ctx.dve.sem, 1)
                    ctx.dve.inc(1)

                    @blk.vector
                    def _(vv, mz=mz):
                        vv.reciprocal(mzcols[:, mz + 6:mz + 7],
                                      mzcols[:, mz + 5:mz + 6]).then_inc(ctx.dve.sem, 1)
                    ctx.dve.inc(1)

                    @blk.vector
                    def _(vv, nk=nk, mz=mz, pp=pp):
                        vv.tensor_scalar(p_sb[:, pp:pp + nk], p_sb[:, pp:pp + nk],
                                         mzcols[:, mz + 6:mz + 7], None,
                                         ALU.mult).then_inc(ctx.dve.sem, 1)
                    ctx.dve.inc(1)

                    for kb in range(nkb):
                        @blk.tensor
                        def _(t, kb=kb, par=par, pp=pp):
                            t.wait_ge(ctx.dve.sem, ctx.dve.v)
                            t.transpose(ctx.ps_kv[:, par * 128:(par + 1) * 128],
                                        p_sb[:, pp + kb * 128: pp + (kb + 1) * 128],
                                        ctx.ident_bf[:]).then_inc(ctx.pe.sem, 1)
                        ctx.pe.inc(1)

                        @blk.vector
                        def _(vv, kb=kb, par=par, pt0=pt0):
                            vv.wait_ge(ctx.pe.sem, ctx.pe.v)
                            vv.tensor_copy(ptb[:, pt0 + kb * 128: pt0 + (kb + 1) * 128],
                                           ctx.ps_kv[:, par * 128:(par + 1) * 128]
                                           ).then_inc(ctx.dve.sem, 1)
                        ctx.dve.inc(1)

                    @blk.tensor
                    def _(t, base=base, k0=k0, nkb=nkb, par=par, pt0=pt0):
                        t.wait_ge(ctx.dve.sem, ctx.dve.v)
                        for kb in range(nkb):
                            kbg = k0 // 128 + kb
                            mm = t.matmul(ctx.ps_y3[0:64, par * 128:(par + 1) * 128],
                                          vtm[:, kbg * 128 + base: kbg * 128 + base + 64],
                                          ptb[:, pt0 + kb * 128: pt0 + (kb + 1) * 128],
                                          start=(kb == 0), stop=(kb == nkb - 1))
                        mm.then_inc(ctx.pe.sem, 1)
                    ctx.pe.inc(1)

                    @blk.vector
                    def _(vv, base=base, qt=qt, par=par):
                        vv.wait_ge(ctx.pe.sem, ctx.pe.v)
                        vv.tensor_copy(yp[base:base + 64, qt * 128:(qt + 1) * 128],
                                       ctx.ps_y3[0:64, par * 128:(par + 1) * 128]
                                       ).then_inc(ctx.dve.sem, 1)
                    ctx.dve.inc(1)

                for g in range(ntl + 1):
                    if g < ntl:
                        emit_S(g)
                    if g >= 1:
                        emit_stages(g - 1, base)

            ctx.d_wpair_free[(l, hp)] = ctx.d_wpair_last

            @blk.sync
            def _(s, hp=hp, l=l):
                key = (l, hp - 1)
                if ctx.d_wosl_free.get(key):
                    s.wait_ge(ctx.pe.sem, ctx.d_wosl_free[key])
                else:
                    s.wait_ge(ctx.dve.sem, ctx.dve.v)
                s.dma_start(out=wo_sl[:], in_=io["fwo"][l][hp * 128:(hp + 1) * 128, :]
                            ).then_inc(ctx.ld.sem, 16)
            ctx.ld.inc(1)

            for dc in range(NCH):
                for it in range(NT):
                    psx = (ctx.ps_c, ctx.ps_misc)[(dc * NT + it) % 2]

                    @blk.tensor
                    def _(t, dc=dc, it=it, psx=psx):
                        t.wait_ge(ctx.ld.sem, ctx.ld.v)
                        t.wait_ge(ctx.dve.sem, max(ctx.dve.v - 1, 0))
                        t.matmul(psx[:, 0:512], wo_sl[:, dc * 128:(dc + 1) * 128],
                                 yp[:, it * 512:(it + 1) * 512],
                                 start=True, stop=True).then_inc(ctx.pe.sem, 1)
                    ctx.pe.inc(1)

                    @blk.vector
                    def _(vv, dc=dc, it=it, psx=psx):
                        vv.wait_ge(ctx.pe.sem, ctx.pe.v)
                        vv.tensor_tensor(vsl(dc, it * 512, 512), vsl(dc, it * 512, 512),
                                         psx[:, 0:512], ALU.add).then_inc(ctx.dve.sem, 1)
                    ctx.dve.inc(1)
            ctx.d_wosl_free[(l, hp)] = ctx.pe.v

        for it in range(NT):
            emit_tile_stats(ctx, lambda ch, it=it: vsl(ch, it * 512, 512, shift=True),
                            sqf, is_f32=True)
            emit_tile_rows(ctx, 1e-5)
            emit_tile_bcast(ctx, [(ctx.row_a, mu_bt), (ctx.row_c, rho_bt)])
            for ch in range(NCH):
                @blk.vector
                def _(vv, ch=ch, it=it):
                    vv.tensor_tensor(scr_t[:, 0:512], vsl(ch, it * 512, 512, shift=True),
                                     mu_bt[:, 0:512], ALU.subtract)
                    vv.scalar_tensor_tensor(scr_t[:, 0:512], scr_t[:, 0:512],
                                            ctx.bias_pool[:, lngc + ch:lngc + ch + 1],
                                            rho_bt[:, 0:512], op0=ALU.mult, op1=ALU.mult)
                    vv.tensor_scalar(h_bf[:, ch * T + it * 512: ch * T + (it + 1) * 512],
                                     scr_t[:, 0:512],
                                     ctx.bias_pool[:, lnbc + ch:lnbc + ch + 1], None,
                                     ALU.add).then_inc(ctx.dve.sem, 1)
                ctx.dve.inc(1)

            for jh in range(2):
                for jc in range(11):
                    jcg = jh * 11 + jc
                    cw = 42 if jcg == 21 else 128
                    slot = jc % 2

                    @blk.sync
                    def _(g, l=l, jcg=jcg, cw=cw, slot=slot):
                        g.wait_ge(ctx.pe.sem, max(ctx.pe.v - 1, 0))
                        g.dma_start(
                            out=w1a[:, slot * 1024:slot * 1024 + NCH * cw].rearrange(
                                "p (c o) -> p c o", o=cw),
                            in_=io["fw1"][l].rearrange("(c p) o -> p c o", p=128)[
                                :, :, jcg * 128: jcg * 128 + cw]).then_inc(ctx.ld.sem, 16)
                        g.dma_start(
                            out=w1g[:, slot * 1024:slot * 1024 + NCH * cw].rearrange(
                                "p (c o) -> p c o", o=cw),
                            in_=io["fw1"][l].rearrange("(c p) o -> p c o", p=128)[
                                :, :, 2730 + jcg * 128: 2730 + jcg * 128 + cw]
                            ).then_inc(ctx.ld.sem, 16)
                    ctx.ld.inc(2)

                    @blk.tensor
                    def _(t, it=it, jcg=jcg, cw=cw, slot=slot):
                        t.wait_ge(ctx.ld.sem, ctx.ld.v)
                        t.wait_ge(ctx.dve.sem, ctx.dve.v)
                        t.wait_ge(ctx.act.sem, ctx.act.v)
                        for ch in range(NCH):
                            t.matmul(ctx.ps_a[0:cw, 0:512],
                                     w1a[:, slot * 1024 + ch * cw: slot * 1024 + (ch + 1) * cw],
                                     h_bf[:, ch * T + it * 512: ch * T + (it + 1) * 512],
                                     start=(ch == 0), stop=(ch == NCH - 1))
                            mm = t.matmul(ctx.ps_b[0:cw, 0:512],
                                          w1g[:, slot * 1024 + ch * cw: slot * 1024 + (ch + 1) * cw],
                                          h_bf[:, ch * T + it * 512: ch * T + (it + 1) * 512],
                                          start=(ch == 0), stop=(ch == NCH - 1))
                        mm.then_inc(ctx.pe.sem, 1)
                    ctx.pe.inc(1)

                    @blk.scalar
                    def _(s, jcg=jcg, cw=cw):
                        s.wait_ge(ctx.pe.sem, ctx.pe.v)
                        s.activation(gelu_sb[0:cw, 0:512], ctx.ps_b[0:cw, 0:512],
                                     AF.Gelu,
                                     bias=ctx.bias_pool[0:cw, b1gc + jcg:b1gc + jcg + 1]
                                     ).then_inc(ctx.act.sem, 1)
                    ctx.act.inc(1)

                    @blk.vector
                    def _(vv, jc=jc, jcg=jcg, cw=cw):
                        vv.wait_ge(ctx.act.sem, ctx.act.v)
                        vv.scalar_tensor_tensor(act[0:cw, jc * 512:(jc + 1) * 512],
                                                ctx.ps_a[0:cw, 0:512],
                                                ctx.bias_pool[0:cw, b1ac + jcg:b1ac + jcg + 1],
                                                gelu_sb[0:cw, 0:512],
                                                op0=ALU.add, op1=ALU.mult
                                                ).then_inc(ctx.dve.sem, 1)
                    ctx.dve.inc(1)

                for dc in range(NCH):
                    slot = dc % 2

                    @blk.scalar
                    def _(s, l=l, jh=jh, dc=dc, slot=slot):
                        s.wait_ge(ctx.st.sem, ctx.st.v)
                        s.wait_ge(ctx.pe.sem, max(ctx.pe.v - 1, 0))
                        r0 = jh * 1408
                        nfull = 11 if jh == 0 else 10
                        s.dma_start(
                            out=w2sl[:, slot * 1408: slot * 1408 + nfull * 128].rearrange(
                                "p (c d) -> p c d", d=128),
                            in_=io["fw2"][l][0:2688, :].rearrange(
                                "(x p) d -> p x d", p=128)[
                                :, r0 // 128: r0 // 128 + nfull,
                                dc * 128:(dc + 1) * 128]).then_inc(ctx.st.sem, 16)
                        if jh == 1:
                            s.dma_start(
                                out=w2sl[0:42, slot * 1408 + 10 * 128: slot * 1408 + 11 * 128],
                                in_=io["fw2"][l][2688:2730, dc * 128:(dc + 1) * 128]
                                ).then_inc(ctx.st.sem, 16)
                    ctx.st.inc(2 if jh == 1 else 1)

                    psx = (ctx.ps_c, ctx.ps_misc)[dc % 2]

                    @blk.tensor
                    def _(t, it=it, jh=jh, dc=dc, slot=slot, psx=psx):
                        t.wait_ge(ctx.st.sem, ctx.st.v)
                        t.wait_ge(ctx.dve.sem, max(ctx.dve.v - 1, 0))
                        for jc in range(11):
                            jcg = jh * 11 + jc
                            cw = 42 if jcg == 21 else 128
                            mm = t.matmul(psx[:, 0:512],
                                          w2sl[0:cw, slot * 1408 + jc * 128: slot * 1408 + jc * 128 + 128],
                                          act[0:cw, jc * 512:(jc + 1) * 512],
                                          start=(jc == 0), stop=(jc == 10))
                        mm.then_inc(ctx.pe.sem, 1)
                    ctx.pe.inc(1)

                    @blk.vector
                    def _(vv, it=it, jh=jh, dc=dc, psx=psx):
                        vv.wait_ge(ctx.pe.sem, ctx.pe.v)
                        if jh == 1:
                            vv.tensor_scalar(scr_t[:, 0:512], psx[:, 0:512],
                                             ctx.bias_pool[:, b2c + dc:b2c + dc + 1],
                                             None, ALU.add)
                            vv.tensor_tensor(vsl(dc, it * 512, 512),
                                             vsl(dc, it * 512, 512),
                                             scr_t[:, 0:512], ALU.add).then_inc(ctx.dve.sem, 1)
                        else:
                            vv.tensor_tensor(vsl(dc, it * 512, 512),
                                             vsl(dc, it * 512, 512),
                                             psx[:, 0:512], ALU.add).then_inc(ctx.dve.sem, 1)
                    ctx.dve.inc(1)

    for it in range(NT):
        emit_tile_stats(ctx, lambda ch, it=it: vsl(ch, it * 512, 512),
                        sqf, is_f32=True)
        emit_tile_rows(ctx, 1e-5)
        emit_tile_bcast(ctx, [(ctx.row_a, mu_bt), (ctx.row_c, rho_bt)])

        for ch in range(NCH):
            slot = ch % 2

            @blk.sync
            def _(s, ch=ch, it=it, slot=slot):
                s.wait_ge(ctx.dve.sem, ctx.dve.v)
                s.wait_ge(ctx.st.sem, ctx.st.v)
                s.dma_start(out=xtile[:, slot * 512:(slot + 1) * 512],
                            in_=io[src][b, ch * 128:(ch + 1) * 128,
                                        it * 512:(it + 1) * 512]).then_inc(ctx.ld.sem, 16)
            ctx.ld.inc(1)

            @blk.vector
            def _(vv, ch=ch, it=it, slot=slot):
                vv.wait_ge(ctx.ld.sem, ctx.ld.v)
                vv.tensor_tensor(scr_t[:, 0:512], vsl(ch, it * 512, 512),
                                 mu_bt[:, 0:512], ALU.subtract)
                vv.tensor_tensor(scr_t[:, 0:512], scr_t[:, 0:512], rho_bt[:, 0:512],
                                 ALU.mult)
                vv.tensor_tensor(xtile[:, slot * 512:(slot + 1) * 512],
                                 xtile[:, slot * 512:(slot + 1) * 512],
                                 scr_t[:, 0:512], ALU.add).then_inc(ctx.dve.sem, 1)
            ctx.dve.inc(1)

            @blk.scalar
            def _(s, ch=ch, it=it, slot=slot):
                s.wait_ge(ctx.dve.sem, ctx.dve.v)
                s.dma_start(out=io[dst][b, ch * 128:(ch + 1) * 128,
                                        it * 512:(it + 1) * 512],
                            in_=xtile[:, slot * 512:(slot + 1) * 512]).then_inc(ctx.st.sem, 16)
            ctx.st.inc(1)

    barrier(ctx)
    stack.close()


def emit_copy_out(ctx, io, src="x_cur"):
    blk = ctx.block
    for b in range(BPC):
        @blk.scalar
        def _(s, b=b):
            s.dma_start(out=io["out"][b], in_=io[src][b]).then_inc(ctx.st.sem, 16)
        ctx.st.inc(1)
    barrier(ctx)



PHASE_ORDER = ("a", "b", "c", "d")


def build_program(T=T_FULL, phases=("prep", "a"), batches=(0, 1), debug=False,
                  paranoid=False):
    nc = bass.Bass("TRN2", target_bir_lowering=False)
    io = declare_ios(nc, T)
    ctx = Ctx()
    ctx.nc = nc
    ctx.T = T
    ctx.debug = debug
    ctx.paranoid = paranoid

    stack = contextlib.ExitStack()

    def sb(name, shape, dt):
        return stack.enter_context(nc.sbuf_tensor(name, shape, dt))

    def ps(name, shape, dt=F32):
        return stack.enter_context(nc.psum_tensor(name, shape, dt))

    with stack:
        ctx.ident_bf = sb("ident_bf", [128, 128], BF16)
        ctx.ident_f32 = sb("ident_f32", [128, 128], F32)
        ctx.mask_up = sb("mask_up", [128, 128], F32)
        ctx.mask_low = sb("mask_low", [128, 128], F32)
        ctx.sel8 = sb("sel8", [8, 8 * 128], F32)
        ctx.ones_bf = sb("ones_bf", [128, 128], BF16)
        ctx.ones_f32 = sb("ones_f32", [128, 128], F32)
        ctx.bias_pool = sb("bias_pool", [128, 704], F32)
        ctx.wgA = sb("wgA", [128, 8 * 96], F32)
        ctx.wg_colsum = sb("wg_colsum", [96, 1], F32)
        ctx.went_bf = sb("went_bf", [128, 8], BF16)
        ctx.cent_sb = sb("cent_sb", [128, 8 * 8], F32)
        ctx.zero_col = sb("zero_col", [128, 8], F32)

        ctx.ps_a = ps("ps_a", [128, 512])
        ctx.ps_b = ps("ps_b", [128, 512])
        ctx.ps_c = ps("ps_c", [128, 512])
        ctx.ps_misc = ps("ps_misc", [128, 512])
        ctx.ps_st = ps("ps_st", [128, 128])
        ctx.ps_kv = ps("ps_kv", [128, 256], BF16)
        ctx.ps_y3 = ps("ps_y3", [128, 384])
        ctx.ps_tr = ps("ps_tr", [128, 256], BF16)

        with (
            nc.semaphore("s_ld") as s_ld,
            nc.semaphore("s_lw") as s_lw,
            nc.semaphore("s_st") as s_st,
            nc.semaphore("s_pe") as s_pe,
            nc.semaphore("s_dve") as s_dve,
            nc.semaphore("s_act") as s_act,
            nc.semaphore("s_pool") as s_pool,
            nc.Block() as block,
        ):
            ctx.ld = Ctr(s_ld, 16)
            ctx.lw = Ctr(s_lw, 16)
            ctx.st = Ctr(s_st, 16)
            ctx.pe = Ctr(s_pe)
            ctx.dve = Ctr(s_dve)
            ctx.act = Ctr(s_act)
            ctx.pool = Ctr(s_pool)
            ctx.block = SelfWaitBlock(block, ctx)

            if "prep" in phases:
                emit_prep(ctx, io)
            active = [p for p in PHASE_ORDER if p in phases]
            cur = "x"
            for pi, p in enumerate(active):
                for b in batches:
                    if p == "a":
                        emit_phase_a(ctx, io, b, src=cur, dst="x_cur")
                    elif p == "b":
                        emit_phase_b(ctx, io, b, src=cur, dst="x_cur")
                    elif p == "c":
                        emit_phase_c(ctx, io, b, src=cur, dst="x_cur")
                    elif p == "d":
                        emit_phase_d(ctx, io, b, src=cur, dst="x_cur")
                cur = "x_cur"
            emit_copy_out(ctx, io, src=cur if active else "x")

    return nc, io, ctx


def kernel(**inputs):
    T = T_FULL
    nc, io, ctx = build_program(T=T, phases=("prep", "a", "b", "c", "d"), batches=(0, 1))
    consts = make_host_consts(T)
    x = inputs["L0_attn_x"]
    x_fm = np.ascontiguousarray(np.transpose(x, (0, 2, 1)))
    in_maps = []
    for c in range(NCORES):
        m = {"x": x_fm[c * BPC:(c + 1) * BPC]}
        for name in WEIGHT_NAMES:
            m[name] = np.ascontiguousarray(inputs[name])
        m.update(consts)
        in_maps.append(m)
    res = run_bass_kernel_spmd(nc, in_maps, core_ids=list(range(NCORES)))
    out_fm = np.stack([r["out"] for r in res.results]).reshape(NCORES * BPC, D, T)
    return np.ascontiguousarray(np.transpose(out_fm, (0, 2, 1)))


# version 06 (final)
# speedup vs baseline: 1.0163x; 1.0134x over previous
import math
import contextlib
import numpy as np
import ml_dtypes
import concourse.bass as bass
import concourse.mybir as mybir
from concourse.bass_utils import run_bass_kernel_spmd

F32 = mybir.dt.float32
BF16 = mybir.dt.bfloat16
AF = mybir.ActivationFunctionType
ALU = mybir.AluOpType
AX = mybir.AxisListType

NCORES = 8
BPC = 2
D = 1024
NCH = D // 128
T_FULL = 2048
NEG = -1e30

A_NH, A_DH, A_CHUNK = 8, 128, 128
A_SCALE = 1.0 / math.sqrt(A_DH)
LN_A_SCALE = math.log(A_SCALE)

MOE_H, MOE_ES, MOE_ER, MOE_K = 4096, 2, 8, 2

C_NH, C_HD = 16, 64
C_SCALE = (1.0 / math.sqrt(C_HD)) / 0.1

F_HEADS, F_DH, F_WINDOW, F_HID = 8, 64, 512, 2730
F_QK_SCALE = 8.0

WEIGHT_NAMES = [
    "L0_attn_norm_w", "L0_attn_Wq", "L0_attn_Wk", "L0_attn_Wv",
    "L0_attn_Wbeta", "L0_attn_bbeta", "L0_attn_Wig", "L0_attn_big",
    "L0_attn_Wog", "L0_attn_bog", "L0_attn_Wout", "L0_attn_bout",
    "L0_ffn_g_shared", "L0_ffn_W1_s", "L0_ffn_b1_s", "L0_ffn_W2_s", "L0_ffn_b2_s",
    "L0_ffn_g_routed", "L0_ffn_W1_r", "L0_ffn_b1_r", "L0_ffn_W2_r", "L0_ffn_b2_r",
    "L0_ffn_centroids",
    "L1_attn_ln_g", "L1_attn_ln_b", "L1_attn_w_qkv", "L1_attn_b_qkv",
    "L1_attn_w_ent", "L1_attn_b_ent", "L1_attn_w_out", "L1_attn_b_out",
    "L1_ffn_wqkv", "L1_ffn_q_scale", "L1_ffn_k_scale", "L1_ffn_wo",
    "L1_ffn_ln_g", "L1_ffn_ln_b", "L1_ffn_w1", "L1_ffn_b1", "L1_ffn_w2", "L1_ffn_b2",
]

INPUT_SHAPES = {
    "L0_attn_norm_w": (1024,), "L0_attn_Wq": (1024, 1024), "L0_attn_Wk": (1024, 1024),
    "L0_attn_Wv": (1024, 1024), "L0_attn_Wbeta": (8, 1024), "L0_attn_bbeta": (8,),
    "L0_attn_Wig": (8, 1024), "L0_attn_big": (8,), "L0_attn_Wog": (8, 1024),
    "L0_attn_bog": (8,), "L0_attn_Wout": (1024, 1024), "L0_attn_bout": (1024,),
    "L0_ffn_g_shared": (1024,), "L0_ffn_W1_s": (2, 1024, 4096), "L0_ffn_b1_s": (2, 4096),
    "L0_ffn_W2_s": (2, 4096, 1024), "L0_ffn_b2_s": (2, 1024),
    "L0_ffn_g_routed": (1024,), "L0_ffn_W1_r": (8, 1024, 4096), "L0_ffn_b1_r": (8, 4096),
    "L0_ffn_W2_r": (8, 4096, 1024), "L0_ffn_b2_r": (8, 1024),
    "L0_ffn_centroids": (1024, 8),
    "L1_attn_ln_g": (1024,), "L1_attn_ln_b": (1024,), "L1_attn_w_qkv": (3072, 1024),
    "L1_attn_b_qkv": (3072,), "L1_attn_w_ent": (1, 1024), "L1_attn_b_ent": (1,),
    "L1_attn_w_out": (1024, 1024), "L1_attn_b_out": (1024,),
    "L1_ffn_wqkv": (2, 1024, 1536), "L1_ffn_q_scale": (2, 64), "L1_ffn_k_scale": (2, 64),
    "L1_ffn_wo": (2, 512, 1024), "L1_ffn_ln_g": (2, 1024), "L1_ffn_ln_b": (2, 1024),
    "L1_ffn_w1": (2, 1024, 5460), "L1_ffn_b1": (2, 5460), "L1_ffn_w2": (2, 2730, 1024),
    "L1_ffn_b2": (2, 1024),
}


class Ctr:
    def __init__(self, sem, step=1):
        self.sem, self.v, self.step = sem, 0, step

    def inc(self, n=1):
        self.v += n * self.step
        return self.v


class Ctx:
    pass


class SelfWaitBlock:

    def __init__(self, blk, ctx):
        self._blk = blk
        self._ctx = ctx

    def _wrap(self, dec, ctrs):
        ctx = self._ctx
        if getattr(ctx, "paranoid", False):
            ctrs = (ctx.ld, ctx.lw, ctx.st, ctx.pe, ctx.dve, ctx.act, ctx.pool)

        def decorate(fn):
            vals = [(c.sem, c.v) for c in ctrs if c.v > 0]

            def wrapped(e):
                for sem, v in vals:
                    e.wait_ge(sem, v)
                fn(e)
            dec(wrapped)
        return decorate

    @property
    def tensor(self):
        return self._wrap(self._blk.tensor, (self._ctx.pe,))

    @property
    def vector(self):
        return self._wrap(self._blk.vector, (self._ctx.dve,))

    @property
    def scalar(self):
        return self._wrap(self._blk.scalar, (self._ctx.act, self._ctx.st))

    @property
    def sync(self):
        return self._wrap(self._blk.sync, (self._ctx.ld,))

    @property
    def gpsimd(self):
        return self._wrap(self._blk.gpsimd, (self._ctx.pool, self._ctx.lw))


def make_host_consts(T):
    c = {}
    j = np.arange(64, dtype=np.float64)
    inv_freq = 1.0 / (10000.0 ** (j / 64.0))
    t = np.arange(T, dtype=np.float64)
    ang = inv_freq[:, None] * t[None, :]
    c["rope0_cos"] = np.vstack([np.cos(ang), np.cos(ang)]).astype(ml_dtypes.bfloat16)
    c["rope0_sin"] = np.vstack([np.sin(ang), np.sin(ang)]).astype(ml_dtypes.bfloat16)

    half_idx = np.arange(0, F_DH, 2, dtype=np.float64)
    inv_freq2 = 1.0 / (10000.0 ** (half_idx / F_DH))
    ang2 = t[None, :] * inv_freq2[:, None]
    base = (half_idx + 0.4 * F_DH) / (1.4 * F_DH)
    power = (t - T // 2) / float(F_WINDOW // 2)
    scale = base[:, None] ** power[None, :]
    cos2, sin2 = np.cos(ang2), np.sin(ang2)
    def dup(a):
        return np.vstack([a, a]).astype(ml_dtypes.bfloat16)
    c["xpos_qcos"] = dup(cos2 * scale * F_QK_SCALE)
    c["xpos_qsin"] = dup(sin2 * scale * F_QK_SCALE)
    c["xpos_kcos"] = dup(cos2 / scale)
    c["xpos_ksin"] = dup(sin2 / scale)
    return c


def declare_ios(nc, T):
    io = {}
    io["x"] = nc.dram_tensor("x", [BPC, D, T], F32, kind="ExternalInput").ap()
    io["out"] = nc.dram_tensor("out", [BPC, D, T], F32, kind="ExternalOutput").ap()
    for name in WEIGHT_NAMES:
        io[name] = nc.dram_tensor(name, list(INPUT_SHAPES[name]), F32,
                                  kind="ExternalInput").ap()
    for name in ["rope0_cos", "rope0_sin"]:
        io[name] = nc.dram_tensor(name, [128, T], BF16, kind="ExternalInput").ap()
    for name in ["xpos_qcos", "xpos_qsin", "xpos_kcos", "xpos_ksin"]:
        io[name] = nc.dram_tensor(name, [64, T], BF16, kind="ExternalInput").ap()
    io["x_cur"] = nc.dram_tensor("x_cur", [BPC, D, T], F32, kind="Internal").ap()
    io["dbg_gates"] = nc.dram_tensor("dbg_gates", [96, T], F32, kind="ExternalOutput").ap()
    io["dbg_lcum"] = nc.dram_tensor("dbg_lcum", [8, T], F32, kind="ExternalOutput").ap()
    io["dbg_ltm"] = nc.dram_tensor("dbg_ltm", [128, (T // 128) * 24], F32, kind="ExternalOutput").ap()
    io["dbg_yall"] = nc.dram_tensor("dbg_yall", [128, NCH * T], BF16, kind="ExternalOutput").ap()
    io["dbg_attn"] = nc.dram_tensor("dbg_attn", [128, NCH * T], BF16, kind="ExternalOutput").ap()
    io["dbg_qh"] = nc.dram_tensor("dbg_qh", [128, T], BF16, kind="ExternalOutput").ap()
    io["dbg_u"] = nc.dram_tensor("dbg_u", [BPC, D, T], F32, kind="ExternalOutput").ap()
    io["dbg_ubf"] = nc.dram_tensor("dbg_ubf", [D, T], BF16, kind="ExternalOutput").ap()
    io["dbg_gv"] = nc.dram_tensor("dbg_gv", [128, (T // 128) * 8], F32, kind="ExternalOutput").ap()
    io["dbg_sc"] = nc.dram_tensor("dbg_sc", [128, (T // 128) * 8], F32, kind="ExternalOutput").ap()
    io["dbg_gem"] = nc.dram_tensor("dbg_gem", [8, T], F32, kind="ExternalOutput").ap()
    io["dbg_acc"] = nc.dram_tensor("dbg_acc", [BPC, D, T], F32, kind="ExternalOutput").ap()
    io["dbg_h"] = nc.dram_tensor("dbg_h", [128, NCH * T], BF16, kind="ExternalOutput").ap()
    io["w1bf"] = nc.dram_tensor("w1bf", [10, 1024, 4096], BF16, kind="Internal").ap()
    io["fqkv"] = nc.dram_tensor("fqkv", [2, 1024, 1536], BF16, kind="Internal").ap()
    io["fwo"] = nc.dram_tensor("fwo", [2, 512, 1024], BF16, kind="Internal").ap()
    io["fw1"] = nc.dram_tensor("fw1", [2, 1024, 5460], BF16, kind="Internal").ap()
    io["fw2"] = nc.dram_tensor("fw2", [2, 2730, 1024], BF16, kind="Internal").ap()
    io["w2bf"] = nc.dram_tensor("w2bf", [10, 4096, 1024], BF16, kind="Internal").ap()
    io["wqT"] = nc.dram_tensor("wqT", [1024, 1024], BF16, kind="Internal").ap()
    io["wkT"] = nc.dram_tensor("wkT", [1024, 1024], BF16, kind="Internal").ap()
    io["wvT"] = nc.dram_tensor("wvT", [1024, 1024], BF16, kind="Internal").ap()
    io["woT"] = nc.dram_tensor("woT", [1024, 1024], BF16, kind="Internal").ap()
    io["qkvT"] = nc.dram_tensor("qkvT", [1024, 3072], BF16, kind="Internal").ap()
    io["woutT"] = nc.dram_tensor("woutT", [1024, 1024], BF16, kind="Internal").ap()
    return io


def barrier(ctx):
    blk = ctx.block
    ctrs = (ctx.ld, ctx.lw, ctx.st, ctx.pe, ctx.dve, ctx.act, ctx.pool)
    vals = [(c.sem, c.v) for c in ctrs]
    for dec in (blk.tensor, blk.vector, blk.scalar, blk.sync, blk.gpsimd):
        @dec
        def _(e):
            for sem, v in vals:
                if v > 0:
                    e.wait_ge(sem, v)



def emit_prep(ctx, io):
    nc, blk = ctx.nc, ctx.block
    stack = contextlib.ExitStack()
    stage_row = stack.enter_context(nc.sbuf_tensor("stage_row", [1, 4096], F32))
    wg_rows = stack.enter_context(nc.sbuf_tensor("wg_rows", [96, 1024], F32))
    wt_stage = stack.enter_context(nc.sbuf_tensor("wt_stage", [128, 256], BF16))
    wt_out = stack.enter_context(nc.sbuf_tensor("wt_out", [128, 256], BF16))

    @blk.gpsimd
    def _(g):
        memsets = [(ctx.ident_bf[:], 0.0), (ctx.ident_f32[:], 0.0),
                   (ctx.mask_up[:], 0.0), (ctx.mask_low[:], 0.0),
                   (ctx.sel8[:], 0.0), (wg_rows[:], 0.0),
                   (ctx.ones_bf[:], 1.0), (ctx.ones_f32[:], 1.0),
                   (stage_row[:], 0.0), (ctx.zero_col[:], 0.0)]
        for ap, val in memsets:
            g.memset(ap, val).then_inc(ctx.pool.sem, 1)
        g.memset(ctx.zero_col[:, 1:2], LN_A_SCALE).then_inc(ctx.pool.sem, 1)
        n1 = ctx.pool.inc(len(memsets) + 1)
        g.wait_ge(ctx.pool.sem, n1)
        for ident in (ctx.ident_bf, ctx.ident_f32):
            g.affine_select(out=ident[:], in_=ident[:], compare_op=ALU.not_equal,
                            fill=1.0, base=0, pattern=[[-1, 128]],
                            channel_multiplier=1).then_inc(ctx.pool.sem, 1)
        g.affine_select(out=ctx.mask_up[:], in_=ctx.mask_up[:], compare_op=ALU.is_ge,
                        fill=NEG, base=0, pattern=[[1, 128]],
                        channel_multiplier=-1).then_inc(ctx.pool.sem, 1)
        g.affine_select(out=ctx.mask_low[:], in_=ctx.mask_low[:], compare_op=ALU.is_ge,
                        fill=NEG, base=0, pattern=[[-1, 128]],
                        channel_multiplier=1).then_inc(ctx.pool.sem, 1)
        for h in range(8):
            sl = ctx.sel8[:, h * 128:(h + 1) * 128]
            g.affine_select(out=sl, in_=sl, compare_op=ALU.not_equal, fill=1.0,
                            base=-h, pattern=[[0, 128]],
                            channel_multiplier=1).then_inc(ctx.pool.sem, 1)
        ctx.pool.inc(12)
        g.wait_ge(ctx.pool.sem, ctx.pool.v)

    reg = {}
    specs = []
    col = 0

    def add_vec(name, ap_flat, length):
        nonlocal col
        reg[name] = col
        specs.append((col, ap_flat, length))
        col += (length + 127) // 128

    add_vec("a_norm_w", io["L0_attn_norm_w"], 1024)
    add_vec("a_bout", io["L0_attn_bout"], 1024)
    add_vec("m_gs", io["L0_ffn_g_shared"], 1024)
    add_vec("m_gr", io["L0_ffn_g_routed"], 1024)
    for e in range(MOE_ES):
        add_vec(f"m_b1_s{e}", io["L0_ffn_b1_s"][e], 4096)
        add_vec(f"m_b2_s{e}", io["L0_ffn_b2_s"][e], 1024)
    for e in range(MOE_ER):
        add_vec(f"m_b1_r{e}", io["L0_ffn_b1_r"][e], 4096)
        add_vec(f"m_b2_r{e}", io["L0_ffn_b2_r"][e], 1024)
    add_vec("c_lng", io["L1_attn_ln_g"], 1024)
    add_vec("c_lnb", io["L1_attn_ln_b"], 1024)
    add_vec("c_bqkv", io["L1_attn_b_qkv"], 3072)
    add_vec("c_went", io["L1_attn_w_ent"][0], 1024)
    add_vec("c_bent", io["L1_attn_b_ent"], 1)
    add_vec("c_bout", io["L1_attn_b_out"], 1024)
    for l in range(2):
        add_vec(f"f_qs{l}", io["L1_ffn_q_scale"][l], 64)
        add_vec(f"f_ks{l}", io["L1_ffn_k_scale"][l], 64)
        add_vec(f"f_lng{l}", io["L1_ffn_ln_g"][l], 1024)
        add_vec(f"f_lnb{l}", io["L1_ffn_ln_b"][l], 1024)
        add_vec(f"f_b1a{l}", io["L1_ffn_b1"][l][0:2730], 2730)
        add_vec(f"f_b1g{l}", io["L1_ffn_b1"][l][2730:5460], 2730)
        add_vec(f"f_b2{l}", io["L1_ffn_b2"][l], 1024)
    reg["a_gbias"] = col
    gb_col = col
    col += 1
    ctx.bias_reg = reg
    assert col <= ctx.bias_pool.shape[1], (col, tuple(ctx.bias_pool.shape))

    @blk.sync
    def _(s):
        s.wait_ge(ctx.pool.sem, ctx.pool.v)
        s.dma_start(out=stage_row[0:1, 0:8],
                    in_=io["L0_attn_bbeta"].rearrange("(a b) -> a b", a=1)).then_inc(ctx.ld.sem, 16)
        s.dma_start(out=stage_row[0:1, 32:40],
                    in_=io["L0_attn_big"].rearrange("(a b) -> a b", a=1)).then_inc(ctx.ld.sem, 16)
        s.dma_start(out=stage_row[0:1, 64:72],
                    in_=io["L0_attn_bog"].rearrange("(a b) -> a b", a=1)).then_inc(ctx.ld.sem, 16)
    ctx.ld.inc(3)

    @blk.tensor
    def _(t):
        t.wait_ge(ctx.ld.sem, ctx.ld.v)
        t.wait_ge(ctx.pool.sem, ctx.pool.v)
        t.transpose(ctx.ps_a[:, 0:1], stage_row[0:1, 0:128],
                    ctx.ident_f32[0:1, 0:1]).then_inc(ctx.pe.sem, 1)
    ctx.pe.inc(1)

    @blk.vector
    def _(v):
        v.wait_ge(ctx.pe.sem, ctx.pe.v)
        v.tensor_copy(ctx.bias_pool[:, gb_col:gb_col + 1],
                      ctx.ps_a[:, 0:1]).then_inc(ctx.dve.sem, 1)
    ctx.dve.inc(1)

    group, used, groups = [], 0, []
    for s_ in specs:
        ncols = (s_[2] + 127) // 128
        if used + ncols > 32:
            groups.append(group)
            group, used = [], 0
        group.append((s_, used))
        used += ncols
    groups.append(group)

    for grp in groups:
        @blk.sync
        def _(s, grp=grp):
            s.wait_ge(ctx.dve.sem, ctx.dve.v)
            for (c0, ap_flat, length), off in grp:
                s.dma_start(out=stage_row[0:1, off * 128: off * 128 + length],
                            in_=ap_flat.rearrange("(a b) -> a b", a=1)
                            ).then_inc(ctx.ld.sem, 16)
        ctx.ld.inc(len(grp))

        ncols_g = sum((sp[2] + 127) // 128 for sp, _ in grp)

        @blk.tensor
        def _(t, ncols_g=ncols_g):
            t.wait_ge(ctx.ld.sem, ctx.ld.v)
            t.wait_ge(ctx.dve.sem, ctx.dve.v)
            last = None
            for k in range(ncols_g):
                last = t.transpose(ctx.ps_a[:, k:k + 1],
                                   stage_row[0:1, k * 128:(k + 1) * 128],
                                   ctx.ident_f32[0:1, 0:1])
            last.then_inc(ctx.pe.sem, 1)
        ctx.pe.inc(1)

        @blk.vector
        def _(v, grp=grp):
            v.wait_ge(ctx.pe.sem, ctx.pe.v)
            for n_, ((c0, ap_flat, length), off) in enumerate(grp):
                nch = (length + 127) // 128
                cc = v.tensor_copy(ctx.bias_pool[:, c0:c0 + nch],
                                   ctx.ps_a[:, off:off + nch])
                if n_ == len(grp) - 1:
                    cc.then_inc(ctx.dve.sem, 1)
        ctx.dve.inc(1)

    @blk.sync
    def _(s):
        s.wait_ge(ctx.pool.sem, ctx.pool.v)
        s.dma_start(out=wg_rows[0:8, :], in_=io["L0_attn_Wbeta"]).then_inc(ctx.ld.sem, 16)
        s.dma_start(out=wg_rows[32:40, :], in_=io["L0_attn_Wig"]).then_inc(ctx.ld.sem, 16)
        s.dma_start(out=wg_rows[64:72, :], in_=io["L0_attn_Wog"]).then_inc(ctx.ld.sem, 16)
        s.dma_start(out=ctx.cent_sb[:].rearrange("p (ch e) -> p ch e", ch=8),
                    in_=io["L0_ffn_centroids"].rearrange(
                        "(ch p) e -> p ch e", p=128)).then_inc(ctx.ld.sem, 16)
    ctx.ld.inc(4)

    nw = reg["a_norm_w"]
    for rnd in range(2):
        @blk.tensor
        def _(t, rnd=rnd):
            t.wait_ge(ctx.ld.sem, ctx.ld.v)
            t.wait_ge(ctx.dve.sem, ctx.dve.v)
            last = None
            for k in range(4):
                last = t.transpose(ctx.ps_a[:, k * 96:k * 96 + 96],
                                   wg_rows[0:96, (rnd * 4 + k) * 128:(rnd * 4 + k + 1) * 128],
                                   ctx.ident_f32[0:96, 0:96])
            last.then_inc(ctx.pe.sem, 1)
        ctx.pe.inc(1)

        @blk.vector
        def _(v, rnd=rnd):
            v.wait_ge(ctx.pe.sem, ctx.pe.v)
            for k in range(4):
                kk = rnd * 4 + k
                cc = v.tensor_scalar(ctx.wgA[:, kk * 96:(kk + 1) * 96],
                                     ctx.ps_a[:, k * 96:k * 96 + 96],
                                     ctx.bias_pool[:, nw + kk:nw + kk + 1], None, ALU.mult)
                if k == 3:
                    cc.then_inc(ctx.dve.sem, 1)
        ctx.dve.inc(1)

    @blk.tensor
    def _(t):
        t.wait_ge(ctx.dve.sem, ctx.dve.v)
        for k in range(8):
            mm = t.matmul(ctx.ps_a[0:96, 0:1], ctx.wgA[:, k * 96:(k + 1) * 96],
                          ctx.ones_f32[:, 0:1], start=(k == 0), stop=(k == 7))
        mm.then_inc(ctx.pe.sem, 1)
    ctx.pe.inc(1)

    @blk.vector
    def _(v):
        v.wait_ge(ctx.pe.sem, ctx.pe.v)
        v.tensor_copy(ctx.wg_colsum[0:96, 0:1], ctx.ps_a[0:96, 0:1])
        c = ctx.bias_reg["c_bout"]
        v.tensor_scalar(ctx.bias_pool[:, c:c + 8], ctx.bias_pool[:, c:c + 8],
                        0.1, None, ALU.mult)
        c = ctx.bias_reg["c_went"]
        v.tensor_copy(ctx.went_bf[:], ctx.bias_pool[:, c:c + 8]).then_inc(ctx.dve.sem, 1)
    ctx.dve.inc(1)

    wcv = stack.enter_context(nc.sbuf_tensor("wcv", [128, 2 * 4096], BF16))
    moe_list = ([("L0_ffn_W1_s", "L0_ffn_W2_s", "m_gs", MOE_ES, 0)] +
                [("L0_ffn_W1_r", "L0_ffn_W2_r", "m_gr", MOE_ER, MOE_ES)])
    cvt_jobs = []
    for w1n, w2n, gn, ne, base in moe_list:
        for e in range(ne):
            for ch in range(8):
                cvt_jobs.append((io[w1n][e][ch * 128:(ch + 1) * 128, :],
                                 io["w1bf"][base + e][ch * 128:(ch + 1) * 128, :],
                                 reg[gn] + ch, 4096))
            for hc in range(32):
                cvt_jobs.append((io[w2n][e][hc * 128:(hc + 1) * 128, :],
                                 io["w2bf"][base + e][hc * 128:(hc + 1) * 128, :],
                                 None, 1024))
    for l in range(2):
        for ch in range(8):
            cvt_jobs.append((io["L1_ffn_wqkv"][l][ch * 128:(ch + 1) * 128, :],
                             io["fqkv"][l][ch * 128:(ch + 1) * 128, :], None, 1536))
        for ch in range(4):
            cvt_jobs.append((io["L1_ffn_wo"][l][ch * 128:(ch + 1) * 128, :],
                             io["fwo"][l][ch * 128:(ch + 1) * 128, :], None, 1024))
        for ch in range(8):
            cvt_jobs.append((io["L1_ffn_w1"][l][ch * 128:(ch + 1) * 128, 0:2730],
                             io["fw1"][l][ch * 128:(ch + 1) * 128, 0:2730], None, 2730))
            cvt_jobs.append((io["L1_ffn_w1"][l][ch * 128:(ch + 1) * 128, 2730:5460],
                             io["fw1"][l][ch * 128:(ch + 1) * 128, 2730:5460], None, 2730))
        for ch in range(21):
            cvt_jobs.append((io["L1_ffn_w2"][l][ch * 128:(ch + 1) * 128, :],
                             io["fw2"][l][ch * 128:(ch + 1) * 128, :], None, 1024))
        cvt_jobs.append((io["L1_ffn_w2"][l][2688:2730, :],
                         io["fw2"][l][2688:2730, :], None, 1024))
    for j, (srcap, dstap, gcol, width) in enumerate(cvt_jobs):
        slot = j % 2
        nprt = srcap.shape[0]
        buf = wcv[0:nprt, slot * 4096: slot * 4096 + width]

        @blk.gpsimd
        def _(g, buf=buf, srcap=srcap):
            g.wait_ge(ctx.st.sem, max(ctx.st.v - 16, 0))
            g.dma_start(out=buf, in_=srcap).then_inc(ctx.lw.sem, 16)
        ctx.lw.inc(1)

        if gcol is not None:
            @blk.vector
            def _(v, buf=buf, gcol=gcol):
                v.wait_ge(ctx.lw.sem, ctx.lw.v)
                v.tensor_scalar(buf, buf, ctx.bias_pool[:, gcol:gcol + 1], None,
                                ALU.mult).then_inc(ctx.dve.sem, 1)
            ctx.dve.inc(1)

        @blk.scalar
        def _(s, buf=buf, dstap=dstap, gcol=gcol):
            if gcol is not None:
                s.wait_ge(ctx.dve.sem, ctx.dve.v)
            else:
                s.wait_ge(ctx.lw.sem, ctx.lw.v)
            s.dma_start(out=dstap, in_=buf).then_inc(ctx.st.sem, 16)
        ctx.st.inc(1)

    jobs = [
        (io["L0_attn_Wq"], io["wqT"], 1024, 1024, 1.0),
        (io["L0_attn_Wk"], io["wkT"], 1024, 1024, 1.0),
        (io["L0_attn_Wv"], io["wvT"], 1024, 1024, 1.0),
        (io["L0_attn_Wout"], io["woT"], 1024, 1024, 1.0),
        (io["L1_attn_w_qkv"], io["qkvT"], 3072, 1024, 1.0),
        (io["L1_attn_w_out"], io["woutT"], 1024, 1024, 0.1),
    ]
    for src, dst, R, C, scale in jobs:
        for rb in range(R // 128):
            for cb in range(C // 128):
                slot = (rb * (C // 128) + cb) % 2
                stg = wt_stage[:, slot * 128:(slot + 1) * 128]
                out_sb = wt_out[:, slot * 128:(slot + 1) * 128]
                psl = ctx.ps_tr[:, slot * 128:(slot + 1) * 128]

                @blk.gpsimd
                def _(g, stg=stg, src=src, rb=rb, cb=cb):
                    g.wait_ge(ctx.pe.sem, max(ctx.pe.v - 1, 0))
                    g.dma_start(out=stg,
                                in_=src[rb * 128:(rb + 1) * 128, cb * 128:(cb + 1) * 128]
                                ).then_inc(ctx.lw.sem, 16)
                ctx.lw.inc(1)

                @blk.tensor
                def _(t, psl=psl, stg=stg):
                    t.wait_ge(ctx.lw.sem, ctx.lw.v)
                    t.wait_ge(ctx.dve.sem, max(ctx.dve.v - 1, 0))
                    t.transpose(psl, stg, ctx.ident_bf[:]).then_inc(ctx.pe.sem, 1)
                ctx.pe.inc(1)

                @blk.vector
                def _(v, out_sb=out_sb, psl=psl, scale=scale):
                    v.wait_ge(ctx.pe.sem, ctx.pe.v)
                    v.wait_ge(ctx.st.sem, max(ctx.st.v - 16, 0))
                    if scale == 1.0:
                        v.tensor_copy(out_sb, psl).then_inc(ctx.dve.sem, 1)
                    else:
                        v.tensor_scalar(out_sb, psl, scale, None,
                                        ALU.mult).then_inc(ctx.dve.sem, 1)
                ctx.dve.inc(1)

                @blk.scalar
                def _(sc, dst=dst, rb=rb, cb=cb, out_sb=out_sb):
                    sc.wait_ge(ctx.dve.sem, ctx.dve.v)
                    sc.dma_start(out=dst[cb * 128:(cb + 1) * 128, rb * 128:(rb + 1) * 128],
                                 in_=out_sb).then_inc(ctx.st.sem, 16)
                ctx.st.inc(1)

    barrier(ctx)
    stack.close()



def emit_tile_stats(ctx, get_chunk, sq_scr, is_f32, extra_mm=None):
    blk = ctx.block
    ones = ctx.ones_f32 if is_f32 else ctx.ones_bf
    pe_hist = {}
    for ch in range(NCH):
        src = get_chunk(ch)
        slot = ch % 2
        scr = sq_scr[:, slot * 512:(slot + 1) * 512]
        wait_pe = pe_hist.get(ch - 2, 0)

        @blk.scalar
        def _(s, src=src, scr=scr, wait_pe=wait_pe, ch=ch):
            if ch == 0:
                s.wait_ge(ctx.dve.sem, ctx.dve.v)
            if wait_pe:
                s.wait_ge(ctx.pe.sem, wait_pe)
            s.activation(scr, src, AF.Square).then_inc(ctx.act.sem, 1)
        ctx.act.inc(1)

        @blk.tensor
        def _(t, src=src, scr=scr, ch=ch):
            t.wait_ge(ctx.act.sem, ctx.act.v)
            if ch == 0:
                t.wait_ge(ctx.dve.sem, ctx.dve.v)
            t.matmul(ctx.ps_b[0:1, 0:512], ones[:, 0:1], src,
                     start=(ch == 0), stop=(ch == NCH - 1))
            mm = t.matmul(ctx.ps_c[0:1, 0:512], ones[:, 0:1], scr,
                          start=(ch == 0), stop=(ch == NCH - 1))
            if extra_mm is not None:
                mm = extra_mm(t, ch) or mm
            mm.then_inc(ctx.pe.sem, 1)
        ctx.pe.inc(1)
        pe_hist[ch] = ctx.pe.v


def emit_tile_rows(ctx, eps):
    blk = ctx.block

    @blk.vector
    def _(v):
        v.wait_ge(ctx.pe.sem, ctx.pe.v)
        v.tensor_scalar(ctx.row_a[0:1, :], ctx.ps_b[0:1, 0:512], 1.0 / D, None, ALU.mult)
        v.tensor_scalar(ctx.row_b[0:1, :], ctx.ps_c[0:1, 0:512], 1.0 / D, None, ALU.mult)
        v.tensor_tensor(ctx.row_d[0:1, :], ctx.row_a[0:1, :], ctx.row_a[0:1, :], ALU.mult)
        v.tensor_tensor(ctx.row_b[0:1, :], ctx.row_b[0:1, :], ctx.row_d[0:1, :], ALU.subtract)
        v.tensor_scalar(ctx.row_c[0:1, :], ctx.row_b[0:1, :], eps, None,
                        ALU.add).then_inc(ctx.dve.sem, 1)
    ctx.dve.inc(1)

    @blk.scalar
    def _(s):
        s.wait_ge(ctx.dve.sem, ctx.dve.v)
        s.activation(ctx.row_c[0:1, :], ctx.row_c[0:1, :], AF.Sqrt).then_inc(ctx.act.sem, 1)
    ctx.act.inc(1)

    @blk.vector
    def _(v):
        v.wait_ge(ctx.act.sem, ctx.act.v)
        v.reciprocal(ctx.row_c[0:1, :], ctx.row_c[0:1, :]).then_inc(ctx.dve.sem, 1)
    ctx.dve.inc(1)


def emit_tile_bcast(ctx, rows_out):
    blk = ctx.block

    @blk.tensor
    def _(t):
        t.wait_ge(ctx.dve.sem, ctx.dve.v)
        mm = None
        for i, (row, out) in enumerate(rows_out):
            psx = (ctx.ps_b, ctx.ps_c)[i % 2]
            mm = t.matmul(psx[:, 0:512], ctx.ones_f32[0:1, :], row[0:1, :],
                          start=True, stop=True)
        mm.then_inc(ctx.pe.sem, 1)
    ctx.pe.inc(1)

    @blk.vector
    def _(v):
        v.wait_ge(ctx.pe.sem, ctx.pe.v)
        cc = None
        for i, (row, out) in enumerate(rows_out):
            psx = (ctx.ps_b, ctx.ps_c)[i % 2]
            cc = v.tensor_copy(out[:, 0:512], psx[:, 0:512])
        cc.then_inc(ctx.dve.sem, 1)
    ctx.dve.inc(1)



def emit_phase_a(ctx, io, b, src="x", dst="x_cur"):
    nc, blk, T = ctx.nc, ctx.block, ctx.T
    NT = T // 512
    NCK = T // A_CHUNK
    stack = contextlib.ExitStack()

    def sb(name, shape, dt):
        return stack.enter_context(nc.sbuf_tensor(f"a{b}_{name}", shape, dt))

    h = sb("h", [128, NCH * T], BF16)
    yall = sb("yall", [128, NCH * T], BF16)
    x_it = sb("x_it", [128, NCH * 512], F32)
    gates = sb("gates", [96, T], F32)
    lcum = sb("lcum", [8, T], F32)
    ogr = sb("ogr", [8, T], F32)
    ones8 = sb("ones8", [8, T], F32)
    ltm = sb("ltm", [128, NCK * 24], F32)
    lnig = sb("lnig", [8, T], F32)
    qh = sb("qh", [128, T], BF16)
    kh = sb("kh", [128, T], BF16)
    vh = sb("vh", [128, T], BF16)
    ropec = sb("ropec", [128, T], BF16)
    ropes = sb("ropes", [128, T], BF16)
    rsc1 = sb("rsc1", [128, T], BF16)
    rsc2 = sb("rsc2", [128, T], BF16)
    sqf = sb("sqf", [128, 1024], F32)
    sqb = sb("sqb", [128, 1024], BF16)
    whb = sb("whb", [128, 3 * 1024], BF16)
    wob = sb("wob", [128, 1024], BF16)
    m_f32 = sb("m_f32", [128, 128], F32)
    m_bf = sb("m_bf", [128, 128], BF16)
    cols = sb("cols", [128, 8], F32)
    ogb = sb("ogb", [128, 128], F32)
    darg = sb("darg", [128, 128], F32)
    aarg = sb("aarg", [128, 128], F32)
    aexp = sb("aexp", [128, 128], F32)
    pt_bf = sb("pt_bf", [128, 128], BF16)
    kt_bf = sb("kt_bf", [128, 128], BF16)
    vt_bf = sb("vt_bf", [128, 128], BF16)
    scr128 = sb("scr128", [128, 128], F32)
    ctx.row_a = sb("row_a", [1, 512], F32)
    ctx.row_b = sb("row_b", [1, 512], F32)
    ctx.row_c = sb("row_c", [1, 512], F32)
    ctx.row_d = sb("row_d", [1, 512], F32)
    mu_bt = sb("mu_bt", [128, 512], F32)
    rho_bt = sb("rho_bt", [128, 512], F32)
    scr_t = sb("scr_t", [128, 512], F32)

    @blk.gpsimd
    def _(g):
        g.memset(ones8[:], 1.0).then_inc(ctx.pool.sem, 1)
    ctx.pool.inc(1)

    gb = ctx.bias_reg["a_gbias"]
    nw = ctx.bias_reg["a_norm_w"]

    for it in range(NT):
        tsl = slice(it * 512, (it + 1) * 512)

        @blk.sync
        def _(s, tsl=tsl):
            s.wait_ge(ctx.dve.sem, ctx.dve.v)
            for ch in range(NCH):
                s.dma_start(out=x_it[:, ch * 512:(ch + 1) * 512],
                            in_=io[src][b, ch * 128:(ch + 1) * 128, tsl]
                            ).then_inc(ctx.ld.sem, 16)
        ctx.ld.inc(NCH)

        @blk.scalar
        def _(s):
            s.wait_ge(ctx.ld.sem, ctx.ld.v)
            s.sem_inc(ctx.act.sem, 1)
        ctx.act.inc(1)

        @blk.tensor
        def _(t):
            t.wait_ge(ctx.ld.sem, ctx.ld.v)
            t.sem_inc(ctx.pe.sem, 1)
        ctx.pe.inc(1)

        def gate_mm(t, ch):
            return t.matmul(ctx.ps_a[0:96, 0:512], ctx.wgA[:, ch * 96:(ch + 1) * 96],
                            x_it[:, ch * 512:(ch + 1) * 512],
                            start=(ch == 0), stop=(ch == NCH - 1))
        emit_tile_stats(ctx, lambda ch: x_it[:, ch * 512:(ch + 1) * 512],
                        sqf, is_f32=True, extra_mm=gate_mm)

        @blk.vector
        def _(v):
            v.wait_ge(ctx.pe.sem, ctx.pe.v)
            v.tensor_scalar(ctx.row_a[0:1, :], ctx.ps_b[0:1, 0:512], 1.0 / D, None, ALU.mult)
            v.tensor_scalar(ctx.row_b[0:1, :], ctx.ps_c[0:1, 0:512], 1.0 / D, None, ALU.mult)
            v.tensor_tensor(ctx.row_d[0:1, :], ctx.row_a[0:1, :], ctx.row_a[0:1, :], ALU.mult)
            v.tensor_tensor(ctx.row_b[0:1, :], ctx.row_b[0:1, :], ctx.row_d[0:1, :], ALU.subtract)
            v.tensor_scalar(ctx.row_c[0:1, :], ctx.row_b[0:1, :], 1e-5, None, ALU.add)
            v.reciprocal(ctx.row_d[0:1, :], ctx.row_c[0:1, :])
            v.tensor_tensor(ctx.row_d[0:1, :], ctx.row_b[0:1, :], ctx.row_d[0:1, :], ALU.mult)
            v.tensor_scalar(ctx.row_d[0:1, :], ctx.row_d[0:1, :], 1e-6, None, ALU.add)
            v.tensor_tensor(ctx.row_c[0:1, :], ctx.row_c[0:1, :], ctx.row_d[0:1, :],
                            ALU.mult).then_inc(ctx.dve.sem, 1)
        ctx.dve.inc(1)

        @blk.scalar
        def _(s):
            s.wait_ge(ctx.dve.sem, ctx.dve.v)
            s.activation(ctx.row_c[0:1, :], ctx.row_c[0:1, :], AF.Sqrt).then_inc(ctx.act.sem, 1)
        ctx.act.inc(1)

        @blk.vector
        def _(v):
            v.wait_ge(ctx.act.sem, ctx.act.v)
            v.reciprocal(ctx.row_c[0:1, :], ctx.row_c[0:1, :]).then_inc(ctx.dve.sem, 1)
        ctx.dve.inc(1)

        emit_tile_bcast(ctx, [(ctx.row_a, mu_bt), (ctx.row_c, rho_bt)])

        @blk.vector
        def _(v):
            v.scalar_tensor_tensor(scr_t[0:96, :], mu_bt[0:96, 0:512],
                                   ctx.wg_colsum[0:96, 0:1], ctx.ps_a[0:96, 0:512],
                                   op0=ALU.mult, op1=ALU.subtract)
            v.tensor_tensor(scr_t[0:96, :], scr_t[0:96, :], rho_bt[0:96, 0:512],
                            ALU.mult).then_inc(ctx.dve.sem, 1)
        ctx.dve.inc(1)

        @blk.scalar
        def _(s, tsl=tsl):
            s.wait_ge(ctx.dve.sem, ctx.dve.v)
            s.activation(gates[0:96, tsl], scr_t[0:96, :], AF.Sigmoid,
                         bias=ctx.bias_pool[0:96, gb:gb + 1],
                         scale=-1.0).then_inc(ctx.act.sem, 1)
        ctx.act.inc(1)

        for ch in range(NCH):
            @blk.vector
            def _(v, ch=ch, it=it):
                if ch == 0:
                    v.wait_ge(ctx.act.sem, ctx.act.v)
                v.tensor_tensor(scr_t[:, 0:512], x_it[:, ch * 512:(ch + 1) * 512],
                                mu_bt[:, 0:512], ALU.subtract)
                v.scalar_tensor_tensor(h[:, ch * T + it * 512: ch * T + (it + 1) * 512],
                                       scr_t[:, 0:512],
                                       ctx.bias_pool[:, nw + ch:nw + ch + 1],
                                       rho_bt[:, 0:512],
                                       op0=ALU.mult, op1=ALU.mult).then_inc(ctx.dve.sem, 1)
            ctx.dve.inc(1)

    @blk.vector
    def _(v):
        v.wait_ge(ctx.act.sem, ctx.act.v)
        v.wait_ge(ctx.pool.sem, ctx.pool.v)
        v.tensor_scalar(lcum[0:8, :T], gates[0:8, :T], 1e-15, None,
                        ALU.max).then_inc(ctx.dve.sem, 1)
    ctx.dve.inc(1)

    @blk.scalar
    def _(s):
        s.wait_ge(ctx.dve.sem, ctx.dve.v)
        s.activation(lcum[0:8, :T], lcum[0:8, :T], AF.Ln).then_inc(ctx.act.sem, 1)
    ctx.act.inc(1)

    @blk.vector
    def _(v):
        v.wait_ge(ctx.act.sem, ctx.act.v)
        v.tensor_tensor_scan(lcum[0:8, :T], ones8[0:8, :T], lcum[0:8, :T], 0.0,
                             op0=ALU.mult, op1=ALU.add).then_inc(ctx.dve.sem, 1)
    ctx.dve.inc(1)

    @blk.gpsimd
    def _(g):
        g.wait_ge(ctx.act.sem, ctx.act.v)
        g.dma_start(out=ogr[0:8, :T], in_=gates[64:72, :T]).then_inc(ctx.lw.sem, 16)
    ctx.lw.inc(1)

    @blk.scalar
    def _(s):
        s.activation(lnig[0:8, :T], gates[32:40, :T], AF.Ln).then_inc(ctx.act.sem, 1)
    ctx.act.inc(1)

    @blk.vector
    def _(v):
        v.wait_ge(ctx.act.sem, ctx.act.v)
        v.tensor_scalar(lnig[0:8, :T], lnig[0:8, :T], LN_A_SCALE, None,
                        ALU.add).then_inc(ctx.dve.sem, 1)
    ctx.dve.inc(1)

    for ck in range(NCK):
        bank_pos = ck % 21

        @blk.tensor
        def _(t, ck=ck, bank_pos=bank_pos):
            t.wait_ge(ctx.dve.sem, ctx.dve.v)
            t.transpose(ctx.ps_b[:, bank_pos * 24:bank_pos * 24 + 8],
                        lcum[0:8, ck * 128:(ck + 1) * 128], ctx.ident_f32[0:8, 0:8])
            t.transpose(ctx.ps_b[:, bank_pos * 24 + 8:bank_pos * 24 + 16],
                        gates[32:40, ck * 128:(ck + 1) * 128],
                        ctx.ident_f32[32:40, 32:40])
            t.transpose(ctx.ps_b[:, bank_pos * 24 + 16:bank_pos * 24 + 24],
                        lnig[0:8, ck * 128:(ck + 1) * 128],
                        ctx.ident_f32[0:8, 0:8]).then_inc(ctx.pe.sem, 1)
        ctx.pe.inc(1)

        if bank_pos == 20 or ck == NCK - 1:
            n_in = bank_pos + 1
            c0 = ck - n_in + 1

            @blk.vector
            def _(v, n_in=n_in, c0=c0):
                v.wait_ge(ctx.pe.sem, ctx.pe.v)
                v.tensor_copy(ltm[:, c0 * 24:(c0 + n_in) * 24],
                              ctx.ps_b[:, 0:n_in * 24]).then_inc(ctx.dve.sem, 1)
            ctx.dve.inc(1)

    @blk.sync
    def _(s):
        s.dma_start(out=ropec[:, :T], in_=io["rope0_cos"]).then_inc(ctx.ld.sem, 16)
        s.dma_start(out=ropes[:, :T], in_=io["rope0_sin"]).then_inc(ctx.ld.sem, 16)
    ctx.ld.inc(2)

    if getattr(ctx, "debug", False) and b == 0:
        @blk.scalar
        def _(s):
            s.wait_ge(ctx.dve.sem, ctx.dve.v)
            s.wait_ge(ctx.act.sem, ctx.act.v)
            s.dma_start(out=io["dbg_gates"], in_=gates[0:96, :T]).then_inc(ctx.st.sem, 16)
            s.dma_start(out=io["dbg_lcum"], in_=lcum[0:8, :T]).then_inc(ctx.st.sem, 16)
            s.dma_start(out=io["dbg_ltm"], in_=ltm[:]).then_inc(ctx.st.sem, 16)
            s.dma_start(out=io["dbg_h"], in_=h[:]).then_inc(ctx.st.sem, 16)
        ctx.st.inc(4)

    whb_free = {}
    for head in range(A_NH):
        @blk.sync
        def _(s, head=head):
            if whb_free.get(head - 1):
                s.wait_ge(ctx.pe.sem, whb_free[head - 1])
            else:
                s.wait_ge(ctx.dve.sem, ctx.dve.v)
            for wi, name in enumerate(("wqT", "wkT", "wvT")):
                s.dma_start(
                    out=whb[:, wi * 1024:(wi + 1) * 1024].rearrange(
                        "p (c e) -> p c e", e=128),
                    in_=io[name].rearrange("(c p) e -> p c e", p=128)[
                        :, :, head * 128:(head + 1) * 128]).then_inc(ctx.ld.sem, 16)
        ctx.ld.inc(3)

        for wi, dstbuf in ((0, qh), (1, kh), (2, vh)):
            for it in range(NT):
                @blk.tensor
                def _(t, wi=wi, it=it):
                    t.wait_ge(ctx.ld.sem, ctx.ld.v)
                    t.wait_ge(ctx.dve.sem, ctx.dve.v)
                    for ch in range(NCH):
                        mm = t.matmul(ctx.ps_a[:, 0:512],
                                      whb[:, wi * 1024 + ch * 128: wi * 1024 + (ch + 1) * 128],
                                      h[:, ch * T + it * 512: ch * T + (it + 1) * 512],
                                      start=(ch == 0), stop=(ch == NCH - 1))
                    mm.then_inc(ctx.pe.sem, 1)
                ctx.pe.inc(1)

                @blk.vector
                def _(v, dstbuf=dstbuf, it=it):
                    v.wait_ge(ctx.pe.sem, ctx.pe.v)
                    v.tensor_copy(dstbuf[:, it * 512:(it + 1) * 512],
                                  ctx.ps_a[:, 0:512]).then_inc(ctx.dve.sem, 1)
                ctx.dve.inc(1)

        whb_free[head] = ctx.pe.v

        for buf in (qh, kh):
            @blk.vector
            def _(v, buf=buf):
                v.tensor_tensor(rsc2[0:64, :T], buf[64:128, :T], ropes[64:128, :T], ALU.mult)
                v.tensor_tensor(rsc1[64:128, :T], buf[0:64, :T], ropes[0:64, :T], ALU.mult)
                v.tensor_tensor(buf[0:64, :T], buf[0:64, :T], ropec[0:64, :T], ALU.mult)
                v.tensor_tensor(buf[64:128, :T], buf[64:128, :T], ropec[64:128, :T], ALU.mult)
                v.tensor_tensor(buf[0:64, :T], buf[0:64, :T], rsc2[0:64, :T], ALU.subtract)
                v.tensor_tensor(buf[64:128, :T], buf[64:128, :T], rsc1[64:128, :T],
                                ALU.add).then_inc(ctx.dve.sem, 1)
            ctx.dve.inc(1)

        @blk.vector
        def _(v):
            v.memset(m_f32[:], 0.0)
            v.memset(m_bf[:], 0.0).then_inc(ctx.dve.sem, 1)
        ctx.dve.inc(1)

        for ck in range(NCK):
            c0 = ck * 128
            csl = slice(c0, c0 + 128)
            lcol = ltm[:, ck * 24 + head: ck * 24 + head + 1]
            igcol = ltm[:, ck * 24 + 8 + head: ck * 24 + 8 + head + 1]
            lnigcol = ltm[:, ck * 24 + 16 + head: ck * 24 + 16 + head + 1]
            sel = ctx.sel8[:, head * 128:(head + 1) * 128]

            @blk.tensor
            def _(t, ck=ck, c0=c0, csl=csl, sel=sel):
                t.wait_ge(ctx.dve.sem, ctx.dve.v)
                t.wait_ge(ctx.lw.sem, ctx.lw.v)
                t.matmul(ctx.ps_misc[:, 0:128], sel, lcum[0:8, csl],
                         start=True, stop=True)
                t.matmul(ctx.ps_misc[:, 128:256], sel, ogr[0:8, csl],
                         start=True, stop=True)
                if ck > 0:
                    t.matmul(ctx.ps_misc[:, 256:257], sel, lcum[0:8, c0 - 1:c0],
                             start=True, stop=True)
                t.matmul(ctx.ps_misc[:, 257:258], sel, lcum[0:8, c0 + 127:c0 + 128],
                         start=True, stop=True)
                t.matmul(ctx.ps_st[:, 0:128], kh[:, csl], qh[:, csl],
                         start=True, stop=True)
                t.transpose(ctx.ps_kv[:, 0:128], kh[:, csl], ctx.ident_bf[:])
                t.transpose(ctx.ps_kv[:, 128:256], vh[:, csl],
                            ctx.ident_bf[:]).then_inc(ctx.pe.sem, 1)
            ctx.pe.inc(1)

            @blk.vector
            def _(v, ck=ck):
                v.wait_ge(ctx.pe.sem, ctx.pe.v)
                if ck > 0:
                    v.tensor_copy(cols[:, 0:2], ctx.ps_misc[:, 256:258])
                else:
                    v.tensor_copy(cols[:, 0:1], ctx.zero_col[:, 0:1])
                    v.tensor_copy(cols[:, 1:2], ctx.ps_misc[:, 257:258])
                v.tensor_copy(ogb[:, 0:128], ctx.ps_misc[:, 128:256]).then_inc(ctx.dve.sem, 1)
            ctx.dve.inc(1)

            @blk.vector
            def _(v, lcol=lcol):
                v.scalar_tensor_tensor(darg[:, 0:128], ctx.ps_misc[:, 0:128],
                                       lcol, ctx.mask_up[:],
                                       op0=ALU.subtract, op1=ALU.add)
                v.tensor_scalar(aarg[:, 0:128], ctx.ps_misc[:, 0:128],
                                cols[:, 0:1], None, ALU.subtract)
                v.tensor_tensor(cols[:, 3:4], cols[:, 1:2], lcol, ALU.subtract)
                v.tensor_tensor(cols[:, 4:5], cols[:, 1:2], cols[:, 0:1],
                                ALU.subtract).then_inc(ctx.dve.sem, 1)
            ctx.dve.inc(1)

            @blk.scalar
            def _(s, lnigcol=lnigcol):
                s.wait_ge(ctx.dve.sem, ctx.dve.v)
                s.activation(darg[:, 0:128], darg[:, 0:128], AF.Exp,
                             bias=ctx.zero_col[:, 1:2])
                s.activation(aexp[:, 0:128], aarg[:, 0:128], AF.Exp)
                s.activation(cols[:, 5:6], cols[:, 3:4], AF.Exp, bias=lnigcol)
                s.activation(cols[:, 7:8], cols[:, 4:5], AF.Exp).then_inc(ctx.act.sem, 1)
            ctx.act.inc(1)

            @blk.vector
            def _(v, igcol=igcol):
                v.wait_ge(ctx.act.sem, ctx.act.v)
                v.scalar_tensor_tensor(pt_bf[:, 0:128], ctx.ps_st[:, 0:128],
                                       igcol, darg[:, 0:128],
                                       op0=ALU.mult, op1=ALU.mult)
                v.tensor_scalar(kt_bf[:, 0:128], ctx.ps_kv[:, 0:128],
                                cols[:, 5:6], None, ALU.mult)
                v.tensor_copy(vt_bf[:, 0:128], ctx.ps_kv[:, 128:256]).then_inc(ctx.dve.sem, 1)
            ctx.dve.inc(1)

            @blk.tensor
            def _(t, csl=csl):
                t.wait_ge(ctx.dve.sem, ctx.dve.v)
                t.matmul(ctx.ps_y3[:, 0:128], vt_bf[:, 0:128], pt_bf[:, 0:128],
                         start=True, stop=True)
                t.matmul(ctx.ps_y3[:, 128:256], m_bf[:], qh[:, csl],
                         start=True, stop=True)
                t.matmul(ctx.ps_y3[:, 256:384], kt_bf[:, 0:128], vt_bf[:, 0:128],
                         start=True, stop=True).then_inc(ctx.pe.sem, 1)
            ctx.pe.inc(1)

            @blk.vector
            def _(v, head=head, c0=c0):
                v.wait_ge(ctx.pe.sem, ctx.pe.v)
                v.tensor_tensor(scr128[:, 0:128], ctx.ps_y3[:, 128:256],
                                aexp[:, 0:128], ALU.mult)
                v.tensor_tensor(scr128[:, 0:128], scr128[:, 0:128],
                                ctx.ps_y3[:, 0:128], ALU.add)
                v.tensor_tensor(yall[:, head * T + c0: head * T + c0 + 128],
                                scr128[:, 0:128], ogb[:, 0:128], ALU.mult)
                v.tensor_scalar(scr128[:, 0:128], m_f32[:], cols[:, 7:8],
                                None, ALU.mult)
                v.tensor_tensor(m_f32[:], scr128[:, 0:128],
                                ctx.ps_y3[:, 256:384], ALU.add)
                v.tensor_copy(m_bf[:], m_f32[:]).then_inc(ctx.dve.sem, 1)
            ctx.dve.inc(1)

    if getattr(ctx, "debug", False) and b == 0:
        @blk.scalar
        def _(s):
            s.wait_ge(ctx.dve.sem, ctx.dve.v)
            s.dma_start(out=io["dbg_yall"], in_=yall[:]).then_inc(ctx.st.sem, 16)
            s.dma_start(out=io["dbg_qh"], in_=qh[:, :T]).then_inc(ctx.st.sem, 16)
        ctx.st.inc(2)

    attn = h
    bout = ctx.bias_reg["a_bout"]
    wob_free = {}
    for ch in range(NCH):
        @blk.sync
        def _(s, ch=ch):
            if wob_free.get(ch - 1):
                s.wait_ge(ctx.pe.sem, wob_free[ch - 1])
            else:
                s.wait_ge(ctx.dve.sem, ctx.dve.v)
            s.dma_start(
                out=wob[:].rearrange("p (e d) -> p e d", d=128),
                in_=io["woT"].rearrange("(e p) d -> p e d", p=128)[
                    :, :, ch * 128:(ch + 1) * 128]).then_inc(ctx.ld.sem, 16)
        ctx.ld.inc(1)

        for it in range(NT):
            @blk.tensor
            def _(t, ch=ch, it=it):
                t.wait_ge(ctx.ld.sem, ctx.ld.v)
                t.wait_ge(ctx.dve.sem, ctx.dve.v)
                for e in range(NCH):
                    mm = t.matmul(ctx.ps_a[:, 0:512],
                                  wob[:, e * 128:(e + 1) * 128],
                                  yall[:, e * T + it * 512: e * T + (it + 1) * 512],
                                  start=(e == 0), stop=(e == NCH - 1))
                mm.then_inc(ctx.pe.sem, 1)
            ctx.pe.inc(1)

            @blk.vector
            def _(v, ch=ch, it=it):
                v.wait_ge(ctx.pe.sem, ctx.pe.v)
                v.tensor_scalar(attn[:, ch * T + it * 512: ch * T + (it + 1) * 512],
                                ctx.ps_a[:, 0:512],
                                ctx.bias_pool[:, bout + ch:bout + ch + 1], None,
                                ALU.add).then_inc(ctx.dve.sem, 1)
            ctx.dve.inc(1)

    if getattr(ctx, "debug", False) and b == 0:
        @blk.scalar
        def _(s):
            s.wait_ge(ctx.dve.sem, ctx.dve.v)
            s.dma_start(out=io["dbg_attn"], in_=attn[:]).then_inc(ctx.st.sem, 16)
        ctx.st.inc(1)

    for it in range(NT):
        tsl = slice(it * 512, (it + 1) * 512)
        emit_tile_stats(ctx,
                        lambda ch, it=it: attn[:, ch * T + it * 512: ch * T + (it + 1) * 512],
                        sqb, is_f32=False)
        emit_tile_rows(ctx, 1e-5)
        emit_tile_bcast(ctx, [(ctx.row_a, mu_bt), (ctx.row_c, rho_bt)])

        @blk.sync
        def _(s, tsl=tsl):
            s.wait_ge(ctx.dve.sem, ctx.dve.v)
            for ch in range(NCH):
                s.dma_start(out=x_it[:, ch * 512:(ch + 1) * 512],
                            in_=io[src][b, ch * 128:(ch + 1) * 128, tsl]
                            ).then_inc(ctx.ld.sem, 16)
        ctx.ld.inc(NCH)

        for ch in range(NCH):
            @blk.vector
            def _(v, ch=ch, it=it):
                if ch == 0:
                    v.wait_ge(ctx.ld.sem, ctx.ld.v)
                v.tensor_tensor(scr_t[:, 0:512],
                                attn[:, ch * T + it * 512: ch * T + (it + 1) * 512],
                                mu_bt[:, 0:512], ALU.subtract)
                v.tensor_tensor(scr_t[:, 0:512], scr_t[:, 0:512], rho_bt[:, 0:512], ALU.mult)
                v.tensor_tensor(x_it[:, ch * 512:(ch + 1) * 512],
                                x_it[:, ch * 512:(ch + 1) * 512],
                                scr_t[:, 0:512], ALU.add).then_inc(ctx.dve.sem, 1)
            ctx.dve.inc(1)

        @blk.scalar
        def _(s, tsl=tsl):
            s.wait_ge(ctx.dve.sem, ctx.dve.v)
            for ch in range(NCH):
                s.dma_start(out=io[dst][b, ch * 128:(ch + 1) * 128, tsl],
                            in_=x_it[:, ch * 512:(ch + 1) * 512]).then_inc(ctx.st.sem, 16)
        ctx.st.inc(NCH)

    barrier(ctx)
    stack.close()


def emit_phase_b(ctx, io, b, src="x_cur", dst="x_cur"):
    nc, blk, T = ctx.nc, ctx.block, ctx.T
    NT = T // 512
    NTC = T // 128
    EPSF = 1.1920929e-07
    stack = contextlib.ExitStack()

    def sb(name, shape, dt):
        return stack.enter_context(nc.sbuf_tensor(f"b{b}_{name}", shape, dt))

    acc = sb("acc", [128, NCH * T], F32)
    u_bf = sb("u_bf", [128, NCH * T], BF16)
    act = sb("act", [128, 8 * T], BF16)
    w1b = sb("w1b", [128, 2 * 1024], BF16)
    w2q = sb("w2q", [128, 8 * 1024], BF16)
    gbc_t = sb("gbc_t", [128, 512], F32)
    gate_em = sb("gate_em", [8, T], F32)
    scores = sb("scores", [128, NTC * 8], F32)
    gv_tm = sb("gv_tm", [128, NTC * 8], F32)
    x_it = sb("x_it", [128, NCH * 512], F32)
    sqf = sb("sqf", [128, 1024], F32)
    ctx.row_a = sb("row_a", [1, 512], F32)
    ctx.row_b = sb("row_b", [1, 512], F32)
    ctx.row_c = sb("row_c", [1, 512], F32)
    ctx.row_d = sb("row_d", [1, 512], F32)
    mu_bt = sb("mu_bt", [128, 512], F32)
    rho_bt = sb("rho_bt", [128, 512], F32)
    rho2_bt = sb("rho2_bt", [128, 512], F32)
    scr_t = sb("scr_t", [128, 512], F32)
    col_t = sb("col_t", [128, 4 * (T // 128)], F32)

    for it in range(NT):
        tsl = slice(it * 512, (it + 1) * 512)

        @blk.sync
        def _(s, tsl=tsl):
            s.wait_ge(ctx.dve.sem, ctx.dve.v)
            for ch in range(NCH):
                s.dma_start(out=x_it[:, ch * 512:(ch + 1) * 512],
                            in_=io[src][b, ch * 128:(ch + 1) * 128, tsl]
                            ).then_inc(ctx.ld.sem, 16)
        ctx.ld.inc(NCH)

        @blk.scalar
        def _(s):
            s.wait_ge(ctx.ld.sem, ctx.ld.v)
            s.sem_inc(ctx.act.sem, 1)
        ctx.act.inc(1)

        emit_tile_stats(ctx, lambda ch: x_it[:, ch * 512:(ch + 1) * 512],
                        sqf, is_f32=True)

        @blk.vector
        def _(v):
            v.wait_ge(ctx.pe.sem, ctx.pe.v)
            v.tensor_scalar(ctx.row_a[0:1, :], ctx.ps_b[0:1, 0:512], 1.0 / D, None, ALU.mult)
            v.tensor_scalar(ctx.row_b[0:1, :], ctx.ps_c[0:1, 0:512], 1.0 / D, None, ALU.mult)
            v.tensor_tensor(ctx.row_d[0:1, :], ctx.row_a[0:1, :], ctx.row_a[0:1, :], ALU.mult)
            v.tensor_tensor(ctx.row_b[0:1, :], ctx.row_b[0:1, :], ctx.row_d[0:1, :], ALU.subtract)
            v.tensor_scalar(ctx.row_c[0:1, :], ctx.row_b[0:1, :], 1e-5, None, ALU.add)
            v.reciprocal(ctx.row_d[0:1, :], ctx.row_c[0:1, :])
            v.tensor_tensor(ctx.row_d[0:1, :], ctx.row_b[0:1, :], ctx.row_d[0:1, :], ALU.mult)
            v.tensor_scalar(ctx.row_d[0:1, :], ctx.row_d[0:1, :], EPSF, None,
                            ALU.add).then_inc(ctx.dve.sem, 1)
        ctx.dve.inc(1)

        @blk.scalar
        def _(s):
            s.wait_ge(ctx.dve.sem, ctx.dve.v)
            s.activation(ctx.row_c[0:1, :], ctx.row_c[0:1, :], AF.Sqrt)
            s.activation(ctx.row_d[0:1, :], ctx.row_d[0:1, :], AF.Sqrt).then_inc(ctx.act.sem, 1)
        ctx.act.inc(1)

        @blk.vector
        def _(v):
            v.wait_ge(ctx.act.sem, ctx.act.v)
            v.reciprocal(ctx.row_c[0:1, :], ctx.row_c[0:1, :])
            v.reciprocal(ctx.row_d[0:1, :], ctx.row_d[0:1, :]).then_inc(ctx.dve.sem, 1)
        ctx.dve.inc(1)

        emit_tile_bcast(ctx, [(ctx.row_a, mu_bt), (ctx.row_c, rho_bt)])
        emit_tile_bcast(ctx, [(ctx.row_d, rho2_bt)])

        for ch in range(NCH):
            @blk.vector
            def _(v, ch=ch, it=it):
                v.tensor_tensor(scr_t[:, 0:512], x_it[:, ch * 512:(ch + 1) * 512],
                                mu_bt[:, 0:512], ALU.subtract)
                v.tensor_tensor(acc[:, ch * T + it * 512: ch * T + (it + 1) * 512],
                                scr_t[:, 0:512], rho_bt[:, 0:512], ALU.mult)
                v.tensor_tensor(u_bf[:, ch * T + it * 512: ch * T + (it + 1) * 512],
                                acc[:, ch * T + it * 512: ch * T + (it + 1) * 512],
                                rho2_bt[:, 0:512], ALU.mult).then_inc(ctx.dve.sem, 1)
            ctx.dve.inc(1)

    for tc in range(NTC):
        @blk.tensor
        def _(t, tc=tc):
            t.wait_ge(ctx.dve.sem, ctx.dve.v)
            t.wait_ge(ctx.act.sem, ctx.act.v)
            for ch in range(NCH):
                mm = t.matmul(ctx.ps_misc[0:128, 0:8],
                              acc[:, ch * T + tc * 128: ch * T + (tc + 1) * 128],
                              ctx.cent_sb[:, ch * 8:(ch + 1) * 8],
                              start=(ch == 0), stop=(ch == NCH - 1))
            mm.then_inc(ctx.pe.sem, 1)
        ctx.pe.inc(1)

        @blk.scalar
        def _(s, tc=tc):
            s.wait_ge(ctx.pe.sem, ctx.pe.v)
            s.activation(scores[:, tc * 8:(tc + 1) * 8], ctx.ps_misc[0:128, 0:8],
                         AF.Sigmoid).then_inc(ctx.act.sem, 1)
        ctx.act.inc(1)

    ct = col_t

    @blk.vector
    def _(v):
        v.wait_ge(ctx.act.sem, ctx.act.v)
        for tc in range(NTC):
            cc = v.tensor_reduce(ct[:, tc * 4:tc * 4 + 1],
                                 scores[:, tc * 8:(tc + 1) * 8], axis=AX.X, op=ALU.max)
            cc2 = v.tensor_reduce(ct[:, tc * 4 + 2:tc * 4 + 3],
                                  scores[:, tc * 8:(tc + 1) * 8], axis=AX.X, op=ALU.add)
            if tc == NTC - 1:
                cc2.then_inc(ctx.dve.sem, 1)
    ctx.dve.inc(1)

    @blk.vector
    def _(v):
        for tc in range(NTC):
            cc = v.tensor_scalar(gv_tm[:, tc * 8:(tc + 1) * 8],
                                 scores[:, tc * 8:(tc + 1) * 8],
                                 ct[:, tc * 4:tc * 4 + 1], 1e30,
                                 ALU.is_equal, op1=ALU.mult)
            if tc == NTC - 1:
                cc.then_inc(ctx.dve.sem, 1)
    ctx.dve.inc(1)

    @blk.vector
    def _(v):
        for tc in range(NTC):
            cc = v.tensor_tensor(gv_tm[:, tc * 8:(tc + 1) * 8],
                                 scores[:, tc * 8:(tc + 1) * 8],
                                 gv_tm[:, tc * 8:(tc + 1) * 8], ALU.subtract)
            if tc == NTC - 1:
                cc.then_inc(ctx.dve.sem, 1)
    ctx.dve.inc(1)

    @blk.vector
    def _(v):
        for tc in range(NTC):
            cc = v.tensor_reduce(ct[:, tc * 4 + 1:tc * 4 + 2],
                                 gv_tm[:, tc * 8:(tc + 1) * 8], axis=AX.X, op=ALU.max)
            cc2 = v.reciprocal(ct[:, tc * 4 + 3:tc * 4 + 4],
                               ct[:, tc * 4 + 2:tc * 4 + 3])
            if tc == NTC - 1:
                cc2.then_inc(ctx.dve.sem, 1)
    ctx.dve.inc(1)

    @blk.vector
    def _(v):
        for tc in range(NTC):
            cc = v.tensor_scalar(gv_tm[:, tc * 8:(tc + 1) * 8],
                                 scores[:, tc * 8:(tc + 1) * 8],
                                 ct[:, tc * 4 + 1:tc * 4 + 2], None, ALU.is_ge)
            if tc == NTC - 1:
                cc.then_inc(ctx.dve.sem, 1)
    ctx.dve.inc(1)

    @blk.vector
    def _(v):
        for tc in range(NTC):
            cc = v.tensor_tensor(gv_tm[:, tc * 8:(tc + 1) * 8],
                                 scores[:, tc * 8:(tc + 1) * 8],
                                 gv_tm[:, tc * 8:(tc + 1) * 8], ALU.mult)
            if tc == NTC - 1:
                cc.then_inc(ctx.dve.sem, 1)
    ctx.dve.inc(1)

    @blk.vector
    def _(v):
        for tc in range(NTC):
            cc = v.tensor_scalar(gv_tm[:, tc * 8:(tc + 1) * 8],
                                 gv_tm[:, tc * 8:(tc + 1) * 8],
                                 ct[:, tc * 4 + 3:tc * 4 + 4], None, ALU.mult)
            if tc == NTC - 1:
                cc.then_inc(ctx.dve.sem, 1)
    ctx.dve.inc(1)

    for tc in range(NTC):
        bank_pos = tc % 4

        @blk.tensor
        def _(t, tc=tc, bank_pos=bank_pos):
            t.wait_ge(ctx.dve.sem, ctx.dve.v)
            t.transpose(ctx.ps_b[0:8, bank_pos * 128:(bank_pos + 1) * 128],
                        gv_tm[:, tc * 8:(tc + 1) * 8],
                        ctx.ident_f32[:]).then_inc(ctx.pe.sem, 1)
        ctx.pe.inc(1)

        if bank_pos == 3 or tc == NTC - 1:
            n_in = bank_pos + 1
            c0 = tc - n_in + 1

            @blk.vector
            def _(v, n_in=n_in, c0=c0):
                v.wait_ge(ctx.pe.sem, ctx.pe.v)
                v.tensor_copy(gate_em[0:8, c0 * 128:(c0 + n_in) * 128],
                              ctx.ps_b[0:8, 0:n_in * 128]).then_inc(ctx.dve.sem, 1)
            ctx.dve.inc(1)

    if getattr(ctx, "debug", False) and b == 0:
        @blk.scalar
        def _(s):
            s.wait_ge(ctx.dve.sem, ctx.dve.v)
            for ch in range(NCH):
                s.dma_start(out=io["dbg_u"][0, ch * 128:(ch + 1) * 128, :],
                            in_=acc[:, ch * T:(ch + 1) * T]).then_inc(ctx.st.sem, 16)
            s.dma_start(out=io["dbg_gv"], in_=gv_tm[:, 0:(T // 128) * 8]).then_inc(ctx.st.sem, 16)
            s.dma_start(out=io["dbg_sc"], in_=scores[:, 0:(T // 128) * 8]).then_inc(ctx.st.sem, 16)
            s.dma_start(out=io["dbg_gem"], in_=gate_em[0:8, :T]).then_inc(ctx.st.sem, 16)
        ctx.st.inc(NCH + 3)

        @blk.gpsimd
        def _(g):
            g.wait_ge(ctx.dve.sem, ctx.dve.v)
            g.dma_start(out=io["dbg_ubf"].rearrange("(c p) t -> p c t", p=128),
                        in_=u_bf[:].rearrange("p (c t) -> p c t", t=T)).then_inc(ctx.lw.sem, 16)
        ctx.lw.inc(1)

    ctx.w2_evac_hist = {}
    ctx.w2_g2 = 0
    NQ = 4
    HQ = MOE_H // NQ
    QC = HQ // 128
    for e in range(10):
        routed = e >= 2
        b1c = ctx.bias_reg[f"m_b1_r{e - 2}" if routed else f"m_b1_s{e}"]
        b2c = ctx.bias_reg[f"m_b2_r{e - 2}" if routed else f"m_b2_s{e}"]
        for q in range(NQ):
            h0 = q * HQ
            @blk.sync
            def _(s, e=e, h0=h0):
                s.wait_ge(ctx.pe.sem, ctx.pe.v)
                s.dma_start(out=w2q[:].rearrange("p (c d) -> p c d", d=1024),
                            in_=io["w2bf"][e].rearrange("(c p) d -> p c d", p=128)[
                                :, h0 // 128:(h0 + HQ) // 128, :]
                            ).then_inc(ctx.ld.sem, 16)
            ctx.ld.inc(1)

            gelu_hist = {}
            qdve = ctx.dve.v
            for hc in range(QC):
                slot = hc % 2
                wsl = w1b[:, slot * 1024:(slot + 1) * 1024]

                @blk.sync
                def _(s, e=e, h0=h0, hc=hc, wsl=wsl):
                    s.wait_ge(ctx.pe.sem, max(ctx.pe.v - NT, 0))
                    s.dma_start(
                        out=wsl.rearrange("p (c h) -> p c h", h=128),
                        in_=io["w1bf"][e].rearrange("(c p) h -> p c h", p=128)[
                            :, :, h0 + hc * 128: h0 + (hc + 1) * 128]
                        ).then_inc(ctx.ld.sem, 16)
                ctx.ld.inc(1)

                for it in range(NT):
                    g = hc * NT + it
                    psx = (ctx.ps_a, ctx.ps_b, ctx.ps_c, ctx.ps_misc)[g % 4]
                    bank_free = gelu_hist.get(g - 4, 0)

                    @blk.tensor
                    def _(t, wsl=wsl, psx=psx, bank_free=bank_free, qdve=qdve):
                        t.wait_ge(ctx.ld.sem, ctx.ld.v)
                        t.wait_ge(ctx.dve.sem, qdve)
                        if bank_free:
                            t.wait_ge(ctx.act.sem, bank_free)
                        for ch in range(NCH):
                            mm = t.matmul(psx[:, 0:512],
                                          wsl[:, ch * 128:(ch + 1) * 128],
                                          u_bf[:, ch * T + it * 512: ch * T + (it + 1) * 512],
                                          start=(ch == 0), stop=(ch == NCH - 1))
                        mm.then_inc(ctx.pe.sem, 1)
                    ctx.pe.inc(1)

                    @blk.scalar
                    def _(s, hc=hc, it=it, b1c=b1c, q=q, psx=psx):
                        s.wait_ge(ctx.pe.sem, ctx.pe.v)
                        gcol = b1c + q * QC + hc
                        s.activation(act[:, hc * T + it * 512: hc * T + (it + 1) * 512],
                                     psx[:, 0:512], AF.Gelu,
                                     bias=ctx.bias_pool[:, gcol:gcol + 1]
                                     ).then_inc(ctx.act.sem, 1)
                    ctx.act.inc(1)
                    gelu_hist[g] = ctx.act.v

            for it in range(NT):
                if routed:
                    @blk.tensor
                    def _(t, e=e, it=it):
                        t.wait_ge(ctx.dve.sem, ctx.dve.v)
                        t.matmul(ctx.ps_y3[:, 0:128],
                                 ctx.sel8[:, (e - 2) * 128:(e - 1) * 128],
                                 gate_em[0:8, it * 512:it * 512 + 128],
                                 start=True, stop=True)
                        t.matmul(ctx.ps_y3[:, 128:256],
                                 ctx.sel8[:, (e - 2) * 128:(e - 1) * 128],
                                 gate_em[0:8, it * 512 + 128:it * 512 + 256],
                                 start=True, stop=True).then_inc(ctx.pe.sem, 1)
                        t.matmul(ctx.ps_st[:, 0:128],
                                 ctx.sel8[:, (e - 2) * 128:(e - 1) * 128],
                                 gate_em[0:8, it * 512 + 256:it * 512 + 384],
                                 start=True, stop=True)
                        t.matmul(ctx.ps_st[:, 0:128] if False else ctx.ps_y3[:, 256:384],
                                 ctx.sel8[:, (e - 2) * 128:(e - 1) * 128],
                                 gate_em[0:8, it * 512 + 384:it * 512 + 512],
                                 start=True, stop=True).then_inc(ctx.pe.sem, 1)
                    ctx.pe.inc(2)

                    @blk.vector
                    def _(v):
                        v.wait_ge(ctx.pe.sem, ctx.pe.v)
                        v.tensor_copy(gbc_t[:, 0:128], ctx.ps_y3[:, 0:128])
                        v.tensor_copy(gbc_t[:, 128:256], ctx.ps_y3[:, 128:256])
                        v.tensor_copy(gbc_t[:, 256:384], ctx.ps_st[:, 0:128])
                        v.tensor_copy(gbc_t[:, 384:512],
                                      ctx.ps_y3[:, 256:384]).then_inc(ctx.dve.sem, 1)
                    ctx.dve.inc(1)

                for dc in range(NCH):
                    g2 = ctx.w2_g2
                    ctx.w2_g2 += 1
                    psx = (ctx.ps_a, ctx.ps_b, ctx.ps_c, ctx.ps_misc)[g2 % 4]
                    bank_free2 = ctx.w2_evac_hist.get(g2 - 4, 0)

                    @blk.tensor
                    def _(t, dc=dc, it=it, psx=psx, bank_free2=bank_free2):
                        t.wait_ge(ctx.ld.sem, ctx.ld.v)
                        if bank_free2:
                            t.wait_ge(ctx.dve.sem, bank_free2)
                        t.wait_ge(ctx.act.sem, ctx.act.v)
                        for hc in range(QC):
                            mm = t.matmul(psx[:, 0:512],
                                          w2q[:, hc * 1024 + dc * 128: hc * 1024 + (dc + 1) * 128],
                                          act[:, hc * T + it * 512: hc * T + (it + 1) * 512],
                                          start=(hc == 0), stop=(hc == QC - 1))
                        mm.then_inc(ctx.pe.sem, 1)
                    ctx.pe.inc(1)

                    @blk.vector
                    def _(v, dc=dc, it=it, psx=psx, q=q, b2c=b2c, routed=routed):
                        v.wait_ge(ctx.pe.sem, ctx.pe.v)
                        dst_sl = acc[:, dc * T + it * 512: dc * T + (it + 1) * 512]
                        if routed:
                            if q == 0:
                                v.scalar_tensor_tensor(scr_t[:, 0:512], psx[:, 0:512],
                                                       ctx.bias_pool[:, b2c + dc:b2c + dc + 1],
                                                       gbc_t[:, 0:512],
                                                       op0=ALU.add, op1=ALU.mult)
                            else:
                                v.tensor_tensor(scr_t[:, 0:512], psx[:, 0:512],
                                                gbc_t[:, 0:512], ALU.mult)
                            v.tensor_tensor(dst_sl, dst_sl, scr_t[:, 0:512],
                                            ALU.add).then_inc(ctx.dve.sem, 1)
                        else:
                            if q == 0:
                                v.tensor_scalar(scr_t[:, 0:512], psx[:, 0:512],
                                                ctx.bias_pool[:, b2c + dc:b2c + dc + 1],
                                                None, ALU.add)
                                v.tensor_tensor(dst_sl, dst_sl, scr_t[:, 0:512],
                                                ALU.add).then_inc(ctx.dve.sem, 1)
                            else:
                                v.tensor_tensor(dst_sl, dst_sl, psx[:, 0:512],
                                                ALU.add).then_inc(ctx.dve.sem, 1)
                    ctx.dve.inc(1)
                    ctx.w2_evac_hist[g2] = ctx.dve.v

    if getattr(ctx, "debug", False) and b == 0:
        @blk.scalar
        def _(s):
            s.wait_ge(ctx.dve.sem, ctx.dve.v)
            for ch in range(NCH):
                s.dma_start(out=io["dbg_acc"][0, ch * 128:(ch + 1) * 128, :],
                            in_=acc[:, ch * T:(ch + 1) * T]).then_inc(ctx.st.sem, 16)
        ctx.st.inc(NCH)

    for it in range(NT):
        tsl = slice(it * 512, (it + 1) * 512)
        emit_tile_stats(ctx, lambda ch, it=it: acc[:, ch * T + it * 512: ch * T + (it + 1) * 512],
                        sqf, is_f32=True)
        emit_tile_rows(ctx, 1e-5)
        emit_tile_bcast(ctx, [(ctx.row_a, mu_bt), (ctx.row_c, rho_bt)])

        @blk.sync
        def _(s, tsl=tsl):
            s.wait_ge(ctx.dve.sem, ctx.dve.v)
            for ch in range(NCH):
                s.dma_start(out=x_it[:, ch * 512:(ch + 1) * 512],
                            in_=io[src][b, ch * 128:(ch + 1) * 128, tsl]
                            ).then_inc(ctx.ld.sem, 16)
        ctx.ld.inc(NCH)

        for ch in range(NCH):
            @blk.vector
            def _(v, ch=ch, it=it):
                if ch == 0:
                    v.wait_ge(ctx.ld.sem, ctx.ld.v)
                v.tensor_tensor(scr_t[:, 0:512],
                                acc[:, ch * T + it * 512: ch * T + (it + 1) * 512],
                                mu_bt[:, 0:512], ALU.subtract)
                v.tensor_tensor(scr_t[:, 0:512], scr_t[:, 0:512], rho_bt[:, 0:512], ALU.mult)
                v.tensor_tensor(x_it[:, ch * 512:(ch + 1) * 512],
                                x_it[:, ch * 512:(ch + 1) * 512],
                                scr_t[:, 0:512], ALU.add).then_inc(ctx.dve.sem, 1)
            ctx.dve.inc(1)

        @blk.scalar
        def _(s, tsl=tsl):
            s.wait_ge(ctx.dve.sem, ctx.dve.v)
            for ch in range(NCH):
                s.dma_start(out=io[dst][b, ch * 128:(ch + 1) * 128, tsl],
                            in_=x_it[:, ch * 512:(ch + 1) * 512]).then_inc(ctx.st.sem, 16)
        ctx.st.inc(NCH)

    barrier(ctx)
    stack.close()


def emit_phase_c(ctx, io, b, src="x_cur", dst="x_cur"):
    nc, blk, T = ctx.nc, ctx.block, ctx.T
    NT = T // 512
    NTC = T // 128
    NQT = T // 128
    stack = contextlib.ExitStack()

    def sb(name, shape, dt):
        return stack.enter_context(nc.sbuf_tensor(f"c{b}_{name}", shape, dt))

    xn = sb("xn", [128, NCH * T], BF16)
    yall = sb("yall", [128, NCH * T], BF16)
    x_it = sb("x_it", [128, NCH * 512], F32)
    q2 = sb("q2", [128, T], BF16)
    k2 = sb("k2", [128, T], BF16)
    vtm = sb("vtm", [128, T], BF16)
    p_sb = sb("p_sb", [128, T], BF16)
    s_sb = sb("s_sb", [128, 2 * T], F32)
    ptb = sb("ptb", [128, 16 * 128], BF16)
    wpair = sb("wpair", [128, 3 * 1024], BF16)
    wob = sb("wob", [128, 1024], BF16)
    ge_row = sb("ge_row", [1, T], F32)
    gcol = sb("gcol", [128, NTC], F32)
    mzcols = sb("mzcols", [128, 16], F32)
    sqf = sb("sqf", [128, 1024], F32)
    sqb = sb("sqb", [128, 1024], BF16)
    ctx.row_a = sb("row_a", [1, 512], F32)
    ctx.row_b = sb("row_b", [1, 512], F32)
    ctx.row_c = sb("row_c", [1, 512], F32)
    ctx.row_d = sb("row_d", [1, 512], F32)
    mu_bt = sb("mu_bt", [128, 512], F32)
    rho_bt = sb("rho_bt", [128, 512], F32)
    scr_t = sb("scr_t", [128, 512], F32)

    lng = ctx.bias_reg["c_lng"]
    lnb = ctx.bias_reg["c_lnb"]
    bqkv = ctx.bias_reg["c_bqkv"]
    bent = ctx.bias_reg["c_bent"]
    bout = ctx.bias_reg["c_bout"]

    for it in range(NT):
        tsl = slice(it * 512, (it + 1) * 512)

        @blk.sync
        def _(s, tsl=tsl):
            s.wait_ge(ctx.dve.sem, ctx.dve.v)
            for ch in range(NCH):
                s.dma_start(out=x_it[:, ch * 512:(ch + 1) * 512],
                            in_=io[src][b, ch * 128:(ch + 1) * 128, tsl]
                            ).then_inc(ctx.ld.sem, 16)
        ctx.ld.inc(NCH)

        @blk.scalar
        def _(s):
            s.wait_ge(ctx.ld.sem, ctx.ld.v)
            s.sem_inc(ctx.act.sem, 1)
        ctx.act.inc(1)

        emit_tile_stats(ctx, lambda ch: x_it[:, ch * 512:(ch + 1) * 512],
                        sqf, is_f32=True)

        @blk.vector
        def _(v):
            v.wait_ge(ctx.pe.sem, ctx.pe.v)
            v.tensor_scalar(ctx.row_a[0:1, :], ctx.ps_b[0:1, 0:512], 1.0 / D, None, ALU.mult)
            v.tensor_scalar(ctx.row_b[0:1, :], ctx.ps_c[0:1, 0:512], 1.0 / D, None, ALU.mult)
            v.tensor_tensor(ctx.row_d[0:1, :], ctx.row_a[0:1, :], ctx.row_a[0:1, :], ALU.mult)
            v.tensor_tensor(ctx.row_b[0:1, :], ctx.row_b[0:1, :], ctx.row_d[0:1, :], ALU.subtract)
            v.tensor_scalar(ctx.row_c[0:1, :], ctx.row_b[0:1, :], 1e-5, None, ALU.add)
            v.reciprocal(ctx.row_d[0:1, :], ctx.row_c[0:1, :])
            v.tensor_tensor(ctx.row_d[0:1, :], ctx.row_b[0:1, :], ctx.row_d[0:1, :], ALU.mult)
            v.tensor_scalar(ctx.row_d[0:1, :], ctx.row_d[0:1, :], 1e-6, None, ALU.add)
            v.tensor_tensor(ctx.row_c[0:1, :], ctx.row_c[0:1, :], ctx.row_d[0:1, :],
                            ALU.mult).then_inc(ctx.dve.sem, 1)
        ctx.dve.inc(1)

        @blk.scalar
        def _(s):
            s.wait_ge(ctx.dve.sem, ctx.dve.v)
            s.activation(ctx.row_c[0:1, :], ctx.row_c[0:1, :], AF.Sqrt).then_inc(ctx.act.sem, 1)
        ctx.act.inc(1)

        @blk.vector
        def _(v):
            v.wait_ge(ctx.act.sem, ctx.act.v)
            v.reciprocal(ctx.row_c[0:1, :], ctx.row_c[0:1, :]).then_inc(ctx.dve.sem, 1)
        ctx.dve.inc(1)

        emit_tile_bcast(ctx, [(ctx.row_a, mu_bt), (ctx.row_c, rho_bt)])

        for ch in range(NCH):
            @blk.vector
            def _(v, ch=ch, it=it):
                v.tensor_tensor(scr_t[:, 0:512], x_it[:, ch * 512:(ch + 1) * 512],
                                mu_bt[:, 0:512], ALU.subtract)
                v.scalar_tensor_tensor(scr_t[:, 0:512], scr_t[:, 0:512],
                                       ctx.bias_pool[:, lng + ch:lng + ch + 1],
                                       rho_bt[:, 0:512], op0=ALU.mult, op1=ALU.mult)
                v.tensor_scalar(xn[:, ch * T + it * 512: ch * T + (it + 1) * 512],
                                scr_t[:, 0:512],
                                ctx.bias_pool[:, lnb + ch:lnb + ch + 1], None,
                                ALU.add).then_inc(ctx.dve.sem, 1)
            ctx.dve.inc(1)

    for it in range(NT):
        @blk.tensor
        def _(t, it=it):
            t.wait_ge(ctx.dve.sem, ctx.dve.v)
            t.wait_ge(ctx.act.sem, ctx.act.v)
            for ch in range(NCH):
                mm = t.matmul(ctx.ps_misc[0:1, 0:512],
                              ctx.went_bf[:, ch:ch + 1],
                              xn[:, ch * T + it * 512: ch * T + (it + 1) * 512],
                              start=(ch == 0), stop=(ch == NCH - 1))
            mm.then_inc(ctx.pe.sem, 1)
        ctx.pe.inc(1)

        @blk.scalar
        def _(s, it=it):
            s.wait_ge(ctx.pe.sem, ctx.pe.v)
            s.activation(ge_row[0:1, it * 512:(it + 1) * 512], ctx.ps_misc[0:1, 0:512],
                         AF.Sigmoid,
                         bias=ctx.bias_pool[0:1, bent:bent + 1]).then_inc(ctx.act.sem, 1)
        ctx.act.inc(1)

    @blk.vector
    def _(v):
        v.wait_ge(ctx.act.sem, ctx.act.v)
        v.tensor_scalar(ge_row[0:1, :T], ge_row[0:1, :T], 2.0, 0.1, ALU.min,
                        op1=ALU.max).then_inc(ctx.dve.sem, 1)
    ctx.dve.inc(1)

    for tc in range(NTC):
        bank_pos = tc % 32

        @blk.tensor
        def _(t, tc=tc, bank_pos=bank_pos):
            t.wait_ge(ctx.dve.sem, ctx.dve.v)
            t.transpose(ctx.ps_b[:, bank_pos:bank_pos + 1],
                        ge_row[0:1, tc * 128:(tc + 1) * 128],
                        ctx.ident_f32[0:1, 0:1]).then_inc(ctx.pe.sem, 1)
        ctx.pe.inc(1)

        if bank_pos == 31 or tc == NTC - 1:
            n_in = bank_pos + 1
            c0 = tc - n_in + 1

            @blk.vector
            def _(v, n_in=n_in, c0=c0):
                v.wait_ge(ctx.pe.sem, ctx.pe.v)
                v.tensor_copy(gcol[:, c0:c0 + n_in],
                              ctx.ps_b[:, 0:n_in]).then_inc(ctx.dve.sem, 1)
            ctx.dve.inc(1)

    wpair_free = {}
    for hp in range(C_NH // 2):
        @blk.sync
        def _(s, hp=hp):
            if wpair_free.get(hp - 1):
                s.wait_ge(ctx.pe.sem, wpair_free[hp - 1])
            else:
                s.wait_ge(ctx.dve.sem, ctx.dve.v)
            for wi in range(3):
                s.dma_start(
                    out=wpair[:, wi * 1024:(wi + 1) * 1024].rearrange(
                        "p (c e) -> p c e", e=128),
                    in_=io["qkvT"].rearrange("(c p) e -> p c e", p=128)[
                        :, :, wi * 1024 + hp * 128: wi * 1024 + (hp + 1) * 128]
                    ).then_inc(ctx.ld.sem, 16)
        ctx.ld.inc(3)

        for wi, dstbuf in ((0, q2), (1, k2)):
            for it in range(NT):
                @blk.tensor
                def _(t, wi=wi, it=it):
                    t.wait_ge(ctx.ld.sem, ctx.ld.v)
                    t.wait_ge(ctx.dve.sem, ctx.dve.v)
                    for ch in range(NCH):
                        mm = t.matmul(ctx.ps_a[:, 0:512],
                                      wpair[:, wi * 1024 + ch * 128: wi * 1024 + (ch + 1) * 128],
                                      xn[:, ch * T + it * 512: ch * T + (it + 1) * 512],
                                      start=(ch == 0), stop=(ch == NCH - 1))
                    mm.then_inc(ctx.pe.sem, 1)
                ctx.pe.inc(1)

                @blk.vector
                def _(v, wi=wi, dstbuf=dstbuf, it=it, hp=hp):
                    v.wait_ge(ctx.pe.sem, ctx.pe.v)
                    bcol = bqkv + (wi * 1024 + hp * 128) // 128
                    if wi == 0:
                        v.tensor_scalar(dstbuf[:, it * 512:(it + 1) * 512],
                                        ctx.ps_a[:, 0:512],
                                        ctx.bias_pool[:, bcol:bcol + 1], C_SCALE,
                                        ALU.add, op1=ALU.mult).then_inc(ctx.dve.sem, 1)
                    else:
                        v.tensor_scalar(dstbuf[:, it * 512:(it + 1) * 512],
                                        ctx.ps_a[:, 0:512],
                                        ctx.bias_pool[:, bcol:bcol + 1], None,
                                        ALU.add).then_inc(ctx.dve.sem, 1)
                ctx.dve.inc(1)

        for it in range(NT):
            @blk.tensor
            def _(t, it=it):
                t.wait_ge(ctx.ld.sem, ctx.ld.v)
                t.wait_ge(ctx.dve.sem, ctx.dve.v)
                for ch in range(NCH):
                    mm = t.matmul(ctx.ps_a[:, 0:512],
                                  wpair[:, 2 * 1024 + ch * 128: 2 * 1024 + (ch + 1) * 128],
                                  xn[:, ch * T + it * 512: ch * T + (it + 1) * 512],
                                  start=(ch == 0), stop=(ch == NCH - 1))
                mm.then_inc(ctx.pe.sem, 1)
            ctx.pe.inc(1)

            @blk.vector
            def _(v, it=it, hp=hp):
                v.wait_ge(ctx.pe.sem, ctx.pe.v)
                bcol = bqkv + (2 * 1024 + hp * 128) // 128
                v.tensor_scalar(p_sb[:, it * 512:(it + 1) * 512],
                                ctx.ps_a[:, 0:512],
                                ctx.bias_pool[:, bcol:bcol + 1], None,
                                ALU.add).then_inc(ctx.dve.sem, 1)
            ctx.dve.inc(1)
        wpair_free[hp] = ctx.pe.v

        for tc in range(NTC):
            slot = tc % 2

            @blk.tensor
            def _(t, tc=tc, slot=slot):
                t.wait_ge(ctx.dve.sem, ctx.dve.v)
                t.transpose(ctx.ps_kv[:, slot * 128:(slot + 1) * 128],
                            p_sb[:, tc * 128:(tc + 1) * 128],
                            ctx.ident_bf[:]).then_inc(ctx.pe.sem, 1)
            ctx.pe.inc(1)

            @blk.vector
            def _(v, tc=tc, slot=slot):
                v.wait_ge(ctx.pe.sem, ctx.pe.v)
                v.tensor_scalar(vtm[:, tc * 128:(tc + 1) * 128],
                                ctx.ps_kv[:, slot * 128:(slot + 1) * 128],
                                gcol[:, tc:tc + 1], None,
                                ALU.mult).then_inc(ctx.dve.sem, 1)
            ctx.dve.inc(1)

        for hh in range(2):
            base = 64 * hh
            s_hist, copy_hist, exp_hist = {}, {}, {}
            qk_ready = ctx.dve.v
            sbanks = (ctx.ps_a, ctx.ps_b, ctx.ps_c, ctx.ps_misc)

            def emit_S(g, base=base):
                qi = g
                nk = (qi + 1) * 128
                nb = (nk + 511) // 512

                @blk.tensor
                def _(t, qi=qi, nk=nk, nb=nb, g=g):
                    t.wait_ge(ctx.dve.sem, max(copy_hist.get(g - 1, 0), qk_ready))
                    last = None
                    for bk in range(nb):
                        kw = min(512, nk - bk * 512)
                        last = t.matmul(sbanks[bk][:, 0:kw],
                                        q2[base:base + 64, qi * 128:(qi + 1) * 128],
                                        k2[base:base + 64, bk * 512:bk * 512 + kw],
                                        start=True, stop=True)
                    last.then_inc(ctx.pe.sem, 1)
                ctx.pe.inc(1)
                s_hist[g] = ctx.pe.v

            def emit_copy(g, base=base):
                qi = g
                nk = (qi + 1) * 128
                nb = (nk + 511) // 512
                par = g % 2
                sp = par * T
                mz = par * 8

                @blk.vector
                def _(vv, nk=nk, nb=nb, g=g, sp=sp, mz=mz):
                    vv.wait_ge(ctx.pe.sem, s_hist[g])
                    for bk in range(nb):
                        kw = min(512, nk - bk * 512)
                        vv.tensor_copy(s_sb[:, sp + bk * 512: sp + bk * 512 + kw],
                                       sbanks[bk][:, 0:kw])
                    vv.tensor_tensor(s_sb[:, sp + nk - 128: sp + nk],
                                     s_sb[:, sp + nk - 128: sp + nk],
                                     ctx.mask_low[:], ALU.add)
                    last = None
                    for bk in range(nb):
                        kw = min(512, nk - bk * 512)
                        last = vv.tensor_reduce(mzcols[:, mz + bk:mz + bk + 1],
                                                s_sb[:, sp + bk * 512: sp + bk * 512 + kw],
                                                axis=AX.X, op=ALU.max)
                    last.then_inc(ctx.dve.sem, 1)
                ctx.dve.inc(1)
                copy_hist[g] = ctx.dve.v

            def emit_stages(g, base=base):
                qi = g
                nk = (qi + 1) * 128
                nb = (nk + 511) // 512
                par = g % 2
                sp = par * T
                mz = par * 8
                nkb = qi + 1

                @blk.vector
                def _(vv, nb=nb, mz=mz):
                    vv.tensor_reduce(mzcols[:, mz + 4:mz + 5], mzcols[:, mz:mz + nb],
                                     axis=AX.X, op=ALU.max,
                                     negate=True).then_inc(ctx.dve.sem, 1)
                ctx.dve.inc(1)

                @blk.scalar
                def _(s, nk=nk, sp=sp, mz=mz):
                    s.wait_ge(ctx.dve.sem, ctx.dve.v)
                    s.activation(p_sb[:, 0:nk], s_sb[:, sp:sp + nk], AF.Exp,
                                 bias=mzcols[:, mz + 4:mz + 5],
                                 accum_out=mzcols[:, mz + 5:mz + 6]).then_inc(ctx.act.sem, 1)
                ctx.act.inc(1)
                exp_hist[g] = ctx.act.v

                @blk.vector
                def _(vv, mz=mz):
                    vv.wait_ge(ctx.act.sem, ctx.act.v)
                    vv.reciprocal(mzcols[:, mz + 6:mz + 7],
                                  mzcols[:, mz + 5:mz + 6]).then_inc(ctx.dve.sem, 1)
                ctx.dve.inc(1)

                @blk.vector
                def _(vv, nk=nk, mz=mz):
                    vv.tensor_scalar(p_sb[:, 0:nk], p_sb[:, 0:nk],
                                     mzcols[:, mz + 6:mz + 7], None,
                                     ALU.mult).then_inc(ctx.dve.sem, 1)
                ctx.dve.inc(1)

                for kb2 in range(0, nkb, 2):
                    npair = min(2, nkb - kb2)

                    @blk.tensor
                    def _(t, kb2=kb2, npair=npair):
                        t.wait_ge(ctx.dve.sem, ctx.dve.v)
                        last = None
                        for j in range(npair):
                            last = t.transpose(
                                ctx.ps_kv[:, j * 128:(j + 1) * 128],
                                p_sb[:, (kb2 + j) * 128:(kb2 + j + 1) * 128],
                                ctx.ident_bf[:])
                        last.then_inc(ctx.pe.sem, 1)
                    ctx.pe.inc(1)

                    @blk.vector
                    def _(vv, kb2=kb2, npair=npair):
                        vv.wait_ge(ctx.pe.sem, ctx.pe.v)
                        vv.tensor_copy(ptb[:, kb2 * 128:(kb2 + npair) * 128],
                                       ctx.ps_kv[:, 0:npair * 128]).then_inc(ctx.dve.sem, 1)
                    ctx.dve.inc(1)

                @blk.tensor
                def _(t, qi=qi, nkb=nkb, par=par):
                    t.wait_ge(ctx.dve.sem, ctx.dve.v)
                    for kb in range(nkb):
                        mm = t.matmul(ctx.ps_y3[0:64, par * 128:(par + 1) * 128],
                                      vtm[:, kb * 128 + base: kb * 128 + base + 64],
                                      ptb[:, kb * 128:(kb + 1) * 128],
                                      start=(kb == 0), stop=(kb == nkb - 1))
                    mm.then_inc(ctx.pe.sem, 1)
                ctx.pe.inc(1)

                @blk.vector
                def _(vv, qi=qi, par=par):
                    vv.wait_ge(ctx.pe.sem, ctx.pe.v)
                    vv.tensor_copy(yall[base:base + 64,
                                        hp * T + qi * 128: hp * T + (qi + 1) * 128],
                                   ctx.ps_y3[0:64, par * 128:(par + 1) * 128]
                                   ).then_inc(ctx.dve.sem, 1)
                ctx.dve.inc(1)

            for g in range(NQT + 1):
                if g >= 1:
                    emit_copy(g - 1)
                if g < NQT:
                    emit_S(g)
                if g >= 1:
                    emit_stages(g - 1)

    attn = xn
    wob_free = {}
    for ch in range(NCH):
        @blk.sync
        def _(s, ch=ch):
            if wob_free.get(ch - 1):
                s.wait_ge(ctx.pe.sem, wob_free[ch - 1])
            else:
                s.wait_ge(ctx.dve.sem, ctx.dve.v)
            s.dma_start(
                out=wob[:].rearrange("p (e d) -> p e d", d=128),
                in_=io["woutT"].rearrange("(e p) d -> p e d", p=128)[
                    :, :, ch * 128:(ch + 1) * 128]).then_inc(ctx.ld.sem, 16)
        ctx.ld.inc(1)

        for it in range(NT):
            @blk.tensor
            def _(t, ch=ch, it=it):
                t.wait_ge(ctx.ld.sem, ctx.ld.v)
                t.wait_ge(ctx.dve.sem, ctx.dve.v)
                for e in range(NCH):
                    mm = t.matmul(ctx.ps_a[:, 0:512],
                                  wob[:, e * 128:(e + 1) * 128],
                                  yall[:, e * T + it * 512: e * T + (it + 1) * 512],
                                  start=(e == 0), stop=(e == NCH - 1))
                mm.then_inc(ctx.pe.sem, 1)
            ctx.pe.inc(1)

            @blk.vector
            def _(v, ch=ch, it=it):
                v.wait_ge(ctx.pe.sem, ctx.pe.v)
                v.tensor_scalar(attn[:, ch * T + it * 512: ch * T + (it + 1) * 512],
                                ctx.ps_a[:, 0:512],
                                ctx.bias_pool[:, bout + ch:bout + ch + 1], None,
                                ALU.add).then_inc(ctx.dve.sem, 1)
            ctx.dve.inc(1)
        wob_free[ch] = ctx.pe.v

    for it in range(NT):
        tsl = slice(it * 512, (it + 1) * 512)
        emit_tile_stats(ctx,
                        lambda ch, it=it: attn[:, ch * T + it * 512: ch * T + (it + 1) * 512],
                        sqb, is_f32=False)
        emit_tile_rows(ctx, 1e-5)
        emit_tile_bcast(ctx, [(ctx.row_a, mu_bt), (ctx.row_c, rho_bt)])

        @blk.sync
        def _(s, tsl=tsl):
            s.wait_ge(ctx.dve.sem, ctx.dve.v)
            for ch in range(NCH):
                s.dma_start(out=x_it[:, ch * 512:(ch + 1) * 512],
                            in_=io[src][b, ch * 128:(ch + 1) * 128, tsl]
                            ).then_inc(ctx.ld.sem, 16)
        ctx.ld.inc(NCH)

        for ch in range(NCH):
            @blk.vector
            def _(v, ch=ch, it=it):
                if ch == 0:
                    v.wait_ge(ctx.ld.sem, ctx.ld.v)
                v.tensor_tensor(scr_t[:, 0:512],
                                attn[:, ch * T + it * 512: ch * T + (it + 1) * 512],
                                mu_bt[:, 0:512], ALU.subtract)
                v.tensor_tensor(scr_t[:, 0:512], scr_t[:, 0:512], rho_bt[:, 0:512], ALU.mult)
                v.tensor_tensor(x_it[:, ch * 512:(ch + 1) * 512],
                                x_it[:, ch * 512:(ch + 1) * 512],
                                scr_t[:, 0:512], ALU.add).then_inc(ctx.dve.sem, 1)
            ctx.dve.inc(1)

        @blk.scalar
        def _(s, tsl=tsl):
            s.wait_ge(ctx.dve.sem, ctx.dve.v)
            for ch in range(NCH):
                s.dma_start(out=io[dst][b, ch * 128:(ch + 1) * 128, tsl],
                            in_=x_it[:, ch * 512:(ch + 1) * 512]).then_inc(ctx.st.sem, 16)
        ctx.st.inc(NCH)

    barrier(ctx)
    stack.close()


def emit_phase_d(ctx, io, b, src="x_cur", dst="x_cur"):
    nc, blk, T = ctx.nc, ctx.block, ctx.T
    NT = T // 512
    NTC = T // 128
    VW = T + 8
    NW = T // F_WINDOW
    stack = contextlib.ExitStack()

    def sb(name, shape, dt):
        return stack.enter_context(nc.sbuf_tensor(f"d{b}_{name}", shape, dt))

    v = sb("v", [128, NCH * VW], F32)
    h_bf = sb("h_bf", [128, NCH * T], BF16)
    act = sb("act", [128, 11 * 512], BF16)
    yp = sb("yp", [128, T], BF16)
    qh = sb("qh", [64, T], BF16)
    kh = sb("kh", [64, T], BF16)
    vfm = sb("vfm", [128, T], BF16)
    vtm = sb("vtm", [128, T], BF16)
    p_sb = sb("p_sb", [128, 1024], BF16)
    ptb = sb("ptb", [128, 5 * 128], BF16)
    wpair = sb("wpair", [128, 3 * 1024], BF16)
    wo_sl = sb("wo_sl", [128, 1024], BF16)
    w1a = sb("w1a", [128, 2 * 1024], BF16)
    w1g = sb("w1g", [128, 2 * 1024], BF16)
    w2sl = sb("w2sl", [128, 2 * 11 * 128], BF16)
    gelu_sb = sb("gelu_sb", [128, 512], BF16)
    xq_c = sb("xq_c", [64, T], BF16)
    xq_s = sb("xq_s", [64, T], BF16)
    xk_c = sb("xk_c", [64, T], BF16)
    xk_s = sb("xk_s", [64, T], BF16)
    rs1 = sb("rs1", [64, 512], BF16)
    rs2 = sb("rs2", [64, 512], BF16)
    mzcols = sb("mzcols", [128, 16], F32)
    sqf = sb("sqf", [128, 1024], F32)
    sqb = sb("sqb", [64, 512], BF16)
    xtile = sb("xtile", [128, 2 * 512], F32)
    ctx.row_a = sb("row_a", [1, 512], F32)
    ctx.row_b = sb("row_b", [1, 512], F32)
    ctx.row_c = sb("row_c", [1, 512], F32)
    ctx.row_d = sb("row_d", [1, 512], F32)
    mu_bt = sb("mu_bt", [128, 512], F32)
    rho_bt = sb("rho_bt", [128, 512], F32)
    scr_t = sb("scr_t", [128, 512], F32)

    def vsl(ch, t0, n, shift=False):
        off = 0 if (shift and ch >= NCH // 2) else 1
        return v[:, ch * VW + off + t0: ch * VW + off + t0 + n]

    ctx.d_wpair_free = {}
    ctx.d_wosl_free = {}
    ctx.d_wpair_last = 0

    @blk.gpsimd
    def _(g):
        g.memset(v[:], 0.0).then_inc(ctx.pool.sem, 1)
    ctx.pool.inc(1)

    @blk.sync
    def _(s):
        s.wait_ge(ctx.pool.sem, ctx.pool.v)
        s.dma_start(out=xq_c[:, :T], in_=io["xpos_qcos"]).then_inc(ctx.ld.sem, 16)
        s.dma_start(out=xq_s[:, :T], in_=io["xpos_qsin"]).then_inc(ctx.ld.sem, 16)
        s.dma_start(out=xk_c[:, :T], in_=io["xpos_kcos"]).then_inc(ctx.ld.sem, 16)
        s.dma_start(out=xk_s[:, :T], in_=io["xpos_ksin"]).then_inc(ctx.ld.sem, 16)
        for ch in range(NCH):
            for it in range(NT):
                s.dma_start(out=vsl(ch, it * 512, 512),
                            in_=io[src][b, ch * 128:(ch + 1) * 128,
                                        it * 512:(it + 1) * 512]).then_inc(ctx.ld.sem, 16)
    ctx.ld.inc(4 + NCH * NT)

    @blk.scalar
    def _(s):
        s.wait_ge(ctx.ld.sem, ctx.ld.v)
        s.sem_inc(ctx.act.sem, 1)
    ctx.act.inc(1)

    for it in range(NT):
        emit_tile_stats(ctx, lambda ch, it=it: vsl(ch, it * 512, 512),
                        sqf, is_f32=True)
        emit_tile_rows(ctx, 1e-5)
        emit_tile_bcast(ctx, [(ctx.row_a, mu_bt), (ctx.row_c, rho_bt)])
        for ch in range(NCH):
            @blk.vector
            def _(vv, ch=ch, it=it):
                vv.tensor_tensor(scr_t[:, 0:512], vsl(ch, it * 512, 512),
                                 mu_bt[:, 0:512], ALU.subtract)
                vv.tensor_tensor(vsl(ch, it * 512, 512), scr_t[:, 0:512],
                                 rho_bt[:, 0:512], ALU.mult).then_inc(ctx.dve.sem, 1)
            ctx.dve.inc(1)

    for l in range(2):
        qsc = ctx.bias_reg[f"f_qs{l}"]
        ksc = ctx.bias_reg[f"f_ks{l}"]
        lngc = ctx.bias_reg[f"f_lng{l}"]
        lnbc = ctx.bias_reg[f"f_lnb{l}"]
        b1ac = ctx.bias_reg[f"f_b1a{l}"]
        b1gc = ctx.bias_reg[f"f_b1g{l}"]
        b2c = ctx.bias_reg[f"f_b2{l}"]

        for ch in range(NCH):
            @blk.vector
            def _(vv, ch=ch):
                last = None
                for it in range(NT):
                    last = vv.tensor_copy(
                        h_bf[:, ch * T + it * 512: ch * T + (it + 1) * 512],
                        vsl(ch, it * 512, 512, shift=True))
                last.then_inc(ctx.dve.sem, 1)
            ctx.dve.inc(1)

        for hp in range(F_HEADS // 2):
            @blk.sync
            def _(s, hp=hp, l=l):
                key = (l, hp - 1)
                if ctx.d_wpair_free.get(key):
                    s.wait_ge(ctx.pe.sem, ctx.d_wpair_free[key])
                else:
                    s.wait_ge(ctx.dve.sem, ctx.dve.v)
                for wi in range(3):
                    s.dma_start(
                        out=wpair[:, wi * 1024:(wi + 1) * 1024].rearrange(
                            "p (c e) -> p c e", e=128),
                        in_=io["fqkv"][l].rearrange("(c p) e -> p c e", p=128)[
                            :, :, wi * 512 + hp * 128: wi * 512 + (hp + 1) * 128]
                        ).then_inc(ctx.ld.sem, 16)
            ctx.ld.inc(3)

            for it in range(NT):
                @blk.tensor
                def _(t, it=it):
                    t.wait_ge(ctx.ld.sem, ctx.ld.v)
                    t.wait_ge(ctx.dve.sem, ctx.dve.v)
                    for ch in range(NCH):
                        mm = t.matmul(ctx.ps_a[:, 0:512],
                                      wpair[:, 2 * 1024 + ch * 128: 2 * 1024 + (ch + 1) * 128],
                                      h_bf[:, ch * T + it * 512: ch * T + (it + 1) * 512],
                                      start=(ch == 0), stop=(ch == NCH - 1))
                    mm.then_inc(ctx.pe.sem, 1)
                ctx.pe.inc(1)

                @blk.vector
                def _(vv, it=it):
                    vv.wait_ge(ctx.pe.sem, ctx.pe.v)
                    vv.tensor_copy(vfm[:, it * 512:(it + 1) * 512],
                                   ctx.ps_a[:, 0:512]).then_inc(ctx.dve.sem, 1)
                ctx.dve.inc(1)
            for tc in range(NTC):
                slot = tc % 2

                @blk.tensor
                def _(t, tc=tc, slot=slot):
                    t.wait_ge(ctx.dve.sem, ctx.dve.v)
                    t.transpose(ctx.ps_kv[:, slot * 128:(slot + 1) * 128],
                                vfm[:, tc * 128:(tc + 1) * 128],
                                ctx.ident_bf[:]).then_inc(ctx.pe.sem, 1)
                ctx.pe.inc(1)

                @blk.vector
                def _(vv, tc=tc, slot=slot):
                    vv.wait_ge(ctx.pe.sem, ctx.pe.v)
                    vv.tensor_copy(vtm[:, tc * 128:(tc + 1) * 128],
                                   ctx.ps_kv[:, slot * 128:(slot + 1) * 128]
                                   ).then_inc(ctx.dve.sem, 1)
                ctx.dve.inc(1)

            for hh in range(2):
                base = 64 * hh
                for (wi, fin, tabc, tabs, scol) in (
                        (0, qh, xq_c, xq_s, qsc), (1, kh, xk_c, xk_s, ksc)):
                    apply_hist = {}
                    for it in range(NT):
                        psx = (ctx.ps_a, ctx.ps_misc)[it % 2]
                        bank_free = apply_hist.get(it - 2, 0)

                        @blk.tensor
                        def _(t, wi=wi, it=it, base=base, psx=psx, bank_free=bank_free):
                            if bank_free:
                                t.wait_ge(ctx.dve.sem, bank_free)
                            t.wait_ge(ctx.dve.sem, max(ctx.dve.v - 2, 0))
                            for ch in range(NCH):
                                mm = t.matmul(
                                    psx[0:64, 0:512],
                                    wpair[:, wi * 1024 + ch * 128 + base:
                                          wi * 1024 + ch * 128 + base + 64],
                                    h_bf[:, ch * T + it * 512: ch * T + (it + 1) * 512],
                                    start=(ch == 0), stop=(ch == NCH - 1))
                            mm.then_inc(ctx.pe.sem, 1)
                        ctx.pe.inc(1)

                        @blk.scalar
                        def _(s, it=it, psx=psx):
                            s.wait_ge(ctx.pe.sem, ctx.pe.v)
                            s.activation(sqb[0:64, 0:512], psx[0:64, 0:512],
                                         AF.Square).then_inc(ctx.act.sem, 1)
                        ctx.act.inc(1)

                        @blk.tensor
                        def _(t, it=it):
                            t.wait_ge(ctx.act.sem, ctx.act.v)
                            t.wait_ge(ctx.dve.sem, ctx.dve.v)
                            t.matmul(ctx.ps_b[0:1, 0:512], ctx.ones_bf[0:64, 0:1],
                                     sqb[0:64, 0:512], start=True,
                                     stop=True).then_inc(ctx.pe.sem, 1)
                        ctx.pe.inc(1)

                        @blk.vector
                        def _(vv):
                            vv.wait_ge(ctx.pe.sem, ctx.pe.v)
                            vv.tensor_copy(ctx.row_a[0:1, :], ctx.ps_b[0:1, 0:512]
                                           ).then_inc(ctx.dve.sem, 1)
                        ctx.dve.inc(1)

                        @blk.scalar
                        def _(s):
                            s.wait_ge(ctx.dve.sem, ctx.dve.v)
                            s.activation(ctx.row_a[0:1, :], ctx.row_a[0:1, :],
                                         AF.Sqrt).then_inc(ctx.act.sem, 1)
                        ctx.act.inc(1)

                        @blk.vector
                        def _(vv):
                            vv.wait_ge(ctx.act.sem, ctx.act.v)
                            vv.tensor_scalar(ctx.row_a[0:1, :], ctx.row_a[0:1, :],
                                             1e-12, None, ALU.max)
                            vv.reciprocal(ctx.row_a[0:1, :],
                                          ctx.row_a[0:1, :]).then_inc(ctx.dve.sem, 1)
                        ctx.dve.inc(1)

                        @blk.tensor
                        def _(t):
                            t.wait_ge(ctx.dve.sem, ctx.dve.v)
                            t.matmul(ctx.ps_b[0:64, 0:512], ctx.ones_f32[0:1, 0:64],
                                     ctx.row_a[0:1, :], start=True,
                                     stop=True).then_inc(ctx.pe.sem, 1)
                        ctx.pe.inc(1)

                        @blk.vector
                        def _(vv):
                            vv.wait_ge(ctx.pe.sem, ctx.pe.v)
                            vv.tensor_copy(mu_bt[0:64, 0:512],
                                           ctx.ps_b[0:64, 0:512]).then_inc(ctx.dve.sem, 1)
                        ctx.dve.inc(1)

                        @blk.vector
                        def _(vv, fin=fin, scol=scol, it=it, psx=psx):
                            vv.scalar_tensor_tensor(fin[:, it * 512:(it + 1) * 512],
                                                    psx[0:64, 0:512],
                                                    ctx.bias_pool[0:64, scol:scol + 1],
                                                    mu_bt[0:64, 0:512],
                                                    op0=ALU.mult,
                                                    op1=ALU.mult).then_inc(ctx.dve.sem, 1)
                        ctx.dve.inc(1)
                        apply_hist[it] = ctx.dve.v

                    if wi == 1:
                        ctx.d_wpair_last = ctx.pe.v

                    for it in range(NT):
                        ts2 = slice(it * 512, (it + 1) * 512)

                        @blk.vector
                        def _(vv, fin=fin, tabc=tabc, tabs=tabs, ts2=ts2):
                            vv.tensor_tensor(rs2[0:32, :], fin[32:64, ts2],
                                             tabs[32:64, ts2], ALU.mult)
                            vv.tensor_tensor(rs1[32:64, :], fin[0:32, ts2],
                                             tabs[0:32, ts2], ALU.mult)
                            vv.tensor_tensor(fin[0:32, ts2], fin[0:32, ts2],
                                             tabc[0:32, ts2], ALU.mult)
                            vv.tensor_tensor(fin[32:64, ts2], fin[32:64, ts2],
                                             tabc[32:64, ts2], ALU.mult)
                            vv.tensor_tensor(fin[0:32, ts2], fin[0:32, ts2],
                                             rs2[0:32, :], ALU.subtract)
                            vv.tensor_tensor(fin[32:64, ts2], fin[32:64, ts2],
                                             rs1[32:64, :], ALU.add).then_inc(ctx.dve.sem, 1)
                        ctx.dve.inc(1)

                tiles = []
                for w in range(NW):
                    for qti in range(4):
                        tiles.append((w, qti))
                ntl = len(tiles)
                s_hist, mm_hist, exp_hist = {}, {}, {}
                qk_ready = ctx.dve.v

                def emit_S(g):
                    w, qti = tiles[g]
                    qt = w * 4 + qti
                    if w == 0:
                        k0, nkb = 0, qti + 1
                    else:
                        k0, nkb = (w - 1) * 512 + qti * 128, 5
                    nk = nkb * 128
                    nb = (nk + 511) // 512
                    par = g % 2
                    sbk = ((ctx.ps_a, ctx.ps_b), (ctx.ps_c, ctx.ps_misc))[par]

                    @blk.tensor
                    def _(t, qt=qt, k0=k0, nk=nk, nb=nb, sbk=sbk, g=g):
                        t.wait_ge(ctx.dve.sem, max(mm_hist.get(g - 2, 0), qk_ready))
                        if exp_hist.get(g - 2, 0):
                            t.wait_ge(ctx.act.sem, exp_hist[g - 2])
                        last = None
                        for bk in range(nb):
                            kw = min(512, nk - bk * 512)
                            last = t.matmul(sbk[bk][:, 0:kw],
                                            qh[:, qt * 128:(qt + 1) * 128],
                                            kh[:, k0 + bk * 512: k0 + bk * 512 + kw],
                                            start=True, stop=True)
                        last.then_inc(ctx.pe.sem, 1)
                    ctx.pe.inc(1)
                    s_hist[g] = ctx.pe.v

                def emit_stages(g, base):
                    w, qti = tiles[g]
                    qt = w * 4 + qti
                    if w == 0:
                        k0, nkb = 0, qti + 1
                    else:
                        k0, nkb = (w - 1) * 512 + qti * 128, 5
                    nk = nkb * 128
                    nb = (nk + 511) // 512
                    par = g % 2
                    sbk = ((ctx.ps_a, ctx.ps_b), (ctx.ps_c, ctx.ps_misc))[par]
                    mz = par * 8
                    pp = 0
                    pt0 = 0

                    @blk.vector
                    def _(vv, nkb=nkb, nb=nb, nk=nk, sbk=sbk, g=g, w=w, mz=mz):
                        vv.wait_ge(ctx.pe.sem, s_hist[g])
                        off = ((nkb - 1) * 128) % 512
                        vv.tensor_tensor(sbk[(nkb - 1) * 128 // 512][:, off:off + 128],
                                         sbk[(nkb - 1) * 128 // 512][:, off:off + 128],
                                         ctx.mask_low[:], ALU.add)
                        if w > 0:
                            vv.tensor_tensor(sbk[0][:, 0:128], sbk[0][:, 0:128],
                                             ctx.mask_up[:], ALU.add)
                        last = None
                        for bk in range(nb):
                            kw = min(512, nk - bk * 512)
                            last = vv.tensor_reduce(mzcols[:, mz + bk:mz + bk + 1],
                                                    sbk[bk][:, 0:kw], axis=AX.X,
                                                    op=ALU.max)
                        last.then_inc(ctx.dve.sem, 1)
                    ctx.dve.inc(1)
                    mm_hist[g] = ctx.dve.v

                    @blk.vector
                    def _(vv, nb=nb, mz=mz):
                        vv.tensor_reduce(mzcols[:, mz + 2:mz + 3], mzcols[:, mz:mz + nb],
                                         axis=AX.X, op=ALU.max,
                                         negate=True).then_inc(ctx.dve.sem, 1)
                    ctx.dve.inc(1)

                    @blk.scalar
                    def _(s, nb=nb, nk=nk, sbk=sbk, mz=mz, pp=pp):
                        s.wait_ge(ctx.dve.sem, ctx.dve.v)
                        last = None
                        for bk in range(nb):
                            kw = min(512, nk - bk * 512)
                            last = s.activation(p_sb[:, pp + bk * 512: pp + bk * 512 + kw],
                                                sbk[bk][:, 0:kw], AF.Exp,
                                                bias=mzcols[:, mz + 2:mz + 3],
                                                accum_out=mzcols[:, mz + 3 + bk:mz + 4 + bk])
                        last.then_inc(ctx.act.sem, 1)
                    ctx.act.inc(1)
                    exp_hist[g] = ctx.act.v

                    @blk.vector
                    def _(vv, nb=nb, mz=mz):
                        vv.wait_ge(ctx.act.sem, ctx.act.v)
                        vv.tensor_reduce(mzcols[:, mz + 5:mz + 6], mzcols[:, mz + 3:mz + 3 + nb],
                                         axis=AX.X, op=ALU.add).then_inc(ctx.dve.sem, 1)
                    ctx.dve.inc(1)

                    @blk.vector
                    def _(vv, mz=mz):
                        vv.reciprocal(mzcols[:, mz + 6:mz + 7],
                                      mzcols[:, mz + 5:mz + 6]).then_inc(ctx.dve.sem, 1)
                    ctx.dve.inc(1)

                    @blk.vector
                    def _(vv, nk=nk, mz=mz, pp=pp):
                        vv.tensor_scalar(p_sb[:, pp:pp + nk], p_sb[:, pp:pp + nk],
                                         mzcols[:, mz + 6:mz + 7], None,
                                         ALU.mult).then_inc(ctx.dve.sem, 1)
                    ctx.dve.inc(1)

                    for kb in range(nkb):
                        @blk.tensor
                        def _(t, kb=kb, par=par, pp=pp):
                            t.wait_ge(ctx.dve.sem, ctx.dve.v)
                            t.transpose(ctx.ps_kv[:, par * 128:(par + 1) * 128],
                                        p_sb[:, pp + kb * 128: pp + (kb + 1) * 128],
                                        ctx.ident_bf[:]).then_inc(ctx.pe.sem, 1)
                        ctx.pe.inc(1)

                        @blk.vector
                        def _(vv, kb=kb, par=par, pt0=pt0):
                            vv.wait_ge(ctx.pe.sem, ctx.pe.v)
                            vv.tensor_copy(ptb[:, pt0 + kb * 128: pt0 + (kb + 1) * 128],
                                           ctx.ps_kv[:, par * 128:(par + 1) * 128]
                                           ).then_inc(ctx.dve.sem, 1)
                        ctx.dve.inc(1)

                    @blk.tensor
                    def _(t, base=base, k0=k0, nkb=nkb, par=par, pt0=pt0):
                        t.wait_ge(ctx.dve.sem, ctx.dve.v)
                        for kb in range(nkb):
                            kbg = k0 // 128 + kb
                            mm = t.matmul(ctx.ps_y3[0:64, par * 128:(par + 1) * 128],
                                          vtm[:, kbg * 128 + base: kbg * 128 + base + 64],
                                          ptb[:, pt0 + kb * 128: pt0 + (kb + 1) * 128],
                                          start=(kb == 0), stop=(kb == nkb - 1))
                        mm.then_inc(ctx.pe.sem, 1)
                    ctx.pe.inc(1)

                    @blk.vector
                    def _(vv, base=base, qt=qt, par=par):
                        vv.wait_ge(ctx.pe.sem, ctx.pe.v)
                        vv.tensor_copy(yp[base:base + 64, qt * 128:(qt + 1) * 128],
                                       ctx.ps_y3[0:64, par * 128:(par + 1) * 128]
                                       ).then_inc(ctx.dve.sem, 1)
                    ctx.dve.inc(1)

                for g in range(ntl + 1):
                    if g < ntl:
                        emit_S(g)
                    if g >= 1:
                        emit_stages(g - 1, base)

            ctx.d_wpair_free[(l, hp)] = ctx.d_wpair_last

            @blk.sync
            def _(s, hp=hp, l=l):
                key = (l, hp - 1)
                if ctx.d_wosl_free.get(key):
                    s.wait_ge(ctx.pe.sem, ctx.d_wosl_free[key])
                else:
                    s.wait_ge(ctx.dve.sem, ctx.dve.v)
                s.dma_start(out=wo_sl[:], in_=io["fwo"][l][hp * 128:(hp + 1) * 128, :]
                            ).then_inc(ctx.ld.sem, 16)
            ctx.ld.inc(1)

            for dc in range(NCH):
                for it in range(NT):
                    psx = (ctx.ps_c, ctx.ps_misc)[(dc * NT + it) % 2]

                    @blk.tensor
                    def _(t, dc=dc, it=it, psx=psx):
                        t.wait_ge(ctx.ld.sem, ctx.ld.v)
                        t.wait_ge(ctx.dve.sem, max(ctx.dve.v - 1, 0))
                        t.matmul(psx[:, 0:512], wo_sl[:, dc * 128:(dc + 1) * 128],
                                 yp[:, it * 512:(it + 1) * 512],
                                 start=True, stop=True).then_inc(ctx.pe.sem, 1)
                    ctx.pe.inc(1)

                    @blk.vector
                    def _(vv, dc=dc, it=it, psx=psx):
                        vv.wait_ge(ctx.pe.sem, ctx.pe.v)
                        vv.tensor_tensor(vsl(dc, it * 512, 512), vsl(dc, it * 512, 512),
                                         psx[:, 0:512], ALU.add).then_inc(ctx.dve.sem, 1)
                    ctx.dve.inc(1)
            ctx.d_wosl_free[(l, hp)] = ctx.pe.v

        for it in range(NT):
            emit_tile_stats(ctx, lambda ch, it=it: vsl(ch, it * 512, 512, shift=True),
                            sqf, is_f32=True)
            emit_tile_rows(ctx, 1e-5)
            emit_tile_bcast(ctx, [(ctx.row_a, mu_bt), (ctx.row_c, rho_bt)])
            for ch in range(NCH):
                @blk.vector
                def _(vv, ch=ch, it=it):
                    vv.tensor_tensor(scr_t[:, 0:512], vsl(ch, it * 512, 512, shift=True),
                                     mu_bt[:, 0:512], ALU.subtract)
                    vv.scalar_tensor_tensor(scr_t[:, 0:512], scr_t[:, 0:512],
                                            ctx.bias_pool[:, lngc + ch:lngc + ch + 1],
                                            rho_bt[:, 0:512], op0=ALU.mult, op1=ALU.mult)
                    vv.tensor_scalar(h_bf[:, ch * T + it * 512: ch * T + (it + 1) * 512],
                                     scr_t[:, 0:512],
                                     ctx.bias_pool[:, lnbc + ch:lnbc + ch + 1], None,
                                     ALU.add).then_inc(ctx.dve.sem, 1)
                ctx.dve.inc(1)

            for jh in range(2):
                for jc in range(11):
                    jcg = jh * 11 + jc
                    cw = 42 if jcg == 21 else 128
                    slot = jc % 2

                    @blk.sync
                    def _(g, l=l, jcg=jcg, cw=cw, slot=slot):
                        g.wait_ge(ctx.pe.sem, max(ctx.pe.v - 1, 0))
                        g.dma_start(
                            out=w1a[:, slot * 1024:slot * 1024 + NCH * cw].rearrange(
                                "p (c o) -> p c o", o=cw),
                            in_=io["fw1"][l].rearrange("(c p) o -> p c o", p=128)[
                                :, :, jcg * 128: jcg * 128 + cw]).then_inc(ctx.ld.sem, 16)
                        g.dma_start(
                            out=w1g[:, slot * 1024:slot * 1024 + NCH * cw].rearrange(
                                "p (c o) -> p c o", o=cw),
                            in_=io["fw1"][l].rearrange("(c p) o -> p c o", p=128)[
                                :, :, 2730 + jcg * 128: 2730 + jcg * 128 + cw]
                            ).then_inc(ctx.ld.sem, 16)
                    ctx.ld.inc(2)

                    @blk.tensor
                    def _(t, it=it, jcg=jcg, cw=cw, slot=slot):
                        t.wait_ge(ctx.ld.sem, ctx.ld.v)
                        t.wait_ge(ctx.dve.sem, ctx.dve.v)
                        t.wait_ge(ctx.act.sem, ctx.act.v)
                        for ch in range(NCH):
                            t.matmul(ctx.ps_a[0:cw, 0:512],
                                     w1a[:, slot * 1024 + ch * cw: slot * 1024 + (ch + 1) * cw],
                                     h_bf[:, ch * T + it * 512: ch * T + (it + 1) * 512],
                                     start=(ch == 0), stop=(ch == NCH - 1))
                            mm = t.matmul(ctx.ps_b[0:cw, 0:512],
                                          w1g[:, slot * 1024 + ch * cw: slot * 1024 + (ch + 1) * cw],
                                          h_bf[:, ch * T + it * 512: ch * T + (it + 1) * 512],
                                          start=(ch == 0), stop=(ch == NCH - 1))
                        mm.then_inc(ctx.pe.sem, 1)
                    ctx.pe.inc(1)

                    @blk.scalar
                    def _(s, jcg=jcg, cw=cw):
                        s.wait_ge(ctx.pe.sem, ctx.pe.v)
                        s.activation(gelu_sb[0:cw, 0:512], ctx.ps_b[0:cw, 0:512],
                                     AF.Gelu,
                                     bias=ctx.bias_pool[0:cw, b1gc + jcg:b1gc + jcg + 1]
                                     ).then_inc(ctx.act.sem, 1)
                    ctx.act.inc(1)

                    @blk.vector
                    def _(vv, jc=jc, jcg=jcg, cw=cw):
                        vv.wait_ge(ctx.act.sem, ctx.act.v)
                        vv.scalar_tensor_tensor(act[0:cw, jc * 512:(jc + 1) * 512],
                                                ctx.ps_a[0:cw, 0:512],
                                                ctx.bias_pool[0:cw, b1ac + jcg:b1ac + jcg + 1],
                                                gelu_sb[0:cw, 0:512],
                                                op0=ALU.add, op1=ALU.mult
                                                ).then_inc(ctx.dve.sem, 1)
                    ctx.dve.inc(1)

                for dc in range(NCH):
                    slot = dc % 2

                    @blk.scalar
                    def _(s, l=l, jh=jh, dc=dc, slot=slot):
                        s.wait_ge(ctx.st.sem, ctx.st.v)
                        s.wait_ge(ctx.pe.sem, max(ctx.pe.v - 1, 0))
                        r0 = jh * 1408
                        nfull = 11 if jh == 0 else 10
                        s.dma_start(
                            out=w2sl[:, slot * 1408: slot * 1408 + nfull * 128].rearrange(
                                "p (c d) -> p c d", d=128),
                            in_=io["fw2"][l][0:2688, :].rearrange(
                                "(x p) d -> p x d", p=128)[
                                :, r0 // 128: r0 // 128 + nfull,
                                dc * 128:(dc + 1) * 128]).then_inc(ctx.st.sem, 16)
                        if jh == 1:
                            s.dma_start(
                                out=w2sl[0:42, slot * 1408 + 10 * 128: slot * 1408 + 11 * 128],
                                in_=io["fw2"][l][2688:2730, dc * 128:(dc + 1) * 128]
                                ).then_inc(ctx.st.sem, 16)
                    ctx.st.inc(2 if jh == 1 else 1)

                    psx = (ctx.ps_c, ctx.ps_misc)[dc % 2]

                    @blk.tensor
                    def _(t, it=it, jh=jh, dc=dc, slot=slot, psx=psx):
                        t.wait_ge(ctx.st.sem, ctx.st.v)
                        t.wait_ge(ctx.dve.sem, max(ctx.dve.v - 1, 0))
                        for jc in range(11):
                            jcg = jh * 11 + jc
                            cw = 42 if jcg == 21 else 128
                            mm = t.matmul(psx[:, 0:512],
                                          w2sl[0:cw, slot * 1408 + jc * 128: slot * 1408 + jc * 128 + 128],
                                          act[0:cw, jc * 512:(jc + 1) * 512],
                                          start=(jc == 0), stop=(jc == 10))
                        mm.then_inc(ctx.pe.sem, 1)
                    ctx.pe.inc(1)

                    @blk.vector
                    def _(vv, it=it, jh=jh, dc=dc, psx=psx):
                        vv.wait_ge(ctx.pe.sem, ctx.pe.v)
                        if jh == 1:
                            vv.tensor_scalar(scr_t[:, 0:512], psx[:, 0:512],
                                             ctx.bias_pool[:, b2c + dc:b2c + dc + 1],
                                             None, ALU.add)
                            vv.tensor_tensor(vsl(dc, it * 512, 512),
                                             vsl(dc, it * 512, 512),
                                             scr_t[:, 0:512], ALU.add).then_inc(ctx.dve.sem, 1)
                        else:
                            vv.tensor_tensor(vsl(dc, it * 512, 512),
                                             vsl(dc, it * 512, 512),
                                             psx[:, 0:512], ALU.add).then_inc(ctx.dve.sem, 1)
                    ctx.dve.inc(1)

    for it in range(NT):
        emit_tile_stats(ctx, lambda ch, it=it: vsl(ch, it * 512, 512),
                        sqf, is_f32=True)
        emit_tile_rows(ctx, 1e-5)
        emit_tile_bcast(ctx, [(ctx.row_a, mu_bt), (ctx.row_c, rho_bt)])

        for ch in range(NCH):
            slot = ch % 2

            @blk.sync
            def _(s, ch=ch, it=it, slot=slot):
                s.wait_ge(ctx.dve.sem, ctx.dve.v)
                s.wait_ge(ctx.st.sem, ctx.st.v)
                s.dma_start(out=xtile[:, slot * 512:(slot + 1) * 512],
                            in_=io[src][b, ch * 128:(ch + 1) * 128,
                                        it * 512:(it + 1) * 512]).then_inc(ctx.ld.sem, 16)
            ctx.ld.inc(1)

            @blk.vector
            def _(vv, ch=ch, it=it, slot=slot):
                vv.wait_ge(ctx.ld.sem, ctx.ld.v)
                vv.tensor_tensor(scr_t[:, 0:512], vsl(ch, it * 512, 512),
                                 mu_bt[:, 0:512], ALU.subtract)
                vv.tensor_tensor(scr_t[:, 0:512], scr_t[:, 0:512], rho_bt[:, 0:512],
                                 ALU.mult)
                vv.tensor_tensor(xtile[:, slot * 512:(slot + 1) * 512],
                                 xtile[:, slot * 512:(slot + 1) * 512],
                                 scr_t[:, 0:512], ALU.add).then_inc(ctx.dve.sem, 1)
            ctx.dve.inc(1)

            @blk.scalar
            def _(s, ch=ch, it=it, slot=slot):
                s.wait_ge(ctx.dve.sem, ctx.dve.v)
                s.dma_start(out=io[dst][b, ch * 128:(ch + 1) * 128,
                                        it * 512:(it + 1) * 512],
                            in_=xtile[:, slot * 512:(slot + 1) * 512]).then_inc(ctx.st.sem, 16)
            ctx.st.inc(1)

    barrier(ctx)
    stack.close()


def emit_copy_out(ctx, io, src="x_cur"):
    blk = ctx.block
    for b in range(BPC):
        @blk.scalar
        def _(s, b=b):
            s.dma_start(out=io["out"][b], in_=io[src][b]).then_inc(ctx.st.sem, 16)
        ctx.st.inc(1)
    barrier(ctx)



PHASE_ORDER = ("a", "b", "c", "d")


def build_program(T=T_FULL, phases=("prep", "a"), batches=(0, 1), debug=False,
                  paranoid=False):
    nc = bass.Bass("TRN2", target_bir_lowering=False)
    io = declare_ios(nc, T)
    ctx = Ctx()
    ctx.nc = nc
    ctx.T = T
    ctx.debug = debug
    ctx.paranoid = paranoid

    stack = contextlib.ExitStack()

    def sb(name, shape, dt):
        return stack.enter_context(nc.sbuf_tensor(name, shape, dt))

    def ps(name, shape, dt=F32):
        return stack.enter_context(nc.psum_tensor(name, shape, dt))

    with stack:
        ctx.ident_bf = sb("ident_bf", [128, 128], BF16)
        ctx.ident_f32 = sb("ident_f32", [128, 128], F32)
        ctx.mask_up = sb("mask_up", [128, 128], F32)
        ctx.mask_low = sb("mask_low", [128, 128], F32)
        ctx.sel8 = sb("sel8", [8, 8 * 128], F32)
        ctx.ones_bf = sb("ones_bf", [128, 128], BF16)
        ctx.ones_f32 = sb("ones_f32", [128, 128], F32)
        ctx.bias_pool = sb("bias_pool", [128, 704], F32)
        ctx.wgA = sb("wgA", [128, 8 * 96], F32)
        ctx.wg_colsum = sb("wg_colsum", [96, 1], F32)
        ctx.went_bf = sb("went_bf", [128, 8], BF16)
        ctx.cent_sb = sb("cent_sb", [128, 8 * 8], F32)
        ctx.zero_col = sb("zero_col", [128, 8], F32)

        ctx.ps_a = ps("ps_a", [128, 512])
        ctx.ps_b = ps("ps_b", [128, 512])
        ctx.ps_c = ps("ps_c", [128, 512])
        ctx.ps_misc = ps("ps_misc", [128, 512])
        ctx.ps_st = ps("ps_st", [128, 128])
        ctx.ps_kv = ps("ps_kv", [128, 256], BF16)
        ctx.ps_y3 = ps("ps_y3", [128, 384])
        ctx.ps_tr = ps("ps_tr", [128, 256], BF16)

        with (
            nc.semaphore("s_ld") as s_ld,
            nc.semaphore("s_lw") as s_lw,
            nc.semaphore("s_st") as s_st,
            nc.semaphore("s_pe") as s_pe,
            nc.semaphore("s_dve") as s_dve,
            nc.semaphore("s_act") as s_act,
            nc.semaphore("s_pool") as s_pool,
            nc.Block() as block,
        ):
            ctx.ld = Ctr(s_ld, 16)
            ctx.lw = Ctr(s_lw, 16)
            ctx.st = Ctr(s_st, 16)
            ctx.pe = Ctr(s_pe)
            ctx.dve = Ctr(s_dve)
            ctx.act = Ctr(s_act)
            ctx.pool = Ctr(s_pool)
            ctx.block = SelfWaitBlock(block, ctx)

            if "prep" in phases:
                emit_prep(ctx, io)
            active = [p for p in PHASE_ORDER if p in phases]
            cur = "x"
            for pi, p in enumerate(active):
                for b in batches:
                    if p == "a":
                        emit_phase_a(ctx, io, b, src=cur, dst="x_cur")
                    elif p == "b":
                        emit_phase_b(ctx, io, b, src=cur, dst="x_cur")
                    elif p == "c":
                        emit_phase_c(ctx, io, b, src=cur, dst="x_cur")
                    elif p == "d":
                        emit_phase_d(ctx, io, b, src=cur, dst="x_cur")
                cur = "x_cur"
            emit_copy_out(ctx, io, src=cur if active else "x")

    return nc, io, ctx


def kernel(**inputs):
    T = T_FULL
    nc, io, ctx = build_program(T=T, phases=("prep", "a", "b", "c", "d"), batches=(0, 1))
    consts = make_host_consts(T)
    x = inputs["L0_attn_x"]
    x_fm = np.ascontiguousarray(np.transpose(x, (0, 2, 1)))
    in_maps = []
    for c in range(NCORES):
        m = {"x": x_fm[c * BPC:(c + 1) * BPC]}
        for name in WEIGHT_NAMES:
            m[name] = np.ascontiguousarray(inputs[name])
        m.update(consts)
        in_maps.append(m)
    res = run_bass_kernel_spmd(nc, in_maps, core_ids=list(range(NCORES)))
    out_fm = np.stack([r["out"] for r in res.results]).reshape(NCORES * BPC, D, T)
    return np.ascontiguousarray(np.transpose(out_fm, (0, 2, 1)))
